# version 02
# baseline (speedup 1.0000x reference)
import math
import numpy as np
import ml_dtypes
import concourse.bass as bass
import concourse.mybir as mybir
from concourse.bass_utils import run_bass_kernel_spmd

F32 = mybir.dt.float32
BF16 = mybir.dt.bfloat16
AF = mybir.ActivationFunctionType
ALU = mybir.AluOpType
AX = mybir.AxisListType.X
BF = ml_dtypes.bfloat16

NCORE = 8
B, S, D = 4, 4096, 2048
H, HD = 16, 128
TPC = 2048
CH = 512
NCH = TPC // CH
NFT = D // 128
GAMMA = 0.96875
FH = 8192
LAT = 1024
INTER = 1024
NE = 32
SHH = 4096
EPS = 1e-5

_prog_cache = {}
STRICT = True


def _mkemit(block, ctrs):
    def wrap(engfn):
        def emit(f):
            def g(e):
                if STRICT:
                    for c in ctrs.values():
                        c.wait(e)
                f(e)
            engfn(g)
        return emit
    return {k: wrap(getattr(block, k)) for k in ["sync", "gpsimd", "tensor", "vector", "scalar"]}


class Ctr:
    def __init__(self, sem):
        self.sem = sem
        self.n = 0

    def inc(self, instr, k=1):
        instr.then_inc(self.sem, k)
        self.n += k

    def wait(self, eng):
        if self.n > 0:
            eng.wait_ge(self.sem, self.n)


def _dft_consts():
    k = np.arange(HD, dtype=np.float64)
    th = 2.0 * np.pi * np.outer(k, k) / HD
    C = np.cos(th)
    Sn = -np.sin(th)
    return (C.astype(np.float32), Sn.astype(np.float32),
            (C / HD).astype(np.float32), (Sn / HD).astype(np.float32))


def _rot_const():
    R = np.zeros((128, 128), dtype=np.float32)
    for j in range(0, 128, 2):
        R[j, j + 1] = -1.0
        R[j + 1, j] = 1.0
    return np.ascontiguousarray(R.T)


def _emit_bf16_const(nc, g_ctr, name, arr_f32, gp):
    d = nc.inline_tensor(np.ascontiguousarray(arr_f32.astype(np.float32)), name=name).ap()
    t = nc.alloc_sbuf_tensor(name + "_sb", list(arr_f32.shape), BF16)
    g_ctr.inc(gp.dma_start(out=t.ap(), in_=d), 16)
    return t


def _ln_stats_chunk(nc, em, bufs, ctrs, c):
    xc, xb2, ones_col, ones_row, st = bufs["xc"], bufs["xb2"], bufs["ones_col"], bufs["ones_row"], bufs["stats"]
    ps_s1, ps_s2, ps_bm, ps_br = bufs["ps_s1"], bufs["ps_s2"], bufs["ps_bm"], bufs["ps_br"]
    LD, MM, VV, SS, GG = ctrs["LD"], ctrs["MM"], ctrs["VV"], ctrs["SS"], ctrs["GG"]
    mean, ex2, tmp, mhf, rstd = (st.ap()[0:1, i * CH:(i + 1) * CH] for i in range(5))
    mh, ml, rh, rl = (bufs["stats_bf"].ap()[0:1, i * CH:(i + 1) * CH] for i in range(4))
    ST = ctrs["ST"]

    def sc1(s):
        LD.wait(s)
        MM.wait(s)
        ST.wait(s)
        SS.inc(s.activation(xb2.ap()[:, :], xc.ap()[:, :], AF.Square))
    em["scalar"](sc1)

    def tn1(t):
        SS.wait(t)
        for ft in range(NFT):
            MM.inc(t.matmul(ps_s2.ap()[0:1, 0:CH], ones_col.ap()[:, 0:1],
                            xb2.ap()[:, ft * CH:(ft + 1) * CH],
                            start=(ft == 0), stop=(ft == NFT - 1)))
    em["tensor"](tn1)

    def vc1(v):
        MM.wait(v)
        VV.inc(v.tensor_scalar(ex2, ps_s2.ap()[0:1, 0:CH], 1.0 / D, None, ALU.mult))
    em["vector"](vc1)

    def vc2(v):
        VV.wait(v)
        MM.wait(v)
        VV.inc(v.tensor_copy(xb2.ap()[:, :], xc.ap()[:, :]))
    em["vector"](vc2)

    def tn2(t):
        VV.wait(t)
        for ft in range(NFT):
            MM.inc(t.matmul(ps_s1.ap()[0:1, 0:CH], ones_col.ap()[:, 0:1],
                            xb2.ap()[:, ft * CH:(ft + 1) * CH],
                            start=(ft == 0), stop=(ft == NFT - 1)))
    em["tensor"](tn2)

    def vst(v):
        MM.wait(v)
        VV.inc(v.tensor_scalar(mean, ps_s1.ap()[0:1, 0:CH], 1.0 / D, None, ALU.mult))
        VV.wait(v)
        VV.inc(v.tensor_tensor(tmp, mean, mean, ALU.mult))
        VV.wait(v)
        VV.inc(v.tensor_tensor(ex2, ex2, tmp, ALU.subtract))
        VV.wait(v)
        VV.inc(v.tensor_scalar(ex2, ex2, EPS, None, ALU.add))
    em["vector"](vst)

    def sst(s):
        VV.wait(s)
        SS.inc(s.activation(tmp, ex2, AF.Ln))
        SS.wait(s)
        SS.inc(s.activation(rstd, tmp, AF.Exp, scale=-0.5))
    em["scalar"](sst)

    def vhl(v):
        SS.wait(v)
        VV.inc(v.tensor_copy(mh, mean))
        VV.wait(v)
        VV.inc(v.tensor_copy(mhf, mh))
        VV.wait(v)
        VV.inc(v.tensor_tensor(tmp, mean, mhf, ALU.subtract))
        VV.wait(v)
        VV.inc(v.tensor_copy(ml, tmp))
        VV.inc(v.tensor_copy(rh, rstd))
        VV.wait(v)
        VV.inc(v.tensor_copy(mhf, rh))
        VV.wait(v)
        VV.inc(v.tensor_tensor(tmp, rstd, mhf, ALU.subtract))
        VV.wait(v)
        VV.inc(v.tensor_copy(rl, tmp))
    em["vector"](vhl)

    def tbc(t):
        VV.wait(t)
        MM.inc(t.matmul(ps_bm.ap()[:, 0:CH], ones_row.ap()[0:1, :], mh, start=True, stop=False))
        MM.inc(t.matmul(ps_bm.ap()[:, 0:CH], ones_row.ap()[0:1, :], ml, start=False, stop=True))
        MM.inc(t.matmul(ps_br.ap()[:, 0:CH], ones_row.ap()[0:1, :], rh, start=True, stop=False))
        MM.inc(t.matmul(ps_br.ap()[:, 0:CH], ones_row.ap()[0:1, :], rl, start=False, stop=True))
    em["tensor"](tbc)

    def vxn(v):
        MM.wait(v)
        for ft in range(NFT):
            sl = slice(ft * CH, (ft + 1) * CH)
            VV.inc(v.tensor_tensor(xc.ap()[:, sl], xc.ap()[:, sl], ps_bm.ap()[:, 0:CH], ALU.subtract))
        VV.wait(v)
        for ft in range(NFT):
            sl = slice(ft * CH, (ft + 1) * CH)
            VV.inc(v.tensor_tensor(xb2.ap()[:, sl], xc.ap()[:, sl], ps_br.ap()[:, 0:CH], ALU.mult))
    em["vector"](vxn)


def build_L1():
    nc = bass.Bass("TRN2", target_bir_lowering=False, debug=False)
    xfm = nc.dram_tensor("xfm", [D, TPC], F32, kind="ExternalInput").ap()
    WqT = nc.dram_tensor("WqT", [D, D], F32, kind="ExternalInput").ap()
    WkvT = nc.dram_tensor("WkvT", [D, D], F32, kind="ExternalInput").ap()
    xn_o = nc.dram_tensor("xn", [D, TPC], BF16, kind="ExternalOutput").ap()
    qn_o = nc.dram_tensor("qn", [TPC, D], BF16, kind="ExternalOutput").ap()
    kvn_o = nc.dram_tensor("kvn", [TPC, D], BF16, kind="ExternalOutput").ap()

    import contextlib
    with contextlib.ExitStack() as ex:
        sb = lambda *a: ex.enter_context(nc.sbuf_tensor(*a))
        pt = lambda *a: ex.enter_context(nc.psum_tensor(*a))
        sem = lambda n: ex.enter_context(nc.semaphore(n))
        wq_sb = sb("wq_sb", [128, NFT * D], BF16)
        wkv_sb = sb("wkv_sb", [128, NFT * D], BF16)
        xc = sb("xc", [128, NFT * CH], F32)
        xb2 = sb("xb2", [128, NFT * CH], BF16)
        stats = sb("stats", [1, 5 * CH], F32)
        stats_bf = sb("stats_bf", [1, 4 * CH], BF16)
        ones_col = sb("ones_col", [128, 1], BF16)
        ones_row = sb("ones_row", [1, 128], BF16)
        qbuf = sb("qbuf", [128, D], BF16)
        qnbuf = sb("qnbuf", [128, D], BF16)
        qsq = sb("qsq", [128, D], F32)
        s16 = sb("s16", [128, 16], F32)
        rn16 = sb("rn16", [128, 16], F32)
        ps_s1 = pt("ps_s1", [1, CH], F32)
        ps_s2 = pt("ps_s2", [1, CH], F32)
        ps_bm = pt("ps_bm", [128, CH], F32)
        ps_br = pt("ps_br", [128, CH], F32)
        ps_q0 = pt("ps_q0", [128, CH], F32)
        ps_q1 = pt("ps_q1", [128, CH], F32)
        sLD, sMM, sVV, sSS, sGG, sST = (sem(n) for n in ["LD", "MM", "VV", "SS", "GG", "ST"])
        block = ex.enter_context(nc.Block())
        ctrs = {k: Ctr(s) for k, s in
                [("LD", sLD), ("MM", sMM), ("VV", sVV), ("SS", sSS), ("GG", sGG), ("ST", sST)]}
        LD, MM, VV, SS, GG, ST = (ctrs[k] for k in ["LD", "MM", "VV", "SS", "GG", "ST"])
        em = _mkemit(block, ctrs)
        bufs = dict(xc=xc, xb2=xb2, ones_col=ones_col, ones_row=ones_row, stats=stats,
                    stats_bf=stats_bf, ps_s1=ps_s1, ps_s2=ps_s2, ps_bm=ps_bm, ps_br=ps_br)

        ones_c_d = nc.inline_tensor(np.ones((128, 1), dtype=np.float32), name="onescol").ap()
        ones_r_d = nc.inline_tensor(np.ones((1, 128), dtype=np.float32), name="onesrow").ap()

        def g0(g):
            for j in range(4):
                GG.inc(g.dma_start(
                    out=wq_sb.ap()[:, j * 4 * D:(j + 1) * 4 * D],
                    in_=WqT.rearrange("(ft p) o -> p ft o", p=128)[:, j * 4:(j + 1) * 4, :]), 16)
                GG.inc(g.dma_start(
                    out=wkv_sb.ap()[:, j * 4 * D:(j + 1) * 4 * D],
                    in_=WkvT.rearrange("(ft p) o -> p ft o", p=128)[:, j * 4:(j + 1) * 4, :]), 16)
            GG.inc(g.dma_start(out=ones_col.ap()[:, :], in_=ones_c_d[:, :]), 16)
            GG.inc(g.dma_start(out=ones_row.ap()[:, :], in_=ones_r_d[:, :]), 16)
        em["gpsimd"](g0)

        for c in range(NCH):
            def ld(s, c=c):
                ST.wait(s)
                VV.wait(s)
                SS.wait(s)
                MM.wait(s)
                LD.inc(s.dma_start(
                    out=xc.ap().rearrange("p (ft c) -> p ft c", ft=NFT),
                    in_=xfm.rearrange("(ft p) t -> p ft t", p=128)[:, :, c * CH:(c + 1) * CH]), 16)
            em["sync"](ld)

            _ln_stats_chunk(nc, em, bufs, ctrs, c)

            def st_xn(s, c=c):
                VV.wait(s)
                ST.inc(s.dma_start(
                    out=xn_o.rearrange("(ft p) t -> p ft t", p=128)[:, :, c * CH:(c + 1) * CH],
                    in_=xb2.ap().rearrange("p (ft c) -> p ft c", ft=NFT)), 16)
            em["sync"](st_xn)

            for (wsb, park) in ((wq_sb, qn_o), (wkv_sb, kvn_o)):
                for ts in range(CH // 128):
                    for fq in range(4):
                        ps = (ps_q0, ps_q1)[fq % 2]

                        def tq(t, wsb=wsb, ts=ts, c=c, fq=fq, ps=ps):
                            GG.wait(t)
                            VV.wait(t)
                            SS.wait(t)
                            for ft in range(NFT):
                                MM.inc(t.matmul(
                                    ps.ap()[:, 0:CH],
                                    xb2.ap()[:, ft * CH + ts * 128: ft * CH + ts * 128 + 128],
                                    wsb.ap()[:, ft * D + fq * CH: ft * D + (fq + 1) * CH],
                                    start=(ft == 0), stop=(ft == NFT - 1)))
                        em["tensor"](tq)

                        def ev(s, fq=fq, ps=ps):
                            MM.wait(s)
                            SS.inc(s.copy(qbuf.ap()[:, fq * CH:(fq + 1) * CH], ps.ap()[:, 0:CH]))
                        em["scalar"](ev)
                    def vn(v, ts=ts):
                        SS.wait(v)
                        VV.inc(v.tensor_tensor(qsq.ap()[:, :], qbuf.ap()[:, :], qbuf.ap()[:, :], ALU.mult))
                        VV.wait(v)
                        VV.inc(v.tensor_reduce(s16.ap()[:, :],
                                               qsq.ap().rearrange("p (h f) -> p h f", h=H),
                                               AX, ALU.add))
                    em["vector"](vn)

                    def sn(s, ts=ts):
                        VV.wait(s)
                        SS.inc(s.activation(rn16.ap()[:, :], s16.ap()[:, :], AF.Ln, scale=float(HD)))
                        SS.wait(s)
                        SS.inc(s.activation(rn16.ap()[:, :], rn16.ap()[:, :], AF.Exp, scale=-0.5))
                    em["scalar"](sn)

                    def vq(v, ts=ts):
                        SS.wait(v)
                        ST.wait(v)
                        for h in range(H):
                            VV.inc(v.tensor_scalar(qnbuf.ap()[:, h * HD:(h + 1) * HD],
                                                   qbuf.ap()[:, h * HD:(h + 1) * HD],
                                                   rn16.ap()[:, h:h + 1], None, ALU.mult))
                    em["vector"](vq)

                    def stq(s, park=park, ts=ts, c=c):
                        VV.wait(s)
                        ST.inc(s.dma_start(out=park[c * CH + ts * 128: c * CH + ts * 128 + 128, :],
                                           in_=qnbuf.ap()[:, :]), 16)
                        ST.wait(s)
                    em["sync"](stq)
    return nc




def build_L2():
    nc = bass.Bass("TRN2", target_bir_lowering=False, debug=False)
    kin = nc.dram_tensor("kvnfs", [256, B, S], BF16, kind="ExternalInput").ap()
    cout = nc.dram_tensor("cfs", [256, B, S], F32, kind="ExternalOutput").ap()
    import contextlib
    with contextlib.ExitStack() as ex:
        tin = ex.enter_context(nc.sbuf_tensor("tin", [128, S], BF16))
        tout = ex.enter_context(nc.sbuf_tensor("tout", [128, S], F32))
        sLD = ex.enter_context(nc.semaphore("LD"))
        sVV = ex.enter_context(nc.semaphore("VV"))
        sST = ex.enter_context(nc.semaphore("ST"))
        block = ex.enter_context(nc.Block())
        ctrs = {"LD": Ctr(sLD), "VV": Ctr(sVV), "ST": Ctr(sST)}
        LD, VV, ST = ctrs["LD"], ctrs["VV"], ctrs["ST"]
        em = _mkemit(block, ctrs)
        for fh in range(2):
            for b in range(B):
                def ld(s, fh=fh, b=b):
                    VV.wait(s)
                    LD.inc(s.dma_start(out=tin.ap()[:, :], in_=kin[fh * 128:(fh + 1) * 128, b, :]), 16)
                em["sync"](ld)

                def vs(v):
                    LD.wait(v)
                    ST.wait(v)
                    VV.inc(v.tensor_tensor_scan(tout.ap()[:, :], tin.ap()[:, :], tin.ap()[:, :],
                                                0.0, ALU.add, ALU.bypass))
                em["vector"](vs)

                def st(s, fh=fh, b=b):
                    VV.wait(s)
                    ST.inc(s.dma_start(out=cout[fh * 128:(fh + 1) * 128, b, :], in_=tout.ap()[:, :]), 16)
                em["scalar"](st)

        def fin(s):
            ST.wait(s)
        em["sync"](fin)
    return nc


def run_L2(p, parks):
    kvn_full = np.empty((B, S, D), dtype=BF)
    for i in range(NCORE):
        b, p0 = _tok_owner(i)
        kvn_full[b, p0:p0 + TPC, :] = parks[i]["kvn"]
    kf = np.ascontiguousarray(kvn_full.transpose(2, 0, 1))
    in_maps = [{"kvnfs": np.ascontiguousarray(kf[i * 256:(i + 1) * 256])} for i in range(NCORE)]
    r = _run("L2", build_L2, in_maps)
    c_full = np.concatenate([r[i]["cfs"] for i in range(NCORE)], axis=0)
    c_core = []
    for i in range(NCORE):
        b, p0 = _tok_owner(i)
        c_core.append(np.ascontiguousarray(c_full[:, b, p0:p0 + TPC]))
    return c_core




def build_L3():
    nc = bass.Bass("TRN2", target_bir_lowering=False, debug=False)
    qn_i = nc.dram_tensor("qn", [TPC, D], BF16, kind="ExternalInput").ap()
    c_i = nc.dram_tensor("cfm", [D, TPC], F32, kind="ExternalInput").ap()
    x_i = nc.dram_tensor("xfm", [D, TPC], F32, kind="ExternalInput").ap()
    wo_i = nc.dram_tensor("WoT", [D, D], F32, kind="ExternalInput").ap()
    o_x1p = nc.dram_tensor("x1p", [D, TPC], F32, kind="ExternalOutput").ap()
    Cf, SN, Ci, SNi = _dft_consts()
    import contextlib
    with contextlib.ExitStack() as ex:
        sb = lambda *a: ex.enter_context(nc.sbuf_tensor(*a))
        pt = lambda *a: ex.enter_context(nc.psum_tensor(*a))
        wo_sb = sb("wo_sb", [128, NFT * D], BF16)
        qn_fm = sb("qn_fm", [128, NFT * CH], BF16)
        c_bf = sb("c_bf", [128, NFT * CH], BF16)
        xc = sb("xc", [128, NFT * CH], F32)
        vh_bf = sb("vh_bf", [128, NFT * CH], BF16)
        cfr = sb("cfr", [128, CH], F32)
        cfi = sb("cfi", [128, CH], F32)
        t1 = sb("t1", [128, CH], F32)
        t2 = sb("t2", [128, CH], F32)
        qvr = sb("qvr", [128, CH], BF16)
        qvi = sb("qvi", [128, CH], BF16)
        outb = sb("outb", [128, CH], F32)
        cdft = sb("cdft", [128, 4 * 128], BF16)
        ps_qr = pt("ps_qr", [128, CH], F32)
        ps_qi = pt("ps_qi", [128, CH], F32)
        ps_cr = pt("ps_cr", [128, CH], F32)
        ps_ci = pt("ps_ci", [128, CH], F32)
        ps_vh = pt("ps_vh", [128, CH], F32)
        ps_a0 = pt("ps_a0", [128, CH], F32)
        ps_a1 = pt("ps_a1", [128, CH], F32)
        sems = {k: ex.enter_context(nc.semaphore(k)) for k in ["LD", "MM", "VV", "SS", "GG", "ST"]}
        block = ex.enter_context(nc.Block())
        ctrs = {k: Ctr(s) for k, s in sems.items()}
        LD, MM, VV, SS, GG, ST = (ctrs[k] for k in ["LD", "MM", "VV", "SS", "GG", "ST"])
        em = _mkemit(block, ctrs)
        dft_d = nc.inline_tensor(np.concatenate([Cf, SN, Ci, SNi], axis=1), name="dftc").ap()
        CfS = cdft.ap()[:, 0:128]
        SNS = cdft.ap()[:, 128:256]
        CiS = cdft.ap()[:, 256:384]
        SNiS = cdft.ap()[:, 384:512]

        def g0(g):
            GG.inc(g.dma_start(out=cdft.ap()[:, :], in_=dft_d), 16)
            for j in range(4):
                GG.inc(g.dma_start(
                    out=wo_sb.ap()[:, j * 4 * D:(j + 1) * 4 * D],
                    in_=wo_i.rearrange("(ft p) o -> p ft o", p=128)[:, j * 4:(j + 1) * 4, :]), 16)
        em["gpsimd"](g0)

        for c in range(NCH):
            def ldq(s, c=c):
                for ft in range(NFT):
                    LD.inc(s.dma_start(out=qn_fm.ap()[:, ft * CH:(ft + 1) * CH],
                                       in_=qn_i[c * CH:(c + 1) * CH, ft * 128:(ft + 1) * 128],
                                       transpose=True), 16)
                LD.inc(s.dma_start(
                    out=xc.ap().rearrange("p (ft t) -> p ft t", ft=NFT),
                    in_=x_i.rearrange("(ft p) t -> p ft t", p=128)[:, :, c * CH:(c + 1) * CH]), 16)
            em["sync"](ldq)

            def ldc(g, c=c):
                GG.inc(g.dma_start(
                    out=c_bf.ap().rearrange("p (ft t) -> p ft t", ft=NFT),
                    in_=c_i.rearrange("(ft p) t -> p ft t", p=128)[:, :, c * CH:(c + 1) * CH]), 16)
            em["gpsimd"](ldc)

            for ft in range(NFT):
                def tf(t, ft=ft):
                    LD.wait(t); GG.wait(t); VV.wait(t)
                    MM.inc(t.matmul(ps_qr.ap()[:, :], CfS, qn_fm.ap()[:, ft * CH:(ft + 1) * CH],
                                    start=True, stop=True))
                    MM.inc(t.matmul(ps_qi.ap()[:, :], SNS, qn_fm.ap()[:, ft * CH:(ft + 1) * CH],
                                    start=True, stop=True))
                    MM.inc(t.matmul(ps_cr.ap()[:, :], CfS, c_bf.ap()[:, ft * CH:(ft + 1) * CH],
                                    start=True, stop=True))
                    MM.inc(t.matmul(ps_ci.ap()[:, :], SNS, c_bf.ap()[:, ft * CH:(ft + 1) * CH],
                                    start=True, stop=True))
                em["tensor"](tf)

                def se(s, ft=ft):
                    MM.wait(s)
                    SS.inc(s.copy(cfr.ap()[:, :], ps_cr.ap()[:, :]))
                    SS.inc(s.copy(cfi.ap()[:, :], ps_ci.ap()[:, :]))
                em["scalar"](se)

                def vc(v, ft=ft):
                    SS.wait(v); MM.wait(v)
                    VV.inc(v.tensor_tensor(t1.ap()[:, :], ps_qr.ap()[:, :], cfr.ap()[:, :], ALU.mult))
                    VV.inc(v.tensor_tensor(t2.ap()[:, :], ps_qi.ap()[:, :], cfi.ap()[:, :], ALU.mult))
                    VV.wait(v)
                    VV.inc(v.tensor_tensor(qvr.ap()[:, :], t1.ap()[:, :], t2.ap()[:, :], ALU.subtract))
                    VV.inc(v.tensor_tensor(t1.ap()[:, :], ps_qr.ap()[:, :], cfi.ap()[:, :], ALU.mult))
                    VV.inc(v.tensor_tensor(t2.ap()[:, :], ps_qi.ap()[:, :], cfr.ap()[:, :], ALU.mult))
                    VV.wait(v)
                    VV.inc(v.tensor_tensor(qvi.ap()[:, :], t1.ap()[:, :], t2.ap()[:, :], ALU.add))
                em["vector"](vc)

                def ti(t, ft=ft):
                    VV.wait(t)
                    MM.inc(t.matmul(ps_vh.ap()[:, :], CiS, qvr.ap()[:, :], start=True, stop=False))
                    MM.inc(t.matmul(ps_vh.ap()[:, :], SNiS, qvi.ap()[:, :], start=False, stop=True))
                em["tensor"](ti)

                def ve(v, ft=ft):
                    MM.wait(v)
                    VV.inc(v.tensor_copy(vh_bf.ap()[:, ft * CH:(ft + 1) * CH], ps_vh.ap()[:, :]))
                em["vector"](ve)

            for dot in range(NFT):
                ps = (ps_a0, ps_a1)[dot % 2]

                def tw(t, dot=dot, ps=ps):
                    VV.wait(t); SS.wait(t)
                    for ft in range(NFT):
                        MM.inc(t.matmul(ps.ap()[:, :],
                                        wo_sb.ap()[:, ft * D + dot * 128: ft * D + dot * 128 + 128],
                                        vh_bf.ap()[:, ft * CH:(ft + 1) * CH],
                                        start=(ft == 0), stop=(ft == NFT - 1)))
                em["tensor"](tw)

                def vo(v, dot=dot, ps=ps):
                    MM.wait(v); ST.wait(v)
                    VV.inc(v.tensor_tensor(outb.ap()[:, :], ps.ap()[:, :],
                                           xc.ap()[:, dot * CH:(dot + 1) * CH], ALU.add))
                em["vector"](vo)

                def so(s, dot=dot, c=c):
                    VV.wait(s)
                    ST.inc(s.dma_start(out=o_x1p[dot * 128:(dot + 1) * 128, c * CH:(c + 1) * CH],
                                       in_=outb.ap()[:, :]), 16)
                em["sync"](so)

        def fin(s):
            ST.wait(s)
        em["sync"](fin)
    return nc


def run_L3(p, parks, c_core):
    in_maps = [{"qn": parks[i]["qn"], "cfm": c_core[i], "xfm": p["xfm"][i], "WoT": p["WoT"]}
               for i in range(NCORE)]
    return _run("L3", build_L3, in_maps)




def build_L4():
    nc = bass.Bass("TRN2", target_bir_lowering=False, debug=False)
    xn_i = nc.dram_tensor("xn", [D, TPC], BF16, kind="ExternalInput").ap()
    x1p_i = nc.dram_tensor("x1p", [D, TPC], F32, kind="ExternalInput").ap()
    wg_i = nc.dram_tensor("WgT", [D, FH], F32, kind="ExternalInput").ap()
    wu_i = nc.dram_tensor("WuT", [D, FH], F32, kind="ExternalInput").ap()
    wd_i = nc.dram_tensor("WdT", [FH, D], F32, kind="ExternalInput").ap()
    o_x1 = nc.dram_tensor("x1", [D, TPC], F32, kind="ExternalOutput").ap()
    HT = 1024
    import contextlib
    with contextlib.ExitStack() as ex:
        sb = lambda *a: ex.enter_context(nc.sbuf_tensor(*a))
        pt = lambda *a: ex.enter_context(nc.psum_tensor(*a))
        xn_sb = sb("xn_sb", [128, NFT * TPC], BF16)
        acc = sb("acc", [128, NFT * HT], F32)
        gated = sb("gated", [128, 16 * HT], BF16)
        wg_t = sb("wg_t", [128, NFT * 128], BF16)
        wu_t = sb("wu_t", [128, NFT * 128], BF16)
        wd_t = sb("wd_t", [128, 16 * 128], BF16)
        gs = sb("gs", [128, CH], F32)
        ps_g = pt("ps_g", [128, CH], F32)
        ps_u = pt("ps_u", [128, CH], F32)
        ps_d0 = pt("ps_d0", [128, CH], F32)
        ps_d1 = pt("ps_d1", [128, CH], F32)
        sems = {k: ex.enter_context(nc.semaphore(k)) for k in ["LD", "MM", "VV", "SS", "GG", "ST"]}
        block = ex.enter_context(nc.Block())
        ctrs = {k: Ctr(s) for k, s in sems.items()}
        LD, MM, VV, SS, GG, ST = (ctrs[k] for k in ["LD", "MM", "VV", "SS", "GG", "ST"])
        em = _mkemit(block, ctrs)

        def g0(g):
            for j in range(4):
                GG.inc(g.dma_start(
                    out=xn_sb.ap().rearrange("p (ft t) -> p ft t", ft=NFT)[:, j * 4:(j + 1) * 4, :],
                    in_=xn_i.rearrange("(ft p) t -> p ft t", p=128)[:, j * 4:(j + 1) * 4, :]), 16)
        em["gpsimd"](g0)

        for hf in range(2):
            def lda(s, hf=hf):
                ST.wait(s)
                LD.inc(s.dma_start(
                    out=acc.ap().rearrange("p (ft t) -> p ft t", ft=NFT),
                    in_=x1p_i.rearrange("(ft p) t -> p ft t", p=128)[:, :, hf * HT:(hf + 1) * HT]), 16)
            em["sync"](lda)

            for sl in range(4):
                for ht in range(16):
                    hid0 = sl * 2048 + ht * 128

                    def ldw(g, hid0=hid0):
                        MM.wait(g)
                        GG.inc(g.dma_start(
                            out=wg_t.ap().rearrange("p (ft o) -> p ft o", ft=NFT),
                            in_=wg_i.rearrange("(ft p) h -> p ft h", p=128)[:, :, hid0:hid0 + 128]), 16)
                        GG.inc(g.dma_start(
                            out=wu_t.ap().rearrange("p (ft o) -> p ft o", ft=NFT),
                            in_=wu_i.rearrange("(ft p) h -> p ft h", p=128)[:, :, hid0:hid0 + 128]), 16)
                    em["gpsimd"](ldw)

                    for tc in range(2):
                        def tg(t, hf=hf, tc=tc):
                            GG.wait(t); VV.wait(t); SS.wait(t)
                            tok0 = hf * HT + tc * CH
                            for ft in range(NFT):
                                MM.inc(t.matmul(ps_g.ap()[:, :], wg_t.ap()[:, ft * 128:(ft + 1) * 128],
                                                xn_sb.ap()[:, ft * TPC + tok0: ft * TPC + tok0 + CH],
                                                start=(ft == 0), stop=(ft == NFT - 1)))
                            for ft in range(NFT):
                                MM.inc(t.matmul(ps_u.ap()[:, :], wu_t.ap()[:, ft * 128:(ft + 1) * 128],
                                                xn_sb.ap()[:, ft * TPC + tok0: ft * TPC + tok0 + CH],
                                                start=(ft == 0), stop=(ft == NFT - 1)))
                        em["tensor"](tg)

                        def sg(s):
                            MM.wait(s)
                            SS.inc(s.activation(gs.ap()[:, :], ps_g.ap()[:, :], AF.Silu))
                        em["scalar"](sg)

                        def vg(v, ht=ht, tc=tc):
                            SS.wait(v); MM.wait(v)
                            VV.inc(v.tensor_tensor(gated.ap()[:, ht * HT + tc * CH: ht * HT + tc * CH + CH],
                                                   gs.ap()[:, :], ps_u.ap()[:, :], ALU.mult))
                        em["vector"](vg)

                for dot in range(16):
                    def ldd(g, sl=sl, dot=dot):
                        MM.wait(g)
                        GG.inc(g.dma_start(
                            out=wd_t.ap().rearrange("p (ht o) -> p ht o", ht=16),
                            in_=wd_i[sl * 2048:(sl + 1) * 2048, dot * 128:(dot + 1) * 128]
                                .rearrange("(ht p) o -> p ht o", p=128)), 16)
                    em["gpsimd"](ldd)

                    for tc in range(2):
                        ps = (ps_d0, ps_d1)[tc]

                        def td(t, hf=hf, tc=tc, ps=ps):
                            GG.wait(t); VV.wait(t)
                            for ht in range(16):
                                MM.inc(t.matmul(ps.ap()[:, :], wd_t.ap()[:, ht * 128:(ht + 1) * 128],
                                                gated.ap()[:, ht * HT + tc * CH: ht * HT + tc * CH + CH],
                                                start=(ht == 0), stop=(ht == 15)))
                        em["tensor"](td)

                        def vd(v, dot=dot, tc=tc, ps=ps):
                            MM.wait(v); LD.wait(v)
                            sl_ = slice(dot * HT + tc * CH, dot * HT + tc * CH + CH)
                            VV.inc(v.tensor_tensor(acc.ap()[:, sl_], acc.ap()[:, sl_], ps.ap()[:, :], ALU.add))
                        em["vector"](vd)

            def sta(s, hf=hf):
                VV.wait(s)
                ST.inc(s.dma_start(
                    out=o_x1.rearrange("(ft p) t -> p ft t", p=128)[:, :, hf * HT:(hf + 1) * HT],
                    in_=acc.ap().rearrange("p (ft t) -> p ft t", ft=NFT)), 16)
            em["sync"](sta)

        def fin(s):
            ST.wait(s)
        em["sync"](fin)
    return nc


def run_L4(p, parks, r3):
    in_maps = [{"xn": parks[i]["xn"], "x1p": r3[i]["x1p"], "WgT": p["WgT"], "WuT": p["WuT"],
                "WdT": p["WdT"]} for i in range(NCORE)]
    return _run("L4", build_L4, in_maps)




def build_L5():
    nc = bass.Bass("TRN2", target_bir_lowering=False, debug=False)
    x1_i = nc.dram_tensor("x1", [D, TPC], F32, kind="ExternalInput").ap()
    WQ_i = nc.dram_tensor("WQ", [D, D], F32, kind="ExternalInput").ap()
    WK_i = nc.dram_tensor("WK", [D, D], F32, kind="ExternalInput").ap()
    WV_i = nc.dram_tensor("WV", [D, D], F32, kind="ExternalInput").ap()
    gw_i = nc.dram_tensor("gwT", [D, NE], F32, kind="ExternalInput").ap()
    bias_i = nc.dram_tensor("biasT", [128, NE], F32, kind="ExternalInput").ap()
    fc1_i = nc.dram_tensor("fc1T", [D, LAT], F32, kind="ExternalInput").ap()
    qc_i = nc.dram_tensor("qcos", [D, TPC], F32, kind="ExternalInput").ap()
    qs_i = nc.dram_tensor("qsin", [D, TPC], F32, kind="ExternalInput").ap()
    kc_i = nc.dram_tensor("kcos", [D, TPC], F32, kind="ExternalInput").ap()
    ks_i = nc.dram_tensor("ksin", [D, TPC], F32, kind="ExternalInput").ap()
    o_xn1 = nc.dram_tensor("xn1", [D, TPC], BF16, kind="ExternalOutput").ap()
    o_Qx = nc.dram_tensor("Qx", [D, TPC], BF16, kind="ExternalOutput").ap()
    o_Kx = nc.dram_tensor("Kx", [D, TPC], BF16, kind="ExternalOutput").ap()
    o_V = nc.dram_tensor("Vtm", [TPC, D], BF16, kind="ExternalOutput").ap()
    o_cw = nc.dram_tensor("cw", [TPC, NE], F32, kind="ExternalOutput").ap()
    o_xlat = nc.dram_tensor("xlat", [LAT, TPC], BF16, kind="ExternalOutput").ap()
    BIGC = 65536.0
    import contextlib
    with contextlib.ExitStack() as ex:
        sb = lambda *a: ex.enter_context(nc.sbuf_tensor(*a))
        pt = lambda *a: ex.enter_context(nc.psum_tensor(*a))
        xn1_sb = sb("xn1_sb", [128, NFT * TPC], BF16)
        w_sb = sb("w_sb", [128, NFT * D], BF16)
        xc = sb("xc", [128, NFT * CH], F32)
        sq = sb("sq", [128, NFT * CH], BF16)
        stats = sb("stats", [1, 5 * CH], F32)
        stats_bf = sb("stats_bf", [1, 4 * CH], BF16)
        ones_col = sb("ones_col", [128, 1], BF16)
        ones_row = sb("ones_row", [1, 128], BF16)
        rt_sb = sb("rt_sb", [128, 128], BF16)
        gw_sb = sb("gw_sb", [128, NFT * NE], BF16)
        bias_sb = sb("bias_sb", [128, NE], F32)
        scores = sb("scores", [128, NE], F32)
        sbt = sb("sbt", [128, NE], F32)
        m8 = sb("m8", [128, 8], F32)
        gsb = sb("gsb", [128, 8], F32)
        g4 = sb("g4", [128, 8], F32)
        gmask = sb("gmask", [128, 4], F32)
        ngm = sb("ngm", [128, 4], F32)
        me = sb("me", [128, NE], F32)
        selm = sb("selm", [128, NE], F32)
        cwt = sb("cwt", [128, NE], F32)
        rsum = sb("rsum", [128, 2], F32)
        qsb = sb("qsb", [128, CH], BF16)
        tq1 = sb("tq1", [128, CH], F32)
        tq2 = sb("tq2", [128, CH], F32)
        qxb = sb("qxb", [128, CH], BF16)
        ctab = sb("ctab", [128, CH], F32)
        stab = sb("stab", [128, CH], F32)
        ps_q = pt("ps_q", [128, CH], F32)
        ps_rot = pt("ps_rot", [128, CH], F32)
        ps_lg = pt("ps_lg", [128, NE], F32)
        ps_s1 = pt("ps_s1", [1, CH], F32)
        ps_s2 = pt("ps_s2", [1, CH], F32)
        ps_bm = pt("ps_bm", [128, CH], F32)
        ps_br = pt("ps_br", [128, CH], F32)
        sems = {k: ex.enter_context(nc.semaphore(k)) for k in ["LD", "MM", "VV", "SS", "GG", "ST"]}
        block = ex.enter_context(nc.Block())
        ctrs = {k: Ctr(s) for k, s in sems.items()}
        LD, MM, VV, SS, GG, ST = (ctrs[k] for k in ["LD", "MM", "VV", "SS", "GG", "ST"])
        em = _mkemit(block, ctrs)
        ones_c_d = nc.inline_tensor(np.ones((128, 1), dtype=np.float32), name="onescol").ap()
        ones_r_d = nc.inline_tensor(np.ones((1, 128), dtype=np.float32), name="onesrow").ap()
        rt_d = nc.inline_tensor(_rot_const(), name="rotT").ap()

        def g0(g):
            GG.inc(g.dma_start(out=ones_col.ap()[:, :], in_=ones_c_d[:, :]), 16)
            GG.inc(g.dma_start(out=ones_row.ap()[:, :], in_=ones_r_d[:, :]), 16)
            GG.inc(g.dma_start(out=rt_sb.ap()[:, :], in_=rt_d), 16)
            GG.inc(g.dma_start(out=gw_sb.ap().rearrange("p (ft e) -> p ft e", ft=NFT),
                               in_=gw_i.rearrange("(ft p) e -> p ft e", p=128)), 16)
        em["gpsimd"](g0)

        def s0(s):
            LD.inc(s.dma_start(out=bias_sb.ap()[:, :], in_=bias_i), 16)
        em["sync"](s0)

        def v0(v):
            VV.inc(v.memset(gsb.ap()[:, 4:8], -1e30))
        em["vector"](v0)

        bufs = dict(xc=xc, xb2=sq, ones_col=ones_col, ones_row=ones_row, stats=stats,
                    stats_bf=stats_bf, ps_s1=ps_s1, ps_s2=ps_s2, ps_bm=ps_bm, ps_br=ps_br)
        for c in range(NCH):
            def ld(s, c=c):
                ST.wait(s); VV.wait(s); SS.wait(s); MM.wait(s)
                LD.inc(s.dma_start(
                    out=xc.ap().rearrange("p (ft t) -> p ft t", ft=NFT),
                    in_=x1_i.rearrange("(ft p) t -> p ft t", p=128)[:, :, c * CH:(c + 1) * CH]), 16)
            em["sync"](ld)
            _ln_stats_chunk(nc, em, bufs, ctrs, c)
            def vcp(v, c=c):
                VV.wait(v)
                for ft in range(NFT):
                    VV.inc(v.tensor_copy(xn1_sb.ap()[:, ft * TPC + c * CH: ft * TPC + (c + 1) * CH],
                                         sq.ap()[:, ft * CH:(ft + 1) * CH]))
            em["vector"](vcp)

            def st_xn(s, c=c):
                VV.wait(s)
                ST.inc(s.dma_start(
                    out=o_xn1.rearrange("(ft p) t -> p ft t", p=128)[:, :, c * CH:(c + 1) * CH],
                    in_=sq.ap().rearrange("p (ft t) -> p ft t", ft=NFT)), 16)
            em["sync"](st_xn)

        for tt in range(TPC // 128):
            def tr(t, tt=tt):
                GG.wait(t); VV.wait(t); SS.wait(t); LD.wait(t)
                for ft in range(NFT):
                    MM.inc(t.matmul(ps_lg.ap()[:, :],
                                    xn1_sb.ap()[:, ft * TPC + tt * 128: ft * TPC + tt * 128 + 128],
                                    gw_sb.ap()[:, ft * NE:(ft + 1) * NE],
                                    start=(ft == 0), stop=(ft == NFT - 1)))
            em["tensor"](tr)

            def sr(s):
                MM.wait(s)
                SS.inc(s.activation(scores.ap()[:, :], ps_lg.ap()[:, :], AF.Sigmoid))
            em["scalar"](sr)

            def vr(v, tt=tt):
                SS.wait(v); ST.wait(v)
                VV.inc(v.tensor_tensor(sbt.ap()[:, :], scores.ap()[:, :], bias_sb.ap()[:, :], ALU.add))
                VV.wait(v)
                for g in range(4):
                    VV.inc(v.max(m8.ap()[:, 0:8], sbt.ap()[:, g * 8:(g + 1) * 8]))
                    VV.wait(v)
                    VV.inc(v.tensor_tensor(gsb.ap()[:, g:g + 1], m8.ap()[:, 0:1], m8.ap()[:, 1:2], ALU.add))
                    VV.wait(v)
                VV.inc(v.max(g4.ap()[:, 0:8], gsb.ap()[:, 0:8]))
                VV.wait(v)
                VV.inc(v.tensor_scalar(gmask.ap()[:, :], gsb.ap()[:, 0:4], g4.ap()[:, 1:2], None, ALU.is_ge))
                VV.wait(v)
                VV.inc(v.tensor_scalar(ngm.ap()[:, :], gmask.ap()[:, :], -1.0, 1.0, ALU.mult, ALU.add))
                VV.wait(v)
                gmv = gmask.ap()[:, :].unsqueeze(2).broadcast_to([128, 4, 8])
                ngv = ngm.ap()[:, :].unsqueeze(2).broadcast_to([128, 4, 8])
                mev = me.ap()[:, :].rearrange("p (g e) -> p g e", g=4)
                sbv = sbt.ap()[:, :].rearrange("p (g e) -> p g e", g=4)
                VV.inc(v.tensor_tensor(mev, sbv, gmv, ALU.mult))
                VV.wait(v)
                VV.inc(v.scalar_tensor_tensor(mev, ngv, -BIGC, mev, ALU.mult, ALU.add))
                VV.wait(v)
                VV.inc(v.max(m8.ap()[:, 0:8], me.ap()[:, :]))
                VV.wait(v)
                VV.inc(v.tensor_scalar(selm.ap()[:, :], me.ap()[:, :], m8.ap()[:, 7:8], None, ALU.is_ge))
                VV.wait(v)
                VV.inc(v.tensor_tensor(cwt.ap()[:, :], scores.ap()[:, :], selm.ap()[:, :], ALU.mult))
                VV.wait(v)
                VV.inc(v.reduce_sum(rsum.ap()[:, 0:1], cwt.ap()[:, :], AX))
                VV.wait(v)
                VV.inc(v.reciprocal(rsum.ap()[:, 1:2], rsum.ap()[:, 0:1]))
                VV.wait(v)
                VV.inc(v.tensor_scalar(cwt.ap()[:, :], cwt.ap()[:, :], rsum.ap()[:, 1:2], 2.5,
                                       ALU.mult, ALU.mult))
            em["vector"](vr)

            def str_(s, tt=tt):
                VV.wait(s)
                ST.inc(s.dma_start(out=o_cw[tt * 128:(tt + 1) * 128, :], in_=cwt.ap()[:, :]), 16)
            em["sync"](str_)

        for (wi, ci, si, park) in ((WQ_i, qc_i, qs_i, o_Qx), (WK_i, kc_i, ks_i, o_Kx)):
            def gw(g, wi=wi):
                MM.wait(g); VV.wait(g); SS.wait(g)
                for j in range(4):
                    GG.inc(g.dma_start(
                        out=w_sb.ap().rearrange("p (ft o) -> p ft o", ft=NFT)[:, j * 4:(j + 1) * 4, :],
                        in_=wi.rearrange("(ft p) o -> p ft o", p=128)[:, j * 4:(j + 1) * 4, :]), 16)
            em["gpsimd"](gw)
            for c in range(NCH):
                for dqt in range(NFT):
                    def ldt(s, ci=ci, si=si, c=c, dqt=dqt):
                        VV.wait(s)
                        LD.inc(s.dma_start(out=ctab.ap()[:, :],
                                           in_=ci[dqt * 128:(dqt + 1) * 128, c * CH:(c + 1) * CH]), 16)
                        LD.inc(s.dma_start(out=stab.ap()[:, :],
                                           in_=si[dqt * 128:(dqt + 1) * 128, c * CH:(c + 1) * CH]), 16)
                    em["sync"](ldt)

                    def tp(t, c=c, dqt=dqt):
                        GG.wait(t); VV.wait(t); SS.wait(t)
                        for ft in range(NFT):
                            MM.inc(t.matmul(ps_q.ap()[:, :],
                                            w_sb.ap()[:, ft * D + dqt * 128: ft * D + dqt * 128 + 128],
                                            xn1_sb.ap()[:, ft * TPC + c * CH: ft * TPC + (c + 1) * CH],
                                            start=(ft == 0), stop=(ft == NFT - 1)))
                    em["tensor"](tp)

                    def sq_(s):
                        MM.wait(s)
                        SS.inc(s.copy(qsb.ap()[:, :], ps_q.ap()[:, :]))
                    em["scalar"](sq_)

                    def trot(t):
                        SS.wait(t)
                        MM.inc(t.matmul(ps_rot.ap()[:, :], rt_sb.ap()[:, :], qsb.ap()[:, :],
                                        start=True, stop=True))
                    em["tensor"](trot)

                    def vx(v):
                        MM.wait(v); LD.wait(v); ST.wait(v)
                        VV.inc(v.tensor_tensor(tq1.ap()[:, :], qsb.ap()[:, :], ctab.ap()[:, :], ALU.mult))
                        VV.inc(v.tensor_tensor(tq2.ap()[:, :], ps_rot.ap()[:, :], stab.ap()[:, :], ALU.mult))
                        VV.wait(v)
                        VV.inc(v.tensor_tensor(qxb.ap()[:, :], tq1.ap()[:, :], tq2.ap()[:, :], ALU.add))
                    em["vector"](vx)

                    def stq(s, park=park, c=c, dqt=dqt):
                        VV.wait(s)
                        ST.inc(s.dma_start(out=park[dqt * 128:(dqt + 1) * 128, c * CH:(c + 1) * CH],
                                           in_=qxb.ap()[:, :]), 16)
                    em["sync"](stq)

        def gv(g):
            MM.wait(g); VV.wait(g); SS.wait(g)
            for j in range(4):
                GG.inc(g.dma_start(
                    out=w_sb.ap().rearrange("p (ft o) -> p ft o", ft=NFT)[:, j * 4:(j + 1) * 4, :],
                    in_=WV_i.rearrange("(ft p) o -> p ft o", p=128)[:, j * 4:(j + 1) * 4, :]), 16)
        em["gpsimd"](gv)
        for tt in range(TPC // 128):
            for dvs in range(4):
                def tv(t, tt=tt, dvs=dvs):
                    GG.wait(t); VV.wait(t)
                    for ft in range(NFT):
                        MM.inc(t.matmul(ps_q.ap()[:, :],
                                        xn1_sb.ap()[:, ft * TPC + tt * 128: ft * TPC + tt * 128 + 128],
                                        w_sb.ap()[:, ft * D + dvs * CH: ft * D + (dvs + 1) * CH],
                                        start=(ft == 0), stop=(ft == NFT - 1)))
                em["tensor"](tv)

                def vv_(v):
                    MM.wait(v); ST.wait(v)
                    VV.inc(v.tensor_copy(qxb.ap()[:, :], ps_q.ap()[:, :]))
                em["vector"](vv_)

                def sv(s, tt=tt, dvs=dvs):
                    VV.wait(s)
                    ST.inc(s.dma_start(out=o_V[tt * 128:(tt + 1) * 128, dvs * CH:(dvs + 1) * CH],
                                       in_=qxb.ap()[:, :]), 16)
                em["sync"](sv)

        def gl(g):
            MM.wait(g); VV.wait(g)
            GG.inc(g.dma_start(
                out=w_sb.ap()[:, 0:NFT * LAT].rearrange("p (ft o) -> p ft o", ft=NFT),
                in_=fc1_i.rearrange("(ft p) o -> p ft o", p=128)), 16)
        em["gpsimd"](gl)
        for latt in range(LAT // 128):
            for c in range(NCH):
                def tl(t, latt=latt, c=c):
                    GG.wait(t); VV.wait(t)
                    for ft in range(NFT):
                        MM.inc(t.matmul(ps_q.ap()[:, :],
                                        w_sb.ap()[:, ft * LAT + latt * 128: ft * LAT + latt * 128 + 128],
                                        xn1_sb.ap()[:, ft * TPC + c * CH: ft * TPC + (c + 1) * CH],
                                        start=(ft == 0), stop=(ft == NFT - 1)))
                em["tensor"](tl)

                def vl(v):
                    MM.wait(v); ST.wait(v)
                    VV.inc(v.tensor_copy(qxb.ap()[:, :], ps_q.ap()[:, :]))
                em["vector"](vl)

                def sl_(s, latt=latt, c=c):
                    VV.wait(s)
                    ST.inc(s.dma_start(out=o_xlat[latt * 128:(latt + 1) * 128, c * CH:(c + 1) * CH],
                                       in_=qxb.ap()[:, :]), 16)
                em["sync"](sl_)

        def fin(s):
            ST.wait(s)
        em["sync"](fin)
    return nc


def run_L5(p, r4):
    tabs = _xpos_tables()
    in_maps = []
    for i in range(NCORE):
        h = i % 2
        qc, qs, kc, ks = tabs[h]
        in_maps.append({"x1": r4[i]["x1"], "WQ": p["WQ"], "WK": p["WK"], "WV": p["WV"],
                        "gwT": p["gwT"], "biasT": p["biasT"], "fc1T": p["fc1T"],
                        "qcos": qc, "qsin": qs, "kcos": kc, "ksin": ks})
    return _run("L5", build_L5, in_maps)




def _decay_tiles():
    lg = np.log(np.float64(GAMMA))
    dm = np.arange(128)[:, None]
    dq = np.arange(512)[None, :]
    full = np.exp(lg * (dq - dm)).astype(np.float32)
    diags = []
    for o in range(4):
        v = np.exp(lg * (dq - 128 * o - dm))
        v = np.where(dm + 128 * o <= dq, v, 0.0)
        diags.append(v.astype(np.float32))
    return full, diags


def build_L6():
    nc = bass.Bass("TRN2", target_bir_lowering=False, debug=False)
    Q_i = nc.dram_tensor("Qb", [D, S], BF16, kind="ExternalInput").ap()
    K_i = nc.dram_tensor("Kb", [D, S], BF16, kind="ExternalInput").ap()
    V_i = nc.dram_tensor("Vbh", [S, 1024], BF16, kind="ExternalInput").ap()
    o_att = nc.dram_tensor("attc", [1024, S], F32, kind="ExternalOutput").ap()
    Tfull, Tdiags = _decay_tiles()
    NMT = S // 128
    import contextlib
    with contextlib.ExitStack() as ex:
        sb = lambda *a: ex.enter_context(nc.sbuf_tensor(*a))
        pt = lambda *a: ex.enter_context(nc.psum_tensor(*a))
        qc_sb = sb("qc_sb", [128, NFT * CH], BF16)
        kt_sb = sb("kt_sb", [128, NFT * 128], BF16)
        vt_sb = sb("vt_sb", [128, 1024], BF16)
        slab = sb("slab", [128, NMT * CH], BF16)
        tdk = sb("tdk", [128, 5 * CH], F32)
        ob = sb("ob", [128, CH], F32)
        ps_at = pt("ps_at", [128, CH], F32)
        ps_o = [pt(f"ps_o{j}", [128, CH], F32) for j in range(4)]
        sems = {k: ex.enter_context(nc.semaphore(k)) for k in ["LD", "MM", "VV", "SS", "GG", "ST"]}
        block = ex.enter_context(nc.Block())
        ctrs = {k: Ctr(s) for k, s in sems.items()}
        LD, MM, VV, SS, GG, ST = (ctrs[k] for k in ["LD", "MM", "VV", "SS", "GG", "ST"])
        em = _mkemit(block, ctrs)
        td_d = nc.inline_tensor(np.concatenate([Tfull] + Tdiags, axis=1), name="dectiles").ap()

        def g0(g):
            GG.inc(g.dma_start(out=tdk.ap()[:, :], in_=td_d), 16)
        em["gpsimd"](g0)

        lg = math.log(GAMMA)
        for qc in range(S // CH):
            nmt = 4 * (qc + 1)

            def ldq(s, qc=qc):
                MM.wait(s)
                LD.inc(s.dma_start(
                    out=qc_sb.ap().rearrange("p (ft t) -> p ft t", ft=NFT),
                    in_=Q_i.rearrange("(ft p) t -> p ft t", p=128)[:, :, qc * CH:(qc + 1) * CH]), 16)
            em["sync"](ldq)

            for mt in range(nmt):
                def ldk(g, mt=mt):
                    MM.wait(g)
                    GG.inc(g.dma_start(
                        out=kt_sb.ap().rearrange("p (ft t) -> p ft t", ft=NFT),
                        in_=K_i.rearrange("(ft p) t -> p ft t", p=128)[:, :, mt * 128:(mt + 1) * 128]), 16)
                em["gpsimd"](ldk)

                def ta(t, mt=mt):
                    GG.wait(t); LD.wait(t); VV.wait(t)
                    for ft in range(NFT):
                        MM.inc(t.matmul(ps_at.ap()[:, :],
                                        kt_sb.ap()[:, ft * 128:(ft + 1) * 128],
                                        qc_sb.ap()[:, ft * CH:(ft + 1) * CH],
                                        start=(ft == 0), stop=(ft == NFT - 1)))
                em["tensor"](ta)

                dist = qc * 512 - mt * 128
                if mt < 4 * qc:
                    cimm = float(np.float32(np.exp(np.float64(lg) * dist)))
                    tsl = tdk.ap()[:, 0:CH]
                else:
                    cimm = 1.0
                    o = mt - 4 * qc
                    tsl = tdk.ap()[:, (1 + o) * CH:(2 + o) * CH]

                def va(v, mt=mt, cimm=cimm, tsl=tsl):
                    MM.wait(v)
                    VV.inc(v.scalar_tensor_tensor(slab.ap()[:, mt * CH:(mt + 1) * CH],
                                                  ps_at.ap()[:, :], cimm, tsl, ALU.mult, ALU.mult))
                em["vector"](va)

            for hp in range(2):
                for mt in range(nmt):
                    def ldv(g, mt=mt, hp=hp):
                        MM.wait(g)
                        GG.inc(g.dma_start(out=vt_sb.ap()[:, 0:CH],
                                           in_=V_i[mt * 128:(mt + 1) * 128, hp * CH:(hp + 1) * CH]), 16)
                    em["gpsimd"](ldv)

                    def tb(t, mt=mt, hp=hp, first=(mt == 0), last=(mt == nmt - 1)):
                        GG.wait(t); VV.wait(t); SS.wait(t)
                        for j in range(4):
                            MM.inc(t.matmul(ps_o[j].ap()[:, :],
                                            vt_sb.ap()[:, j * 128:(j + 1) * 128],
                                            slab.ap()[:, mt * CH:(mt + 1) * CH],
                                            start=first, stop=last))
                    em["tensor"](tb)

                for j in range(4):
                    def vo(v, j=j):
                        MM.wait(v); ST.wait(v)
                        VV.inc(v.tensor_copy(ob.ap()[:, :], ps_o[j].ap()[:, :]))
                    em["vector"](vo)

                    def so(s, j=j, hp=hp, qc=qc):
                        VV.wait(s)
                        dv0 = hp * CH + j * 128
                        ST.inc(s.dma_start(out=o_att[dv0:dv0 + 128, qc * CH:(qc + 1) * CH],
                                           in_=ob.ap()[:, :]), 16)
                    em["sync"](so)

        def fin(s):
            ST.wait(s)
        em["sync"](fin)
    return nc


def run_L6(p, r5):
    in_maps = []
    for i in range(NCORE):
        b, h = i // 2, i % 2
        Qb = np.concatenate([np.asarray(r5[2 * b]["Qx"]), np.asarray(r5[2 * b + 1]["Qx"])], axis=1)
        Kb = np.concatenate([np.asarray(r5[2 * b]["Kx"]), np.asarray(r5[2 * b + 1]["Kx"])], axis=1)
        Vb = np.concatenate([np.asarray(r5[2 * b]["Vtm"]), np.asarray(r5[2 * b + 1]["Vtm"])], axis=0)
        in_maps.append({"Qb": np.ascontiguousarray(Qb), "Kb": np.ascontiguousarray(Kb),
                        "Vbh": np.ascontiguousarray(Vb[:, h * 1024:(h + 1) * 1024])})
    r = _run("L6", build_L6, in_maps)
    attx = []
    for i in range(NCORE):
        b, p0 = _tok_owner(i)
        af = np.concatenate([r[2 * b]["attc"], r[2 * b + 1]["attc"]], axis=0)
        attx.append(np.ascontiguousarray(af[:, p0:p0 + TPC]))
    return attx




def build_L7():
    nc = bass.Bass("TRN2", target_bir_lowering=False, debug=False)
    xn1_i = nc.dram_tensor("xn1", [D, TPC], BF16, kind="ExternalInput").ap()
    xlat_i = nc.dram_tensor("xlat", [LAT, TPC], BF16, kind="ExternalInput").ap()
    cw_i = nc.dram_tensor("cw", [TPC, NE], F32, kind="ExternalInput").ap()
    w1_i = nc.dram_tensor("w1T", [NE, LAT, INTER], F32, kind="ExternalInput").ap()
    w2_i = nc.dram_tensor("w2T", [NE, INTER, LAT], F32, kind="ExternalInput").ap()
    fc2_i = nc.dram_tensor("fc2T", [LAT, D], F32, kind="ExternalInput").ap()
    su_i = nc.dram_tensor("suT", [D, SHH], F32, kind="ExternalInput").ap()
    sd_i = nc.dram_tensor("sdT", [SHH, D], F32, kind="ExternalInput").ap()
    x1_i = nc.dram_tensor("x1", [D, TPC], F32, kind="ExternalInput").ap()
    att_i = nc.dram_tensor("attx", [D, TPC], F32, kind="ExternalInput").ap()
    o_x2 = nc.dram_tensor("x2", [D, TPC], F32, kind="ExternalOutput").ap()
    HT = 1024
    NTT = TPC // 128
    import contextlib
    with contextlib.ExitStack() as ex:
        sb = lambda *a: ex.enter_context(nc.sbuf_tensor(*a))
        pt = lambda *a: ex.enter_context(nc.psum_tensor(*a))
        yfm = sb("yfm", [128, 8 * TPC], BF16)
        cw_sb = sb("cw_sb", [128, NTT * NE], F32)
        hr = sb("hr", [128, CH], F32)
        ob = sb("ob", [128, CH], F32)
        ident = sb("ident", [128, 128], BF16)
        ps_a = pt("ps_a", [128, CH], F32)
        ps_b = pt("ps_b", [128, CH], F32)
        ps_t = pt("ps_t", [128, 128], BF16)
        sems = {k: ex.enter_context(nc.semaphore(k)) for k in ["LD", "MM", "VV", "SS", "GG", "ST"]}
        block = ex.enter_context(nc.Block())
        ctrs = {k: Ctr(s) for k, s in sems.items()}
        LD, MM, VV, SS, GG, ST = (ctrs[k] for k in ["LD", "MM", "VV", "SS", "GG", "ST"])
        em = _mkemit(block, ctrs)
        id_d = nc.inline_tensor(np.eye(128, dtype=np.float32), name="ident128").ap()

        def g0(g):
            GG.inc(g.dma_start(out=ident.ap()[:, :], in_=id_d), 16)
            GG.inc(g.dma_start(out=cw_sb.ap().rearrange("p (tt e) -> p tt e", tt=NTT),
                               in_=cw_i.rearrange("(tt p) e -> p tt e", p=128)), 16)
        em["gpsimd"](g0)

        with contextlib.ExitStack() as ex2:
            sb2 = lambda *a: ex2.enter_context(nc.sbuf_tensor(*a))
            y_acc = sb2("y_acc", [128, NTT * LAT], F32)
            xlat_sb = sb2("xlat_sb", [128, 8 * TPC], BF16)
            h_sb = sb2("h_sb", [128, 8 * TPC], BF16)
            w1_sb = sb2("w1_sb", [128, 8 * INTER], BF16)
            w2_sb = sb2("w2_sb", [128, 8 * LAT], BF16)

            def g1(g):
                for j in range(2):
                    GG.inc(g.dma_start(
                        out=xlat_sb.ap().rearrange("p (lt t) -> p lt t", lt=8)[:, j * 4:(j + 1) * 4, :],
                        in_=xlat_i.rearrange("(lt p) t -> p lt t", p=128)[:, j * 4:(j + 1) * 4, :]), 16)
            em["gpsimd"](g1)

            def vz(v):
                VV.inc(v.memset(y_acc.ap()[:, :], 0.0))
            em["vector"](vz)

            for e in range(NE):
                def gw(g, e=e):
                    MM.wait(g)
                    GG.inc(g.dma_start(out=w1_sb.ap().rearrange("p (lt i) -> p lt i", lt=8),
                                       in_=w1_i[e].rearrange("(lt p) i -> p lt i", p=128)), 16)
                    GG.inc(g.dma_start(out=w2_sb.ap().rearrange("p (it l) -> p it l", it=8),
                                       in_=w2_i[e].rearrange("(it p) l -> p it l", p=128)), 16)
                em["gpsimd"](gw)

                for it in range(8):
                    for tc in range(NCH):
                        def th(t, it=it, tc=tc):
                            GG.wait(t); VV.wait(t); SS.wait(t)
                            for lt in range(8):
                                MM.inc(t.matmul(ps_a.ap()[:, :],
                                                w1_sb.ap()[:, lt * INTER + it * 128: lt * INTER + it * 128 + 128],
                                                xlat_sb.ap()[:, lt * TPC + tc * CH: lt * TPC + (tc + 1) * CH],
                                                start=(lt == 0), stop=(lt == 7)))
                        em["tensor"](th)

                        def sh(s):
                            MM.wait(s)
                            SS.inc(s.activation(hr.ap()[:, :], ps_a.ap()[:, :], AF.Relu))
                        em["scalar"](sh)

                        def vh(v, it=it, tc=tc):
                            SS.wait(v)
                            VV.inc(v.tensor_tensor(h_sb.ap()[:, it * TPC + tc * CH: it * TPC + (tc + 1) * CH],
                                                   hr.ap()[:, :], hr.ap()[:, :], ALU.mult))
                        em["vector"](vh)

                for tt in range(NTT):
                    for ls in range(2):
                        def ty(t, tt=tt, ls=ls):
                            VV.wait(t)
                            for it in range(8):
                                MM.inc(t.matmul(ps_b.ap()[:, :],
                                                h_sb.ap()[:, it * TPC + tt * 128: it * TPC + tt * 128 + 128],
                                                w2_sb.ap()[:, it * LAT + ls * CH: it * LAT + (ls + 1) * CH],
                                                start=(it == 0), stop=(it == 7)))
                        em["tensor"](ty)

                        def vy(v, tt=tt, ls=ls, e=e):
                            MM.wait(v)
                            dst = y_acc.ap()[:, tt * LAT + ls * CH: tt * LAT + (ls + 1) * CH]
                            VV.inc(v.scalar_tensor_tensor(dst, ps_b.ap()[:, :],
                                                          cw_sb.ap()[:, tt * NE + e: tt * NE + e + 1],
                                                          dst, ALU.mult, ALU.add))
                        em["vector"](vy)

            for tt in range(NTT):
                def vcast(v, tt=tt):
                    MM.wait(v); VV.wait(v)
                    VV.inc(v.tensor_copy(h_sb.ap()[:, tt * LAT:(tt + 1) * LAT],
                                         y_acc.ap()[:, tt * LAT:(tt + 1) * LAT]))
                em["vector"](vcast)
            for tt in range(NTT):
                for lt in range(8):
                    def tt_(t, tt=tt, lt=lt):
                        VV.wait(t); SS.wait(t)
                        MM.inc(t.transpose(ps_t.ap()[:, :],
                                           h_sb.ap()[:, tt * LAT + lt * 128: tt * LAT + lt * 128 + 128],
                                           ident.ap()[:, :]))
                    em["tensor"](tt_)

                    def vtr(v, tt=tt, lt=lt):
                        MM.wait(v)
                        VV.inc(v.tensor_copy(yfm.ap()[:, lt * TPC + tt * 128: lt * TPC + tt * 128 + 128],
                                             ps_t.ap()[:, :]))
                    em["vector"](vtr)

        with contextlib.ExitStack() as ex3:
            sb3 = lambda *a: ex3.enter_context(nc.sbuf_tensor(*a))
            acc = sb3("acc", [128, NFT * HT], F32)
            shg = sb3("shg", [128, 16 * CH], BF16)
            fc2_sb = sb3("fc2_sb", [128, 8 * D], BF16)
            xn1_sl = sb3("xn1_sl", [128, NFT * CH], BF16)
            su_t = sb3("su_t", [128, NFT * 128], BF16)
            sd_t = sb3("sd_t", [128, 16 * 128], BF16)
            atile = sb3("atile", [128, HT], F32)

            def g2(g):
                for j in range(4):
                    GG.inc(g.dma_start(
                        out=fc2_sb.ap().rearrange("p (lt o) -> p lt o", lt=8)[:, j * 2:(j + 1) * 2, :],
                        in_=fc2_i.rearrange("(lt p) o -> p lt o", p=128)[:, j * 2:(j + 1) * 2, :]), 16)
            em["gpsimd"](g2)

            for hf in range(2):
                def lda(s, hf=hf):
                    ST.wait(s); VV.wait(s)
                    LD.inc(s.dma_start(
                        out=acc.ap().rearrange("p (ft t) -> p ft t", ft=NFT),
                        in_=x1_i.rearrange("(ft p) t -> p ft t", p=128)[:, :, hf * HT:(hf + 1) * HT]), 16)
                em["sync"](lda)
                for ft in range(NFT):
                    def lat_(s, hf=hf, ft=ft):
                        VV.wait(s)
                        LD.inc(s.dma_start(out=atile.ap()[:, :],
                                           in_=att_i[ft * 128:(ft + 1) * 128, hf * HT:(hf + 1) * HT]), 16)
                    em["sync"](lat_)

                    def vat(v, ft=ft):
                        LD.wait(v)
                        sl_ = slice(ft * HT, (ft + 1) * HT)
                        VV.inc(v.tensor_tensor(acc.ap()[:, sl_], acc.ap()[:, sl_], atile.ap()[:, :], ALU.add))
                    em["vector"](vat)

                for tc in range(2):
                    tok0 = hf * HT + tc * CH

                    def ldx(s, tok0=tok0):
                        MM.wait(s)
                        LD.inc(s.dma_start(
                            out=xn1_sl.ap().rearrange("p (ft t) -> p ft t", ft=NFT),
                            in_=xn1_i.rearrange("(ft p) t -> p ft t", p=128)[:, :, tok0:tok0 + CH]), 16)
                    em["sync"](ldx)

                    for s2 in range(2):
                        for ht in range(16):
                            hid0 = s2 * 2048 + ht * 128

                            def lsu(g, hid0=hid0):
                                MM.wait(g)
                                GG.inc(g.dma_start(
                                    out=su_t.ap().rearrange("p (ft o) -> p ft o", ft=NFT),
                                    in_=su_i.rearrange("(ft p) h -> p ft h", p=128)[:, :, hid0:hid0 + 128]), 16)
                            em["gpsimd"](lsu)

                            def tsu(t, ht=ht):
                                GG.wait(t); LD.wait(t); VV.wait(t); SS.wait(t)
                                for ft in range(NFT):
                                    MM.inc(t.matmul(ps_a.ap()[:, :],
                                                    su_t.ap()[:, ft * 128:(ft + 1) * 128],
                                                    xn1_sl.ap()[:, ft * CH:(ft + 1) * CH],
                                                    start=(ft == 0), stop=(ft == NFT - 1)))
                            em["tensor"](tsu)

                            def ssu(s):
                                MM.wait(s)
                                SS.inc(s.activation(hr.ap()[:, :], ps_a.ap()[:, :], AF.Relu))
                            em["scalar"](ssu)

                            def vsu(v, ht=ht):
                                SS.wait(v)
                                VV.inc(v.tensor_tensor(shg.ap()[:, ht * CH:(ht + 1) * CH],
                                                       hr.ap()[:, :], hr.ap()[:, :], ALU.mult))
                            em["vector"](vsu)

                        for dot in range(16):
                            def lsd(g, s2=s2, dot=dot):
                                MM.wait(g)
                                GG.inc(g.dma_start(
                                    out=sd_t.ap().rearrange("p (ht o) -> p ht o", ht=16),
                                    in_=sd_i[s2 * 2048:(s2 + 1) * 2048, dot * 128:(dot + 1) * 128]
                                        .rearrange("(ht p) o -> p ht o", p=128)), 16)
                            em["gpsimd"](lsd)

                            def tsd(t, dot=dot):
                                GG.wait(t); VV.wait(t)
                                for ht in range(16):
                                    MM.inc(t.matmul(ps_b.ap()[:, :],
                                                    sd_t.ap()[:, ht * 128:(ht + 1) * 128],
                                                    shg.ap()[:, ht * CH:(ht + 1) * CH],
                                                    start=(ht == 0), stop=(ht == 15)))
                            em["tensor"](tsd)

                            def vsd(v, dot=dot, tc=tc):
                                MM.wait(v); LD.wait(v)
                                sl_ = slice(dot * HT + tc * CH, dot * HT + tc * CH + CH)
                                VV.inc(v.tensor_tensor(acc.ap()[:, sl_], acc.ap()[:, sl_],
                                                       ps_b.ap()[:, :], ALU.add))
                            em["vector"](vsd)

                    for dot in range(16):
                        def tr2(t, dot=dot, tok0=tok0):
                            GG.wait(t); VV.wait(t)
                            for lt in range(8):
                                MM.inc(t.matmul(ps_a.ap()[:, :],
                                                fc2_sb.ap()[:, lt * D + dot * 128: lt * D + dot * 128 + 128],
                                                yfm.ap()[:, lt * TPC + tok0: lt * TPC + tok0 + CH],
                                                start=(lt == 0), stop=(lt == 7)))
                        em["tensor"](tr2)

                        def vr2(v, dot=dot, tc=tc):
                            MM.wait(v)
                            sl_ = slice(dot * HT + tc * CH, dot * HT + tc * CH + CH)
                            VV.inc(v.tensor_tensor(acc.ap()[:, sl_], acc.ap()[:, sl_],
                                                   ps_a.ap()[:, :], ALU.add))
                        em["vector"](vr2)

                def sta(s, hf=hf):
                    VV.wait(s)
                    ST.inc(s.dma_start(
                        out=o_x2.rearrange("(ft p) t -> p ft t", p=128)[:, :, hf * HT:(hf + 1) * HT],
                        in_=acc.ap().rearrange("p (ft t) -> p ft t", ft=NFT)), 16)
                em["sync"](sta)

            def fin(s):
                ST.wait(s)
            em["sync"](fin)
    return nc


def run_L7(p, r5, r4, attx):
    in_maps = [{"xn1": r5[i]["xn1"], "xlat": r5[i]["xlat"], "cw": r5[i]["cw"],
                "w1T": p["w1T"], "w2T": p["w2T"], "fc2T": p["fc2T"], "suT": p["suT"],
                "sdT": p["sdT"], "x1": r4[i]["x1"], "attx": attx[i]} for i in range(NCORE)]
    return _run("L7", build_L7, in_maps)



def _tok_owner(i):
    return i // 2, (i % 2) * TPC


def _prep_basic(inputs):
    p = {}
    x = np.ascontiguousarray(inputs["L0_attn_x"], dtype=np.float32)
    p["xfm"] = []
    for i in range(NCORE):
        b, p0 = _tok_owner(i)
        p["xfm"].append(np.ascontiguousarray(x[b, p0:p0 + TPC, :].T))
    p["WqT"] = np.ascontiguousarray(inputs["L0_attn_W_q"].T.astype(np.float32))
    p["WkvT"] = np.ascontiguousarray(inputs["L0_attn_W_kv"].T.astype(np.float32))
    p["WoT"] = np.ascontiguousarray(inputs["L0_attn_W_o"].T.astype(np.float32))
    p["WgT"] = np.ascontiguousarray(inputs["L0_ffn_w_gate"].T.astype(np.float32))
    p["WuT"] = np.ascontiguousarray(inputs["L0_ffn_w_up"].T.astype(np.float32))
    p["WdT"] = np.ascontiguousarray(inputs["L0_ffn_w_down"].T.astype(np.float32))
    p["WQ"] = np.ascontiguousarray(inputs["L1_attn_W_Q"].astype(np.float32))
    p["WK"] = np.ascontiguousarray(inputs["L1_attn_W_K"].astype(np.float32))
    p["WV"] = np.ascontiguousarray(inputs["L1_attn_W_V"].astype(np.float32))
    p["gwT"] = np.ascontiguousarray(inputs["L1_ffn_gate_w"].T.astype(np.float32))
    p["biasT"] = np.ascontiguousarray(np.tile(inputs["L1_ffn_gate_bias"].astype(np.float32)[None, :], (128, 1)))
    p["fc1T"] = np.ascontiguousarray(inputs["L1_ffn_fc1_w"].T.astype(np.float32))
    p["fc2T"] = np.ascontiguousarray(inputs["L1_ffn_fc2_w"].T.astype(np.float32))
    p["w1T"] = np.ascontiguousarray(inputs["L1_ffn_w1"].transpose(0, 2, 1).astype(np.float32))
    p["w2T"] = np.ascontiguousarray(inputs["L1_ffn_w2"].transpose(0, 2, 1).astype(np.float32))
    p["suT"] = np.ascontiguousarray(inputs["L1_ffn_shared_up_w"].T.astype(np.float32))
    p["sdT"] = np.ascontiguousarray(inputs["L1_ffn_shared_down_w"].T.astype(np.float32))
    return p


def _xpos_tables():
    half = D // 2
    j = np.arange(half, dtype=np.float64)
    scale_vec = (2.0 * j + 0.4 * D) / (1.4 * D)
    inv_freq = 1.0 / (10000.0 ** (j / half))
    out = []
    for h in range(2):
        pos = np.arange(h * TPC, (h + 1) * TPC, dtype=np.float64)
        scale = scale_vec[None, :] ** (pos[:, None] / 512.0)
        sinus = pos[:, None] * inv_freq[None, :]
        sin, cos = np.sin(sinus), np.cos(sinus)
        qc = np.repeat(cos * scale, 2, axis=1).T
        qs = np.repeat(sin * scale, 2, axis=1).T
        kc = np.repeat(cos / scale, 2, axis=1).T
        ks = np.repeat(sin / scale, 2, axis=1).T
        out.append(tuple(np.ascontiguousarray(a.astype(np.float32)) for a in (qc, qs, kc, ks)))
    return out


def _run(name, build_fn, in_maps):
    if name not in _prog_cache:
        _prog_cache[name] = build_fn()
    res = run_bass_kernel_spmd(_prog_cache[name], in_maps, core_ids=list(range(NCORE)))
    return res.results


def run_L1(p):
    in_maps = [{"xfm": p["xfm"][i], "WqT": p["WqT"], "WkvT": p["WkvT"]} for i in range(NCORE)]
    return _run("L1", build_L1, in_maps)


def kernel(**inputs):
    out, _ = kernel_fused(**inputs)
    return out



FDEBUG = False

RGPAIR = [[0, 1], [2, 3], [4, 5], [6, 7]]
NJOB = [4 * (qc + 1) + 16 for qc in range(4)]
NJOBS = sum(NJOB)


def _dtiles_fused(h):
    lg = np.log(np.float64(GAMMA))
    dm = np.arange(128)[:, None]
    dq = np.arange(512)[None, :]
    out = np.zeros((NJOBS, 128, 512), dtype=np.float32)
    j = 0
    for qc in range(4):
        q0g = h * TPC + qc * 512
        for mt in range(NJOB[qc]):
            m0 = mt * 128
            expo = (q0g + dq) - (m0 + dm)
            v = np.where(expo >= 0, np.exp(lg * expo), 0.0)
            out[j] = v.astype(np.float32)
            j += 1
    return np.ascontiguousarray(out)


def build_FUSED():
    nc = bass.Bass("TRN2", target_bir_lowering=False, debug=False)
    import contextlib
    EI = lambda n, s, d: nc.dram_tensor(n, s, d, kind="ExternalInput").ap()
    xfm = EI("xfm", [D, TPC], F32)
    WqT = EI("WqT", [D, D], F32); WkvT = EI("WkvT", [D, D], F32); WoT = EI("WoT", [D, D], F32)
    WgT = EI("WgT", [D, FH], F32); WuT = EI("WuT", [D, FH], F32); WdT = EI("WdT", [FH, D], F32)
    WQ = EI("WQ", [D, D], F32); WK = EI("WK", [D, D], F32); WV = EI("WV", [D, D], F32)
    gwT = EI("gwT", [D, NE], F32); biasT = EI("biasT", [128, NE], F32)
    fc1T = EI("fc1T", [D, LAT], F32); fc2T = EI("fc2T", [LAT, D], F32)
    w1T = EI("w1T", [NE, LAT, INTER], F32); w2T = EI("w2T", [NE, INTER, LAT], F32)
    suT = EI("suT", [D, SHH], F32); sdT = EI("sdT", [SHH, D], F32)
    qc_i = EI("qcos", [D, TPC], F32); qs_i = EI("qsin", [D, TPC], F32)
    kc_i = EI("kcos", [D, TPC], F32); ks_i = EI("ksin", [D, TPC], F32)
    dt_i = EI("dtiles", [NJOBS, 128, 512], F32)
    isc_i = EI("isc", [128, 1], F32)
    o_x2 = nc.dram_tensor("x2", [D, TPC], F32, kind="ExternalOutput").ap()
    pkind = "ExternalOutput" if FDEBUG else "Internal"
    PK = lambda n, s, d: nc.dram_tensor(n, s, d, kind=pkind).ap()
    xn_p = PK("xn_p", [D, TPC], BF16)
    qn_p = PK("qn_p", [TPC, D], BF16)
    kvn_p = PK("kvn_p", [TPC, D], BF16)
    c_p = PK("c_p", [D, TPC], F32)
    x1p_p = PK("x1p_p", [D, TPC], F32)
    x1_p = PK("x1_p", [D, TPC], F32)
    xn1_p = PK("xn1_p", [D, TPC], BF16)
    Qx_p = PK("Qx_p", [D, TPC], BF16)
    cw_p = PK("cw_p", [TPC, NE], F32)
    xlat_p = PK("xlat_p", [LAT, TPC], BF16)
    attx_p = PK("attx_p", [D, TPC], F32)
    kc_c = nc.dram_tensor("kc_c", [D, TPC], BF16).ap()
    vc_c = nc.dram_tensor("vc_c", [TPC, D], BF16).ap()
    ctot_c = nc.dram_tensor("ctot_c", [1, D], F32).ap()
    KG = nc.dram_tensor("KG", [2 * D, TPC], BF16).ap()
    VG = nc.dram_tensor("VG", [2 * TPC, D], BF16).ap()
    GTOT = nc.dram_tensor("GTOT", [2, D], F32).ap()
    Cf, SN, Ci, SNi = _dft_consts()
    HT = 1024
    NTT = TPC // 128

    with contextlib.ExitStack() as ex:
        sems = {k: ex.enter_context(nc.semaphore(k)) for k in ["LD", "MM", "VV", "SS", "GG", "ST", "CC"]}
        block = ex.enter_context(nc.Block())
        ctrs = {k: Ctr(s) for k, s in sems.items()}
        LD, MM, VV, SS, GG, ST, CC = (ctrs[k] for k in ["LD", "MM", "VV", "SS", "GG", "ST", "CC"])
        em = _mkemit(block, ctrs)
        ones_c_d = nc.inline_tensor(np.ones((128, 1), dtype=np.float32), name="onescol").ap()
        ones_r_d = nc.inline_tensor(np.ones((1, 128), dtype=np.float32), name="onesrow").ap()
        rt_d = nc.inline_tensor(_rot_const(), name="rotT").ap()
        dft_d = nc.inline_tensor(np.concatenate([Cf, SN, Ci, SNi], axis=1), name="dftc").ap()
        id_d = nc.inline_tensor(np.eye(128, dtype=np.float32), name="ident128").ap()

        with contextlib.ExitStack() as px:
            sb = lambda *a: px.enter_context(nc.sbuf_tensor(*a))
            pt = lambda *a: px.enter_context(nc.psum_tensor(*a))
            wq_sb = sb("p1wq", [128, NFT * D], BF16)
            wkv_sb = sb("p1wkv", [128, NFT * D], BF16)
            xc = sb("p1xc", [128, NFT * CH], F32)
            xb2 = sb("p1xb2", [128, NFT * CH], BF16)
            stats = sb("p1stats", [1, 5 * CH], F32)
            stats_bf = sb("p1statsbf", [1, 4 * CH], BF16)
            ones_col = sb("p1onesc", [128, 1], BF16)
            ones_row = sb("p1onesr", [1, 128], BF16)
            qbuf = sb("p1qbuf", [128, D], BF16)
            qnbuf = sb("p1qnbuf", [128, D], BF16)
            qsq = sb("p1qsq", [128, D], F32)
            s16 = sb("p1s16", [128, 16], F32)
            rn16 = sb("p1rn16", [128, 16], F32)
            ps_s1 = pt("p1s1", [1, CH], F32)
            ps_s2 = pt("p1s2", [1, CH], F32)
            ps_bm = pt("p1bm", [128, CH], F32)
            ps_br = pt("p1br", [128, CH], F32)
            ps_q0 = pt("p1q0", [128, CH], F32)
            ps_q1 = pt("p1q1", [128, CH], F32)
            bufs = dict(xc=xc, xb2=xb2, ones_col=ones_col, ones_row=ones_row, stats=stats,
                        stats_bf=stats_bf, ps_s1=ps_s1, ps_s2=ps_s2, ps_bm=ps_bm, ps_br=ps_br)

            def g0(g):
                for j in range(4):
                    GG.inc(g.dma_start(
                        out=wq_sb.ap()[:, j * 4 * D:(j + 1) * 4 * D],
                        in_=WqT.rearrange("(ft p) o -> p ft o", p=128)[:, j * 4:(j + 1) * 4, :]), 16)
                    GG.inc(g.dma_start(
                        out=wkv_sb.ap()[:, j * 4 * D:(j + 1) * 4 * D],
                        in_=WkvT.rearrange("(ft p) o -> p ft o", p=128)[:, j * 4:(j + 1) * 4, :]), 16)
                GG.inc(g.dma_start(out=ones_col.ap()[:, :], in_=ones_c_d[:, :]), 16)
                GG.inc(g.dma_start(out=ones_row.ap()[:, :], in_=ones_r_d[:, :]), 16)
            em["gpsimd"](g0)

            for c in range(NCH):
                def ld(s, c=c):
                    ST.wait(s); VV.wait(s); SS.wait(s); MM.wait(s)
                    LD.inc(s.dma_start(
                        out=xc.ap().rearrange("p (ft c) -> p ft c", ft=NFT),
                        in_=xfm.rearrange("(ft p) t -> p ft t", p=128)[:, :, c * CH:(c + 1) * CH]), 16)
                em["sync"](ld)

                _ln_stats_chunk(nc, em, bufs, ctrs, c)

                def st_xn(s, c=c):
                    VV.wait(s)
                    ST.inc(s.dma_start(
                        out=xn_p.rearrange("(ft p) t -> p ft t", p=128)[:, :, c * CH:(c + 1) * CH],
                        in_=xb2.ap().rearrange("p (ft c) -> p ft c", ft=NFT)), 16)
                em["sync"](st_xn)

                for (wsb, park) in ((wq_sb, qn_p), (wkv_sb, kvn_p)):
                    for ts in range(CH // 128):
                        for fq in range(4):
                            ps = (ps_q0, ps_q1)[fq % 2]

                            def tq(t, wsb=wsb, ts=ts, c=c, fq=fq, ps=ps):
                                GG.wait(t); VV.wait(t); SS.wait(t)
                                for ft in range(NFT):
                                    MM.inc(t.matmul(
                                        ps.ap()[:, 0:CH],
                                        xb2.ap()[:, ft * CH + ts * 128: ft * CH + ts * 128 + 128],
                                        wsb.ap()[:, ft * D + fq * CH: ft * D + (fq + 1) * CH],
                                        start=(ft == 0), stop=(ft == NFT - 1)))
                            em["tensor"](tq)

                            def ev(s, fq=fq, ps=ps):
                                MM.wait(s)
                                SS.inc(s.copy(qbuf.ap()[:, fq * CH:(fq + 1) * CH], ps.ap()[:, 0:CH]))
                            em["scalar"](ev)

                        def vn(v, ts=ts):
                            SS.wait(v)
                            VV.inc(v.tensor_tensor(qsq.ap()[:, :], qbuf.ap()[:, :], qbuf.ap()[:, :], ALU.mult))
                            VV.wait(v)
                            VV.inc(v.tensor_reduce(s16.ap()[:, :],
                                                   qsq.ap().rearrange("p (h f) -> p h f", h=H),
                                                   AX, ALU.add))
                        em["vector"](vn)

                        def sn(s, ts=ts):
                            VV.wait(s)
                            SS.inc(s.activation(rn16.ap()[:, :], s16.ap()[:, :], AF.Ln, scale=float(HD)))
                            SS.wait(s)
                            SS.inc(s.activation(rn16.ap()[:, :], rn16.ap()[:, :], AF.Exp, scale=-0.5))
                        em["scalar"](sn)

                        def vq(v, ts=ts):
                            SS.wait(v)
                            ST.wait(v)
                            for h in range(H):
                                VV.inc(v.tensor_scalar(qnbuf.ap()[:, h * HD:(h + 1) * HD],
                                                       qbuf.ap()[:, h * HD:(h + 1) * HD],
                                                       rn16.ap()[:, h:h + 1], None, ALU.mult))
                        em["vector"](vq)

                        def stq(s, park=park, ts=ts, c=c):
                            VV.wait(s)
                            ST.inc(s.dma_start(out=park[c * CH + ts * 128: c * CH + ts * 128 + 128, :],
                                               in_=qnbuf.ap()[:, :]), 16)
                            ST.wait(s)
                        em["sync"](stq)

        with contextlib.ExitStack() as px:
            sb = lambda *a: px.enter_context(nc.sbuf_tensor(*a))
            tin = sb("pcin", [128, TPC], BF16)
            tout = sb("pcout", [128, TPC], F32)
            carry = sb("pccar", [128, NFT], F32)
            iscs = sb("pcisc", [128, 1], F32)
            for ft in range(NFT):
                def ldk(s, ft=ft):
                    VV.wait(s); ST.wait(s)
                    LD.inc(s.dma_start(out=tin.ap()[:, :],
                                       in_=kvn_p[0:TPC, ft * 128:(ft + 1) * 128], transpose=True), 16)
                em["sync"](ldk)

                def vs(v, ft=ft):
                    LD.wait(v); ST.wait(v)
                    VV.inc(v.tensor_tensor_scan(tout.ap()[:, :], tin.ap()[:, :], tin.ap()[:, :],
                                                0.0, ALU.add, ALU.bypass))
                em["vector"](vs)

                def stc(s, ft=ft):
                    VV.wait(s)
                    ST.inc(s.dma_start(out=c_p[ft * 128:(ft + 1) * 128, :], in_=tout.ap()[:, :]), 16)
                    ST.inc(s.dma_start(out=ctot_c[0:1, ft * 128:(ft + 1) * 128].rearrange("a p -> p a"),
                                       in_=tout.ap()[:, TPC - 1:TPC],
                                       allow_slow_non_contiguous=True), 16)
                em["sync"](stc)

            def cc1(g):
                ST.wait(g)
                CC.inc(g.collective_compute("AllGather", ALU.bypass, replica_groups=RGPAIR,
                                            ins=[ctot_c.opt()], outs=[GTOT.opt()]))
            em["gpsimd"](cc1)

            def ldcar(s):
                CC.wait(s)
                LD.inc(s.dma_start(out=carry.ap()[:, :],
                                   in_=GTOT[0:1, :].rearrange("a (ft p) -> p (a ft)", p=128),
                                   allow_slow_non_contiguous=True), 16)
                LD.inc(s.dma_start(out=iscs.ap()[:, :], in_=isc_i), 16)
            em["sync"](ldcar)

            def vcar(v):
                LD.wait(v)
                VV.inc(v.tensor_scalar(carry.ap()[:, :], carry.ap()[:, :], iscs.ap()[:, 0:1], None, ALU.mult))
            em["vector"](vcar)

            for ft in range(NFT):
                def ldc2(s, ft=ft):
                    VV.wait(s); ST.wait(s)
                    LD.inc(s.dma_start(out=tout.ap()[:, :], in_=c_p[ft * 128:(ft + 1) * 128, :]), 16)
                em["sync"](ldc2)

                def vad(v, ft=ft):
                    LD.wait(v); ST.wait(v)
                    VV.inc(v.tensor_scalar(tout.ap()[:, :], tout.ap()[:, :],
                                           carry.ap()[:, ft:ft + 1], None, ALU.add))
                em["vector"](vad)

                def stc2(s, ft=ft):
                    VV.wait(s)
                    ST.inc(s.dma_start(out=c_p[ft * 128:(ft + 1) * 128, :], in_=tout.ap()[:, :]), 16)
                em["sync"](stc2)

        with contextlib.ExitStack() as px:
            sb = lambda *a: px.enter_context(nc.sbuf_tensor(*a))
            pt = lambda *a: px.enter_context(nc.psum_tensor(*a))
            wo_sb = sb("p3wo", [128, NFT * D], BF16)
            qn_fm = sb("p3qn", [128, NFT * CH], BF16)
            c_bf = sb("p3c", [128, NFT * CH], BF16)
            xc = sb("p3xc", [128, NFT * CH], F32)
            vh_bf = sb("p3vh", [128, NFT * CH], BF16)
            cfr = sb("p3cfr", [128, CH], F32)
            cfi = sb("p3cfi", [128, CH], F32)
            t1 = sb("p3t1", [128, CH], F32)
            t2 = sb("p3t2", [128, CH], F32)
            qvr = sb("p3qvr", [128, CH], BF16)
            qvi = sb("p3qvi", [128, CH], BF16)
            outb = sb("p3outb", [128, CH], F32)
            cdft = sb("p3cdft", [128, 4 * 128], BF16)
            ps_qr = pt("p3qr", [128, CH], F32)
            ps_qi = pt("p3qi", [128, CH], F32)
            ps_cr = pt("p3cr", [128, CH], F32)
            ps_ci = pt("p3ci", [128, CH], F32)
            ps_vh = pt("p3vhp", [128, CH], F32)
            ps_a0 = pt("p3a0", [128, CH], F32)
            ps_a1 = pt("p3a1", [128, CH], F32)
            CfS = cdft.ap()[:, 0:128]
            SNS = cdft.ap()[:, 128:256]
            CiS = cdft.ap()[:, 256:384]
            SNiS = cdft.ap()[:, 384:512]

            def g0(g):
                GG.inc(g.dma_start(out=cdft.ap()[:, :], in_=dft_d), 16)
                for j in range(4):
                    GG.inc(g.dma_start(
                        out=wo_sb.ap()[:, j * 4 * D:(j + 1) * 4 * D],
                        in_=WoT.rearrange("(ft p) o -> p ft o", p=128)[:, j * 4:(j + 1) * 4, :]), 16)
            em["gpsimd"](g0)

            for c in range(NCH):
                def ldq(s, c=c):
                    MM.wait(s); VV.wait(s)
                    for ft in range(NFT):
                        LD.inc(s.dma_start(out=qn_fm.ap()[:, ft * CH:(ft + 1) * CH],
                                           in_=qn_p[c * CH:(c + 1) * CH, ft * 128:(ft + 1) * 128],
                                           transpose=True), 16)
                    LD.inc(s.dma_start(
                        out=xc.ap().rearrange("p (ft t) -> p ft t", ft=NFT),
                        in_=xfm.rearrange("(ft p) t -> p ft t", p=128)[:, :, c * CH:(c + 1) * CH]), 16)
                em["sync"](ldq)

                def ldc(g, c=c):
                    MM.wait(g)
                    GG.inc(g.dma_start(
                        out=c_bf.ap().rearrange("p (ft t) -> p ft t", ft=NFT),
                        in_=c_p.rearrange("(ft p) t -> p ft t", p=128)[:, :, c * CH:(c + 1) * CH]), 16)
                em["gpsimd"](ldc)

                for ft in range(NFT):
                    def tf(t, ft=ft):
                        LD.wait(t); GG.wait(t); VV.wait(t)
                        MM.inc(t.matmul(ps_qr.ap()[:, :], CfS, qn_fm.ap()[:, ft * CH:(ft + 1) * CH],
                                        start=True, stop=True))
                        MM.inc(t.matmul(ps_qi.ap()[:, :], SNS, qn_fm.ap()[:, ft * CH:(ft + 1) * CH],
                                        start=True, stop=True))
                        MM.inc(t.matmul(ps_cr.ap()[:, :], CfS, c_bf.ap()[:, ft * CH:(ft + 1) * CH],
                                        start=True, stop=True))
                        MM.inc(t.matmul(ps_ci.ap()[:, :], SNS, c_bf.ap()[:, ft * CH:(ft + 1) * CH],
                                        start=True, stop=True))
                    em["tensor"](tf)

                    def se(s, ft=ft):
                        MM.wait(s)
                        SS.inc(s.copy(cfr.ap()[:, :], ps_cr.ap()[:, :]))
                        SS.inc(s.copy(cfi.ap()[:, :], ps_ci.ap()[:, :]))
                    em["scalar"](se)

                    def vc(v, ft=ft):
                        SS.wait(v); MM.wait(v)
                        VV.inc(v.tensor_tensor(t1.ap()[:, :], ps_qr.ap()[:, :], cfr.ap()[:, :], ALU.mult))
                        VV.inc(v.tensor_tensor(t2.ap()[:, :], ps_qi.ap()[:, :], cfi.ap()[:, :], ALU.mult))
                        VV.wait(v)
                        VV.inc(v.tensor_tensor(qvr.ap()[:, :], t1.ap()[:, :], t2.ap()[:, :], ALU.subtract))
                        VV.inc(v.tensor_tensor(t1.ap()[:, :], ps_qr.ap()[:, :], cfi.ap()[:, :], ALU.mult))
                        VV.inc(v.tensor_tensor(t2.ap()[:, :], ps_qi.ap()[:, :], cfr.ap()[:, :], ALU.mult))
                        VV.wait(v)
                        VV.inc(v.tensor_tensor(qvi.ap()[:, :], t1.ap()[:, :], t2.ap()[:, :], ALU.add))
                    em["vector"](vc)

                    def ti(t, ft=ft):
                        VV.wait(t)
                        MM.inc(t.matmul(ps_vh.ap()[:, :], CiS, qvr.ap()[:, :], start=True, stop=False))
                        MM.inc(t.matmul(ps_vh.ap()[:, :], SNiS, qvi.ap()[:, :], start=False, stop=True))
                    em["tensor"](ti)

                    def ve(v, ft=ft):
                        MM.wait(v)
                        VV.inc(v.tensor_copy(vh_bf.ap()[:, ft * CH:(ft + 1) * CH], ps_vh.ap()[:, :]))
                    em["vector"](ve)

                for dot in range(NFT):
                    ps = (ps_a0, ps_a1)[dot % 2]

                    def tw(t, dot=dot, ps=ps):
                        VV.wait(t); SS.wait(t)
                        for ft in range(NFT):
                            MM.inc(t.matmul(ps.ap()[:, :],
                                            wo_sb.ap()[:, ft * D + dot * 128: ft * D + dot * 128 + 128],
                                            vh_bf.ap()[:, ft * CH:(ft + 1) * CH],
                                            start=(ft == 0), stop=(ft == NFT - 1)))
                    em["tensor"](tw)

                    def vo(v, dot=dot, ps=ps):
                        MM.wait(v); ST.wait(v)
                        VV.inc(v.tensor_tensor(outb.ap()[:, :], ps.ap()[:, :],
                                               xc.ap()[:, dot * CH:(dot + 1) * CH], ALU.add))
                    em["vector"](vo)

                    def so(s, dot=dot, c=c):
                        VV.wait(s)
                        ST.inc(s.dma_start(out=x1p_p[dot * 128:(dot + 1) * 128, c * CH:(c + 1) * CH],
                                           in_=outb.ap()[:, :]), 16)
                    em["sync"](so)

        with contextlib.ExitStack() as px:
            sb = lambda *a: px.enter_context(nc.sbuf_tensor(*a))
            pt = lambda *a: px.enter_context(nc.psum_tensor(*a))
            HT4 = 1024
            xn_sb = sb("p4xn", [128, NFT * TPC], BF16)
            acc = sb("p4acc", [128, NFT * HT4], F32)
            gated = sb("p4gt", [128, 16 * HT4], BF16)
            wg_t = sb("p4wg", [128, NFT * 128], BF16)
            wu_t = sb("p4wu", [128, NFT * 128], BF16)
            wd_t = sb("p4wd", [128, 16 * 128], BF16)
            gs = sb("p4gs", [128, CH], F32)
            ps_g = pt("p4g", [128, CH], F32)
            ps_u = pt("p4u", [128, CH], F32)
            ps_d0 = pt("p4d0", [128, CH], F32)
            ps_d1 = pt("p4d1", [128, CH], F32)

            def g0(g):
                ST.wait(g)
                for j in range(4):
                    GG.inc(g.dma_start(
                        out=xn_sb.ap().rearrange("p (ft t) -> p ft t", ft=NFT)[:, j * 4:(j + 1) * 4, :],
                        in_=xn_p.rearrange("(ft p) t -> p ft t", p=128)[:, j * 4:(j + 1) * 4, :]), 16)
            em["gpsimd"](g0)

            for hf in range(2):
                def lda(s, hf=hf):
                    ST.wait(s); VV.wait(s)
                    LD.inc(s.dma_start(
                        out=acc.ap().rearrange("p (ft t) -> p ft t", ft=NFT),
                        in_=x1p_p.rearrange("(ft p) t -> p ft t", p=128)[:, :, hf * HT4:(hf + 1) * HT4]), 16)
                em["sync"](lda)

                for sl in range(4):
                    for ht in range(16):
                        hid0 = sl * 2048 + ht * 128

                        def ldw(g, hid0=hid0):
                            MM.wait(g)
                            GG.inc(g.dma_start(
                                out=wg_t.ap().rearrange("p (ft o) -> p ft o", ft=NFT),
                                in_=WgT.rearrange("(ft p) h -> p ft h", p=128)[:, :, hid0:hid0 + 128]), 16)
                            GG.inc(g.dma_start(
                                out=wu_t.ap().rearrange("p (ft o) -> p ft o", ft=NFT),
                                in_=WuT.rearrange("(ft p) h -> p ft h", p=128)[:, :, hid0:hid0 + 128]), 16)
                        em["gpsimd"](ldw)

                        for tc in range(2):
                            def tg(t, hf=hf, tc=tc):
                                GG.wait(t); VV.wait(t); SS.wait(t)
                                tok0 = hf * HT4 + tc * CH
                                for ft in range(NFT):
                                    MM.inc(t.matmul(ps_g.ap()[:, :], wg_t.ap()[:, ft * 128:(ft + 1) * 128],
                                                    xn_sb.ap()[:, ft * TPC + tok0: ft * TPC + tok0 + CH],
                                                    start=(ft == 0), stop=(ft == NFT - 1)))
                                for ft in range(NFT):
                                    MM.inc(t.matmul(ps_u.ap()[:, :], wu_t.ap()[:, ft * 128:(ft + 1) * 128],
                                                    xn_sb.ap()[:, ft * TPC + tok0: ft * TPC + tok0 + CH],
                                                    start=(ft == 0), stop=(ft == NFT - 1)))
                            em["tensor"](tg)

                            def sg(s):
                                MM.wait(s)
                                SS.inc(s.activation(gs.ap()[:, :], ps_g.ap()[:, :], AF.Silu))
                            em["scalar"](sg)

                            def vg(v, ht=ht, tc=tc):
                                SS.wait(v); MM.wait(v)
                                VV.inc(v.tensor_tensor(gated.ap()[:, ht * HT4 + tc * CH: ht * HT4 + tc * CH + CH],
                                                       gs.ap()[:, :], ps_u.ap()[:, :], ALU.mult))
                            em["vector"](vg)

                    for dot in range(16):
                        def ldd(g, sl=sl, dot=dot):
                            MM.wait(g)
                            GG.inc(g.dma_start(
                                out=wd_t.ap().rearrange("p (ht o) -> p ht o", ht=16),
                                in_=WdT[sl * 2048:(sl + 1) * 2048, dot * 128:(dot + 1) * 128]
                                    .rearrange("(ht p) o -> p ht o", p=128)), 16)
                        em["gpsimd"](ldd)

                        for tc in range(2):
                            ps = (ps_d0, ps_d1)[tc]

                            def td(t, hf=hf, tc=tc, ps=ps):
                                GG.wait(t); VV.wait(t)
                                for ht in range(16):
                                    MM.inc(t.matmul(ps.ap()[:, :], wd_t.ap()[:, ht * 128:(ht + 1) * 128],
                                                    gated.ap()[:, ht * HT4 + tc * CH: ht * HT4 + tc * CH + CH],
                                                    start=(ht == 0), stop=(ht == 15)))
                            em["tensor"](td)

                            def vd(v, dot=dot, tc=tc, ps=ps):
                                MM.wait(v); LD.wait(v)
                                sl_ = slice(dot * HT4 + tc * CH, dot * HT4 + tc * CH + CH)
                                VV.inc(v.tensor_tensor(acc.ap()[:, sl_], acc.ap()[:, sl_], ps.ap()[:, :], ALU.add))
                            em["vector"](vd)

                def sta(s, hf=hf):
                    VV.wait(s)
                    ST.inc(s.dma_start(
                        out=x1_p.rearrange("(ft p) t -> p ft t", p=128)[:, :, hf * HT4:(hf + 1) * HT4],
                        in_=acc.ap().rearrange("p (ft t) -> p ft t", ft=NFT)), 16)
                em["sync"](sta)

        with contextlib.ExitStack() as px:
            sb = lambda *a: px.enter_context(nc.sbuf_tensor(*a))
            pt = lambda *a: px.enter_context(nc.psum_tensor(*a))
            BIGC = 65536.0
            xn1_sb = sb("p5xn1", [128, NFT * TPC], BF16)
            w_sb = sb("p5w", [128, NFT * D], BF16)
            xc = sb("p5xc", [128, NFT * CH], F32)
            sq = sb("p5sq", [128, NFT * CH], BF16)
            stats = sb("p5stats", [1, 5 * CH], F32)
            stats_bf = sb("p5statsbf", [1, 4 * CH], BF16)
            ones_col = sb("p5onesc", [128, 1], BF16)
            ones_row = sb("p5onesr", [1, 128], BF16)
            rt_sb = sb("p5rt", [128, 128], BF16)
            gw_sb = sb("p5gw", [128, NFT * NE], BF16)
            bias_sb = sb("p5bias", [128, NE], F32)
            scores = sb("p5sc", [128, NE], F32)
            sbt = sb("p5sbt", [128, NE], F32)
            m8 = sb("p5m8", [128, 8], F32)
            gsb = sb("p5gsb", [128, 8], F32)
            g4 = sb("p5g4", [128, 8], F32)
            gmask = sb("p5gm", [128, 4], F32)
            ngm = sb("p5ngm", [128, 4], F32)
            me = sb("p5me", [128, NE], F32)
            selm = sb("p5sel", [128, NE], F32)
            cwt = sb("p5cwt", [128, NE], F32)
            rsum = sb("p5rs", [128, 2], F32)
            qsb = sb("p5qsb", [128, CH], BF16)
            tq1 = sb("p5tq1", [128, CH], F32)
            tq2 = sb("p5tq2", [128, CH], F32)
            qxb = sb("p5qxb", [128, CH], BF16)
            ctab = sb("p5ct", [128, CH], F32)
            stab = sb("p5st", [128, CH], F32)
            ps_q = pt("p5q", [128, CH], F32)
            ps_rot = pt("p5rot", [128, CH], F32)
            ps_lg = pt("p5lg", [128, NE], F32)
            ps_s1 = pt("p5s1", [1, CH], F32)
            ps_s2 = pt("p5s2", [1, CH], F32)
            ps_bm = pt("p5bm", [128, CH], F32)
            ps_br = pt("p5br", [128, CH], F32)

            def g0(g):
                ST.wait(g)
                GG.inc(g.dma_start(out=ones_col.ap()[:, :], in_=ones_c_d[:, :]), 16)
                GG.inc(g.dma_start(out=ones_row.ap()[:, :], in_=ones_r_d[:, :]), 16)
                GG.inc(g.dma_start(out=rt_sb.ap()[:, :], in_=rt_d), 16)
                GG.inc(g.dma_start(out=gw_sb.ap().rearrange("p (ft e) -> p ft e", ft=NFT),
                                   in_=gwT.rearrange("(ft p) e -> p ft e", p=128)), 16)
            em["gpsimd"](g0)

            def s0(s):
                LD.inc(s.dma_start(out=bias_sb.ap()[:, :], in_=biasT), 16)
            em["sync"](s0)

            def v0(v):
                VV.inc(v.memset(gsb.ap()[:, 4:8], -1e30))
            em["vector"](v0)

            bufs = dict(xc=xc, xb2=sq, ones_col=ones_col, ones_row=ones_row, stats=stats,
                        stats_bf=stats_bf, ps_s1=ps_s1, ps_s2=ps_s2, ps_bm=ps_bm, ps_br=ps_br)
            for c in range(NCH):
                def ld(s, c=c):
                    ST.wait(s); VV.wait(s); SS.wait(s); MM.wait(s)
                    LD.inc(s.dma_start(
                        out=xc.ap().rearrange("p (ft t) -> p ft t", ft=NFT),
                        in_=x1_p.rearrange("(ft p) t -> p ft t", p=128)[:, :, c * CH:(c + 1) * CH]), 16)
                em["sync"](ld)
                _ln_stats_chunk(nc, em, bufs, ctrs, c)

                def vcp(v, c=c):
                    VV.wait(v)
                    for ft in range(NFT):
                        VV.inc(v.tensor_copy(xn1_sb.ap()[:, ft * TPC + c * CH: ft * TPC + (c + 1) * CH],
                                             sq.ap()[:, ft * CH:(ft + 1) * CH]))
                em["vector"](vcp)

                def st_xn(s, c=c):
                    VV.wait(s)
                    ST.inc(s.dma_start(
                        out=xn1_p.rearrange("(ft p) t -> p ft t", p=128)[:, :, c * CH:(c + 1) * CH],
                        in_=sq.ap().rearrange("p (ft t) -> p ft t", ft=NFT)), 16)
                em["sync"](st_xn)

            for tt in range(TPC // 128):
                def tr(t, tt=tt):
                    GG.wait(t); VV.wait(t); SS.wait(t); LD.wait(t)
                    for ft in range(NFT):
                        MM.inc(t.matmul(ps_lg.ap()[:, :],
                                        xn1_sb.ap()[:, ft * TPC + tt * 128: ft * TPC + tt * 128 + 128],
                                        gw_sb.ap()[:, ft * NE:(ft + 1) * NE],
                                        start=(ft == 0), stop=(ft == NFT - 1)))
                em["tensor"](tr)

                def sr(s):
                    MM.wait(s)
                    SS.inc(s.activation(scores.ap()[:, :], ps_lg.ap()[:, :], AF.Sigmoid))
                em["scalar"](sr)

                def vr(v, tt=tt):
                    SS.wait(v); ST.wait(v)
                    VV.inc(v.tensor_tensor(sbt.ap()[:, :], scores.ap()[:, :], bias_sb.ap()[:, :], ALU.add))
                    VV.wait(v)
                    for g in range(4):
                        VV.inc(v.max(m8.ap()[:, 0:8], sbt.ap()[:, g * 8:(g + 1) * 8]))
                        VV.wait(v)
                        VV.inc(v.tensor_tensor(gsb.ap()[:, g:g + 1], m8.ap()[:, 0:1], m8.ap()[:, 1:2], ALU.add))
                        VV.wait(v)
                    VV.inc(v.max(g4.ap()[:, 0:8], gsb.ap()[:, 0:8]))
                    VV.wait(v)
                    VV.inc(v.tensor_scalar(gmask.ap()[:, :], gsb.ap()[:, 0:4], g4.ap()[:, 1:2], None, ALU.is_ge))
                    VV.wait(v)
                    VV.inc(v.tensor_scalar(ngm.ap()[:, :], gmask.ap()[:, :], -1.0, 1.0, ALU.mult, ALU.add))
                    VV.wait(v)
                    gmv = gmask.ap()[:, :].unsqueeze(2).broadcast_to([128, 4, 8])
                    ngv = ngm.ap()[:, :].unsqueeze(2).broadcast_to([128, 4, 8])
                    mev = me.ap()[:, :].rearrange("p (g e) -> p g e", g=4)
                    sbv = sbt.ap()[:, :].rearrange("p (g e) -> p g e", g=4)
                    VV.inc(v.tensor_tensor(mev, sbv, gmv, ALU.mult))
                    VV.wait(v)
                    VV.inc(v.scalar_tensor_tensor(mev, ngv, -BIGC, mev, ALU.mult, ALU.add))
                    VV.wait(v)
                    VV.inc(v.max(m8.ap()[:, 0:8], me.ap()[:, :]))
                    VV.wait(v)
                    VV.inc(v.tensor_scalar(selm.ap()[:, :], me.ap()[:, :], m8.ap()[:, 7:8], None, ALU.is_ge))
                    VV.wait(v)
                    VV.inc(v.tensor_tensor(cwt.ap()[:, :], scores.ap()[:, :], selm.ap()[:, :], ALU.mult))
                    VV.wait(v)
                    VV.inc(v.reduce_sum(rsum.ap()[:, 0:1], cwt.ap()[:, :], AX))
                    VV.wait(v)
                    VV.inc(v.reciprocal(rsum.ap()[:, 1:2], rsum.ap()[:, 0:1]))
                    VV.wait(v)
                    VV.inc(v.tensor_scalar(cwt.ap()[:, :], cwt.ap()[:, :], rsum.ap()[:, 1:2], 2.5,
                                           ALU.mult, ALU.mult))
                em["vector"](vr)

                def str_(s, tt=tt):
                    VV.wait(s)
                    ST.inc(s.dma_start(out=cw_p[tt * 128:(tt + 1) * 128, :], in_=cwt.ap()[:, :]), 16)
                em["sync"](str_)

            for (wi, ci, si, park) in ((WQ, qc_i, qs_i, Qx_p), (WK, kc_i, ks_i, kc_c)):
                def gw(g, wi=wi):
                    MM.wait(g); VV.wait(g); SS.wait(g)
                    for j in range(4):
                        GG.inc(g.dma_start(
                            out=w_sb.ap().rearrange("p (ft o) -> p ft o", ft=NFT)[:, j * 4:(j + 1) * 4, :],
                            in_=wi.rearrange("(ft p) o -> p ft o", p=128)[:, j * 4:(j + 1) * 4, :]), 16)
                em["gpsimd"](gw)
                for c in range(NCH):
                    for dqt in range(NFT):
                        def ldt(s, ci=ci, si=si, c=c, dqt=dqt):
                            VV.wait(s)
                            LD.inc(s.dma_start(out=ctab.ap()[:, :],
                                               in_=ci[dqt * 128:(dqt + 1) * 128, c * CH:(c + 1) * CH]), 16)
                            LD.inc(s.dma_start(out=stab.ap()[:, :],
                                               in_=si[dqt * 128:(dqt + 1) * 128, c * CH:(c + 1) * CH]), 16)
                        em["sync"](ldt)

                        def tp(t, c=c, dqt=dqt):
                            GG.wait(t); VV.wait(t); SS.wait(t)
                            for ft in range(NFT):
                                MM.inc(t.matmul(ps_q.ap()[:, :],
                                                w_sb.ap()[:, ft * D + dqt * 128: ft * D + dqt * 128 + 128],
                                                xn1_sb.ap()[:, ft * TPC + c * CH: ft * TPC + (c + 1) * CH],
                                                start=(ft == 0), stop=(ft == NFT - 1)))
                        em["tensor"](tp)

                        def sq_(s):
                            MM.wait(s)
                            SS.inc(s.copy(qsb.ap()[:, :], ps_q.ap()[:, :]))
                        em["scalar"](sq_)

                        def trot(t):
                            SS.wait(t)
                            MM.inc(t.matmul(ps_rot.ap()[:, :], rt_sb.ap()[:, :], qsb.ap()[:, :],
                                            start=True, stop=True))
                        em["tensor"](trot)

                        def vx(v):
                            MM.wait(v); LD.wait(v); ST.wait(v)
                            VV.inc(v.tensor_tensor(tq1.ap()[:, :], qsb.ap()[:, :], ctab.ap()[:, :], ALU.mult))
                            VV.inc(v.tensor_tensor(tq2.ap()[:, :], ps_rot.ap()[:, :], stab.ap()[:, :], ALU.mult))
                            VV.wait(v)
                            VV.inc(v.tensor_tensor(qxb.ap()[:, :], tq1.ap()[:, :], tq2.ap()[:, :], ALU.add))
                        em["vector"](vx)

                        def stq(s, park=park, c=c, dqt=dqt):
                            VV.wait(s)
                            ST.inc(s.dma_start(out=park[dqt * 128:(dqt + 1) * 128, c * CH:(c + 1) * CH],
                                               in_=qxb.ap()[:, :]), 16)
                        em["sync"](stq)

            def gv(g):
                MM.wait(g); VV.wait(g); SS.wait(g)
                for j in range(4):
                    GG.inc(g.dma_start(
                        out=w_sb.ap().rearrange("p (ft o) -> p ft o", ft=NFT)[:, j * 4:(j + 1) * 4, :],
                        in_=WV.rearrange("(ft p) o -> p ft o", p=128)[:, j * 4:(j + 1) * 4, :]), 16)
            em["gpsimd"](gv)
            for tt in range(TPC // 128):
                for dvs in range(4):
                    def tv(t, tt=tt, dvs=dvs):
                        GG.wait(t); VV.wait(t)
                        for ft in range(NFT):
                            MM.inc(t.matmul(ps_q.ap()[:, :],
                                            xn1_sb.ap()[:, ft * TPC + tt * 128: ft * TPC + tt * 128 + 128],
                                            w_sb.ap()[:, ft * D + dvs * CH: ft * D + (dvs + 1) * CH],
                                            start=(ft == 0), stop=(ft == NFT - 1)))
                    em["tensor"](tv)

                    def vv_(v):
                        MM.wait(v); ST.wait(v)
                        VV.inc(v.tensor_copy(qxb.ap()[:, :], ps_q.ap()[:, :]))
                    em["vector"](vv_)

                    def sv(s, tt=tt, dvs=dvs):
                        VV.wait(s)
                        ST.inc(s.dma_start(out=vc_c[tt * 128:(tt + 1) * 128, dvs * CH:(dvs + 1) * CH],
                                           in_=qxb.ap()[:, :]), 16)
                    em["sync"](sv)

            def gl(g):
                MM.wait(g); VV.wait(g)
                GG.inc(g.dma_start(
                    out=w_sb.ap()[:, 0:NFT * LAT].rearrange("p (ft o) -> p ft o", ft=NFT),
                    in_=fc1T.rearrange("(ft p) o -> p ft o", p=128)), 16)
            em["gpsimd"](gl)
            for latt in range(LAT // 128):
                for c in range(NCH):
                    def tl(t, latt=latt, c=c):
                        GG.wait(t); VV.wait(t)
                        for ft in range(NFT):
                            MM.inc(t.matmul(ps_q.ap()[:, :],
                                            w_sb.ap()[:, ft * LAT + latt * 128: ft * LAT + latt * 128 + 128],
                                            xn1_sb.ap()[:, ft * TPC + c * CH: ft * TPC + (c + 1) * CH],
                                            start=(ft == 0), stop=(ft == NFT - 1)))
                    em["tensor"](tl)

                    def vl(v):
                        MM.wait(v); ST.wait(v)
                        VV.inc(v.tensor_copy(qxb.ap()[:, :], ps_q.ap()[:, :]))
                    em["vector"](vl)

                    def sl_(s, latt=latt, c=c):
                        VV.wait(s)
                        ST.inc(s.dma_start(out=xlat_p[latt * 128:(latt + 1) * 128, c * CH:(c + 1) * CH],
                                           in_=qxb.ap()[:, :]), 16)
                    em["sync"](sl_)

            def cckv(g):
                ST.wait(g)
                CC.inc(g.collective_compute("AllGather", ALU.bypass, replica_groups=RGPAIR,
                                            ins=[kc_c.opt()], outs=[KG.opt()]))
                CC.inc(g.collective_compute("AllGather", ALU.bypass, replica_groups=RGPAIR,
                                            ins=[vc_c.opt()], outs=[VG.opt()]))
            em["gpsimd"](cckv)

        with contextlib.ExitStack() as px:
            sb = lambda *a: px.enter_context(nc.sbuf_tensor(*a))
            pt = lambda *a: px.enter_context(nc.psum_tensor(*a))
            qc_sb = sb("p6q", [128, NFT * CH], BF16)
            kt_sb = sb("p6k", [128, NFT * 128], BF16)
            vt_sb = sb("p6v", [128, D], BF16)
            slab = sb("p6slab", [128, 32 * CH], BF16)
            dtl = sb("p6dt", [128, CH], F32)
            ob = sb("p6ob", [128, CH], F32)
            ps_o = [pt(f"p6o{j}", [128, CH], F32) for j in range(8)]
            ps_at = ps_o[0]

            job = 0
            for qcx in range(4):
                nmt = NJOB[qcx]

                def ldq(s, qcx=qcx):
                    MM.wait(s); CC.wait(s)
                    LD.inc(s.dma_start(
                        out=qc_sb.ap().rearrange("p (ft t) -> p ft t", ft=NFT),
                        in_=Qx_p.rearrange("(ft p) t -> p ft t", p=128)[:, :, qcx * CH:(qcx + 1) * CH]), 16)
                em["sync"](ldq)

                for mt in range(nmt):
                    blk = mt // 16
                    col0 = (mt % 16) * 128

                    def ldk(g, blk=blk, col0=col0):
                        MM.wait(g); CC.wait(g)
                        GG.inc(g.dma_start(
                            out=kt_sb.ap().rearrange("p (ft t) -> p ft t", ft=NFT),
                            in_=KG[blk * D:(blk + 1) * D, col0:col0 + 128]
                                .rearrange("(ft p) t -> p ft t", p=128)), 16)
                    em["gpsimd"](ldk)

                    def ldd(s, job=job):
                        VV.wait(s)
                        LD.inc(s.dma_start(out=dtl.ap()[:, :], in_=dt_i[job]), 16)
                    em["sync"](ldd)

                    def ta(t, mt=mt):
                        GG.wait(t); LD.wait(t); VV.wait(t)
                        for ft in range(NFT):
                            MM.inc(t.matmul(ps_at.ap()[:, :],
                                            kt_sb.ap()[:, ft * 128:(ft + 1) * 128],
                                            qc_sb.ap()[:, ft * CH:(ft + 1) * CH],
                                            start=(ft == 0), stop=(ft == NFT - 1)))
                    em["tensor"](ta)

                    def va(v, mt=mt):
                        MM.wait(v); LD.wait(v)
                        VV.inc(v.tensor_tensor(slab.ap()[:, mt * CH:(mt + 1) * CH],
                                               ps_at.ap()[:, :], dtl.ap()[:, :], ALU.mult))
                    em["vector"](va)
                    job += 1

                for hp in range(2):
                    for mt in range(nmt):
                        blk = mt // 16
                        row0 = (mt % 16) * 128

                        def ldv(g, blk=blk, row0=row0):
                            MM.wait(g)
                            GG.inc(g.dma_start(out=vt_sb.ap()[:, :],
                                               in_=VG[blk * TPC + row0: blk * TPC + row0 + 128, :]), 16)
                        em["gpsimd"](ldv)

                        def tb(t, mt=mt, hp=hp, first=(mt == 0), last=(mt == nmt - 1)):
                            GG.wait(t); VV.wait(t); SS.wait(t)
                            for j in range(8):
                                MM.inc(t.matmul(ps_o[j].ap()[:, :],
                                                vt_sb.ap()[:, hp * 1024 + j * 128: hp * 1024 + (j + 1) * 128],
                                                slab.ap()[:, mt * CH:(mt + 1) * CH],
                                                start=first, stop=last))
                        em["tensor"](tb)

                    for j in range(8):
                        def vo(v, j=j):
                            MM.wait(v); ST.wait(v)
                            VV.inc(v.tensor_copy(ob.ap()[:, :], ps_o[j].ap()[:, :]))
                        em["vector"](vo)

                        def so(s, j=j, hp=hp, qcx=qcx):
                            VV.wait(s)
                            dv0 = hp * 1024 + j * 128
                            ST.inc(s.dma_start(out=attx_p[dv0:dv0 + 128, qcx * CH:(qcx + 1) * CH],
                                               in_=ob.ap()[:, :]), 16)
                        em["sync"](so)

        with contextlib.ExitStack() as px:
            sb = lambda *a: px.enter_context(nc.sbuf_tensor(*a))
            pt = lambda *a: px.enter_context(nc.psum_tensor(*a))
            HT7 = 1024
            yfm = sb("p7yfm", [128, 8 * TPC], BF16)
            cw_sb = sb("p7cw", [128, NTT * NE], F32)
            hr = sb("p7hr", [128, CH], F32)
            ob = sb("p7ob", [128, CH], F32)
            ident = sb("p7id", [128, 128], BF16)
            ps_a = pt("p7a", [128, CH], F32)
            ps_b = pt("p7b", [128, CH], F32)
            ps_t = pt("p7t", [128, 128], BF16)

            def g0(g):
                ST.wait(g)
                GG.inc(g.dma_start(out=ident.ap()[:, :], in_=id_d), 16)
                GG.inc(g.dma_start(out=cw_sb.ap().rearrange("p (tt e) -> p tt e", tt=NTT),
                                   in_=cw_p.rearrange("(tt p) e -> p tt e", p=128)), 16)
            em["gpsimd"](g0)

            with contextlib.ExitStack() as ex2:
                sb2 = lambda *a: ex2.enter_context(nc.sbuf_tensor(*a))
                y_acc = sb2("p7ya", [128, NTT * LAT], F32)
                xlat_sb = sb2("p7xl", [128, 8 * TPC], BF16)
                h_sb = sb2("p7h", [128, 8 * TPC], BF16)
                w1_sb = sb2("p7w1", [128, 8 * INTER], BF16)
                w2_sb = sb2("p7w2", [128, 8 * LAT], BF16)

                def g1(g):
                    for j in range(2):
                        GG.inc(g.dma_start(
                            out=xlat_sb.ap().rearrange("p (lt t) -> p lt t", lt=8)[:, j * 4:(j + 1) * 4, :],
                            in_=xlat_p.rearrange("(lt p) t -> p lt t", p=128)[:, j * 4:(j + 1) * 4, :]), 16)
                em["gpsimd"](g1)

                def vz(v):
                    VV.inc(v.memset(y_acc.ap()[:, :], 0.0))
                em["vector"](vz)

                for e in range(NE):
                    def gw(g, e=e):
                        MM.wait(g)
                        GG.inc(g.dma_start(out=w1_sb.ap().rearrange("p (lt i) -> p lt i", lt=8),
                                           in_=w1T[e].rearrange("(lt p) i -> p lt i", p=128)), 16)
                        GG.inc(g.dma_start(out=w2_sb.ap().rearrange("p (it l) -> p it l", it=8),
                                           in_=w2T[e].rearrange("(it p) l -> p it l", p=128)), 16)
                    em["gpsimd"](gw)

                    for it in range(8):
                        for tc in range(NCH):
                            def th(t, it=it, tc=tc):
                                GG.wait(t); VV.wait(t); SS.wait(t)
                                for lt in range(8):
                                    MM.inc(t.matmul(ps_a.ap()[:, :],
                                                    w1_sb.ap()[:, lt * INTER + it * 128: lt * INTER + it * 128 + 128],
                                                    xlat_sb.ap()[:, lt * TPC + tc * CH: lt * TPC + (tc + 1) * CH],
                                                    start=(lt == 0), stop=(lt == 7)))
                            em["tensor"](th)

                            def sh(s):
                                MM.wait(s)
                                SS.inc(s.activation(hr.ap()[:, :], ps_a.ap()[:, :], AF.Relu))
                            em["scalar"](sh)

                            def vh(v, it=it, tc=tc):
                                SS.wait(v)
                                VV.inc(v.tensor_tensor(h_sb.ap()[:, it * TPC + tc * CH: it * TPC + (tc + 1) * CH],
                                                       hr.ap()[:, :], hr.ap()[:, :], ALU.mult))
                            em["vector"](vh)

                    for tt in range(NTT):
                        for ls in range(2):
                            def ty(t, tt=tt, ls=ls):
                                VV.wait(t)
                                for it in range(8):
                                    MM.inc(t.matmul(ps_b.ap()[:, :],
                                                    h_sb.ap()[:, it * TPC + tt * 128: it * TPC + tt * 128 + 128],
                                                    w2_sb.ap()[:, it * LAT + ls * CH: it * LAT + (ls + 1) * CH],
                                                    start=(it == 0), stop=(it == 7)))
                            em["tensor"](ty)

                            def vy(v, tt=tt, ls=ls, e=e):
                                MM.wait(v)
                                dst = y_acc.ap()[:, tt * LAT + ls * CH: tt * LAT + (ls + 1) * CH]
                                VV.inc(v.scalar_tensor_tensor(dst, ps_b.ap()[:, :],
                                                              cw_sb.ap()[:, tt * NE + e: tt * NE + e + 1],
                                                              dst, ALU.mult, ALU.add))
                            em["vector"](vy)

                for tt in range(NTT):
                    def vcast(v, tt=tt):
                        MM.wait(v); VV.wait(v)
                        VV.inc(v.tensor_copy(h_sb.ap()[:, tt * LAT:(tt + 1) * LAT],
                                             y_acc.ap()[:, tt * LAT:(tt + 1) * LAT]))
                    em["vector"](vcast)
                for tt in range(NTT):
                    for lt in range(8):
                        def tt_(t, tt=tt, lt=lt):
                            VV.wait(t); SS.wait(t)
                            MM.inc(t.transpose(ps_t.ap()[:, :],
                                               h_sb.ap()[:, tt * LAT + lt * 128: tt * LAT + lt * 128 + 128],
                                               ident.ap()[:, :]))
                        em["tensor"](tt_)

                        def vtr(v, tt=tt, lt=lt):
                            MM.wait(v)
                            VV.inc(v.tensor_copy(yfm.ap()[:, lt * TPC + tt * 128: lt * TPC + tt * 128 + 128],
                                                 ps_t.ap()[:, :]))
                        em["vector"](vtr)

            with contextlib.ExitStack() as ex3:
                sb3 = lambda *a: ex3.enter_context(nc.sbuf_tensor(*a))
                acc = sb3("p7acc", [128, NFT * HT7], F32)
                shg = sb3("p7shg", [128, 16 * CH], BF16)
                fc2_sb = sb3("p7fc2", [128, 8 * D], BF16)
                xn1_sl = sb3("p7xsl", [128, NFT * CH], BF16)
                su_t = sb3("p7su", [128, NFT * 128], BF16)
                sd_t = sb3("p7sd", [128, 16 * 128], BF16)
                atile = sb3("p7at", [128, HT7], F32)

                def g2(g):
                    VV.wait(g); MM.wait(g)
                    for j in range(4):
                        GG.inc(g.dma_start(
                            out=fc2_sb.ap().rearrange("p (lt o) -> p lt o", lt=8)[:, j * 2:(j + 1) * 2, :],
                            in_=fc2T.rearrange("(lt p) o -> p lt o", p=128)[:, j * 2:(j + 1) * 2, :]), 16)
                em["gpsimd"](g2)

                for hf in range(2):
                    def lda(s, hf=hf):
                        ST.wait(s); VV.wait(s)
                        LD.inc(s.dma_start(
                            out=acc.ap().rearrange("p (ft t) -> p ft t", ft=NFT),
                            in_=x1_p.rearrange("(ft p) t -> p ft t", p=128)[:, :, hf * HT7:(hf + 1) * HT7]), 16)
                    em["sync"](lda)
                    for ft in range(NFT):
                        def lat_(s, hf=hf, ft=ft):
                            VV.wait(s)
                            LD.inc(s.dma_start(out=atile.ap()[:, :],
                                               in_=attx_p[ft * 128:(ft + 1) * 128, hf * HT7:(hf + 1) * HT7]), 16)
                        em["sync"](lat_)

                        def vat(v, ft=ft):
                            LD.wait(v)
                            sl_ = slice(ft * HT7, (ft + 1) * HT7)
                            VV.inc(v.tensor_tensor(acc.ap()[:, sl_], acc.ap()[:, sl_], atile.ap()[:, :], ALU.add))
                        em["vector"](vat)

                    for tc in range(2):
                        tok0 = hf * HT7 + tc * CH

                        def ldx(s, tok0=tok0):
                            MM.wait(s)
                            LD.inc(s.dma_start(
                                out=xn1_sl.ap().rearrange("p (ft t) -> p ft t", ft=NFT),
                                in_=xn1_p.rearrange("(ft p) t -> p ft t", p=128)[:, :, tok0:tok0 + CH]), 16)
                        em["sync"](ldx)

                        for s2 in range(2):
                            for ht in range(16):
                                hid0 = s2 * 2048 + ht * 128

                                def lsu(g, hid0=hid0):
                                    MM.wait(g)
                                    GG.inc(g.dma_start(
                                        out=su_t.ap().rearrange("p (ft o) -> p ft o", ft=NFT),
                                        in_=suT.rearrange("(ft p) h -> p ft h", p=128)[:, :, hid0:hid0 + 128]), 16)
                                em["gpsimd"](lsu)

                                def tsu(t, ht=ht):
                                    GG.wait(t); LD.wait(t); VV.wait(t); SS.wait(t)
                                    for ft in range(NFT):
                                        MM.inc(t.matmul(ps_a.ap()[:, :],
                                                        su_t.ap()[:, ft * 128:(ft + 1) * 128],
                                                        xn1_sl.ap()[:, ft * CH:(ft + 1) * CH],
                                                        start=(ft == 0), stop=(ft == NFT - 1)))
                                em["tensor"](tsu)

                                def ssu(s):
                                    MM.wait(s)
                                    SS.inc(s.activation(hr.ap()[:, :], ps_a.ap()[:, :], AF.Relu))
                                em["scalar"](ssu)

                                def vsu(v, ht=ht):
                                    SS.wait(v)
                                    VV.inc(v.tensor_tensor(shg.ap()[:, ht * CH:(ht + 1) * CH],
                                                           hr.ap()[:, :], hr.ap()[:, :], ALU.mult))
                                em["vector"](vsu)

                            for dot in range(16):
                                def lsd(g, s2=s2, dot=dot):
                                    MM.wait(g)
                                    GG.inc(g.dma_start(
                                        out=sd_t.ap().rearrange("p (ht o) -> p ht o", ht=16),
                                        in_=sdT[s2 * 2048:(s2 + 1) * 2048, dot * 128:(dot + 1) * 128]
                                            .rearrange("(ht p) o -> p ht o", p=128)), 16)
                                em["gpsimd"](lsd)

                                def tsd(t, dot=dot):
                                    GG.wait(t); VV.wait(t)
                                    for ht in range(16):
                                        MM.inc(t.matmul(ps_b.ap()[:, :],
                                                        sd_t.ap()[:, ht * 128:(ht + 1) * 128],
                                                        shg.ap()[:, ht * CH:(ht + 1) * CH],
                                                        start=(ht == 0), stop=(ht == 15)))
                                em["tensor"](tsd)

                                def vsd(v, dot=dot, tc=tc):
                                    MM.wait(v); LD.wait(v)
                                    sl_ = slice(dot * HT7 + tc * CH, dot * HT7 + tc * CH + CH)
                                    VV.inc(v.tensor_tensor(acc.ap()[:, sl_], acc.ap()[:, sl_],
                                                           ps_b.ap()[:, :], ALU.add))
                                em["vector"](vsd)

                        for dot in range(16):
                            def tr2(t, dot=dot, tok0=tok0):
                                GG.wait(t); VV.wait(t)
                                for lt in range(8):
                                    MM.inc(t.matmul(ps_a.ap()[:, :],
                                                    fc2_sb.ap()[:, lt * D + dot * 128: lt * D + dot * 128 + 128],
                                                    yfm.ap()[:, lt * TPC + tok0: lt * TPC + tok0 + CH],
                                                    start=(lt == 0), stop=(lt == 7)))
                            em["tensor"](tr2)

                            def vr2(v, dot=dot, tc=tc):
                                MM.wait(v)
                                sl_ = slice(dot * HT7 + tc * CH, dot * HT7 + tc * CH + CH)
                                VV.inc(v.tensor_tensor(acc.ap()[:, sl_], acc.ap()[:, sl_],
                                                       ps_a.ap()[:, :], ALU.add))
                            em["vector"](vr2)

                    def sta(s, hf=hf):
                        VV.wait(s)
                        ST.inc(s.dma_start(
                            out=o_x2.rearrange("(ft p) t -> p ft t", p=128)[:, :, hf * HT7:(hf + 1) * HT7],
                            in_=acc.ap().rearrange("p (ft t) -> p ft t", ft=NFT)), 16)
                    em["sync"](sta)

                def fin(s):
                    ST.wait(s)
                em["sync"](fin)
    return nc


def kernel_fused(**inputs):
    p = _prep_basic(inputs)
    tabs = _xpos_tables()
    dts = [_dtiles_fused(h) for h in range(2)]
    iscs = [np.zeros((128, 1), np.float32), np.ones((128, 1), np.float32)]
    in_maps = []
    for i in range(NCORE):
        h = i % 2
        qc, qs, kc, ks = tabs[h]
        in_maps.append({
            "xfm": p["xfm"][i], "WqT": p["WqT"], "WkvT": p["WkvT"], "WoT": p["WoT"],
            "WgT": p["WgT"], "WuT": p["WuT"], "WdT": p["WdT"],
            "WQ": p["WQ"], "WK": p["WK"], "WV": p["WV"],
            "gwT": p["gwT"], "biasT": p["biasT"], "fc1T": p["fc1T"], "fc2T": p["fc2T"],
            "w1T": p["w1T"], "w2T": p["w2T"], "suT": p["suT"], "sdT": p["sdT"],
            "qcos": qc, "qsin": qs, "kcos": kc, "ksin": ks,
            "dtiles": dts[h], "isc": iscs[h],
        })
    r = _run("FUSED", build_FUSED, in_maps)
    out = np.empty((B, S, D), dtype=np.float32)
    for i in range(NCORE):
        b, p0 = _tok_owner(i)
        out[b, p0:p0 + TPC, :] = np.asarray(r[i]["x2"], dtype=np.float32).T
    return out, r


# version 03
# speedup vs baseline: 1.2823x; 1.2823x over previous
import math
import numpy as np
import ml_dtypes
import concourse.bass as bass
import concourse.mybir as mybir
from concourse.bass_utils import run_bass_kernel_spmd

F32 = mybir.dt.float32
BF16 = mybir.dt.bfloat16
AF = mybir.ActivationFunctionType
ALU = mybir.AluOpType
AX = mybir.AxisListType.X
BF = ml_dtypes.bfloat16

NCORE = 8
B, S, D = 4, 4096, 2048
H, HD = 16, 128
TPC = 2048
CH = 512
NCH = TPC // CH
NFT = D // 128
GAMMA = 0.96875
FH = 8192
LAT = 1024
INTER = 1024
NE = 32
SHH = 4096
EPS = 1e-5

_prog_cache = {}
STRICT = "sb"


def _mkemit(block, ctrs):
    def wrap(engfn, name):
        def emit(f):
            def g(e):
                if STRICT is True or (not isinstance(STRICT, bool) and name in STRICT):
                    for c in ctrs.values():
                        c.wait(e)
                elif STRICT == "xw":
                    if name == "vector":
                        ctrs["SS"].wait(e)
                    elif name == "scalar":
                        ctrs["VV"].wait(e)
                elif STRICT == "vxw":
                    if name == "vector":
                        for c in ctrs.values():
                            c.wait(e)
                    elif name == "scalar":
                        ctrs["VV"].wait(e)
                elif STRICT == "sb":
                    if name == "scalar":
                        for c in ctrs.values():
                            c.wait(e)
                f(e)
            engfn(g)
        return emit
    return {k: wrap(getattr(block, k), k) for k in ["sync", "gpsimd", "tensor", "vector", "scalar"]}


def _phase_barrier(em, ctrs):
    for eng in ["sync", "gpsimd", "tensor", "vector", "scalar"]:
        def bar(e):
            for c in ctrs.values():
                c.wait(e)
        em[eng](bar)


class Ctr:
    def __init__(self, sem):
        self.sem = sem
        self.n = 0

    def inc(self, instr, k=1):
        instr.then_inc(self.sem, k)
        self.n += k

    def wait(self, eng):
        if self.n > 0:
            eng.wait_ge(self.sem, self.n)


def _dft_consts():
    k = np.arange(HD, dtype=np.float64)
    th = 2.0 * np.pi * np.outer(k, k) / HD
    C = np.cos(th)
    Sn = -np.sin(th)
    return (C.astype(np.float32), Sn.astype(np.float32),
            (C / HD).astype(np.float32), (Sn / HD).astype(np.float32))


def _rot_const():
    R = np.zeros((128, 128), dtype=np.float32)
    for j in range(0, 128, 2):
        R[j, j + 1] = -1.0
        R[j + 1, j] = 1.0
    return np.ascontiguousarray(R.T)


def _emit_bf16_const(nc, g_ctr, name, arr_f32, gp):
    d = nc.inline_tensor(np.ascontiguousarray(arr_f32.astype(np.float32)), name=name).ap()
    t = nc.alloc_sbuf_tensor(name + "_sb", list(arr_f32.shape), BF16)
    g_ctr.inc(gp.dma_start(out=t.ap(), in_=d), 16)
    return t


def _ln_stats_chunk(nc, em, bufs, ctrs, c):
    xc, xb2, ones_col, ones_row, st = bufs["xc"], bufs["xb2"], bufs["ones_col"], bufs["ones_row"], bufs["stats"]
    ps_s1, ps_s2, ps_bm, ps_br = bufs["ps_s1"], bufs["ps_s2"], bufs["ps_bm"], bufs["ps_br"]
    LD, MM, VV, SS, GG = ctrs["LD"], ctrs["MM"], ctrs["VV"], ctrs["SS"], ctrs["GG"]
    mean, ex2, tmp, mhf, rstd = (st.ap()[0:1, i * CH:(i + 1) * CH] for i in range(5))
    mh, ml, rh, rl = (bufs["stats_bf"].ap()[0:1, i * CH:(i + 1) * CH] for i in range(4))
    ST = ctrs["ST"]

    def sc1(s):
        LD.wait(s)
        MM.wait(s)
        ST.wait(s)
        SS.inc(s.activation(xb2.ap()[:, :], xc.ap()[:, :], AF.Square))
    em["scalar"](sc1)

    def tn1(t):
        SS.wait(t)
        for ft in range(NFT):
            MM.inc(t.matmul(ps_s2.ap()[0:1, 0:CH], ones_col.ap()[:, 0:1],
                            xb2.ap()[:, ft * CH:(ft + 1) * CH],
                            start=(ft == 0), stop=(ft == NFT - 1)))
    em["tensor"](tn1)

    def vc1(v):
        MM.wait(v)
        VV.inc(v.tensor_scalar(ex2, ps_s2.ap()[0:1, 0:CH], 1.0 / D, None, ALU.mult))
    em["vector"](vc1)

    def vc2(v):
        VV.wait(v)
        MM.wait(v)
        VV.inc(v.tensor_copy(xb2.ap()[:, :], xc.ap()[:, :]))
    em["vector"](vc2)

    def tn2(t):
        VV.wait(t)
        for ft in range(NFT):
            MM.inc(t.matmul(ps_s1.ap()[0:1, 0:CH], ones_col.ap()[:, 0:1],
                            xb2.ap()[:, ft * CH:(ft + 1) * CH],
                            start=(ft == 0), stop=(ft == NFT - 1)))
    em["tensor"](tn2)

    def vst(v):
        MM.wait(v)
        VV.inc(v.tensor_scalar(mean, ps_s1.ap()[0:1, 0:CH], 1.0 / D, None, ALU.mult))
        VV.wait(v)
        VV.inc(v.tensor_tensor(tmp, mean, mean, ALU.mult))
        VV.wait(v)
        VV.inc(v.tensor_tensor(ex2, ex2, tmp, ALU.subtract))
        VV.wait(v)
        VV.inc(v.tensor_scalar(ex2, ex2, EPS, None, ALU.add))
    em["vector"](vst)

    def sst(s):
        VV.wait(s)
        SS.inc(s.activation(tmp, ex2, AF.Ln))
        SS.wait(s)
        SS.inc(s.activation(rstd, tmp, AF.Exp, scale=-0.5))
    em["scalar"](sst)

    def vhl(v):
        SS.wait(v)
        VV.inc(v.tensor_copy(mh, mean))
        VV.wait(v)
        VV.inc(v.tensor_copy(mhf, mh))
        VV.wait(v)
        VV.inc(v.tensor_tensor(tmp, mean, mhf, ALU.subtract))
        VV.wait(v)
        VV.inc(v.tensor_copy(ml, tmp))
        VV.inc(v.tensor_copy(rh, rstd))
        VV.wait(v)
        VV.inc(v.tensor_copy(mhf, rh))
        VV.wait(v)
        VV.inc(v.tensor_tensor(tmp, rstd, mhf, ALU.subtract))
        VV.wait(v)
        VV.inc(v.tensor_copy(rl, tmp))
    em["vector"](vhl)

    def tbc(t):
        VV.wait(t)
        MM.inc(t.matmul(ps_bm.ap()[:, 0:CH], ones_row.ap()[0:1, :], mh, start=True, stop=False))
        MM.inc(t.matmul(ps_bm.ap()[:, 0:CH], ones_row.ap()[0:1, :], ml, start=False, stop=True))
        MM.inc(t.matmul(ps_br.ap()[:, 0:CH], ones_row.ap()[0:1, :], rh, start=True, stop=False))
        MM.inc(t.matmul(ps_br.ap()[:, 0:CH], ones_row.ap()[0:1, :], rl, start=False, stop=True))
    em["tensor"](tbc)

    def vxn(v):
        MM.wait(v)
        for ft in range(NFT):
            sl = slice(ft * CH, (ft + 1) * CH)
            VV.inc(v.tensor_tensor(xc.ap()[:, sl], xc.ap()[:, sl], ps_bm.ap()[:, 0:CH], ALU.subtract))
        VV.wait(v)
        for ft in range(NFT):
            sl = slice(ft * CH, (ft + 1) * CH)
            VV.inc(v.tensor_tensor(xb2.ap()[:, sl], xc.ap()[:, sl], ps_br.ap()[:, 0:CH], ALU.mult))
    em["vector"](vxn)


def build_L1():
    nc = bass.Bass("TRN2", target_bir_lowering=False, debug=False)
    xfm = nc.dram_tensor("xfm", [D, TPC], F32, kind="ExternalInput").ap()
    WqT = nc.dram_tensor("WqT", [D, D], F32, kind="ExternalInput").ap()
    WkvT = nc.dram_tensor("WkvT", [D, D], F32, kind="ExternalInput").ap()
    xn_o = nc.dram_tensor("xn", [D, TPC], BF16, kind="ExternalOutput").ap()
    qn_o = nc.dram_tensor("qn", [TPC, D], BF16, kind="ExternalOutput").ap()
    kvn_o = nc.dram_tensor("kvn", [TPC, D], BF16, kind="ExternalOutput").ap()

    import contextlib
    with contextlib.ExitStack() as ex:
        sb = lambda *a: ex.enter_context(nc.sbuf_tensor(*a))
        pt = lambda *a: ex.enter_context(nc.psum_tensor(*a))
        sem = lambda n: ex.enter_context(nc.semaphore(n))
        wq_sb = sb("wq_sb", [128, NFT * D], BF16)
        wkv_sb = sb("wkv_sb", [128, NFT * D], BF16)
        xc = sb("xc", [128, NFT * CH], F32)
        xb2 = sb("xb2", [128, NFT * CH], BF16)
        stats = sb("stats", [1, 5 * CH], F32)
        stats_bf = sb("stats_bf", [1, 4 * CH], BF16)
        ones_col = sb("ones_col", [128, 1], BF16)
        ones_row = sb("ones_row", [1, 128], BF16)
        qbuf = sb("qbuf", [128, D], BF16)
        qnbuf = sb("qnbuf", [128, D], BF16)
        qsq = sb("qsq", [128, D], F32)
        s16 = sb("s16", [128, 16], F32)
        rn16 = sb("rn16", [128, 16], F32)
        ps_s1 = pt("ps_s1", [1, CH], F32)
        ps_s2 = pt("ps_s2", [1, CH], F32)
        ps_bm = pt("ps_bm", [128, CH], F32)
        ps_br = pt("ps_br", [128, CH], F32)
        ps_q0 = pt("ps_q0", [128, CH], F32)
        ps_q1 = pt("ps_q1", [128, CH], F32)
        sLD, sMM, sVV, sSS, sGG, sST = (sem(n) for n in ["LD", "MM", "VV", "SS", "GG", "ST"])
        block = ex.enter_context(nc.Block())
        ctrs = {k: Ctr(s) for k, s in
                [("LD", sLD), ("MM", sMM), ("VV", sVV), ("SS", sSS), ("GG", sGG), ("ST", sST)]}
        LD, MM, VV, SS, GG, ST = (ctrs[k] for k in ["LD", "MM", "VV", "SS", "GG", "ST"])
        em = _mkemit(block, ctrs)
        bufs = dict(xc=xc, xb2=xb2, ones_col=ones_col, ones_row=ones_row, stats=stats,
                    stats_bf=stats_bf, ps_s1=ps_s1, ps_s2=ps_s2, ps_bm=ps_bm, ps_br=ps_br)

        ones_c_d = nc.inline_tensor(np.ones((128, 1), dtype=np.float32), name="onescol").ap()
        ones_r_d = nc.inline_tensor(np.ones((1, 128), dtype=np.float32), name="onesrow").ap()

        def g0(g):
            for j in range(4):
                GG.inc(g.dma_start(
                    out=wq_sb.ap()[:, j * 4 * D:(j + 1) * 4 * D],
                    in_=WqT.rearrange("(ft p) o -> p ft o", p=128)[:, j * 4:(j + 1) * 4, :]), 16)
                GG.inc(g.dma_start(
                    out=wkv_sb.ap()[:, j * 4 * D:(j + 1) * 4 * D],
                    in_=WkvT.rearrange("(ft p) o -> p ft o", p=128)[:, j * 4:(j + 1) * 4, :]), 16)
            GG.inc(g.dma_start(out=ones_col.ap()[:, :], in_=ones_c_d[:, :]), 16)
            GG.inc(g.dma_start(out=ones_row.ap()[:, :], in_=ones_r_d[:, :]), 16)
        em["gpsimd"](g0)

        for c in range(NCH):
            def ld(s, c=c):
                ST.wait(s)
                VV.wait(s)
                SS.wait(s)
                MM.wait(s)
                LD.inc(s.dma_start(
                    out=xc.ap().rearrange("p (ft c) -> p ft c", ft=NFT),
                    in_=xfm.rearrange("(ft p) t -> p ft t", p=128)[:, :, c * CH:(c + 1) * CH]), 16)
            em["sync"](ld)

            _ln_stats_chunk(nc, em, bufs, ctrs, c)

            def st_xn(s, c=c):
                VV.wait(s)
                ST.inc(s.dma_start(
                    out=xn_o.rearrange("(ft p) t -> p ft t", p=128)[:, :, c * CH:(c + 1) * CH],
                    in_=xb2.ap().rearrange("p (ft c) -> p ft c", ft=NFT)), 16)
            em["sync"](st_xn)

            for (wsb, park) in ((wq_sb, qn_o), (wkv_sb, kvn_o)):
                for ts in range(CH // 128):
                    for fq in range(4):
                        ps = (ps_q0, ps_q1)[fq % 2]

                        def tq(t, wsb=wsb, ts=ts, c=c, fq=fq, ps=ps):
                            GG.wait(t)
                            VV.wait(t)
                            SS.wait(t)
                            for ft in range(NFT):
                                MM.inc(t.matmul(
                                    ps.ap()[:, 0:CH],
                                    xb2.ap()[:, ft * CH + ts * 128: ft * CH + ts * 128 + 128],
                                    wsb.ap()[:, ft * D + fq * CH: ft * D + (fq + 1) * CH],
                                    start=(ft == 0), stop=(ft == NFT - 1)))
                        em["tensor"](tq)

                        def ev(s, fq=fq, ps=ps):
                            MM.wait(s)
                            SS.inc(s.copy(qbuf.ap()[:, fq * CH:(fq + 1) * CH], ps.ap()[:, 0:CH]))
                        em["scalar"](ev)
                    def vn(v, ts=ts):
                        SS.wait(v)
                        VV.inc(v.tensor_tensor(qsq.ap()[:, :], qbuf.ap()[:, :], qbuf.ap()[:, :], ALU.mult))
                        VV.wait(v)
                        VV.inc(v.tensor_reduce(s16.ap()[:, :],
                                               qsq.ap().rearrange("p (h f) -> p h f", h=H),
                                               AX, ALU.add))
                    em["vector"](vn)

                    def sn(s, ts=ts):
                        VV.wait(s)
                        SS.inc(s.activation(rn16.ap()[:, :], s16.ap()[:, :], AF.Ln, scale=float(HD)))
                        SS.wait(s)
                        SS.inc(s.activation(rn16.ap()[:, :], rn16.ap()[:, :], AF.Exp, scale=-0.5))
                    em["scalar"](sn)

                    def vq(v, ts=ts):
                        SS.wait(v)
                        ST.wait(v)
                        for h in range(H):
                            VV.inc(v.tensor_scalar(qnbuf.ap()[:, h * HD:(h + 1) * HD],
                                                   qbuf.ap()[:, h * HD:(h + 1) * HD],
                                                   rn16.ap()[:, h:h + 1], None, ALU.mult))
                    em["vector"](vq)

                    def stq(s, park=park, ts=ts, c=c):
                        VV.wait(s)
                        ST.inc(s.dma_start(out=park[c * CH + ts * 128: c * CH + ts * 128 + 128, :],
                                           in_=qnbuf.ap()[:, :]), 16)
                        ST.wait(s)
                    em["sync"](stq)
    return nc




def build_L2():
    nc = bass.Bass("TRN2", target_bir_lowering=False, debug=False)
    kin = nc.dram_tensor("kvnfs", [256, B, S], BF16, kind="ExternalInput").ap()
    cout = nc.dram_tensor("cfs", [256, B, S], F32, kind="ExternalOutput").ap()
    import contextlib
    with contextlib.ExitStack() as ex:
        tin = ex.enter_context(nc.sbuf_tensor("tin", [128, S], BF16))
        tout = ex.enter_context(nc.sbuf_tensor("tout", [128, S], F32))
        sLD = ex.enter_context(nc.semaphore("LD"))
        sVV = ex.enter_context(nc.semaphore("VV"))
        sST = ex.enter_context(nc.semaphore("ST"))
        block = ex.enter_context(nc.Block())
        ctrs = {"LD": Ctr(sLD), "VV": Ctr(sVV), "ST": Ctr(sST)}
        LD, VV, ST = ctrs["LD"], ctrs["VV"], ctrs["ST"]
        em = _mkemit(block, ctrs)
        for fh in range(2):
            for b in range(B):
                def ld(s, fh=fh, b=b):
                    VV.wait(s)
                    LD.inc(s.dma_start(out=tin.ap()[:, :], in_=kin[fh * 128:(fh + 1) * 128, b, :]), 16)
                em["sync"](ld)

                def vs(v):
                    LD.wait(v)
                    ST.wait(v)
                    VV.inc(v.tensor_tensor_scan(tout.ap()[:, :], tin.ap()[:, :], tin.ap()[:, :],
                                                0.0, ALU.add, ALU.bypass))
                em["vector"](vs)

                def st(s, fh=fh, b=b):
                    VV.wait(s)
                    ST.inc(s.dma_start(out=cout[fh * 128:(fh + 1) * 128, b, :], in_=tout.ap()[:, :]), 16)
                em["scalar"](st)

        def fin(s):
            ST.wait(s)
        em["sync"](fin)
    return nc


def run_L2(p, parks):
    kvn_full = np.empty((B, S, D), dtype=BF)
    for i in range(NCORE):
        b, p0 = _tok_owner(i)
        kvn_full[b, p0:p0 + TPC, :] = parks[i]["kvn"]
    kf = np.ascontiguousarray(kvn_full.transpose(2, 0, 1))
    in_maps = [{"kvnfs": np.ascontiguousarray(kf[i * 256:(i + 1) * 256])} for i in range(NCORE)]
    r = _run("L2", build_L2, in_maps)
    c_full = np.concatenate([r[i]["cfs"] for i in range(NCORE)], axis=0)
    c_core = []
    for i in range(NCORE):
        b, p0 = _tok_owner(i)
        c_core.append(np.ascontiguousarray(c_full[:, b, p0:p0 + TPC]))
    return c_core




def build_L3():
    nc = bass.Bass("TRN2", target_bir_lowering=False, debug=False)
    qn_i = nc.dram_tensor("qn", [TPC, D], BF16, kind="ExternalInput").ap()
    c_i = nc.dram_tensor("cfm", [D, TPC], F32, kind="ExternalInput").ap()
    x_i = nc.dram_tensor("xfm", [D, TPC], F32, kind="ExternalInput").ap()
    wo_i = nc.dram_tensor("WoT", [D, D], F32, kind="ExternalInput").ap()
    o_x1p = nc.dram_tensor("x1p", [D, TPC], F32, kind="ExternalOutput").ap()
    Cf, SN, Ci, SNi = _dft_consts()
    import contextlib
    with contextlib.ExitStack() as ex:
        sb = lambda *a: ex.enter_context(nc.sbuf_tensor(*a))
        pt = lambda *a: ex.enter_context(nc.psum_tensor(*a))
        wo_sb = sb("wo_sb", [128, NFT * D], BF16)
        qn_fm = sb("qn_fm", [128, NFT * CH], BF16)
        c_bf = sb("c_bf", [128, NFT * CH], BF16)
        xc = sb("xc", [128, NFT * CH], F32)
        vh_bf = sb("vh_bf", [128, NFT * CH], BF16)
        cfr = sb("cfr", [128, CH], F32)
        cfi = sb("cfi", [128, CH], F32)
        t1 = sb("t1", [128, CH], F32)
        t2 = sb("t2", [128, CH], F32)
        qvr = sb("qvr", [128, CH], BF16)
        qvi = sb("qvi", [128, CH], BF16)
        outb = sb("outb", [128, CH], F32)
        cdft = sb("cdft", [128, 4 * 128], BF16)
        ps_qr = pt("ps_qr", [128, CH], F32)
        ps_qi = pt("ps_qi", [128, CH], F32)
        ps_cr = pt("ps_cr", [128, CH], F32)
        ps_ci = pt("ps_ci", [128, CH], F32)
        ps_vh = pt("ps_vh", [128, CH], F32)
        ps_a0 = pt("ps_a0", [128, CH], F32)
        ps_a1 = pt("ps_a1", [128, CH], F32)
        sems = {k: ex.enter_context(nc.semaphore(k)) for k in ["LD", "MM", "VV", "SS", "GG", "ST"]}
        block = ex.enter_context(nc.Block())
        ctrs = {k: Ctr(s) for k, s in sems.items()}
        LD, MM, VV, SS, GG, ST = (ctrs[k] for k in ["LD", "MM", "VV", "SS", "GG", "ST"])
        em = _mkemit(block, ctrs)
        dft_d = nc.inline_tensor(np.concatenate([Cf, SN, Ci, SNi], axis=1), name="dftc").ap()
        CfS = cdft.ap()[:, 0:128]
        SNS = cdft.ap()[:, 128:256]
        CiS = cdft.ap()[:, 256:384]
        SNiS = cdft.ap()[:, 384:512]

        def g0(g):
            GG.inc(g.dma_start(out=cdft.ap()[:, :], in_=dft_d), 16)
            for j in range(4):
                GG.inc(g.dma_start(
                    out=wo_sb.ap()[:, j * 4 * D:(j + 1) * 4 * D],
                    in_=wo_i.rearrange("(ft p) o -> p ft o", p=128)[:, j * 4:(j + 1) * 4, :]), 16)
        em["gpsimd"](g0)

        for c in range(NCH):
            def ldq(s, c=c):
                for ft in range(NFT):
                    LD.inc(s.dma_start(out=qn_fm.ap()[:, ft * CH:(ft + 1) * CH],
                                       in_=qn_i[c * CH:(c + 1) * CH, ft * 128:(ft + 1) * 128],
                                       transpose=True), 16)
                LD.inc(s.dma_start(
                    out=xc.ap().rearrange("p (ft t) -> p ft t", ft=NFT),
                    in_=x_i.rearrange("(ft p) t -> p ft t", p=128)[:, :, c * CH:(c + 1) * CH]), 16)
            em["sync"](ldq)

            def ldc(g, c=c):
                GG.inc(g.dma_start(
                    out=c_bf.ap().rearrange("p (ft t) -> p ft t", ft=NFT),
                    in_=c_i.rearrange("(ft p) t -> p ft t", p=128)[:, :, c * CH:(c + 1) * CH]), 16)
            em["gpsimd"](ldc)

            for ft in range(NFT):
                def tf(t, ft=ft):
                    LD.wait(t); GG.wait(t); VV.wait(t)
                    MM.inc(t.matmul(ps_qr.ap()[:, :], CfS, qn_fm.ap()[:, ft * CH:(ft + 1) * CH],
                                    start=True, stop=True))
                    MM.inc(t.matmul(ps_qi.ap()[:, :], SNS, qn_fm.ap()[:, ft * CH:(ft + 1) * CH],
                                    start=True, stop=True))
                    MM.inc(t.matmul(ps_cr.ap()[:, :], CfS, c_bf.ap()[:, ft * CH:(ft + 1) * CH],
                                    start=True, stop=True))
                    MM.inc(t.matmul(ps_ci.ap()[:, :], SNS, c_bf.ap()[:, ft * CH:(ft + 1) * CH],
                                    start=True, stop=True))
                em["tensor"](tf)

                def se(s, ft=ft):
                    MM.wait(s)
                    SS.inc(s.copy(cfr.ap()[:, :], ps_cr.ap()[:, :]))
                    SS.inc(s.copy(cfi.ap()[:, :], ps_ci.ap()[:, :]))
                em["scalar"](se)

                def vc(v, ft=ft):
                    SS.wait(v); MM.wait(v)
                    VV.inc(v.tensor_tensor(t1.ap()[:, :], ps_qr.ap()[:, :], cfr.ap()[:, :], ALU.mult))
                    VV.inc(v.tensor_tensor(t2.ap()[:, :], ps_qi.ap()[:, :], cfi.ap()[:, :], ALU.mult))
                    VV.wait(v)
                    VV.inc(v.tensor_tensor(qvr.ap()[:, :], t1.ap()[:, :], t2.ap()[:, :], ALU.subtract))
                    VV.inc(v.tensor_tensor(t1.ap()[:, :], ps_qr.ap()[:, :], cfi.ap()[:, :], ALU.mult))
                    VV.inc(v.tensor_tensor(t2.ap()[:, :], ps_qi.ap()[:, :], cfr.ap()[:, :], ALU.mult))
                    VV.wait(v)
                    VV.inc(v.tensor_tensor(qvi.ap()[:, :], t1.ap()[:, :], t2.ap()[:, :], ALU.add))
                em["vector"](vc)

                def ti(t, ft=ft):
                    VV.wait(t)
                    MM.inc(t.matmul(ps_vh.ap()[:, :], CiS, qvr.ap()[:, :], start=True, stop=False))
                    MM.inc(t.matmul(ps_vh.ap()[:, :], SNiS, qvi.ap()[:, :], start=False, stop=True))
                em["tensor"](ti)

                def ve(v, ft=ft):
                    MM.wait(v)
                    VV.inc(v.tensor_copy(vh_bf.ap()[:, ft * CH:(ft + 1) * CH], ps_vh.ap()[:, :]))
                em["vector"](ve)

            for dot in range(NFT):
                ps = (ps_a0, ps_a1)[dot % 2]

                def tw(t, dot=dot, ps=ps):
                    VV.wait(t); SS.wait(t)
                    for ft in range(NFT):
                        MM.inc(t.matmul(ps.ap()[:, :],
                                        wo_sb.ap()[:, ft * D + dot * 128: ft * D + dot * 128 + 128],
                                        vh_bf.ap()[:, ft * CH:(ft + 1) * CH],
                                        start=(ft == 0), stop=(ft == NFT - 1)))
                em["tensor"](tw)

                def vo(v, dot=dot, ps=ps):
                    MM.wait(v); ST.wait(v)
                    VV.inc(v.tensor_tensor(outb.ap()[:, :], ps.ap()[:, :],
                                           xc.ap()[:, dot * CH:(dot + 1) * CH], ALU.add))
                em["vector"](vo)

                def so(s, dot=dot, c=c):
                    VV.wait(s)
                    ST.inc(s.dma_start(out=o_x1p[dot * 128:(dot + 1) * 128, c * CH:(c + 1) * CH],
                                       in_=outb.ap()[:, :]), 16)
                em["sync"](so)

        def fin(s):
            ST.wait(s)
        em["sync"](fin)
    return nc


def run_L3(p, parks, c_core):
    in_maps = [{"qn": parks[i]["qn"], "cfm": c_core[i], "xfm": p["xfm"][i], "WoT": p["WoT"]}
               for i in range(NCORE)]
    return _run("L3", build_L3, in_maps)




def build_L4():
    nc = bass.Bass("TRN2", target_bir_lowering=False, debug=False)
    xn_i = nc.dram_tensor("xn", [D, TPC], BF16, kind="ExternalInput").ap()
    x1p_i = nc.dram_tensor("x1p", [D, TPC], F32, kind="ExternalInput").ap()
    wg_i = nc.dram_tensor("WgT", [D, FH], F32, kind="ExternalInput").ap()
    wu_i = nc.dram_tensor("WuT", [D, FH], F32, kind="ExternalInput").ap()
    wd_i = nc.dram_tensor("WdT", [FH, D], F32, kind="ExternalInput").ap()
    o_x1 = nc.dram_tensor("x1", [D, TPC], F32, kind="ExternalOutput").ap()
    HT = 1024
    import contextlib
    with contextlib.ExitStack() as ex:
        sb = lambda *a: ex.enter_context(nc.sbuf_tensor(*a))
        pt = lambda *a: ex.enter_context(nc.psum_tensor(*a))
        xn_sb = sb("xn_sb", [128, NFT * TPC], BF16)
        acc = sb("acc", [128, NFT * HT], F32)
        gated = sb("gated", [128, 16 * HT], BF16)
        wg_t = sb("wg_t", [128, NFT * 128], BF16)
        wu_t = sb("wu_t", [128, NFT * 128], BF16)
        wd_t = sb("wd_t", [128, 16 * 128], BF16)
        gs = sb("gs", [128, CH], F32)
        ps_g = pt("ps_g", [128, CH], F32)
        ps_u = pt("ps_u", [128, CH], F32)
        ps_d0 = pt("ps_d0", [128, CH], F32)
        ps_d1 = pt("ps_d1", [128, CH], F32)
        sems = {k: ex.enter_context(nc.semaphore(k)) for k in ["LD", "MM", "VV", "SS", "GG", "ST"]}
        block = ex.enter_context(nc.Block())
        ctrs = {k: Ctr(s) for k, s in sems.items()}
        LD, MM, VV, SS, GG, ST = (ctrs[k] for k in ["LD", "MM", "VV", "SS", "GG", "ST"])
        em = _mkemit(block, ctrs)

        def g0(g):
            for j in range(4):
                GG.inc(g.dma_start(
                    out=xn_sb.ap().rearrange("p (ft t) -> p ft t", ft=NFT)[:, j * 4:(j + 1) * 4, :],
                    in_=xn_i.rearrange("(ft p) t -> p ft t", p=128)[:, j * 4:(j + 1) * 4, :]), 16)
        em["gpsimd"](g0)

        for hf in range(2):
            def lda(s, hf=hf):
                ST.wait(s)
                LD.inc(s.dma_start(
                    out=acc.ap().rearrange("p (ft t) -> p ft t", ft=NFT),
                    in_=x1p_i.rearrange("(ft p) t -> p ft t", p=128)[:, :, hf * HT:(hf + 1) * HT]), 16)
            em["sync"](lda)

            for sl in range(4):
                for ht in range(16):
                    hid0 = sl * 2048 + ht * 128

                    def ldw(g, hid0=hid0):
                        MM.wait(g)
                        GG.inc(g.dma_start(
                            out=wg_t.ap().rearrange("p (ft o) -> p ft o", ft=NFT),
                            in_=wg_i.rearrange("(ft p) h -> p ft h", p=128)[:, :, hid0:hid0 + 128]), 16)
                        GG.inc(g.dma_start(
                            out=wu_t.ap().rearrange("p (ft o) -> p ft o", ft=NFT),
                            in_=wu_i.rearrange("(ft p) h -> p ft h", p=128)[:, :, hid0:hid0 + 128]), 16)
                    em["gpsimd"](ldw)

                    for tc in range(2):
                        def tg(t, hf=hf, tc=tc):
                            GG.wait(t); VV.wait(t); SS.wait(t)
                            tok0 = hf * HT + tc * CH
                            for ft in range(NFT):
                                MM.inc(t.matmul(ps_g.ap()[:, :], wg_t.ap()[:, ft * 128:(ft + 1) * 128],
                                                xn_sb.ap()[:, ft * TPC + tok0: ft * TPC + tok0 + CH],
                                                start=(ft == 0), stop=(ft == NFT - 1)))
                            for ft in range(NFT):
                                MM.inc(t.matmul(ps_u.ap()[:, :], wu_t.ap()[:, ft * 128:(ft + 1) * 128],
                                                xn_sb.ap()[:, ft * TPC + tok0: ft * TPC + tok0 + CH],
                                                start=(ft == 0), stop=(ft == NFT - 1)))
                        em["tensor"](tg)

                        def sg(s):
                            MM.wait(s)
                            SS.inc(s.activation(gs.ap()[:, :], ps_g.ap()[:, :], AF.Silu))
                        em["scalar"](sg)

                        def vg(v, ht=ht, tc=tc):
                            SS.wait(v); MM.wait(v)
                            VV.inc(v.tensor_tensor(gated.ap()[:, ht * HT + tc * CH: ht * HT + tc * CH + CH],
                                                   gs.ap()[:, :], ps_u.ap()[:, :], ALU.mult))
                        em["vector"](vg)

                for dot in range(16):
                    def ldd(g, sl=sl, dot=dot):
                        MM.wait(g)
                        GG.inc(g.dma_start(
                            out=wd_t.ap().rearrange("p (ht o) -> p ht o", ht=16),
                            in_=wd_i[sl * 2048:(sl + 1) * 2048, dot * 128:(dot + 1) * 128]
                                .rearrange("(ht p) o -> p ht o", p=128)), 16)
                    em["gpsimd"](ldd)

                    for tc in range(2):
                        ps = (ps_d0, ps_d1)[tc]

                        def td(t, hf=hf, tc=tc, ps=ps):
                            GG.wait(t); VV.wait(t)
                            for ht in range(16):
                                MM.inc(t.matmul(ps.ap()[:, :], wd_t.ap()[:, ht * 128:(ht + 1) * 128],
                                                gated.ap()[:, ht * HT + tc * CH: ht * HT + tc * CH + CH],
                                                start=(ht == 0), stop=(ht == 15)))
                        em["tensor"](td)

                        def vd(v, dot=dot, tc=tc, ps=ps):
                            MM.wait(v); LD.wait(v)
                            sl_ = slice(dot * HT + tc * CH, dot * HT + tc * CH + CH)
                            VV.inc(v.tensor_tensor(acc.ap()[:, sl_], acc.ap()[:, sl_], ps.ap()[:, :], ALU.add))
                        em["vector"](vd)

            def sta(s, hf=hf):
                VV.wait(s)
                ST.inc(s.dma_start(
                    out=o_x1.rearrange("(ft p) t -> p ft t", p=128)[:, :, hf * HT:(hf + 1) * HT],
                    in_=acc.ap().rearrange("p (ft t) -> p ft t", ft=NFT)), 16)
            em["sync"](sta)

        def fin(s):
            ST.wait(s)
        em["sync"](fin)
    return nc


def run_L4(p, parks, r3):
    in_maps = [{"xn": parks[i]["xn"], "x1p": r3[i]["x1p"], "WgT": p["WgT"], "WuT": p["WuT"],
                "WdT": p["WdT"]} for i in range(NCORE)]
    return _run("L4", build_L4, in_maps)




def build_L5():
    nc = bass.Bass("TRN2", target_bir_lowering=False, debug=False)
    x1_i = nc.dram_tensor("x1", [D, TPC], F32, kind="ExternalInput").ap()
    WQ_i = nc.dram_tensor("WQ", [D, D], F32, kind="ExternalInput").ap()
    WK_i = nc.dram_tensor("WK", [D, D], F32, kind="ExternalInput").ap()
    WV_i = nc.dram_tensor("WV", [D, D], F32, kind="ExternalInput").ap()
    gw_i = nc.dram_tensor("gwT", [D, NE], F32, kind="ExternalInput").ap()
    bias_i = nc.dram_tensor("biasT", [128, NE], F32, kind="ExternalInput").ap()
    fc1_i = nc.dram_tensor("fc1T", [D, LAT], F32, kind="ExternalInput").ap()
    qc_i = nc.dram_tensor("qcos", [D, TPC], F32, kind="ExternalInput").ap()
    qs_i = nc.dram_tensor("qsin", [D, TPC], F32, kind="ExternalInput").ap()
    kc_i = nc.dram_tensor("kcos", [D, TPC], F32, kind="ExternalInput").ap()
    ks_i = nc.dram_tensor("ksin", [D, TPC], F32, kind="ExternalInput").ap()
    o_xn1 = nc.dram_tensor("xn1", [D, TPC], BF16, kind="ExternalOutput").ap()
    o_Qx = nc.dram_tensor("Qx", [D, TPC], BF16, kind="ExternalOutput").ap()
    o_Kx = nc.dram_tensor("Kx", [D, TPC], BF16, kind="ExternalOutput").ap()
    o_V = nc.dram_tensor("Vtm", [TPC, D], BF16, kind="ExternalOutput").ap()
    o_cw = nc.dram_tensor("cw", [TPC, NE], F32, kind="ExternalOutput").ap()
    o_xlat = nc.dram_tensor("xlat", [LAT, TPC], BF16, kind="ExternalOutput").ap()
    BIGC = 65536.0
    import contextlib
    with contextlib.ExitStack() as ex:
        sb = lambda *a: ex.enter_context(nc.sbuf_tensor(*a))
        pt = lambda *a: ex.enter_context(nc.psum_tensor(*a))
        xn1_sb = sb("xn1_sb", [128, NFT * TPC], BF16)
        w_sb = sb("w_sb", [128, NFT * D], BF16)
        xc = sb("xc", [128, NFT * CH], F32)
        sq = sb("sq", [128, NFT * CH], BF16)
        stats = sb("stats", [1, 5 * CH], F32)
        stats_bf = sb("stats_bf", [1, 4 * CH], BF16)
        ones_col = sb("ones_col", [128, 1], BF16)
        ones_row = sb("ones_row", [1, 128], BF16)
        rt_sb = sb("rt_sb", [128, 128], BF16)
        gw_sb = sb("gw_sb", [128, NFT * NE], BF16)
        bias_sb = sb("bias_sb", [128, NE], F32)
        scores = sb("scores", [128, NE], F32)
        sbt = sb("sbt", [128, NE], F32)
        m8 = sb("m8", [128, 8], F32)
        gsb = sb("gsb", [128, 8], F32)
        g4 = sb("g4", [128, 8], F32)
        gmask = sb("gmask", [128, 4], F32)
        ngm = sb("ngm", [128, 4], F32)
        me = sb("me", [128, NE], F32)
        selm = sb("selm", [128, NE], F32)
        cwt = sb("cwt", [128, NE], F32)
        rsum = sb("rsum", [128, 2], F32)
        qsb = sb("qsb", [128, CH], BF16)
        tq1 = sb("tq1", [128, CH], F32)
        tq2 = sb("tq2", [128, CH], F32)
        qxb = sb("qxb", [128, CH], BF16)
        ctab = sb("ctab", [128, CH], F32)
        stab = sb("stab", [128, CH], F32)
        ps_q = pt("ps_q", [128, CH], F32)
        ps_rot = pt("ps_rot", [128, CH], F32)
        ps_lg = pt("ps_lg", [128, NE], F32)
        ps_s1 = pt("ps_s1", [1, CH], F32)
        ps_s2 = pt("ps_s2", [1, CH], F32)
        ps_bm = pt("ps_bm", [128, CH], F32)
        ps_br = pt("ps_br", [128, CH], F32)
        sems = {k: ex.enter_context(nc.semaphore(k)) for k in ["LD", "MM", "VV", "SS", "GG", "ST"]}
        block = ex.enter_context(nc.Block())
        ctrs = {k: Ctr(s) for k, s in sems.items()}
        LD, MM, VV, SS, GG, ST = (ctrs[k] for k in ["LD", "MM", "VV", "SS", "GG", "ST"])
        em = _mkemit(block, ctrs)
        ones_c_d = nc.inline_tensor(np.ones((128, 1), dtype=np.float32), name="onescol").ap()
        ones_r_d = nc.inline_tensor(np.ones((1, 128), dtype=np.float32), name="onesrow").ap()
        rt_d = nc.inline_tensor(_rot_const(), name="rotT").ap()

        def g0(g):
            GG.inc(g.dma_start(out=ones_col.ap()[:, :], in_=ones_c_d[:, :]), 16)
            GG.inc(g.dma_start(out=ones_row.ap()[:, :], in_=ones_r_d[:, :]), 16)
            GG.inc(g.dma_start(out=rt_sb.ap()[:, :], in_=rt_d), 16)
            GG.inc(g.dma_start(out=gw_sb.ap().rearrange("p (ft e) -> p ft e", ft=NFT),
                               in_=gw_i.rearrange("(ft p) e -> p ft e", p=128)), 16)
        em["gpsimd"](g0)

        def s0(s):
            LD.inc(s.dma_start(out=bias_sb.ap()[:, :], in_=bias_i), 16)
        em["sync"](s0)

        def v0(v):
            VV.inc(v.memset(gsb.ap()[:, 4:8], -1e30))
        em["vector"](v0)

        bufs = dict(xc=xc, xb2=sq, ones_col=ones_col, ones_row=ones_row, stats=stats,
                    stats_bf=stats_bf, ps_s1=ps_s1, ps_s2=ps_s2, ps_bm=ps_bm, ps_br=ps_br)
        for c in range(NCH):
            def ld(s, c=c):
                ST.wait(s); VV.wait(s); SS.wait(s); MM.wait(s)
                LD.inc(s.dma_start(
                    out=xc.ap().rearrange("p (ft t) -> p ft t", ft=NFT),
                    in_=x1_i.rearrange("(ft p) t -> p ft t", p=128)[:, :, c * CH:(c + 1) * CH]), 16)
            em["sync"](ld)
            _ln_stats_chunk(nc, em, bufs, ctrs, c)
            def vcp(v, c=c):
                VV.wait(v)
                for ft in range(NFT):
                    VV.inc(v.tensor_copy(xn1_sb.ap()[:, ft * TPC + c * CH: ft * TPC + (c + 1) * CH],
                                         sq.ap()[:, ft * CH:(ft + 1) * CH]))
            em["vector"](vcp)

            def st_xn(s, c=c):
                VV.wait(s)
                ST.inc(s.dma_start(
                    out=o_xn1.rearrange("(ft p) t -> p ft t", p=128)[:, :, c * CH:(c + 1) * CH],
                    in_=sq.ap().rearrange("p (ft t) -> p ft t", ft=NFT)), 16)
            em["sync"](st_xn)

        for tt in range(TPC // 128):
            def tr(t, tt=tt):
                GG.wait(t); VV.wait(t); SS.wait(t); LD.wait(t)
                for ft in range(NFT):
                    MM.inc(t.matmul(ps_lg.ap()[:, :],
                                    xn1_sb.ap()[:, ft * TPC + tt * 128: ft * TPC + tt * 128 + 128],
                                    gw_sb.ap()[:, ft * NE:(ft + 1) * NE],
                                    start=(ft == 0), stop=(ft == NFT - 1)))
            em["tensor"](tr)

            def sr(s):
                MM.wait(s)
                SS.inc(s.activation(scores.ap()[:, :], ps_lg.ap()[:, :], AF.Sigmoid))
            em["scalar"](sr)

            def vr(v, tt=tt):
                SS.wait(v); ST.wait(v)
                VV.inc(v.tensor_tensor(sbt.ap()[:, :], scores.ap()[:, :], bias_sb.ap()[:, :], ALU.add))
                VV.wait(v)
                for g in range(4):
                    VV.inc(v.max(m8.ap()[:, 0:8], sbt.ap()[:, g * 8:(g + 1) * 8]))
                    VV.wait(v)
                    VV.inc(v.tensor_tensor(gsb.ap()[:, g:g + 1], m8.ap()[:, 0:1], m8.ap()[:, 1:2], ALU.add))
                    VV.wait(v)
                VV.inc(v.max(g4.ap()[:, 0:8], gsb.ap()[:, 0:8]))
                VV.wait(v)
                VV.inc(v.tensor_scalar(gmask.ap()[:, :], gsb.ap()[:, 0:4], g4.ap()[:, 1:2], None, ALU.is_ge))
                VV.wait(v)
                VV.inc(v.tensor_scalar(ngm.ap()[:, :], gmask.ap()[:, :], -1.0, 1.0, ALU.mult, ALU.add))
                VV.wait(v)
                gmv = gmask.ap()[:, :].unsqueeze(2).broadcast_to([128, 4, 8])
                ngv = ngm.ap()[:, :].unsqueeze(2).broadcast_to([128, 4, 8])
                mev = me.ap()[:, :].rearrange("p (g e) -> p g e", g=4)
                sbv = sbt.ap()[:, :].rearrange("p (g e) -> p g e", g=4)
                VV.inc(v.tensor_tensor(mev, sbv, gmv, ALU.mult))
                VV.wait(v)
                VV.inc(v.scalar_tensor_tensor(mev, ngv, -BIGC, mev, ALU.mult, ALU.add))
                VV.wait(v)
                VV.inc(v.max(m8.ap()[:, 0:8], me.ap()[:, :]))
                VV.wait(v)
                VV.inc(v.tensor_scalar(selm.ap()[:, :], me.ap()[:, :], m8.ap()[:, 7:8], None, ALU.is_ge))
                VV.wait(v)
                VV.inc(v.tensor_tensor(cwt.ap()[:, :], scores.ap()[:, :], selm.ap()[:, :], ALU.mult))
                VV.wait(v)
                VV.inc(v.reduce_sum(rsum.ap()[:, 0:1], cwt.ap()[:, :], AX))
                VV.wait(v)
                VV.inc(v.reciprocal(rsum.ap()[:, 1:2], rsum.ap()[:, 0:1]))
                VV.wait(v)
                VV.inc(v.tensor_scalar(cwt.ap()[:, :], cwt.ap()[:, :], rsum.ap()[:, 1:2], 2.5,
                                       ALU.mult, ALU.mult))
            em["vector"](vr)

            def str_(s, tt=tt):
                VV.wait(s)
                ST.inc(s.dma_start(out=o_cw[tt * 128:(tt + 1) * 128, :], in_=cwt.ap()[:, :]), 16)
            em["sync"](str_)

        for (wi, ci, si, park) in ((WQ_i, qc_i, qs_i, o_Qx), (WK_i, kc_i, ks_i, o_Kx)):
            def gw(g, wi=wi):
                MM.wait(g); VV.wait(g); SS.wait(g)
                for j in range(4):
                    GG.inc(g.dma_start(
                        out=w_sb.ap().rearrange("p (ft o) -> p ft o", ft=NFT)[:, j * 4:(j + 1) * 4, :],
                        in_=wi.rearrange("(ft p) o -> p ft o", p=128)[:, j * 4:(j + 1) * 4, :]), 16)
            em["gpsimd"](gw)
            for c in range(NCH):
                for dqt in range(NFT):
                    def ldt(s, ci=ci, si=si, c=c, dqt=dqt):
                        VV.wait(s)
                        LD.inc(s.dma_start(out=ctab.ap()[:, :],
                                           in_=ci[dqt * 128:(dqt + 1) * 128, c * CH:(c + 1) * CH]), 16)
                        LD.inc(s.dma_start(out=stab.ap()[:, :],
                                           in_=si[dqt * 128:(dqt + 1) * 128, c * CH:(c + 1) * CH]), 16)
                    em["sync"](ldt)

                    def tp(t, c=c, dqt=dqt):
                        GG.wait(t); VV.wait(t); SS.wait(t)
                        for ft in range(NFT):
                            MM.inc(t.matmul(ps_q.ap()[:, :],
                                            w_sb.ap()[:, ft * D + dqt * 128: ft * D + dqt * 128 + 128],
                                            xn1_sb.ap()[:, ft * TPC + c * CH: ft * TPC + (c + 1) * CH],
                                            start=(ft == 0), stop=(ft == NFT - 1)))
                    em["tensor"](tp)

                    def sq_(s):
                        MM.wait(s)
                        SS.inc(s.copy(qsb.ap()[:, :], ps_q.ap()[:, :]))
                    em["scalar"](sq_)

                    def trot(t):
                        SS.wait(t)
                        MM.inc(t.matmul(ps_rot.ap()[:, :], rt_sb.ap()[:, :], qsb.ap()[:, :],
                                        start=True, stop=True))
                    em["tensor"](trot)

                    def vx(v):
                        MM.wait(v); LD.wait(v); ST.wait(v)
                        VV.inc(v.tensor_tensor(tq1.ap()[:, :], qsb.ap()[:, :], ctab.ap()[:, :], ALU.mult))
                        VV.inc(v.tensor_tensor(tq2.ap()[:, :], ps_rot.ap()[:, :], stab.ap()[:, :], ALU.mult))
                        VV.wait(v)
                        VV.inc(v.tensor_tensor(qxb.ap()[:, :], tq1.ap()[:, :], tq2.ap()[:, :], ALU.add))
                    em["vector"](vx)

                    def stq(s, park=park, c=c, dqt=dqt):
                        VV.wait(s)
                        ST.inc(s.dma_start(out=park[dqt * 128:(dqt + 1) * 128, c * CH:(c + 1) * CH],
                                           in_=qxb.ap()[:, :]), 16)
                    em["sync"](stq)

        def gv(g):
            MM.wait(g); VV.wait(g); SS.wait(g)
            for j in range(4):
                GG.inc(g.dma_start(
                    out=w_sb.ap().rearrange("p (ft o) -> p ft o", ft=NFT)[:, j * 4:(j + 1) * 4, :],
                    in_=WV_i.rearrange("(ft p) o -> p ft o", p=128)[:, j * 4:(j + 1) * 4, :]), 16)
        em["gpsimd"](gv)
        for tt in range(TPC // 128):
            for dvs in range(4):
                def tv(t, tt=tt, dvs=dvs):
                    GG.wait(t); VV.wait(t)
                    for ft in range(NFT):
                        MM.inc(t.matmul(ps_q.ap()[:, :],
                                        xn1_sb.ap()[:, ft * TPC + tt * 128: ft * TPC + tt * 128 + 128],
                                        w_sb.ap()[:, ft * D + dvs * CH: ft * D + (dvs + 1) * CH],
                                        start=(ft == 0), stop=(ft == NFT - 1)))
                em["tensor"](tv)

                def vv_(v):
                    MM.wait(v); ST.wait(v)
                    VV.inc(v.tensor_copy(qxb.ap()[:, :], ps_q.ap()[:, :]))
                em["vector"](vv_)

                def sv(s, tt=tt, dvs=dvs):
                    VV.wait(s)
                    ST.inc(s.dma_start(out=o_V[tt * 128:(tt + 1) * 128, dvs * CH:(dvs + 1) * CH],
                                       in_=qxb.ap()[:, :]), 16)
                em["sync"](sv)

        def gl(g):
            MM.wait(g); VV.wait(g)
            GG.inc(g.dma_start(
                out=w_sb.ap()[:, 0:NFT * LAT].rearrange("p (ft o) -> p ft o", ft=NFT),
                in_=fc1_i.rearrange("(ft p) o -> p ft o", p=128)), 16)
        em["gpsimd"](gl)
        for latt in range(LAT // 128):
            for c in range(NCH):
                def tl(t, latt=latt, c=c):
                    GG.wait(t); VV.wait(t)
                    for ft in range(NFT):
                        MM.inc(t.matmul(ps_q.ap()[:, :],
                                        w_sb.ap()[:, ft * LAT + latt * 128: ft * LAT + latt * 128 + 128],
                                        xn1_sb.ap()[:, ft * TPC + c * CH: ft * TPC + (c + 1) * CH],
                                        start=(ft == 0), stop=(ft == NFT - 1)))
                em["tensor"](tl)

                def vl(v):
                    MM.wait(v); ST.wait(v)
                    VV.inc(v.tensor_copy(qxb.ap()[:, :], ps_q.ap()[:, :]))
                em["vector"](vl)

                def sl_(s, latt=latt, c=c):
                    VV.wait(s)
                    ST.inc(s.dma_start(out=o_xlat[latt * 128:(latt + 1) * 128, c * CH:(c + 1) * CH],
                                       in_=qxb.ap()[:, :]), 16)
                em["sync"](sl_)

        def fin(s):
            ST.wait(s)
        em["sync"](fin)
    return nc


def run_L5(p, r4):
    tabs = _xpos_tables()
    in_maps = []
    for i in range(NCORE):
        h = i % 2
        qc, qs, kc, ks = tabs[h]
        in_maps.append({"x1": r4[i]["x1"], "WQ": p["WQ"], "WK": p["WK"], "WV": p["WV"],
                        "gwT": p["gwT"], "biasT": p["biasT"], "fc1T": p["fc1T"],
                        "qcos": qc, "qsin": qs, "kcos": kc, "ksin": ks})
    return _run("L5", build_L5, in_maps)




def _decay_tiles():
    lg = np.log(np.float64(GAMMA))
    dm = np.arange(128)[:, None]
    dq = np.arange(512)[None, :]
    full = np.exp(lg * (dq - dm)).astype(np.float32)
    diags = []
    for o in range(4):
        v = np.exp(lg * (dq - 128 * o - dm))
        v = np.where(dm + 128 * o <= dq, v, 0.0)
        diags.append(v.astype(np.float32))
    return full, diags


def build_L6():
    nc = bass.Bass("TRN2", target_bir_lowering=False, debug=False)
    Q_i = nc.dram_tensor("Qb", [D, S], BF16, kind="ExternalInput").ap()
    K_i = nc.dram_tensor("Kb", [D, S], BF16, kind="ExternalInput").ap()
    V_i = nc.dram_tensor("Vbh", [S, 1024], BF16, kind="ExternalInput").ap()
    o_att = nc.dram_tensor("attc", [1024, S], F32, kind="ExternalOutput").ap()
    Tfull, Tdiags = _decay_tiles()
    NMT = S // 128
    import contextlib
    with contextlib.ExitStack() as ex:
        sb = lambda *a: ex.enter_context(nc.sbuf_tensor(*a))
        pt = lambda *a: ex.enter_context(nc.psum_tensor(*a))
        qc_sb = sb("qc_sb", [128, NFT * CH], BF16)
        kt_sb = sb("kt_sb", [128, NFT * 128], BF16)
        vt_sb = sb("vt_sb", [128, 1024], BF16)
        slab = sb("slab", [128, NMT * CH], BF16)
        tdk = sb("tdk", [128, 5 * CH], F32)
        ob = sb("ob", [128, CH], F32)
        ps_at = pt("ps_at", [128, CH], F32)
        ps_o = [pt(f"ps_o{j}", [128, CH], F32) for j in range(4)]
        sems = {k: ex.enter_context(nc.semaphore(k)) for k in ["LD", "MM", "VV", "SS", "GG", "ST"]}
        block = ex.enter_context(nc.Block())
        ctrs = {k: Ctr(s) for k, s in sems.items()}
        LD, MM, VV, SS, GG, ST = (ctrs[k] for k in ["LD", "MM", "VV", "SS", "GG", "ST"])
        em = _mkemit(block, ctrs)
        td_d = nc.inline_tensor(np.concatenate([Tfull] + Tdiags, axis=1), name="dectiles").ap()

        def g0(g):
            GG.inc(g.dma_start(out=tdk.ap()[:, :], in_=td_d), 16)
        em["gpsimd"](g0)

        lg = math.log(GAMMA)
        for qc in range(S // CH):
            nmt = 4 * (qc + 1)

            def ldq(s, qc=qc):
                MM.wait(s)
                LD.inc(s.dma_start(
                    out=qc_sb.ap().rearrange("p (ft t) -> p ft t", ft=NFT),
                    in_=Q_i.rearrange("(ft p) t -> p ft t", p=128)[:, :, qc * CH:(qc + 1) * CH]), 16)
            em["sync"](ldq)

            for mt in range(nmt):
                def ldk(g, mt=mt):
                    MM.wait(g)
                    GG.inc(g.dma_start(
                        out=kt_sb.ap().rearrange("p (ft t) -> p ft t", ft=NFT),
                        in_=K_i.rearrange("(ft p) t -> p ft t", p=128)[:, :, mt * 128:(mt + 1) * 128]), 16)
                em["gpsimd"](ldk)

                def ta(t, mt=mt):
                    GG.wait(t); LD.wait(t); VV.wait(t)
                    for ft in range(NFT):
                        MM.inc(t.matmul(ps_at.ap()[:, :],
                                        kt_sb.ap()[:, ft * 128:(ft + 1) * 128],
                                        qc_sb.ap()[:, ft * CH:(ft + 1) * CH],
                                        start=(ft == 0), stop=(ft == NFT - 1)))
                em["tensor"](ta)

                dist = qc * 512 - mt * 128
                if mt < 4 * qc:
                    cimm = float(np.float32(np.exp(np.float64(lg) * dist)))
                    tsl = tdk.ap()[:, 0:CH]
                else:
                    cimm = 1.0
                    o = mt - 4 * qc
                    tsl = tdk.ap()[:, (1 + o) * CH:(2 + o) * CH]

                def va(v, mt=mt, cimm=cimm, tsl=tsl):
                    MM.wait(v)
                    VV.inc(v.scalar_tensor_tensor(slab.ap()[:, mt * CH:(mt + 1) * CH],
                                                  ps_at.ap()[:, :], cimm, tsl, ALU.mult, ALU.mult))
                em["vector"](va)

            for hp in range(2):
                for mt in range(nmt):
                    def ldv(g, mt=mt, hp=hp):
                        MM.wait(g)
                        GG.inc(g.dma_start(out=vt_sb.ap()[:, 0:CH],
                                           in_=V_i[mt * 128:(mt + 1) * 128, hp * CH:(hp + 1) * CH]), 16)
                    em["gpsimd"](ldv)

                    def tb(t, mt=mt, hp=hp, first=(mt == 0), last=(mt == nmt - 1)):
                        GG.wait(t); VV.wait(t); SS.wait(t)
                        for j in range(4):
                            MM.inc(t.matmul(ps_o[j].ap()[:, :],
                                            vt_sb.ap()[:, j * 128:(j + 1) * 128],
                                            slab.ap()[:, mt * CH:(mt + 1) * CH],
                                            start=first, stop=last))
                    em["tensor"](tb)

                for j in range(4):
                    def vo(v, j=j):
                        MM.wait(v); ST.wait(v)
                        VV.inc(v.tensor_copy(ob.ap()[:, :], ps_o[j].ap()[:, :]))
                    em["vector"](vo)

                    def so(s, j=j, hp=hp, qc=qc):
                        VV.wait(s)
                        dv0 = hp * CH + j * 128
                        ST.inc(s.dma_start(out=o_att[dv0:dv0 + 128, qc * CH:(qc + 1) * CH],
                                           in_=ob.ap()[:, :]), 16)
                    em["sync"](so)

        def fin(s):
            ST.wait(s)
        em["sync"](fin)
    return nc


def run_L6(p, r5):
    in_maps = []
    for i in range(NCORE):
        b, h = i // 2, i % 2
        Qb = np.concatenate([np.asarray(r5[2 * b]["Qx"]), np.asarray(r5[2 * b + 1]["Qx"])], axis=1)
        Kb = np.concatenate([np.asarray(r5[2 * b]["Kx"]), np.asarray(r5[2 * b + 1]["Kx"])], axis=1)
        Vb = np.concatenate([np.asarray(r5[2 * b]["Vtm"]), np.asarray(r5[2 * b + 1]["Vtm"])], axis=0)
        in_maps.append({"Qb": np.ascontiguousarray(Qb), "Kb": np.ascontiguousarray(Kb),
                        "Vbh": np.ascontiguousarray(Vb[:, h * 1024:(h + 1) * 1024])})
    r = _run("L6", build_L6, in_maps)
    attx = []
    for i in range(NCORE):
        b, p0 = _tok_owner(i)
        af = np.concatenate([r[2 * b]["attc"], r[2 * b + 1]["attc"]], axis=0)
        attx.append(np.ascontiguousarray(af[:, p0:p0 + TPC]))
    return attx




def build_L7():
    nc = bass.Bass("TRN2", target_bir_lowering=False, debug=False)
    xn1_i = nc.dram_tensor("xn1", [D, TPC], BF16, kind="ExternalInput").ap()
    xlat_i = nc.dram_tensor("xlat", [LAT, TPC], BF16, kind="ExternalInput").ap()
    cw_i = nc.dram_tensor("cw", [TPC, NE], F32, kind="ExternalInput").ap()
    w1_i = nc.dram_tensor("w1T", [NE, LAT, INTER], F32, kind="ExternalInput").ap()
    w2_i = nc.dram_tensor("w2T", [NE, INTER, LAT], F32, kind="ExternalInput").ap()
    fc2_i = nc.dram_tensor("fc2T", [LAT, D], F32, kind="ExternalInput").ap()
    su_i = nc.dram_tensor("suT", [D, SHH], F32, kind="ExternalInput").ap()
    sd_i = nc.dram_tensor("sdT", [SHH, D], F32, kind="ExternalInput").ap()
    x1_i = nc.dram_tensor("x1", [D, TPC], F32, kind="ExternalInput").ap()
    att_i = nc.dram_tensor("attx", [D, TPC], F32, kind="ExternalInput").ap()
    o_x2 = nc.dram_tensor("x2", [D, TPC], F32, kind="ExternalOutput").ap()
    HT = 1024
    NTT = TPC // 128
    import contextlib
    with contextlib.ExitStack() as ex:
        sb = lambda *a: ex.enter_context(nc.sbuf_tensor(*a))
        pt = lambda *a: ex.enter_context(nc.psum_tensor(*a))
        yfm = sb("yfm", [128, 8 * TPC], BF16)
        cw_sb = sb("cw_sb", [128, NTT * NE], F32)
        hr = sb("hr", [128, CH], F32)
        ob = sb("ob", [128, CH], F32)
        ident = sb("ident", [128, 128], BF16)
        ps_a = pt("ps_a", [128, CH], F32)
        ps_b = pt("ps_b", [128, CH], F32)
        ps_t = pt("ps_t", [128, 128], BF16)
        sems = {k: ex.enter_context(nc.semaphore(k)) for k in ["LD", "MM", "VV", "SS", "GG", "ST"]}
        block = ex.enter_context(nc.Block())
        ctrs = {k: Ctr(s) for k, s in sems.items()}
        LD, MM, VV, SS, GG, ST = (ctrs[k] for k in ["LD", "MM", "VV", "SS", "GG", "ST"])
        em = _mkemit(block, ctrs)
        id_d = nc.inline_tensor(np.eye(128, dtype=np.float32), name="ident128").ap()

        def g0(g):
            GG.inc(g.dma_start(out=ident.ap()[:, :], in_=id_d), 16)
            GG.inc(g.dma_start(out=cw_sb.ap().rearrange("p (tt e) -> p tt e", tt=NTT),
                               in_=cw_i.rearrange("(tt p) e -> p tt e", p=128)), 16)
        em["gpsimd"](g0)

        with contextlib.ExitStack() as ex2:
            sb2 = lambda *a: ex2.enter_context(nc.sbuf_tensor(*a))
            y_acc = sb2("y_acc", [128, NTT * LAT], F32)
            xlat_sb = sb2("xlat_sb", [128, 8 * TPC], BF16)
            h_sb = sb2("h_sb", [128, 8 * TPC], BF16)
            w1_sb = sb2("w1_sb", [128, 8 * INTER], BF16)
            w2_sb = sb2("w2_sb", [128, 8 * LAT], BF16)

            def g1(g):
                for j in range(2):
                    GG.inc(g.dma_start(
                        out=xlat_sb.ap().rearrange("p (lt t) -> p lt t", lt=8)[:, j * 4:(j + 1) * 4, :],
                        in_=xlat_i.rearrange("(lt p) t -> p lt t", p=128)[:, j * 4:(j + 1) * 4, :]), 16)
            em["gpsimd"](g1)

            def vz(v):
                VV.inc(v.memset(y_acc.ap()[:, :], 0.0))
            em["vector"](vz)

            for e in range(NE):
                def gw(g, e=e):
                    MM.wait(g)
                    GG.inc(g.dma_start(out=w1_sb.ap().rearrange("p (lt i) -> p lt i", lt=8),
                                       in_=w1_i[e].rearrange("(lt p) i -> p lt i", p=128)), 16)
                    GG.inc(g.dma_start(out=w2_sb.ap().rearrange("p (it l) -> p it l", it=8),
                                       in_=w2_i[e].rearrange("(it p) l -> p it l", p=128)), 16)
                em["gpsimd"](gw)

                for it in range(8):
                    for tc in range(NCH):
                        def th(t, it=it, tc=tc):
                            GG.wait(t); VV.wait(t); SS.wait(t)
                            for lt in range(8):
                                MM.inc(t.matmul(ps_a.ap()[:, :],
                                                w1_sb.ap()[:, lt * INTER + it * 128: lt * INTER + it * 128 + 128],
                                                xlat_sb.ap()[:, lt * TPC + tc * CH: lt * TPC + (tc + 1) * CH],
                                                start=(lt == 0), stop=(lt == 7)))
                        em["tensor"](th)

                        def sh(s):
                            MM.wait(s)
                            SS.inc(s.activation(hr.ap()[:, :], ps_a.ap()[:, :], AF.Relu))
                        em["scalar"](sh)

                        def vh(v, it=it, tc=tc):
                            SS.wait(v)
                            VV.inc(v.tensor_tensor(h_sb.ap()[:, it * TPC + tc * CH: it * TPC + (tc + 1) * CH],
                                                   hr.ap()[:, :], hr.ap()[:, :], ALU.mult))
                        em["vector"](vh)

                for tt in range(NTT):
                    for ls in range(2):
                        def ty(t, tt=tt, ls=ls):
                            VV.wait(t)
                            for it in range(8):
                                MM.inc(t.matmul(ps_b.ap()[:, :],
                                                h_sb.ap()[:, it * TPC + tt * 128: it * TPC + tt * 128 + 128],
                                                w2_sb.ap()[:, it * LAT + ls * CH: it * LAT + (ls + 1) * CH],
                                                start=(it == 0), stop=(it == 7)))
                        em["tensor"](ty)

                        def vy(v, tt=tt, ls=ls, e=e):
                            MM.wait(v)
                            dst = y_acc.ap()[:, tt * LAT + ls * CH: tt * LAT + (ls + 1) * CH]
                            VV.inc(v.scalar_tensor_tensor(dst, ps_b.ap()[:, :],
                                                          cw_sb.ap()[:, tt * NE + e: tt * NE + e + 1],
                                                          dst, ALU.mult, ALU.add))
                        em["vector"](vy)

            for tt in range(NTT):
                def vcast(v, tt=tt):
                    MM.wait(v); VV.wait(v)
                    VV.inc(v.tensor_copy(h_sb.ap()[:, tt * LAT:(tt + 1) * LAT],
                                         y_acc.ap()[:, tt * LAT:(tt + 1) * LAT]))
                em["vector"](vcast)
            for tt in range(NTT):
                for lt in range(8):
                    def tt_(t, tt=tt, lt=lt):
                        VV.wait(t); SS.wait(t)
                        MM.inc(t.transpose(ps_t.ap()[:, :],
                                           h_sb.ap()[:, tt * LAT + lt * 128: tt * LAT + lt * 128 + 128],
                                           ident.ap()[:, :]))
                    em["tensor"](tt_)

                    def vtr(v, tt=tt, lt=lt):
                        MM.wait(v)
                        VV.inc(v.tensor_copy(yfm.ap()[:, lt * TPC + tt * 128: lt * TPC + tt * 128 + 128],
                                             ps_t.ap()[:, :]))
                    em["vector"](vtr)

        with contextlib.ExitStack() as ex3:
            sb3 = lambda *a: ex3.enter_context(nc.sbuf_tensor(*a))
            acc = sb3("acc", [128, NFT * HT], F32)
            shg = sb3("shg", [128, 16 * CH], BF16)
            fc2_sb = sb3("fc2_sb", [128, 8 * D], BF16)
            xn1_sl = sb3("xn1_sl", [128, NFT * CH], BF16)
            su_t = sb3("su_t", [128, NFT * 128], BF16)
            sd_t = sb3("sd_t", [128, 16 * 128], BF16)
            atile = sb3("atile", [128, HT], F32)

            def g2(g):
                for j in range(4):
                    GG.inc(g.dma_start(
                        out=fc2_sb.ap().rearrange("p (lt o) -> p lt o", lt=8)[:, j * 2:(j + 1) * 2, :],
                        in_=fc2_i.rearrange("(lt p) o -> p lt o", p=128)[:, j * 2:(j + 1) * 2, :]), 16)
            em["gpsimd"](g2)

            for hf in range(2):
                def lda(s, hf=hf):
                    ST.wait(s); VV.wait(s)
                    LD.inc(s.dma_start(
                        out=acc.ap().rearrange("p (ft t) -> p ft t", ft=NFT),
                        in_=x1_i.rearrange("(ft p) t -> p ft t", p=128)[:, :, hf * HT:(hf + 1) * HT]), 16)
                em["sync"](lda)
                for ft in range(NFT):
                    def lat_(s, hf=hf, ft=ft):
                        VV.wait(s)
                        LD.inc(s.dma_start(out=atile.ap()[:, :],
                                           in_=att_i[ft * 128:(ft + 1) * 128, hf * HT:(hf + 1) * HT]), 16)
                    em["sync"](lat_)

                    def vat(v, ft=ft):
                        LD.wait(v)
                        sl_ = slice(ft * HT, (ft + 1) * HT)
                        VV.inc(v.tensor_tensor(acc.ap()[:, sl_], acc.ap()[:, sl_], atile.ap()[:, :], ALU.add))
                    em["vector"](vat)

                for tc in range(2):
                    tok0 = hf * HT + tc * CH

                    def ldx(s, tok0=tok0):
                        MM.wait(s)
                        LD.inc(s.dma_start(
                            out=xn1_sl.ap().rearrange("p (ft t) -> p ft t", ft=NFT),
                            in_=xn1_i.rearrange("(ft p) t -> p ft t", p=128)[:, :, tok0:tok0 + CH]), 16)
                    em["sync"](ldx)

                    for s2 in range(2):
                        for ht in range(16):
                            hid0 = s2 * 2048 + ht * 128

                            def lsu(g, hid0=hid0):
                                MM.wait(g)
                                GG.inc(g.dma_start(
                                    out=su_t.ap().rearrange("p (ft o) -> p ft o", ft=NFT),
                                    in_=su_i.rearrange("(ft p) h -> p ft h", p=128)[:, :, hid0:hid0 + 128]), 16)
                            em["gpsimd"](lsu)

                            def tsu(t, ht=ht):
                                GG.wait(t); LD.wait(t); VV.wait(t); SS.wait(t)
                                for ft in range(NFT):
                                    MM.inc(t.matmul(ps_a.ap()[:, :],
                                                    su_t.ap()[:, ft * 128:(ft + 1) * 128],
                                                    xn1_sl.ap()[:, ft * CH:(ft + 1) * CH],
                                                    start=(ft == 0), stop=(ft == NFT - 1)))
                            em["tensor"](tsu)

                            def ssu(s):
                                MM.wait(s)
                                SS.inc(s.activation(hr.ap()[:, :], ps_a.ap()[:, :], AF.Relu))
                            em["scalar"](ssu)

                            def vsu(v, ht=ht):
                                SS.wait(v)
                                VV.inc(v.tensor_tensor(shg.ap()[:, ht * CH:(ht + 1) * CH],
                                                       hr.ap()[:, :], hr.ap()[:, :], ALU.mult))
                            em["vector"](vsu)

                        for dot in range(16):
                            def lsd(g, s2=s2, dot=dot):
                                MM.wait(g)
                                GG.inc(g.dma_start(
                                    out=sd_t.ap().rearrange("p (ht o) -> p ht o", ht=16),
                                    in_=sd_i[s2 * 2048:(s2 + 1) * 2048, dot * 128:(dot + 1) * 128]
                                        .rearrange("(ht p) o -> p ht o", p=128)), 16)
                            em["gpsimd"](lsd)

                            def tsd(t, dot=dot):
                                GG.wait(t); VV.wait(t)
                                for ht in range(16):
                                    MM.inc(t.matmul(ps_b.ap()[:, :],
                                                    sd_t.ap()[:, ht * 128:(ht + 1) * 128],
                                                    shg.ap()[:, ht * CH:(ht + 1) * CH],
                                                    start=(ht == 0), stop=(ht == 15)))
                            em["tensor"](tsd)

                            def vsd(v, dot=dot, tc=tc):
                                MM.wait(v); LD.wait(v)
                                sl_ = slice(dot * HT + tc * CH, dot * HT + tc * CH + CH)
                                VV.inc(v.tensor_tensor(acc.ap()[:, sl_], acc.ap()[:, sl_],
                                                       ps_b.ap()[:, :], ALU.add))
                            em["vector"](vsd)

                    for dot in range(16):
                        def tr2(t, dot=dot, tok0=tok0):
                            GG.wait(t); VV.wait(t)
                            for lt in range(8):
                                MM.inc(t.matmul(ps_a.ap()[:, :],
                                                fc2_sb.ap()[:, lt * D + dot * 128: lt * D + dot * 128 + 128],
                                                yfm.ap()[:, lt * TPC + tok0: lt * TPC + tok0 + CH],
                                                start=(lt == 0), stop=(lt == 7)))
                        em["tensor"](tr2)

                        def vr2(v, dot=dot, tc=tc):
                            MM.wait(v)
                            sl_ = slice(dot * HT + tc * CH, dot * HT + tc * CH + CH)
                            VV.inc(v.tensor_tensor(acc.ap()[:, sl_], acc.ap()[:, sl_],
                                                   ps_a.ap()[:, :], ALU.add))
                        em["vector"](vr2)

                def sta(s, hf=hf):
                    VV.wait(s)
                    ST.inc(s.dma_start(
                        out=o_x2.rearrange("(ft p) t -> p ft t", p=128)[:, :, hf * HT:(hf + 1) * HT],
                        in_=acc.ap().rearrange("p (ft t) -> p ft t", ft=NFT)), 16)
                em["sync"](sta)

            def fin(s):
                ST.wait(s)
            em["sync"](fin)
    return nc


def run_L7(p, r5, r4, attx):
    in_maps = [{"xn1": r5[i]["xn1"], "xlat": r5[i]["xlat"], "cw": r5[i]["cw"],
                "w1T": p["w1T"], "w2T": p["w2T"], "fc2T": p["fc2T"], "suT": p["suT"],
                "sdT": p["sdT"], "x1": r4[i]["x1"], "attx": attx[i]} for i in range(NCORE)]
    return _run("L7", build_L7, in_maps)



def _tok_owner(i):
    return i // 2, (i % 2) * TPC


def _prep_basic(inputs):
    p = {}
    x = np.ascontiguousarray(inputs["L0_attn_x"], dtype=np.float32)
    p["xfm"] = []
    for i in range(NCORE):
        b, p0 = _tok_owner(i)
        p["xfm"].append(np.ascontiguousarray(x[b, p0:p0 + TPC, :].T))
    p["WqT"] = np.ascontiguousarray(inputs["L0_attn_W_q"].T.astype(np.float32))
    p["WkvT"] = np.ascontiguousarray(inputs["L0_attn_W_kv"].T.astype(np.float32))
    p["WoT"] = np.ascontiguousarray(inputs["L0_attn_W_o"].T.astype(np.float32))
    p["WgT"] = np.ascontiguousarray(inputs["L0_ffn_w_gate"].T.astype(np.float32))
    p["WuT"] = np.ascontiguousarray(inputs["L0_ffn_w_up"].T.astype(np.float32))
    p["WdT"] = np.ascontiguousarray(inputs["L0_ffn_w_down"].T.astype(np.float32))
    p["WQ"] = np.ascontiguousarray(inputs["L1_attn_W_Q"].astype(np.float32))
    p["WK"] = np.ascontiguousarray(inputs["L1_attn_W_K"].astype(np.float32))
    p["WV"] = np.ascontiguousarray(inputs["L1_attn_W_V"].astype(np.float32))
    p["gwT"] = np.ascontiguousarray(inputs["L1_ffn_gate_w"].T.astype(np.float32))
    p["biasT"] = np.ascontiguousarray(np.tile(inputs["L1_ffn_gate_bias"].astype(np.float32)[None, :], (128, 1)))
    p["fc1T"] = np.ascontiguousarray(inputs["L1_ffn_fc1_w"].T.astype(np.float32))
    p["fc2T"] = np.ascontiguousarray(inputs["L1_ffn_fc2_w"].T.astype(np.float32))
    p["w1T"] = np.ascontiguousarray(inputs["L1_ffn_w1"].transpose(0, 2, 1).astype(np.float32))
    p["w2T"] = np.ascontiguousarray(inputs["L1_ffn_w2"].transpose(0, 2, 1).astype(np.float32))
    p["suT"] = np.ascontiguousarray(inputs["L1_ffn_shared_up_w"].T.astype(np.float32))
    p["sdT"] = np.ascontiguousarray(inputs["L1_ffn_shared_down_w"].T.astype(np.float32))
    return p


def _xpos_tables():
    half = D // 2
    j = np.arange(half, dtype=np.float64)
    scale_vec = (2.0 * j + 0.4 * D) / (1.4 * D)
    inv_freq = 1.0 / (10000.0 ** (j / half))
    out = []
    for h in range(2):
        pos = np.arange(h * TPC, (h + 1) * TPC, dtype=np.float64)
        scale = scale_vec[None, :] ** (pos[:, None] / 512.0)
        sinus = pos[:, None] * inv_freq[None, :]
        sin, cos = np.sin(sinus), np.cos(sinus)
        qc = np.repeat(cos * scale, 2, axis=1).T
        qs = np.repeat(sin * scale, 2, axis=1).T
        kc = np.repeat(cos / scale, 2, axis=1).T
        ks = np.repeat(sin / scale, 2, axis=1).T
        out.append(tuple(np.ascontiguousarray(a.astype(np.float32)) for a in (qc, qs, kc, ks)))
    return out


def _run(name, build_fn, in_maps):
    if name not in _prog_cache:
        _prog_cache[name] = build_fn()
    res = run_bass_kernel_spmd(_prog_cache[name], in_maps, core_ids=list(range(NCORE)))
    return res.results


def run_L1(p):
    in_maps = [{"xfm": p["xfm"][i], "WqT": p["WqT"], "WkvT": p["WkvT"]} for i in range(NCORE)]
    return _run("L1", build_L1, in_maps)


def kernel(**inputs):
    out, _ = kernel_fused(**inputs)
    return out



FDEBUG = False

RGPAIR = [[0, 1], [2, 3], [4, 5], [6, 7]]
NJOB = [4 * (qc + 1) + 16 for qc in range(4)]
NJOBS = sum(NJOB)


def _dtiles_fused(h):
    lg = np.log(np.float64(GAMMA))
    dm = np.arange(128)[:, None]
    dq = np.arange(512)[None, :]
    out = np.zeros((NJOBS, 128, 512), dtype=np.float32)
    j = 0
    for qc in range(4):
        q0g = h * TPC + qc * 512
        for mt in range(NJOB[qc]):
            m0 = mt * 128
            expo = (q0g + dq) - (m0 + dm)
            v = np.where(expo >= 0, np.exp(lg * expo), 0.0)
            out[j] = v.astype(np.float32)
            j += 1
    return np.ascontiguousarray(out)


def build_FUSED():
    nc = bass.Bass("TRN2", target_bir_lowering=False, debug=False)
    import contextlib
    EI = lambda n, s, d: nc.dram_tensor(n, s, d, kind="ExternalInput").ap()
    xfm = EI("xfm", [D, TPC], F32)
    WqT = EI("WqT", [D, D], F32); WkvT = EI("WkvT", [D, D], F32); WoT = EI("WoT", [D, D], F32)
    WgT = EI("WgT", [D, FH], F32); WuT = EI("WuT", [D, FH], F32); WdT = EI("WdT", [FH, D], F32)
    WQ = EI("WQ", [D, D], F32); WK = EI("WK", [D, D], F32); WV = EI("WV", [D, D], F32)
    gwT = EI("gwT", [D, NE], F32); biasT = EI("biasT", [128, NE], F32)
    fc1T = EI("fc1T", [D, LAT], F32); fc2T = EI("fc2T", [LAT, D], F32)
    w1T = EI("w1T", [NE, LAT, INTER], F32); w2T = EI("w2T", [NE, INTER, LAT], F32)
    suT = EI("suT", [D, SHH], F32); sdT = EI("sdT", [SHH, D], F32)
    qc_i = EI("qcos", [D, TPC], F32); qs_i = EI("qsin", [D, TPC], F32)
    kc_i = EI("kcos", [D, TPC], F32); ks_i = EI("ksin", [D, TPC], F32)
    dt_i = EI("dtiles", [NJOBS, 128, 512], F32)
    isc_i = EI("isc", [128, 1], F32)
    o_x2 = nc.dram_tensor("x2", [D, TPC], F32, kind="ExternalOutput").ap()
    pkind = "ExternalOutput" if FDEBUG else "Internal"
    PK = lambda n, s, d: nc.dram_tensor(n, s, d, kind=pkind).ap()
    xn_p = PK("xn_p", [D, TPC], BF16)
    qn_p = PK("qn_p", [TPC, D], BF16)
    kvn_p = PK("kvn_p", [TPC, D], BF16)
    c_p = PK("c_p", [D, TPC], F32)
    x1p_p = PK("x1p_p", [D, TPC], F32)
    x1_p = PK("x1_p", [D, TPC], F32)
    xn1_p = PK("xn1_p", [D, TPC], BF16)
    Qx_p = PK("Qx_p", [D, TPC], BF16)
    cw_p = PK("cw_p", [TPC, NE], F32)
    xlat_p = PK("xlat_p", [LAT, TPC], BF16)
    attx_p = PK("attx_p", [D, TPC], F32)
    kc_c = nc.dram_tensor("kc_c", [D, TPC], BF16).ap()
    vc_c = nc.dram_tensor("vc_c", [TPC, D], BF16).ap()
    ctot_c = nc.dram_tensor("ctot_c", [1, D], F32).ap()
    KG = nc.dram_tensor("KG", [2 * D, TPC], BF16).ap()
    VG = nc.dram_tensor("VG", [2 * TPC, D], BF16).ap()
    GTOT = nc.dram_tensor("GTOT", [2, D], F32).ap()
    Cf, SN, Ci, SNi = _dft_consts()
    HT = 1024
    NTT = TPC // 128

    with contextlib.ExitStack() as ex:
        sems = {k: ex.enter_context(nc.semaphore(k)) for k in ["LD", "MM", "VV", "SS", "GG", "ST", "CC"]}
        block = ex.enter_context(nc.Block())
        ctrs = {k: Ctr(s) for k, s in sems.items()}
        LD, MM, VV, SS, GG, ST, CC = (ctrs[k] for k in ["LD", "MM", "VV", "SS", "GG", "ST", "CC"])
        em = _mkemit(block, ctrs)
        ones_c_d = nc.inline_tensor(np.ones((128, 1), dtype=np.float32), name="onescol").ap()
        ones_r_d = nc.inline_tensor(np.ones((1, 128), dtype=np.float32), name="onesrow").ap()
        rt_d = nc.inline_tensor(_rot_const(), name="rotT").ap()
        dft_d = nc.inline_tensor(np.concatenate([Cf, SN, Ci, SNi], axis=1), name="dftc").ap()
        id_d = nc.inline_tensor(np.eye(128, dtype=np.float32), name="ident128").ap()

        with contextlib.ExitStack() as px:
            px.enter_context(nc.named_scope("P1"))
            _phase_barrier(em, ctrs)
            sb = lambda *a: px.enter_context(nc.sbuf_tensor(*a))
            pt = lambda *a: px.enter_context(nc.psum_tensor(*a))
            wq_sb = sb("p1wq", [128, NFT * D], BF16)
            wkv_sb = sb("p1wkv", [128, NFT * D], BF16)
            xc = sb("p1xc", [128, NFT * CH], F32)
            xb2 = sb("p1xb2", [128, NFT * CH], BF16)
            stats = sb("p1stats", [1, 5 * CH], F32)
            stats_bf = sb("p1statsbf", [1, 4 * CH], BF16)
            ones_col = sb("p1onesc", [128, 1], BF16)
            ones_row = sb("p1onesr", [1, 128], BF16)
            qbuf = sb("p1qbuf", [128, D], BF16)
            qnbuf = sb("p1qnbuf", [128, D], BF16)
            qsq = sb("p1qsq", [128, D], F32)
            s16 = sb("p1s16", [128, 16], F32)
            rn16 = sb("p1rn16", [128, 16], F32)
            ps_s1 = pt("p1s1", [1, CH], F32)
            ps_s2 = pt("p1s2", [1, CH], F32)
            ps_bm = pt("p1bm", [128, CH], F32)
            ps_br = pt("p1br", [128, CH], F32)
            ps_q0 = pt("p1q0", [128, CH], F32)
            ps_q1 = pt("p1q1", [128, CH], F32)
            bufs = dict(xc=xc, xb2=xb2, ones_col=ones_col, ones_row=ones_row, stats=stats,
                        stats_bf=stats_bf, ps_s1=ps_s1, ps_s2=ps_s2, ps_bm=ps_bm, ps_br=ps_br)

            def g0(g):
                for j in range(4):
                    GG.inc(g.dma_start(
                        out=wq_sb.ap()[:, j * 4 * D:(j + 1) * 4 * D],
                        in_=WqT.rearrange("(ft p) o -> p ft o", p=128)[:, j * 4:(j + 1) * 4, :]), 16)
                    GG.inc(g.dma_start(
                        out=wkv_sb.ap()[:, j * 4 * D:(j + 1) * 4 * D],
                        in_=WkvT.rearrange("(ft p) o -> p ft o", p=128)[:, j * 4:(j + 1) * 4, :]), 16)
                GG.inc(g.dma_start(out=ones_col.ap()[:, :], in_=ones_c_d[:, :]), 16)
                GG.inc(g.dma_start(out=ones_row.ap()[:, :], in_=ones_r_d[:, :]), 16)
            em["gpsimd"](g0)

            for c in range(NCH):
                def ld(s, c=c):
                    ST.wait(s); VV.wait(s); SS.wait(s); MM.wait(s)
                    LD.inc(s.dma_start(
                        out=xc.ap().rearrange("p (ft c) -> p ft c", ft=NFT),
                        in_=xfm.rearrange("(ft p) t -> p ft t", p=128)[:, :, c * CH:(c + 1) * CH]), 16)
                em["sync"](ld)

                _ln_stats_chunk(nc, em, bufs, ctrs, c)

                def st_xn(s, c=c):
                    VV.wait(s)
                    ST.inc(s.dma_start(
                        out=xn_p.rearrange("(ft p) t -> p ft t", p=128)[:, :, c * CH:(c + 1) * CH],
                        in_=xb2.ap().rearrange("p (ft c) -> p ft c", ft=NFT)), 16)
                em["sync"](st_xn)

                for (wsb, park) in ((wq_sb, qn_p), (wkv_sb, kvn_p)):
                    for ts in range(CH // 128):
                        for fq in range(4):
                            ps = (ps_q0, ps_q1)[fq % 2]

                            def tq(t, wsb=wsb, ts=ts, c=c, fq=fq, ps=ps):
                                GG.wait(t); VV.wait(t); SS.wait(t)
                                for ft in range(NFT):
                                    MM.inc(t.matmul(
                                        ps.ap()[:, 0:CH],
                                        xb2.ap()[:, ft * CH + ts * 128: ft * CH + ts * 128 + 128],
                                        wsb.ap()[:, ft * D + fq * CH: ft * D + (fq + 1) * CH],
                                        start=(ft == 0), stop=(ft == NFT - 1)))
                            em["tensor"](tq)

                            def ev(s, fq=fq, ps=ps):
                                MM.wait(s)
                                SS.inc(s.copy(qbuf.ap()[:, fq * CH:(fq + 1) * CH], ps.ap()[:, 0:CH]))
                            em["scalar"](ev)

                        def vn(v, ts=ts):
                            SS.wait(v)
                            VV.inc(v.tensor_tensor(qsq.ap()[:, :], qbuf.ap()[:, :], qbuf.ap()[:, :], ALU.mult))
                            VV.wait(v)
                            VV.inc(v.tensor_reduce(s16.ap()[:, :],
                                                   qsq.ap().rearrange("p (h f) -> p h f", h=H),
                                                   AX, ALU.add))
                        em["vector"](vn)

                        def sn(s, ts=ts):
                            VV.wait(s)
                            SS.inc(s.activation(rn16.ap()[:, :], s16.ap()[:, :], AF.Ln, scale=float(HD)))
                            SS.wait(s)
                            SS.inc(s.activation(rn16.ap()[:, :], rn16.ap()[:, :], AF.Exp, scale=-0.5))
                        em["scalar"](sn)

                        def vq(v, ts=ts):
                            SS.wait(v)
                            ST.wait(v)
                            for h in range(H):
                                VV.inc(v.tensor_scalar(qnbuf.ap()[:, h * HD:(h + 1) * HD],
                                                       qbuf.ap()[:, h * HD:(h + 1) * HD],
                                                       rn16.ap()[:, h:h + 1], None, ALU.mult))
                        em["vector"](vq)

                        def stq(s, park=park, ts=ts, c=c):
                            VV.wait(s)
                            ST.inc(s.dma_start(out=park[c * CH + ts * 128: c * CH + ts * 128 + 128, :],
                                               in_=qnbuf.ap()[:, :]), 16)
                            ST.wait(s)
                        em["sync"](stq)

        with contextlib.ExitStack() as px:
            px.enter_context(nc.named_scope("PC"))
            _phase_barrier(em, ctrs)
            sb = lambda *a: px.enter_context(nc.sbuf_tensor(*a))
            tin = sb("pcin", [128, TPC], BF16)
            tout = sb("pcout", [128, TPC], F32)
            carry = sb("pccar", [128, NFT], F32)
            iscs = sb("pcisc", [128, 1], F32)
            for ft in range(NFT):
                def ldk(s, ft=ft):
                    VV.wait(s); ST.wait(s)
                    LD.inc(s.dma_start(out=tin.ap()[:, :],
                                       in_=kvn_p[0:TPC, ft * 128:(ft + 1) * 128], transpose=True), 16)
                em["sync"](ldk)

                def vs(v, ft=ft):
                    LD.wait(v); ST.wait(v)
                    VV.inc(v.tensor_tensor_scan(tout.ap()[:, :], tin.ap()[:, :], tin.ap()[:, :],
                                                0.0, ALU.add, ALU.bypass))
                em["vector"](vs)

                def stc(s, ft=ft):
                    VV.wait(s)
                    ST.inc(s.dma_start(out=c_p[ft * 128:(ft + 1) * 128, :], in_=tout.ap()[:, :]), 16)
                    ST.inc(s.dma_start(out=ctot_c[0:1, ft * 128:(ft + 1) * 128].rearrange("a p -> p a"),
                                       in_=tout.ap()[:, TPC - 1:TPC],
                                       allow_slow_non_contiguous=True), 16)
                em["sync"](stc)

            def cc1(g):
                ST.wait(g)
                CC.inc(g.collective_compute("AllGather", ALU.bypass, replica_groups=RGPAIR,
                                            ins=[ctot_c.opt()], outs=[GTOT.opt()]))
            em["gpsimd"](cc1)

            def ldcar(s):
                CC.wait(s)
                LD.inc(s.dma_start(out=carry.ap()[:, :],
                                   in_=GTOT[0:1, :].rearrange("a (ft p) -> p (a ft)", p=128),
                                   allow_slow_non_contiguous=True), 16)
                LD.inc(s.dma_start(out=iscs.ap()[:, :], in_=isc_i), 16)
            em["sync"](ldcar)

            def vcar(v):
                LD.wait(v)
                VV.inc(v.tensor_scalar(carry.ap()[:, :], carry.ap()[:, :], iscs.ap()[:, 0:1], None, ALU.mult))
            em["vector"](vcar)

            for ft in range(NFT):
                def ldc2(s, ft=ft):
                    VV.wait(s); ST.wait(s)
                    LD.inc(s.dma_start(out=tout.ap()[:, :], in_=c_p[ft * 128:(ft + 1) * 128, :]), 16)
                em["sync"](ldc2)

                def vad(v, ft=ft):
                    LD.wait(v); ST.wait(v)
                    VV.inc(v.tensor_scalar(tout.ap()[:, :], tout.ap()[:, :],
                                           carry.ap()[:, ft:ft + 1], None, ALU.add))
                em["vector"](vad)

                def stc2(s, ft=ft):
                    VV.wait(s)
                    ST.inc(s.dma_start(out=c_p[ft * 128:(ft + 1) * 128, :], in_=tout.ap()[:, :]), 16)
                em["sync"](stc2)

        with contextlib.ExitStack() as px:
            px.enter_context(nc.named_scope("P3"))
            _phase_barrier(em, ctrs)
            sb = lambda *a: px.enter_context(nc.sbuf_tensor(*a))
            pt = lambda *a: px.enter_context(nc.psum_tensor(*a))
            wo_sb = sb("p3wo", [128, NFT * D], BF16)
            qn_fm = sb("p3qn", [128, NFT * CH], BF16)
            c_bf = sb("p3c", [128, NFT * CH], BF16)
            xc = sb("p3xc", [128, NFT * CH], F32)
            vh_bf = sb("p3vh", [128, NFT * CH], BF16)
            cfr = sb("p3cfr", [128, CH], F32)
            cfi = sb("p3cfi", [128, CH], F32)
            t1 = sb("p3t1", [128, CH], F32)
            t2 = sb("p3t2", [128, CH], F32)
            qvr = sb("p3qvr", [128, CH], BF16)
            qvi = sb("p3qvi", [128, CH], BF16)
            outb = sb("p3outb", [128, CH], F32)
            cdft = sb("p3cdft", [128, 4 * 128], BF16)
            ps_qr = pt("p3qr", [128, CH], F32)
            ps_qi = pt("p3qi", [128, CH], F32)
            ps_cr = pt("p3cr", [128, CH], F32)
            ps_ci = pt("p3ci", [128, CH], F32)
            ps_vh = pt("p3vhp", [128, CH], F32)
            ps_a0 = pt("p3a0", [128, CH], F32)
            ps_a1 = pt("p3a1", [128, CH], F32)
            CfS = cdft.ap()[:, 0:128]
            SNS = cdft.ap()[:, 128:256]
            CiS = cdft.ap()[:, 256:384]
            SNiS = cdft.ap()[:, 384:512]

            def g0(g):
                GG.inc(g.dma_start(out=cdft.ap()[:, :], in_=dft_d), 16)
                for j in range(4):
                    GG.inc(g.dma_start(
                        out=wo_sb.ap()[:, j * 4 * D:(j + 1) * 4 * D],
                        in_=WoT.rearrange("(ft p) o -> p ft o", p=128)[:, j * 4:(j + 1) * 4, :]), 16)
            em["gpsimd"](g0)

            for c in range(NCH):
                def ldq(s, c=c):
                    MM.wait(s); VV.wait(s)
                    for ft in range(NFT):
                        LD.inc(s.dma_start(out=qn_fm.ap()[:, ft * CH:(ft + 1) * CH],
                                           in_=qn_p[c * CH:(c + 1) * CH, ft * 128:(ft + 1) * 128],
                                           transpose=True), 16)
                    LD.inc(s.dma_start(
                        out=xc.ap().rearrange("p (ft t) -> p ft t", ft=NFT),
                        in_=xfm.rearrange("(ft p) t -> p ft t", p=128)[:, :, c * CH:(c + 1) * CH]), 16)
                em["sync"](ldq)

                def ldc(g, c=c):
                    MM.wait(g)
                    GG.inc(g.dma_start(
                        out=c_bf.ap().rearrange("p (ft t) -> p ft t", ft=NFT),
                        in_=c_p.rearrange("(ft p) t -> p ft t", p=128)[:, :, c * CH:(c + 1) * CH]), 16)
                em["gpsimd"](ldc)

                for ft in range(NFT):
                    def tf(t, ft=ft):
                        LD.wait(t); GG.wait(t); VV.wait(t); SS.wait(t)
                        MM.inc(t.matmul(ps_qr.ap()[:, :], CfS, qn_fm.ap()[:, ft * CH:(ft + 1) * CH],
                                        start=True, stop=True))
                        MM.inc(t.matmul(ps_qi.ap()[:, :], SNS, qn_fm.ap()[:, ft * CH:(ft + 1) * CH],
                                        start=True, stop=True))
                        MM.inc(t.matmul(ps_cr.ap()[:, :], CfS, c_bf.ap()[:, ft * CH:(ft + 1) * CH],
                                        start=True, stop=True))
                        MM.inc(t.matmul(ps_ci.ap()[:, :], SNS, c_bf.ap()[:, ft * CH:(ft + 1) * CH],
                                        start=True, stop=True))
                    em["tensor"](tf)

                    def se(s, ft=ft):
                        MM.wait(s)
                        SS.inc(s.copy(cfr.ap()[:, :], ps_cr.ap()[:, :]))
                        SS.inc(s.copy(cfi.ap()[:, :], ps_ci.ap()[:, :]))
                    em["scalar"](se)

                    def vc(v, ft=ft):
                        SS.wait(v); MM.wait(v)
                        VV.inc(v.tensor_tensor(t1.ap()[:, :], ps_qr.ap()[:, :], cfr.ap()[:, :], ALU.mult))
                        VV.inc(v.tensor_tensor(t2.ap()[:, :], ps_qi.ap()[:, :], cfi.ap()[:, :], ALU.mult))
                        VV.wait(v)
                        VV.inc(v.tensor_tensor(qvr.ap()[:, :], t1.ap()[:, :], t2.ap()[:, :], ALU.subtract))
                        VV.inc(v.tensor_tensor(t1.ap()[:, :], ps_qr.ap()[:, :], cfi.ap()[:, :], ALU.mult))
                        VV.inc(v.tensor_tensor(t2.ap()[:, :], ps_qi.ap()[:, :], cfr.ap()[:, :], ALU.mult))
                        VV.wait(v)
                        VV.inc(v.tensor_tensor(qvi.ap()[:, :], t1.ap()[:, :], t2.ap()[:, :], ALU.add))
                    em["vector"](vc)

                    def ti(t, ft=ft):
                        VV.wait(t)
                        MM.inc(t.matmul(ps_vh.ap()[:, :], CiS, qvr.ap()[:, :], start=True, stop=False))
                        MM.inc(t.matmul(ps_vh.ap()[:, :], SNiS, qvi.ap()[:, :], start=False, stop=True))
                    em["tensor"](ti)

                    def ve(v, ft=ft):
                        MM.wait(v)
                        VV.inc(v.tensor_copy(vh_bf.ap()[:, ft * CH:(ft + 1) * CH], ps_vh.ap()[:, :]))
                    em["vector"](ve)

                for dot in range(NFT):
                    ps = (ps_a0, ps_a1)[dot % 2]

                    def tw(t, dot=dot, ps=ps):
                        VV.wait(t); SS.wait(t)
                        for ft in range(NFT):
                            MM.inc(t.matmul(ps.ap()[:, :],
                                            wo_sb.ap()[:, ft * D + dot * 128: ft * D + dot * 128 + 128],
                                            vh_bf.ap()[:, ft * CH:(ft + 1) * CH],
                                            start=(ft == 0), stop=(ft == NFT - 1)))
                    em["tensor"](tw)

                    def vo(v, dot=dot, ps=ps):
                        MM.wait(v); ST.wait(v)
                        VV.inc(v.tensor_tensor(outb.ap()[:, :], ps.ap()[:, :],
                                               xc.ap()[:, dot * CH:(dot + 1) * CH], ALU.add))
                    em["vector"](vo)

                    def so(s, dot=dot, c=c):
                        VV.wait(s)
                        ST.inc(s.dma_start(out=x1p_p[dot * 128:(dot + 1) * 128, c * CH:(c + 1) * CH],
                                           in_=outb.ap()[:, :]), 16)
                    em["sync"](so)

        with contextlib.ExitStack() as px:
            px.enter_context(nc.named_scope("P4"))
            _phase_barrier(em, ctrs)
            sb = lambda *a: px.enter_context(nc.sbuf_tensor(*a))
            pt = lambda *a: px.enter_context(nc.psum_tensor(*a))
            HT4 = 1024
            xn_sb = sb("p4xn", [128, NFT * TPC], BF16)
            acc = sb("p4acc", [128, NFT * HT4], F32)
            gated = sb("p4gt", [128, 16 * HT4], BF16)
            wg_t = sb("p4wg", [128, NFT * 128], BF16)
            wu_t = sb("p4wu", [128, NFT * 128], BF16)
            wd_t = sb("p4wd", [128, 16 * 128], BF16)
            gs = sb("p4gs", [128, CH], F32)
            ps_g = pt("p4g", [128, CH], F32)
            ps_u = pt("p4u", [128, CH], F32)
            ps_d0 = pt("p4d0", [128, CH], F32)
            ps_d1 = pt("p4d1", [128, CH], F32)

            def g0(g):
                ST.wait(g)
                for j in range(4):
                    GG.inc(g.dma_start(
                        out=xn_sb.ap().rearrange("p (ft t) -> p ft t", ft=NFT)[:, j * 4:(j + 1) * 4, :],
                        in_=xn_p.rearrange("(ft p) t -> p ft t", p=128)[:, j * 4:(j + 1) * 4, :]), 16)
            em["gpsimd"](g0)

            for hf in range(2):
                def lda(s, hf=hf):
                    ST.wait(s); VV.wait(s)
                    LD.inc(s.dma_start(
                        out=acc.ap().rearrange("p (ft t) -> p ft t", ft=NFT),
                        in_=x1p_p.rearrange("(ft p) t -> p ft t", p=128)[:, :, hf * HT4:(hf + 1) * HT4]), 16)
                em["sync"](lda)

                for sl in range(4):
                    for ht in range(16):
                        hid0 = sl * 2048 + ht * 128

                        def ldw(g, hid0=hid0):
                            MM.wait(g)
                            GG.inc(g.dma_start(
                                out=wg_t.ap().rearrange("p (ft o) -> p ft o", ft=NFT),
                                in_=WgT.rearrange("(ft p) h -> p ft h", p=128)[:, :, hid0:hid0 + 128]), 16)
                            GG.inc(g.dma_start(
                                out=wu_t.ap().rearrange("p (ft o) -> p ft o", ft=NFT),
                                in_=WuT.rearrange("(ft p) h -> p ft h", p=128)[:, :, hid0:hid0 + 128]), 16)
                        em["gpsimd"](ldw)

                        for tc in range(2):
                            def tg(t, hf=hf, tc=tc):
                                GG.wait(t); VV.wait(t); SS.wait(t)
                                tok0 = hf * HT4 + tc * CH
                                for ft in range(NFT):
                                    MM.inc(t.matmul(ps_g.ap()[:, :], wg_t.ap()[:, ft * 128:(ft + 1) * 128],
                                                    xn_sb.ap()[:, ft * TPC + tok0: ft * TPC + tok0 + CH],
                                                    start=(ft == 0), stop=(ft == NFT - 1)))
                                for ft in range(NFT):
                                    MM.inc(t.matmul(ps_u.ap()[:, :], wu_t.ap()[:, ft * 128:(ft + 1) * 128],
                                                    xn_sb.ap()[:, ft * TPC + tok0: ft * TPC + tok0 + CH],
                                                    start=(ft == 0), stop=(ft == NFT - 1)))
                            em["tensor"](tg)

                            def sg(s):
                                MM.wait(s)
                                SS.inc(s.activation(gs.ap()[:, :], ps_g.ap()[:, :], AF.Silu))
                            em["scalar"](sg)

                            def vg(v, ht=ht, tc=tc):
                                SS.wait(v); MM.wait(v)
                                VV.inc(v.tensor_tensor(gated.ap()[:, ht * HT4 + tc * CH: ht * HT4 + tc * CH + CH],
                                                       gs.ap()[:, :], ps_u.ap()[:, :], ALU.mult))
                            em["vector"](vg)

                    for dot in range(16):
                        def ldd(g, sl=sl, dot=dot):
                            MM.wait(g)
                            GG.inc(g.dma_start(
                                out=wd_t.ap().rearrange("p (ht o) -> p ht o", ht=16),
                                in_=WdT[sl * 2048:(sl + 1) * 2048, dot * 128:(dot + 1) * 128]
                                    .rearrange("(ht p) o -> p ht o", p=128)), 16)
                        em["gpsimd"](ldd)

                        for tc in range(2):
                            ps = (ps_d0, ps_d1)[tc]

                            def td(t, hf=hf, tc=tc, ps=ps):
                                GG.wait(t); VV.wait(t)
                                for ht in range(16):
                                    MM.inc(t.matmul(ps.ap()[:, :], wd_t.ap()[:, ht * 128:(ht + 1) * 128],
                                                    gated.ap()[:, ht * HT4 + tc * CH: ht * HT4 + tc * CH + CH],
                                                    start=(ht == 0), stop=(ht == 15)))
                            em["tensor"](td)

                            def vd(v, dot=dot, tc=tc, ps=ps):
                                MM.wait(v); LD.wait(v)
                                sl_ = slice(dot * HT4 + tc * CH, dot * HT4 + tc * CH + CH)
                                VV.inc(v.tensor_tensor(acc.ap()[:, sl_], acc.ap()[:, sl_], ps.ap()[:, :], ALU.add))
                            em["vector"](vd)

                def sta(s, hf=hf):
                    VV.wait(s)
                    ST.inc(s.dma_start(
                        out=x1_p.rearrange("(ft p) t -> p ft t", p=128)[:, :, hf * HT4:(hf + 1) * HT4],
                        in_=acc.ap().rearrange("p (ft t) -> p ft t", ft=NFT)), 16)
                em["sync"](sta)

        with contextlib.ExitStack() as px:
            px.enter_context(nc.named_scope("P5"))
            _phase_barrier(em, ctrs)
            sb = lambda *a: px.enter_context(nc.sbuf_tensor(*a))
            pt = lambda *a: px.enter_context(nc.psum_tensor(*a))
            BIGC = 65536.0
            xn1_sb = sb("p5xn1", [128, NFT * TPC], BF16)
            w_sb = sb("p5w", [128, NFT * D], BF16)
            xc = sb("p5xc", [128, NFT * CH], F32)
            sq = sb("p5sq", [128, NFT * CH], BF16)
            stats = sb("p5stats", [1, 5 * CH], F32)
            stats_bf = sb("p5statsbf", [1, 4 * CH], BF16)
            ones_col = sb("p5onesc", [128, 1], BF16)
            ones_row = sb("p5onesr", [1, 128], BF16)
            rt_sb = sb("p5rt", [128, 128], BF16)
            gw_sb = sb("p5gw", [128, NFT * NE], BF16)
            bias_sb = sb("p5bias", [128, NE], F32)
            scores = sb("p5sc", [128, NE], F32)
            sbt = sb("p5sbt", [128, NE], F32)
            m8 = sb("p5m8", [128, 8], F32)
            gsb = sb("p5gsb", [128, 8], F32)
            g4 = sb("p5g4", [128, 8], F32)
            gmask = sb("p5gm", [128, 4], F32)
            ngm = sb("p5ngm", [128, 4], F32)
            me = sb("p5me", [128, NE], F32)
            selm = sb("p5sel", [128, NE], F32)
            cwt = sb("p5cwt", [128, NE], F32)
            rsum = sb("p5rs", [128, 2], F32)
            qsb = sb("p5qsb", [128, CH], BF16)
            tq1 = sb("p5tq1", [128, CH], F32)
            tq2 = sb("p5tq2", [128, CH], F32)
            qxb = sb("p5qxb", [128, CH], BF16)
            ctab = sb("p5ct", [128, CH], F32)
            stab = sb("p5st", [128, CH], F32)
            ps_q = pt("p5q", [128, CH], F32)
            ps_rot = pt("p5rot", [128, CH], F32)
            ps_lg = pt("p5lg", [128, NE], F32)
            ps_s1 = pt("p5s1", [1, CH], F32)
            ps_s2 = pt("p5s2", [1, CH], F32)
            ps_bm = pt("p5bm", [128, CH], F32)
            ps_br = pt("p5br", [128, CH], F32)

            def g0(g):
                ST.wait(g)
                GG.inc(g.dma_start(out=ones_col.ap()[:, :], in_=ones_c_d[:, :]), 16)
                GG.inc(g.dma_start(out=ones_row.ap()[:, :], in_=ones_r_d[:, :]), 16)
                GG.inc(g.dma_start(out=rt_sb.ap()[:, :], in_=rt_d), 16)
                GG.inc(g.dma_start(out=gw_sb.ap().rearrange("p (ft e) -> p ft e", ft=NFT),
                                   in_=gwT.rearrange("(ft p) e -> p ft e", p=128)), 16)
            em["gpsimd"](g0)

            def s0(s):
                LD.inc(s.dma_start(out=bias_sb.ap()[:, :], in_=biasT), 16)
            em["sync"](s0)

            def v0(v):
                VV.inc(v.memset(gsb.ap()[:, 4:8], -1e30))
            em["vector"](v0)

            bufs = dict(xc=xc, xb2=sq, ones_col=ones_col, ones_row=ones_row, stats=stats,
                        stats_bf=stats_bf, ps_s1=ps_s1, ps_s2=ps_s2, ps_bm=ps_bm, ps_br=ps_br)
            for c in range(NCH):
                def ld(s, c=c):
                    ST.wait(s); VV.wait(s); SS.wait(s); MM.wait(s)
                    LD.inc(s.dma_start(
                        out=xc.ap().rearrange("p (ft t) -> p ft t", ft=NFT),
                        in_=x1_p.rearrange("(ft p) t -> p ft t", p=128)[:, :, c * CH:(c + 1) * CH]), 16)
                em["sync"](ld)
                _ln_stats_chunk(nc, em, bufs, ctrs, c)

                def vcp(v, c=c):
                    VV.wait(v)
                    for ft in range(NFT):
                        VV.inc(v.tensor_copy(xn1_sb.ap()[:, ft * TPC + c * CH: ft * TPC + (c + 1) * CH],
                                             sq.ap()[:, ft * CH:(ft + 1) * CH]))
                em["vector"](vcp)

                def st_xn(s, c=c):
                    VV.wait(s)
                    ST.inc(s.dma_start(
                        out=xn1_p.rearrange("(ft p) t -> p ft t", p=128)[:, :, c * CH:(c + 1) * CH],
                        in_=sq.ap().rearrange("p (ft t) -> p ft t", ft=NFT)), 16)
                em["sync"](st_xn)

            for tt in range(TPC // 128):
                def tr(t, tt=tt):
                    GG.wait(t); VV.wait(t); SS.wait(t); LD.wait(t)
                    for ft in range(NFT):
                        MM.inc(t.matmul(ps_lg.ap()[:, :],
                                        xn1_sb.ap()[:, ft * TPC + tt * 128: ft * TPC + tt * 128 + 128],
                                        gw_sb.ap()[:, ft * NE:(ft + 1) * NE],
                                        start=(ft == 0), stop=(ft == NFT - 1)))
                em["tensor"](tr)

                def sr(s):
                    MM.wait(s)
                    SS.inc(s.activation(scores.ap()[:, :], ps_lg.ap()[:, :], AF.Sigmoid))
                em["scalar"](sr)

                def vr(v, tt=tt):
                    SS.wait(v); ST.wait(v)
                    VV.inc(v.tensor_tensor(sbt.ap()[:, :], scores.ap()[:, :], bias_sb.ap()[:, :], ALU.add))
                    VV.wait(v)
                    for g in range(4):
                        VV.inc(v.max(m8.ap()[:, 0:8], sbt.ap()[:, g * 8:(g + 1) * 8]))
                        VV.wait(v)
                        VV.inc(v.tensor_tensor(gsb.ap()[:, g:g + 1], m8.ap()[:, 0:1], m8.ap()[:, 1:2], ALU.add))
                        VV.wait(v)
                    VV.inc(v.max(g4.ap()[:, 0:8], gsb.ap()[:, 0:8]))
                    VV.wait(v)
                    VV.inc(v.tensor_scalar(gmask.ap()[:, :], gsb.ap()[:, 0:4], g4.ap()[:, 1:2], None, ALU.is_ge))
                    VV.wait(v)
                    VV.inc(v.tensor_scalar(ngm.ap()[:, :], gmask.ap()[:, :], -1.0, 1.0, ALU.mult, ALU.add))
                    VV.wait(v)
                    gmv = gmask.ap()[:, :].unsqueeze(2).broadcast_to([128, 4, 8])
                    ngv = ngm.ap()[:, :].unsqueeze(2).broadcast_to([128, 4, 8])
                    mev = me.ap()[:, :].rearrange("p (g e) -> p g e", g=4)
                    sbv = sbt.ap()[:, :].rearrange("p (g e) -> p g e", g=4)
                    VV.inc(v.tensor_tensor(mev, sbv, gmv, ALU.mult))
                    VV.wait(v)
                    VV.inc(v.scalar_tensor_tensor(mev, ngv, -BIGC, mev, ALU.mult, ALU.add))
                    VV.wait(v)
                    VV.inc(v.max(m8.ap()[:, 0:8], me.ap()[:, :]))
                    VV.wait(v)
                    VV.inc(v.tensor_scalar(selm.ap()[:, :], me.ap()[:, :], m8.ap()[:, 7:8], None, ALU.is_ge))
                    VV.wait(v)
                    VV.inc(v.tensor_tensor(cwt.ap()[:, :], scores.ap()[:, :], selm.ap()[:, :], ALU.mult))
                    VV.wait(v)
                    VV.inc(v.reduce_sum(rsum.ap()[:, 0:1], cwt.ap()[:, :], AX))
                    VV.wait(v)
                    VV.inc(v.reciprocal(rsum.ap()[:, 1:2], rsum.ap()[:, 0:1]))
                    VV.wait(v)
                    VV.inc(v.tensor_scalar(cwt.ap()[:, :], cwt.ap()[:, :], rsum.ap()[:, 1:2], 2.5,
                                           ALU.mult, ALU.mult))
                em["vector"](vr)

                def str_(s, tt=tt):
                    VV.wait(s)
                    ST.inc(s.dma_start(out=cw_p[tt * 128:(tt + 1) * 128, :], in_=cwt.ap()[:, :]), 16)
                em["sync"](str_)

            for (wi, ci, si, park) in ((WQ, qc_i, qs_i, Qx_p), (WK, kc_i, ks_i, kc_c)):
                def gw(g, wi=wi):
                    MM.wait(g); VV.wait(g); SS.wait(g)
                    for j in range(4):
                        GG.inc(g.dma_start(
                            out=w_sb.ap().rearrange("p (ft o) -> p ft o", ft=NFT)[:, j * 4:(j + 1) * 4, :],
                            in_=wi.rearrange("(ft p) o -> p ft o", p=128)[:, j * 4:(j + 1) * 4, :]), 16)
                em["gpsimd"](gw)
                for c in range(NCH):
                    for dqt in range(NFT):
                        def ldt(s, ci=ci, si=si, c=c, dqt=dqt):
                            VV.wait(s)
                            LD.inc(s.dma_start(out=ctab.ap()[:, :],
                                               in_=ci[dqt * 128:(dqt + 1) * 128, c * CH:(c + 1) * CH]), 16)
                            LD.inc(s.dma_start(out=stab.ap()[:, :],
                                               in_=si[dqt * 128:(dqt + 1) * 128, c * CH:(c + 1) * CH]), 16)
                        em["sync"](ldt)

                        def tp(t, c=c, dqt=dqt):
                            GG.wait(t); VV.wait(t); SS.wait(t)
                            for ft in range(NFT):
                                MM.inc(t.matmul(ps_q.ap()[:, :],
                                                w_sb.ap()[:, ft * D + dqt * 128: ft * D + dqt * 128 + 128],
                                                xn1_sb.ap()[:, ft * TPC + c * CH: ft * TPC + (c + 1) * CH],
                                                start=(ft == 0), stop=(ft == NFT - 1)))
                        em["tensor"](tp)

                        def sq_(s):
                            MM.wait(s)
                            SS.inc(s.copy(qsb.ap()[:, :], ps_q.ap()[:, :]))
                        em["scalar"](sq_)

                        def trot(t):
                            SS.wait(t); VV.wait(t)
                            MM.inc(t.matmul(ps_rot.ap()[:, :], rt_sb.ap()[:, :], qsb.ap()[:, :],
                                            start=True, stop=True))
                        em["tensor"](trot)

                        def vx(v):
                            MM.wait(v); LD.wait(v); ST.wait(v)
                            VV.inc(v.tensor_tensor(tq1.ap()[:, :], qsb.ap()[:, :], ctab.ap()[:, :], ALU.mult))
                            VV.inc(v.tensor_tensor(tq2.ap()[:, :], ps_rot.ap()[:, :], stab.ap()[:, :], ALU.mult))
                            VV.wait(v)
                            VV.inc(v.tensor_tensor(qxb.ap()[:, :], tq1.ap()[:, :], tq2.ap()[:, :], ALU.add))
                        em["vector"](vx)

                        def stq(s, park=park, c=c, dqt=dqt):
                            VV.wait(s)
                            ST.inc(s.dma_start(out=park[dqt * 128:(dqt + 1) * 128, c * CH:(c + 1) * CH],
                                               in_=qxb.ap()[:, :]), 16)
                        em["sync"](stq)

            def gv(g):
                MM.wait(g); VV.wait(g); SS.wait(g)
                for j in range(4):
                    GG.inc(g.dma_start(
                        out=w_sb.ap().rearrange("p (ft o) -> p ft o", ft=NFT)[:, j * 4:(j + 1) * 4, :],
                        in_=WV.rearrange("(ft p) o -> p ft o", p=128)[:, j * 4:(j + 1) * 4, :]), 16)
            em["gpsimd"](gv)
            for tt in range(TPC // 128):
                for dvs in range(4):
                    def tv(t, tt=tt, dvs=dvs):
                        GG.wait(t); VV.wait(t)
                        for ft in range(NFT):
                            MM.inc(t.matmul(ps_q.ap()[:, :],
                                            xn1_sb.ap()[:, ft * TPC + tt * 128: ft * TPC + tt * 128 + 128],
                                            w_sb.ap()[:, ft * D + dvs * CH: ft * D + (dvs + 1) * CH],
                                            start=(ft == 0), stop=(ft == NFT - 1)))
                    em["tensor"](tv)

                    def vv_(v):
                        MM.wait(v); ST.wait(v)
                        VV.inc(v.tensor_copy(qxb.ap()[:, :], ps_q.ap()[:, :]))
                    em["vector"](vv_)

                    def sv(s, tt=tt, dvs=dvs):
                        VV.wait(s)
                        ST.inc(s.dma_start(out=vc_c[tt * 128:(tt + 1) * 128, dvs * CH:(dvs + 1) * CH],
                                           in_=qxb.ap()[:, :]), 16)
                    em["sync"](sv)

            def gl(g):
                MM.wait(g); VV.wait(g)
                GG.inc(g.dma_start(
                    out=w_sb.ap()[:, 0:NFT * LAT].rearrange("p (ft o) -> p ft o", ft=NFT),
                    in_=fc1T.rearrange("(ft p) o -> p ft o", p=128)), 16)
            em["gpsimd"](gl)
            for latt in range(LAT // 128):
                for c in range(NCH):
                    def tl(t, latt=latt, c=c):
                        GG.wait(t); VV.wait(t)
                        for ft in range(NFT):
                            MM.inc(t.matmul(ps_q.ap()[:, :],
                                            w_sb.ap()[:, ft * LAT + latt * 128: ft * LAT + latt * 128 + 128],
                                            xn1_sb.ap()[:, ft * TPC + c * CH: ft * TPC + (c + 1) * CH],
                                            start=(ft == 0), stop=(ft == NFT - 1)))
                    em["tensor"](tl)

                    def vl(v):
                        MM.wait(v); ST.wait(v)
                        VV.inc(v.tensor_copy(qxb.ap()[:, :], ps_q.ap()[:, :]))
                    em["vector"](vl)

                    def sl_(s, latt=latt, c=c):
                        VV.wait(s)
                        ST.inc(s.dma_start(out=xlat_p[latt * 128:(latt + 1) * 128, c * CH:(c + 1) * CH],
                                           in_=qxb.ap()[:, :]), 16)
                    em["sync"](sl_)

            def cckv(g):
                ST.wait(g)
                CC.inc(g.collective_compute("AllGather", ALU.bypass, replica_groups=RGPAIR,
                                            ins=[kc_c.opt()], outs=[KG.opt()]))
                CC.inc(g.collective_compute("AllGather", ALU.bypass, replica_groups=RGPAIR,
                                            ins=[vc_c.opt()], outs=[VG.opt()]))
            em["gpsimd"](cckv)

        with contextlib.ExitStack() as px:
            px.enter_context(nc.named_scope("P6"))
            _phase_barrier(em, ctrs)
            sb = lambda *a: px.enter_context(nc.sbuf_tensor(*a))
            pt = lambda *a: px.enter_context(nc.psum_tensor(*a))
            qc_sb = sb("p6q", [128, NFT * CH], BF16)
            kt_sb = sb("p6k", [128, NFT * 128], BF16)
            vt_sb = sb("p6v", [128, D], BF16)
            slab = sb("p6slab", [128, 32 * CH], BF16)
            dtl = sb("p6dt", [128, CH], F32)
            ob = sb("p6ob", [128, CH], F32)
            ps_o = [pt(f"p6o{j}", [128, CH], F32) for j in range(8)]
            ps_at = ps_o[0]

            job = 0
            for qcx in range(4):
                nmt = NJOB[qcx]

                def ldq(s, qcx=qcx):
                    MM.wait(s); CC.wait(s)
                    LD.inc(s.dma_start(
                        out=qc_sb.ap().rearrange("p (ft t) -> p ft t", ft=NFT),
                        in_=Qx_p.rearrange("(ft p) t -> p ft t", p=128)[:, :, qcx * CH:(qcx + 1) * CH]), 16)
                em["sync"](ldq)

                for mt in range(nmt):
                    blk = mt // 16
                    col0 = (mt % 16) * 128

                    def ldk(g, blk=blk, col0=col0):
                        MM.wait(g); CC.wait(g)
                        GG.inc(g.dma_start(
                            out=kt_sb.ap().rearrange("p (ft t) -> p ft t", ft=NFT),
                            in_=KG[blk * D:(blk + 1) * D, col0:col0 + 128]
                                .rearrange("(ft p) t -> p ft t", p=128)), 16)
                    em["gpsimd"](ldk)

                    def ldd(s, job=job):
                        VV.wait(s)
                        LD.inc(s.dma_start(out=dtl.ap()[:, :], in_=dt_i[job]), 16)
                    em["sync"](ldd)

                    def ta(t, mt=mt):
                        GG.wait(t); LD.wait(t); VV.wait(t)
                        for ft in range(NFT):
                            MM.inc(t.matmul(ps_at.ap()[:, :],
                                            kt_sb.ap()[:, ft * 128:(ft + 1) * 128],
                                            qc_sb.ap()[:, ft * CH:(ft + 1) * CH],
                                            start=(ft == 0), stop=(ft == NFT - 1)))
                    em["tensor"](ta)

                    def va(v, mt=mt):
                        MM.wait(v); LD.wait(v)
                        VV.inc(v.tensor_tensor(slab.ap()[:, mt * CH:(mt + 1) * CH],
                                               ps_at.ap()[:, :], dtl.ap()[:, :], ALU.mult))
                    em["vector"](va)
                    job += 1

                for hp in range(2):
                    for mt in range(nmt):
                        blk = mt // 16
                        row0 = (mt % 16) * 128

                        def ldv(g, blk=blk, row0=row0):
                            MM.wait(g)
                            GG.inc(g.dma_start(out=vt_sb.ap()[:, :],
                                               in_=VG[blk * TPC + row0: blk * TPC + row0 + 128, :]), 16)
                        em["gpsimd"](ldv)

                        def tb(t, mt=mt, hp=hp, first=(mt == 0), last=(mt == nmt - 1)):
                            GG.wait(t); VV.wait(t); SS.wait(t)
                            for j in range(8):
                                MM.inc(t.matmul(ps_o[j].ap()[:, :],
                                                vt_sb.ap()[:, hp * 1024 + j * 128: hp * 1024 + (j + 1) * 128],
                                                slab.ap()[:, mt * CH:(mt + 1) * CH],
                                                start=first, stop=last))
                        em["tensor"](tb)

                    for j in range(8):
                        def vo(v, j=j):
                            MM.wait(v); ST.wait(v)
                            VV.inc(v.tensor_copy(ob.ap()[:, :], ps_o[j].ap()[:, :]))
                        em["vector"](vo)

                        def so(s, j=j, hp=hp, qcx=qcx):
                            VV.wait(s)
                            dv0 = hp * 1024 + j * 128
                            ST.inc(s.dma_start(out=attx_p[dv0:dv0 + 128, qcx * CH:(qcx + 1) * CH],
                                               in_=ob.ap()[:, :]), 16)
                        em["sync"](so)

        with contextlib.ExitStack() as px:
            px.enter_context(nc.named_scope("P7"))
            _phase_barrier(em, ctrs)
            sb = lambda *a: px.enter_context(nc.sbuf_tensor(*a))
            pt = lambda *a: px.enter_context(nc.psum_tensor(*a))
            HT7 = 1024
            yfm = sb("p7yfm", [128, 8 * TPC], BF16)
            cw_sb = sb("p7cw", [128, NTT * NE], F32)
            hr = sb("p7hr", [128, CH], F32)
            ob = sb("p7ob", [128, CH], F32)
            ident = sb("p7id", [128, 128], BF16)
            ps_a = pt("p7a", [128, CH], F32)
            ps_b = pt("p7b", [128, CH], F32)
            ps_t = pt("p7t", [128, 128], BF16)

            def g0(g):
                ST.wait(g)
                GG.inc(g.dma_start(out=ident.ap()[:, :], in_=id_d), 16)
                GG.inc(g.dma_start(out=cw_sb.ap().rearrange("p (tt e) -> p tt e", tt=NTT),
                                   in_=cw_p.rearrange("(tt p) e -> p tt e", p=128)), 16)
            em["gpsimd"](g0)

            with contextlib.ExitStack() as ex2:
                _phase_barrier(em, ctrs)
                sb2 = lambda *a: ex2.enter_context(nc.sbuf_tensor(*a))
                y_acc = sb2("p7ya", [128, NTT * LAT], F32)
                xlat_sb = sb2("p7xl", [128, 8 * TPC], BF16)
                h_sb = sb2("p7h", [128, 8 * TPC], BF16)
                w1_sb = sb2("p7w1", [128, 8 * INTER], BF16)
                w2_sb = sb2("p7w2", [128, 8 * LAT], BF16)

                def g1(g):
                    for j in range(2):
                        GG.inc(g.dma_start(
                            out=xlat_sb.ap().rearrange("p (lt t) -> p lt t", lt=8)[:, j * 4:(j + 1) * 4, :],
                            in_=xlat_p.rearrange("(lt p) t -> p lt t", p=128)[:, j * 4:(j + 1) * 4, :]), 16)
                em["gpsimd"](g1)

                def vz(v):
                    VV.inc(v.memset(y_acc.ap()[:, :], 0.0))
                em["vector"](vz)

                for e in range(NE):
                    def gw(g, e=e):
                        MM.wait(g)
                        GG.inc(g.dma_start(out=w1_sb.ap().rearrange("p (lt i) -> p lt i", lt=8),
                                           in_=w1T[e].rearrange("(lt p) i -> p lt i", p=128)), 16)
                        GG.inc(g.dma_start(out=w2_sb.ap().rearrange("p (it l) -> p it l", it=8),
                                           in_=w2T[e].rearrange("(it p) l -> p it l", p=128)), 16)
                    em["gpsimd"](gw)

                    for it in range(8):
                        for tc in range(NCH):
                            def th(t, it=it, tc=tc):
                                GG.wait(t); VV.wait(t); SS.wait(t)
                                for lt in range(8):
                                    MM.inc(t.matmul(ps_a.ap()[:, :],
                                                    w1_sb.ap()[:, lt * INTER + it * 128: lt * INTER + it * 128 + 128],
                                                    xlat_sb.ap()[:, lt * TPC + tc * CH: lt * TPC + (tc + 1) * CH],
                                                    start=(lt == 0), stop=(lt == 7)))
                            em["tensor"](th)

                            def sh(s):
                                MM.wait(s)
                                SS.inc(s.activation(hr.ap()[:, :], ps_a.ap()[:, :], AF.Relu))
                            em["scalar"](sh)

                            def vh(v, it=it, tc=tc):
                                SS.wait(v)
                                VV.inc(v.tensor_tensor(h_sb.ap()[:, it * TPC + tc * CH: it * TPC + (tc + 1) * CH],
                                                       hr.ap()[:, :], hr.ap()[:, :], ALU.mult))
                            em["vector"](vh)

                    for tt in range(NTT):
                        for ls in range(2):
                            def ty(t, tt=tt, ls=ls):
                                VV.wait(t)
                                for it in range(8):
                                    MM.inc(t.matmul(ps_b.ap()[:, :],
                                                    h_sb.ap()[:, it * TPC + tt * 128: it * TPC + tt * 128 + 128],
                                                    w2_sb.ap()[:, it * LAT + ls * CH: it * LAT + (ls + 1) * CH],
                                                    start=(it == 0), stop=(it == 7)))
                            em["tensor"](ty)

                            def vy(v, tt=tt, ls=ls, e=e):
                                MM.wait(v)
                                dst = y_acc.ap()[:, tt * LAT + ls * CH: tt * LAT + (ls + 1) * CH]
                                VV.inc(v.scalar_tensor_tensor(dst, ps_b.ap()[:, :],
                                                              cw_sb.ap()[:, tt * NE + e: tt * NE + e + 1],
                                                              dst, ALU.mult, ALU.add))
                            em["vector"](vy)

                for tt in range(NTT):
                    def vcast(v, tt=tt):
                        MM.wait(v); VV.wait(v)
                        VV.inc(v.tensor_copy(h_sb.ap()[:, tt * LAT:(tt + 1) * LAT],
                                             y_acc.ap()[:, tt * LAT:(tt + 1) * LAT]))
                    em["vector"](vcast)
                for tt in range(NTT):
                    for lt in range(8):
                        def tt_(t, tt=tt, lt=lt):
                            VV.wait(t); SS.wait(t)
                            MM.inc(t.transpose(ps_t.ap()[:, :],
                                               h_sb.ap()[:, tt * LAT + lt * 128: tt * LAT + lt * 128 + 128],
                                               ident.ap()[:, :]))
                        em["tensor"](tt_)

                        def vtr(v, tt=tt, lt=lt):
                            MM.wait(v)
                            VV.inc(v.tensor_copy(yfm.ap()[:, lt * TPC + tt * 128: lt * TPC + tt * 128 + 128],
                                                 ps_t.ap()[:, :]))
                        em["vector"](vtr)

            with contextlib.ExitStack() as ex3:
                _phase_barrier(em, ctrs)
                sb3 = lambda *a: ex3.enter_context(nc.sbuf_tensor(*a))
                acc = sb3("p7acc", [128, NFT * HT7], F32)
                shg = sb3("p7shg", [128, 16 * CH], BF16)
                fc2_sb = sb3("p7fc2", [128, 8 * D], BF16)
                xn1_sl = sb3("p7xsl", [128, NFT * CH], BF16)
                su_t = sb3("p7su", [128, NFT * 128], BF16)
                sd_t = sb3("p7sd", [128, 16 * 128], BF16)
                atile = sb3("p7at", [128, HT7], F32)

                def g2(g):
                    VV.wait(g); MM.wait(g)
                    for j in range(4):
                        GG.inc(g.dma_start(
                            out=fc2_sb.ap().rearrange("p (lt o) -> p lt o", lt=8)[:, j * 2:(j + 1) * 2, :],
                            in_=fc2T.rearrange("(lt p) o -> p lt o", p=128)[:, j * 2:(j + 1) * 2, :]), 16)
                em["gpsimd"](g2)

                for hf in range(2):
                    def lda(s, hf=hf):
                        ST.wait(s); VV.wait(s)
                        LD.inc(s.dma_start(
                            out=acc.ap().rearrange("p (ft t) -> p ft t", ft=NFT),
                            in_=x1_p.rearrange("(ft p) t -> p ft t", p=128)[:, :, hf * HT7:(hf + 1) * HT7]), 16)
                    em["sync"](lda)
                    for ft in range(NFT):
                        def lat_(s, hf=hf, ft=ft):
                            VV.wait(s)
                            LD.inc(s.dma_start(out=atile.ap()[:, :],
                                               in_=attx_p[ft * 128:(ft + 1) * 128, hf * HT7:(hf + 1) * HT7]), 16)
                        em["sync"](lat_)

                        def vat(v, ft=ft):
                            LD.wait(v)
                            sl_ = slice(ft * HT7, (ft + 1) * HT7)
                            VV.inc(v.tensor_tensor(acc.ap()[:, sl_], acc.ap()[:, sl_], atile.ap()[:, :], ALU.add))
                        em["vector"](vat)

                    for tc in range(2):
                        tok0 = hf * HT7 + tc * CH

                        def ldx(s, tok0=tok0):
                            MM.wait(s)
                            LD.inc(s.dma_start(
                                out=xn1_sl.ap().rearrange("p (ft t) -> p ft t", ft=NFT),
                                in_=xn1_p.rearrange("(ft p) t -> p ft t", p=128)[:, :, tok0:tok0 + CH]), 16)
                        em["sync"](ldx)

                        for s2 in range(2):
                            for ht in range(16):
                                hid0 = s2 * 2048 + ht * 128

                                def lsu(g, hid0=hid0):
                                    MM.wait(g)
                                    GG.inc(g.dma_start(
                                        out=su_t.ap().rearrange("p (ft o) -> p ft o", ft=NFT),
                                        in_=suT.rearrange("(ft p) h -> p ft h", p=128)[:, :, hid0:hid0 + 128]), 16)
                                em["gpsimd"](lsu)

                                def tsu(t, ht=ht):
                                    GG.wait(t); LD.wait(t); VV.wait(t); SS.wait(t)
                                    for ft in range(NFT):
                                        MM.inc(t.matmul(ps_a.ap()[:, :],
                                                        su_t.ap()[:, ft * 128:(ft + 1) * 128],
                                                        xn1_sl.ap()[:, ft * CH:(ft + 1) * CH],
                                                        start=(ft == 0), stop=(ft == NFT - 1)))
                                em["tensor"](tsu)

                                def ssu(s):
                                    MM.wait(s)
                                    SS.inc(s.activation(hr.ap()[:, :], ps_a.ap()[:, :], AF.Relu))
                                em["scalar"](ssu)

                                def vsu(v, ht=ht):
                                    SS.wait(v)
                                    VV.inc(v.tensor_tensor(shg.ap()[:, ht * CH:(ht + 1) * CH],
                                                           hr.ap()[:, :], hr.ap()[:, :], ALU.mult))
                                em["vector"](vsu)

                            for dot in range(16):
                                def lsd(g, s2=s2, dot=dot):
                                    MM.wait(g)
                                    GG.inc(g.dma_start(
                                        out=sd_t.ap().rearrange("p (ht o) -> p ht o", ht=16),
                                        in_=sdT[s2 * 2048:(s2 + 1) * 2048, dot * 128:(dot + 1) * 128]
                                            .rearrange("(ht p) o -> p ht o", p=128)), 16)
                                em["gpsimd"](lsd)

                                def tsd(t, dot=dot):
                                    GG.wait(t); VV.wait(t)
                                    for ht in range(16):
                                        MM.inc(t.matmul(ps_b.ap()[:, :],
                                                        sd_t.ap()[:, ht * 128:(ht + 1) * 128],
                                                        shg.ap()[:, ht * CH:(ht + 1) * CH],
                                                        start=(ht == 0), stop=(ht == 15)))
                                em["tensor"](tsd)

                                def vsd(v, dot=dot, tc=tc):
                                    MM.wait(v); LD.wait(v)
                                    sl_ = slice(dot * HT7 + tc * CH, dot * HT7 + tc * CH + CH)
                                    VV.inc(v.tensor_tensor(acc.ap()[:, sl_], acc.ap()[:, sl_],
                                                           ps_b.ap()[:, :], ALU.add))
                                em["vector"](vsd)

                        for dot in range(16):
                            def tr2(t, dot=dot, tok0=tok0):
                                GG.wait(t); VV.wait(t); SS.wait(t)
                                for lt in range(8):
                                    MM.inc(t.matmul(ps_a.ap()[:, :],
                                                    fc2_sb.ap()[:, lt * D + dot * 128: lt * D + dot * 128 + 128],
                                                    yfm.ap()[:, lt * TPC + tok0: lt * TPC + tok0 + CH],
                                                    start=(lt == 0), stop=(lt == 7)))
                            em["tensor"](tr2)

                            def vr2(v, dot=dot, tc=tc):
                                MM.wait(v)
                                sl_ = slice(dot * HT7 + tc * CH, dot * HT7 + tc * CH + CH)
                                VV.inc(v.tensor_tensor(acc.ap()[:, sl_], acc.ap()[:, sl_],
                                                       ps_a.ap()[:, :], ALU.add))
                            em["vector"](vr2)

                    def sta(s, hf=hf):
                        VV.wait(s)
                        ST.inc(s.dma_start(
                            out=o_x2.rearrange("(ft p) t -> p ft t", p=128)[:, :, hf * HT7:(hf + 1) * HT7],
                            in_=acc.ap().rearrange("p (ft t) -> p ft t", ft=NFT)), 16)
                    em["sync"](sta)

                def fin(s):
                    ST.wait(s)
                em["sync"](fin)
    return nc


def kernel_fused(**inputs):
    p = _prep_basic(inputs)
    tabs = _xpos_tables()
    dts = [_dtiles_fused(h) for h in range(2)]
    iscs = [np.zeros((128, 1), np.float32), np.ones((128, 1), np.float32)]
    in_maps = []
    for i in range(NCORE):
        h = i % 2
        qc, qs, kc, ks = tabs[h]
        in_maps.append({
            "xfm": p["xfm"][i], "WqT": p["WqT"], "WkvT": p["WkvT"], "WoT": p["WoT"],
            "WgT": p["WgT"], "WuT": p["WuT"], "WdT": p["WdT"],
            "WQ": p["WQ"], "WK": p["WK"], "WV": p["WV"],
            "gwT": p["gwT"], "biasT": p["biasT"], "fc1T": p["fc1T"], "fc2T": p["fc2T"],
            "w1T": p["w1T"], "w2T": p["w2T"], "suT": p["suT"], "sdT": p["sdT"],
            "qcos": qc, "qsin": qs, "kcos": kc, "ksin": ks,
            "dtiles": dts[h], "isc": iscs[h],
        })
    r = _run("FUSED", build_FUSED, in_maps)
    out = np.empty((B, S, D), dtype=np.float32)
    for i in range(NCORE):
        b, p0 = _tok_owner(i)
        out[b, p0:p0 + TPC, :] = np.asarray(r[i]["x2"], dtype=np.float32).T
    return out, r


# version 04
# speedup vs baseline: 1.6747x; 1.3061x over previous
import math
import numpy as np
import ml_dtypes
import concourse.bass as bass
import concourse.mybir as mybir
from concourse.bass_utils import run_bass_kernel_spmd

F32 = mybir.dt.float32
BF16 = mybir.dt.bfloat16
AF = mybir.ActivationFunctionType
ALU = mybir.AluOpType
AX = mybir.AxisListType.X
BF = ml_dtypes.bfloat16

NCORE = 8
B, S, D = 4, 4096, 2048
H, HD = 16, 128
TPC = 2048
CH = 512
NCH = TPC // CH
NFT = D // 128
GAMMA = 0.96875
FH = 8192
LAT = 1024
INTER = 1024
NE = 32
SHH = 4096
EPS = 1e-5

_prog_cache = {}
STRICT = "sb"


def _mkemit(block, ctrs):
    def wrap(engfn, name):
        def emit(f):
            def g(e):
                if STRICT is True or (not isinstance(STRICT, bool) and name in STRICT):
                    for c in ctrs.values():
                        c.wait(e)
                elif STRICT == "xw":
                    if name == "vector":
                        ctrs["SS"].wait(e)
                    elif name == "scalar":
                        ctrs["VV"].wait(e)
                elif STRICT == "vxw":
                    if name == "vector":
                        for c in ctrs.values():
                            c.wait(e)
                    elif name == "scalar":
                        ctrs["VV"].wait(e)
                elif STRICT == "sb":
                    if name == "scalar":
                        for c in ctrs.values():
                            c.wait(e)
                f(e)
            engfn(g)
        return emit
    return {k: wrap(getattr(block, k), k) for k in ["sync", "gpsimd", "tensor", "vector", "scalar"]}


def _phase_barrier(em, ctrs):
    for eng in ["sync", "gpsimd", "tensor", "vector", "scalar"]:
        def bar(e):
            for c in ctrs.values():
                c.wait(e)
        em[eng](bar)


class Ctr:
    def __init__(self, sem):
        self.sem = sem
        self.n = 0

    def inc(self, instr, k=1):
        instr.then_inc(self.sem, k)
        self.n += k

    def wait(self, eng):
        if self.n > 0:
            eng.wait_ge(self.sem, self.n)

    def wait_at(self, eng, val):
        if val > 0:
            eng.wait_ge(self.sem, val)


def _dft_consts():
    k = np.arange(HD, dtype=np.float64)
    th = 2.0 * np.pi * np.outer(k, k) / HD
    C = np.cos(th)
    Sn = -np.sin(th)
    return (C.astype(np.float32), Sn.astype(np.float32),
            (C / HD).astype(np.float32), (Sn / HD).astype(np.float32))


def _rot_const():
    R = np.zeros((128, 128), dtype=np.float32)
    for j in range(0, 128, 2):
        R[j, j + 1] = -1.0
        R[j + 1, j] = 1.0
    return np.ascontiguousarray(R.T)


def _emit_bf16_const(nc, g_ctr, name, arr_f32, gp):
    d = nc.inline_tensor(np.ascontiguousarray(arr_f32.astype(np.float32)), name=name).ap()
    t = nc.alloc_sbuf_tensor(name + "_sb", list(arr_f32.shape), BF16)
    g_ctr.inc(gp.dma_start(out=t.ap(), in_=d), 16)
    return t


def _ln_stats_chunk(nc, em, bufs, ctrs, c):
    xc, xb2, ones_col, ones_row, st = bufs["xc"], bufs["xb2"], bufs["ones_col"], bufs["ones_row"], bufs["stats"]
    ps_s1, ps_s2, ps_bm, ps_br = bufs["ps_s1"], bufs["ps_s2"], bufs["ps_bm"], bufs["ps_br"]
    LD, MM, VV, SS, GG = ctrs["LD"], ctrs["MM"], ctrs["VV"], ctrs["SS"], ctrs["GG"]
    mean, ex2, tmp, mhf, rstd = (st.ap()[0:1, i * CH:(i + 1) * CH] for i in range(5))
    mh, ml, rh, rl = (bufs["stats_bf"].ap()[0:1, i * CH:(i + 1) * CH] for i in range(4))
    ST = ctrs["ST"]

    def sc1(s):
        LD.wait(s)
        MM.wait(s)
        ST.wait(s)
        SS.inc(s.activation(xb2.ap()[:, :], xc.ap()[:, :], AF.Square))
    em["scalar"](sc1)

    def tn1(t):
        SS.wait(t)
        for ft in range(NFT):
            MM.inc(t.matmul(ps_s2.ap()[0:1, 0:CH], ones_col.ap()[:, 0:1],
                            xb2.ap()[:, ft * CH:(ft + 1) * CH],
                            start=(ft == 0), stop=(ft == NFT - 1)))
    em["tensor"](tn1)

    def vc1(v):
        MM.wait(v)
        VV.inc(v.tensor_scalar(ex2, ps_s2.ap()[0:1, 0:CH], 1.0 / D, None, ALU.mult))
    em["vector"](vc1)

    def vc2(v):
        VV.wait(v)
        MM.wait(v)
        VV.inc(v.tensor_copy(xb2.ap()[:, :], xc.ap()[:, :]))
    em["vector"](vc2)

    def tn2(t):
        VV.wait(t)
        for ft in range(NFT):
            MM.inc(t.matmul(ps_s1.ap()[0:1, 0:CH], ones_col.ap()[:, 0:1],
                            xb2.ap()[:, ft * CH:(ft + 1) * CH],
                            start=(ft == 0), stop=(ft == NFT - 1)))
    em["tensor"](tn2)

    def vst(v):
        MM.wait(v)
        VV.inc(v.tensor_scalar(mean, ps_s1.ap()[0:1, 0:CH], 1.0 / D, None, ALU.mult))
        VV.wait(v)
        VV.inc(v.tensor_tensor(tmp, mean, mean, ALU.mult))
        VV.wait(v)
        VV.inc(v.tensor_tensor(ex2, ex2, tmp, ALU.subtract))
        VV.wait(v)
        VV.inc(v.tensor_scalar(ex2, ex2, EPS, None, ALU.add))
    em["vector"](vst)

    def sst(s):
        VV.wait(s)
        SS.inc(s.activation(tmp, ex2, AF.Ln))
        SS.wait(s)
        SS.inc(s.activation(rstd, tmp, AF.Exp, scale=-0.5))
    em["scalar"](sst)

    def vhl(v):
        SS.wait(v)
        VV.inc(v.tensor_copy(mh, mean))
        VV.wait(v)
        VV.inc(v.tensor_copy(mhf, mh))
        VV.wait(v)
        VV.inc(v.tensor_tensor(tmp, mean, mhf, ALU.subtract))
        VV.wait(v)
        VV.inc(v.tensor_copy(ml, tmp))
        VV.inc(v.tensor_copy(rh, rstd))
        VV.wait(v)
        VV.inc(v.tensor_copy(mhf, rh))
        VV.wait(v)
        VV.inc(v.tensor_tensor(tmp, rstd, mhf, ALU.subtract))
        VV.wait(v)
        VV.inc(v.tensor_copy(rl, tmp))
    em["vector"](vhl)

    def tbc(t):
        VV.wait(t)
        MM.inc(t.matmul(ps_bm.ap()[:, 0:CH], ones_row.ap()[0:1, :], mh, start=True, stop=False))
        MM.inc(t.matmul(ps_bm.ap()[:, 0:CH], ones_row.ap()[0:1, :], ml, start=False, stop=True))
        MM.inc(t.matmul(ps_br.ap()[:, 0:CH], ones_row.ap()[0:1, :], rh, start=True, stop=False))
        MM.inc(t.matmul(ps_br.ap()[:, 0:CH], ones_row.ap()[0:1, :], rl, start=False, stop=True))
    em["tensor"](tbc)

    def vxn(v):
        MM.wait(v)
        for ft in range(NFT):
            sl = slice(ft * CH, (ft + 1) * CH)
            VV.inc(v.tensor_tensor(xc.ap()[:, sl], xc.ap()[:, sl], ps_bm.ap()[:, 0:CH], ALU.subtract))
        VV.wait(v)
        for ft in range(NFT):
            sl = slice(ft * CH, (ft + 1) * CH)
            VV.inc(v.tensor_tensor(xb2.ap()[:, sl], xc.ap()[:, sl], ps_br.ap()[:, 0:CH], ALU.mult))
    em["vector"](vxn)


def build_L1():
    nc = bass.Bass("TRN2", target_bir_lowering=False, debug=False)
    xfm = nc.dram_tensor("xfm", [D, TPC], F32, kind="ExternalInput").ap()
    WqT = nc.dram_tensor("WqT", [D, D], F32, kind="ExternalInput").ap()
    WkvT = nc.dram_tensor("WkvT", [D, D], F32, kind="ExternalInput").ap()
    xn_o = nc.dram_tensor("xn", [D, TPC], BF16, kind="ExternalOutput").ap()
    qn_o = nc.dram_tensor("qn", [TPC, D], BF16, kind="ExternalOutput").ap()
    kvn_o = nc.dram_tensor("kvn", [TPC, D], BF16, kind="ExternalOutput").ap()

    import contextlib
    with contextlib.ExitStack() as ex:
        sb = lambda *a: ex.enter_context(nc.sbuf_tensor(*a))
        pt = lambda *a: ex.enter_context(nc.psum_tensor(*a))
        sem = lambda n: ex.enter_context(nc.semaphore(n))
        wq_sb = sb("wq_sb", [128, NFT * D], BF16)
        wkv_sb = sb("wkv_sb", [128, NFT * D], BF16)
        xc = sb("xc", [128, NFT * CH], F32)
        xb2 = sb("xb2", [128, NFT * CH], BF16)
        stats = sb("stats", [1, 5 * CH], F32)
        stats_bf = sb("stats_bf", [1, 4 * CH], BF16)
        ones_col = sb("ones_col", [128, 1], BF16)
        ones_row = sb("ones_row", [1, 128], BF16)
        qbuf = sb("qbuf", [128, D], BF16)
        qnbuf = sb("qnbuf", [128, D], BF16)
        qsq = sb("qsq", [128, D], F32)
        s16 = sb("s16", [128, 16], F32)
        rn16 = sb("rn16", [128, 16], F32)
        ps_s1 = pt("ps_s1", [1, CH], F32)
        ps_s2 = pt("ps_s2", [1, CH], F32)
        ps_bm = pt("ps_bm", [128, CH], F32)
        ps_br = pt("ps_br", [128, CH], F32)
        ps_q0 = pt("ps_q0", [128, CH], F32)
        ps_q1 = pt("ps_q1", [128, CH], F32)
        sLD, sMM, sVV, sSS, sGG, sST = (sem(n) for n in ["LD", "MM", "VV", "SS", "GG", "ST"])
        block = ex.enter_context(nc.Block())
        ctrs = {k: Ctr(s) for k, s in
                [("LD", sLD), ("MM", sMM), ("VV", sVV), ("SS", sSS), ("GG", sGG), ("ST", sST)]}
        LD, MM, VV, SS, GG, ST = (ctrs[k] for k in ["LD", "MM", "VV", "SS", "GG", "ST"])
        em = _mkemit(block, ctrs)
        bufs = dict(xc=xc, xb2=xb2, ones_col=ones_col, ones_row=ones_row, stats=stats,
                    stats_bf=stats_bf, ps_s1=ps_s1, ps_s2=ps_s2, ps_bm=ps_bm, ps_br=ps_br)

        ones_c_d = nc.inline_tensor(np.ones((128, 1), dtype=np.float32), name="onescol").ap()
        ones_r_d = nc.inline_tensor(np.ones((1, 128), dtype=np.float32), name="onesrow").ap()

        def g0(g):
            for j in range(4):
                GG.inc(g.dma_start(
                    out=wq_sb.ap()[:, j * 4 * D:(j + 1) * 4 * D],
                    in_=WqT.rearrange("(ft p) o -> p ft o", p=128)[:, j * 4:(j + 1) * 4, :]), 16)
                GG.inc(g.dma_start(
                    out=wkv_sb.ap()[:, j * 4 * D:(j + 1) * 4 * D],
                    in_=WkvT.rearrange("(ft p) o -> p ft o", p=128)[:, j * 4:(j + 1) * 4, :]), 16)
            GG.inc(g.dma_start(out=ones_col.ap()[:, :], in_=ones_c_d[:, :]), 16)
            GG.inc(g.dma_start(out=ones_row.ap()[:, :], in_=ones_r_d[:, :]), 16)
        em["gpsimd"](g0)

        for c in range(NCH):
            def ld(s, c=c):
                ST.wait(s)
                VV.wait(s)
                SS.wait(s)
                MM.wait(s)
                LD.inc(s.dma_start(
                    out=xc.ap().rearrange("p (ft c) -> p ft c", ft=NFT),
                    in_=xfm.rearrange("(ft p) t -> p ft t", p=128)[:, :, c * CH:(c + 1) * CH]), 16)
            em["sync"](ld)

            _ln_stats_chunk(nc, em, bufs, ctrs, c)

            def st_xn(s, c=c):
                VV.wait(s)
                ST.inc(s.dma_start(
                    out=xn_o.rearrange("(ft p) t -> p ft t", p=128)[:, :, c * CH:(c + 1) * CH],
                    in_=xb2.ap().rearrange("p (ft c) -> p ft c", ft=NFT)), 16)
            em["sync"](st_xn)

            for (wsb, park) in ((wq_sb, qn_o), (wkv_sb, kvn_o)):
                for ts in range(CH // 128):
                    for fq in range(4):
                        ps = (ps_q0, ps_q1)[fq % 2]

                        def tq(t, wsb=wsb, ts=ts, c=c, fq=fq, ps=ps):
                            GG.wait(t)
                            VV.wait(t)
                            SS.wait(t)
                            for ft in range(NFT):
                                MM.inc(t.matmul(
                                    ps.ap()[:, 0:CH],
                                    xb2.ap()[:, ft * CH + ts * 128: ft * CH + ts * 128 + 128],
                                    wsb.ap()[:, ft * D + fq * CH: ft * D + (fq + 1) * CH],
                                    start=(ft == 0), stop=(ft == NFT - 1)))
                        em["tensor"](tq)

                        def ev(s, fq=fq, ps=ps):
                            MM.wait(s)
                            SS.inc(s.copy(qbuf.ap()[:, fq * CH:(fq + 1) * CH], ps.ap()[:, 0:CH]))
                        em["scalar"](ev)
                    def vn(v, ts=ts):
                        SS.wait(v)
                        VV.inc(v.tensor_tensor(qsq.ap()[:, :], qbuf.ap()[:, :], qbuf.ap()[:, :], ALU.mult))
                        VV.wait(v)
                        VV.inc(v.tensor_reduce(s16.ap()[:, :],
                                               qsq.ap().rearrange("p (h f) -> p h f", h=H),
                                               AX, ALU.add))
                    em["vector"](vn)

                    def sn(s, ts=ts):
                        VV.wait(s)
                        SS.inc(s.activation(rn16.ap()[:, :], s16.ap()[:, :], AF.Ln, scale=float(HD)))
                        SS.wait(s)
                        SS.inc(s.activation(rn16.ap()[:, :], rn16.ap()[:, :], AF.Exp, scale=-0.5))
                    em["scalar"](sn)

                    def vq(v, ts=ts):
                        SS.wait(v)
                        ST.wait(v)
                        for h in range(H):
                            VV.inc(v.tensor_scalar(qnbuf.ap()[:, h * HD:(h + 1) * HD],
                                                   qbuf.ap()[:, h * HD:(h + 1) * HD],
                                                   rn16.ap()[:, h:h + 1], None, ALU.mult))
                    em["vector"](vq)

                    def stq(s, park=park, ts=ts, c=c):
                        VV.wait(s)
                        ST.inc(s.dma_start(out=park[c * CH + ts * 128: c * CH + ts * 128 + 128, :],
                                           in_=qnbuf.ap()[:, :]), 16)
                        ST.wait(s)
                    em["sync"](stq)
    return nc




def build_L2():
    nc = bass.Bass("TRN2", target_bir_lowering=False, debug=False)
    kin = nc.dram_tensor("kvnfs", [256, B, S], BF16, kind="ExternalInput").ap()
    cout = nc.dram_tensor("cfs", [256, B, S], F32, kind="ExternalOutput").ap()
    import contextlib
    with contextlib.ExitStack() as ex:
        tin = ex.enter_context(nc.sbuf_tensor("tin", [128, S], BF16))
        tout = ex.enter_context(nc.sbuf_tensor("tout", [128, S], F32))
        sLD = ex.enter_context(nc.semaphore("LD"))
        sVV = ex.enter_context(nc.semaphore("VV"))
        sST = ex.enter_context(nc.semaphore("ST"))
        block = ex.enter_context(nc.Block())
        ctrs = {"LD": Ctr(sLD), "VV": Ctr(sVV), "ST": Ctr(sST)}
        LD, VV, ST = ctrs["LD"], ctrs["VV"], ctrs["ST"]
        em = _mkemit(block, ctrs)
        for fh in range(2):
            for b in range(B):
                def ld(s, fh=fh, b=b):
                    VV.wait(s)
                    LD.inc(s.dma_start(out=tin.ap()[:, :], in_=kin[fh * 128:(fh + 1) * 128, b, :]), 16)
                em["sync"](ld)

                def vs(v):
                    LD.wait(v)
                    ST.wait(v)
                    VV.inc(v.tensor_tensor_scan(tout.ap()[:, :], tin.ap()[:, :], tin.ap()[:, :],
                                                0.0, ALU.add, ALU.bypass))
                em["vector"](vs)

                def st(s, fh=fh, b=b):
                    VV.wait(s)
                    ST.inc(s.dma_start(out=cout[fh * 128:(fh + 1) * 128, b, :], in_=tout.ap()[:, :]), 16)
                em["scalar"](st)

        def fin(s):
            ST.wait(s)
        em["sync"](fin)
    return nc


def run_L2(p, parks):
    kvn_full = np.empty((B, S, D), dtype=BF)
    for i in range(NCORE):
        b, p0 = _tok_owner(i)
        kvn_full[b, p0:p0 + TPC, :] = parks[i]["kvn"]
    kf = np.ascontiguousarray(kvn_full.transpose(2, 0, 1))
    in_maps = [{"kvnfs": np.ascontiguousarray(kf[i * 256:(i + 1) * 256])} for i in range(NCORE)]
    r = _run("L2", build_L2, in_maps)
    c_full = np.concatenate([r[i]["cfs"] for i in range(NCORE)], axis=0)
    c_core = []
    for i in range(NCORE):
        b, p0 = _tok_owner(i)
        c_core.append(np.ascontiguousarray(c_full[:, b, p0:p0 + TPC]))
    return c_core




def build_L3():
    nc = bass.Bass("TRN2", target_bir_lowering=False, debug=False)
    qn_i = nc.dram_tensor("qn", [TPC, D], BF16, kind="ExternalInput").ap()
    c_i = nc.dram_tensor("cfm", [D, TPC], F32, kind="ExternalInput").ap()
    x_i = nc.dram_tensor("xfm", [D, TPC], F32, kind="ExternalInput").ap()
    wo_i = nc.dram_tensor("WoT", [D, D], F32, kind="ExternalInput").ap()
    o_x1p = nc.dram_tensor("x1p", [D, TPC], F32, kind="ExternalOutput").ap()
    Cf, SN, Ci, SNi = _dft_consts()
    import contextlib
    with contextlib.ExitStack() as ex:
        sb = lambda *a: ex.enter_context(nc.sbuf_tensor(*a))
        pt = lambda *a: ex.enter_context(nc.psum_tensor(*a))
        wo_sb = sb("wo_sb", [128, NFT * D], BF16)
        qn_fm = sb("qn_fm", [128, NFT * CH], BF16)
        c_bf = sb("c_bf", [128, NFT * CH], BF16)
        xc = sb("xc", [128, NFT * CH], F32)
        vh_bf = sb("vh_bf", [128, NFT * CH], BF16)
        cfr = sb("cfr", [128, CH], F32)
        cfi = sb("cfi", [128, CH], F32)
        t1 = sb("t1", [128, CH], F32)
        t2 = sb("t2", [128, CH], F32)
        qvr = sb("qvr", [128, CH], BF16)
        qvi = sb("qvi", [128, CH], BF16)
        outb = sb("outb", [128, CH], F32)
        cdft = sb("cdft", [128, 4 * 128], BF16)
        ps_qr = pt("ps_qr", [128, CH], F32)
        ps_qi = pt("ps_qi", [128, CH], F32)
        ps_cr = pt("ps_cr", [128, CH], F32)
        ps_ci = pt("ps_ci", [128, CH], F32)
        ps_vh = pt("ps_vh", [128, CH], F32)
        ps_a0 = pt("ps_a0", [128, CH], F32)
        ps_a1 = pt("ps_a1", [128, CH], F32)
        sems = {k: ex.enter_context(nc.semaphore(k)) for k in ["LD", "MM", "VV", "SS", "GG", "ST"]}
        block = ex.enter_context(nc.Block())
        ctrs = {k: Ctr(s) for k, s in sems.items()}
        LD, MM, VV, SS, GG, ST = (ctrs[k] for k in ["LD", "MM", "VV", "SS", "GG", "ST"])
        em = _mkemit(block, ctrs)
        dft_d = nc.inline_tensor(np.concatenate([Cf, SN, Ci, SNi], axis=1), name="dftc").ap()
        CfS = cdft.ap()[:, 0:128]
        SNS = cdft.ap()[:, 128:256]
        CiS = cdft.ap()[:, 256:384]
        SNiS = cdft.ap()[:, 384:512]

        def g0(g):
            GG.inc(g.dma_start(out=cdft.ap()[:, :], in_=dft_d), 16)
            for j in range(4):
                GG.inc(g.dma_start(
                    out=wo_sb.ap()[:, j * 4 * D:(j + 1) * 4 * D],
                    in_=wo_i.rearrange("(ft p) o -> p ft o", p=128)[:, j * 4:(j + 1) * 4, :]), 16)
        em["gpsimd"](g0)

        for c in range(NCH):
            def ldq(s, c=c):
                for ft in range(NFT):
                    LD.inc(s.dma_start(out=qn_fm.ap()[:, ft * CH:(ft + 1) * CH],
                                       in_=qn_i[c * CH:(c + 1) * CH, ft * 128:(ft + 1) * 128],
                                       transpose=True), 16)
                LD.inc(s.dma_start(
                    out=xc.ap().rearrange("p (ft t) -> p ft t", ft=NFT),
                    in_=x_i.rearrange("(ft p) t -> p ft t", p=128)[:, :, c * CH:(c + 1) * CH]), 16)
            em["sync"](ldq)

            def ldc(g, c=c):
                GG.inc(g.dma_start(
                    out=c_bf.ap().rearrange("p (ft t) -> p ft t", ft=NFT),
                    in_=c_i.rearrange("(ft p) t -> p ft t", p=128)[:, :, c * CH:(c + 1) * CH]), 16)
            em["gpsimd"](ldc)

            for ft in range(NFT):
                def tf(t, ft=ft):
                    LD.wait(t); GG.wait(t); VV.wait(t)
                    MM.inc(t.matmul(ps_qr.ap()[:, :], CfS, qn_fm.ap()[:, ft * CH:(ft + 1) * CH],
                                    start=True, stop=True))
                    MM.inc(t.matmul(ps_qi.ap()[:, :], SNS, qn_fm.ap()[:, ft * CH:(ft + 1) * CH],
                                    start=True, stop=True))
                    MM.inc(t.matmul(ps_cr.ap()[:, :], CfS, c_bf.ap()[:, ft * CH:(ft + 1) * CH],
                                    start=True, stop=True))
                    MM.inc(t.matmul(ps_ci.ap()[:, :], SNS, c_bf.ap()[:, ft * CH:(ft + 1) * CH],
                                    start=True, stop=True))
                em["tensor"](tf)

                def se(s, ft=ft):
                    MM.wait(s)
                    SS.inc(s.copy(cfr.ap()[:, :], ps_cr.ap()[:, :]))
                    SS.inc(s.copy(cfi.ap()[:, :], ps_ci.ap()[:, :]))
                em["scalar"](se)

                def vc(v, ft=ft):
                    SS.wait(v); MM.wait(v)
                    VV.inc(v.tensor_tensor(t1.ap()[:, :], ps_qr.ap()[:, :], cfr.ap()[:, :], ALU.mult))
                    VV.inc(v.tensor_tensor(t2.ap()[:, :], ps_qi.ap()[:, :], cfi.ap()[:, :], ALU.mult))
                    VV.wait(v)
                    VV.inc(v.tensor_tensor(qvr.ap()[:, :], t1.ap()[:, :], t2.ap()[:, :], ALU.subtract))
                    VV.inc(v.tensor_tensor(t1.ap()[:, :], ps_qr.ap()[:, :], cfi.ap()[:, :], ALU.mult))
                    VV.inc(v.tensor_tensor(t2.ap()[:, :], ps_qi.ap()[:, :], cfr.ap()[:, :], ALU.mult))
                    VV.wait(v)
                    VV.inc(v.tensor_tensor(qvi.ap()[:, :], t1.ap()[:, :], t2.ap()[:, :], ALU.add))
                em["vector"](vc)

                def ti(t, ft=ft):
                    VV.wait(t)
                    MM.inc(t.matmul(ps_vh.ap()[:, :], CiS, qvr.ap()[:, :], start=True, stop=False))
                    MM.inc(t.matmul(ps_vh.ap()[:, :], SNiS, qvi.ap()[:, :], start=False, stop=True))
                em["tensor"](ti)

                def ve(v, ft=ft):
                    MM.wait(v)
                    VV.inc(v.tensor_copy(vh_bf.ap()[:, ft * CH:(ft + 1) * CH], ps_vh.ap()[:, :]))
                em["vector"](ve)

            for dot in range(NFT):
                ps = (ps_a0, ps_a1)[dot % 2]

                def tw(t, dot=dot, ps=ps):
                    VV.wait(t); SS.wait(t)
                    for ft in range(NFT):
                        MM.inc(t.matmul(ps.ap()[:, :],
                                        wo_sb.ap()[:, ft * D + dot * 128: ft * D + dot * 128 + 128],
                                        vh_bf.ap()[:, ft * CH:(ft + 1) * CH],
                                        start=(ft == 0), stop=(ft == NFT - 1)))
                em["tensor"](tw)

                def vo(v, dot=dot, ps=ps):
                    MM.wait(v); ST.wait(v)
                    VV.inc(v.tensor_tensor(outb.ap()[:, :], ps.ap()[:, :],
                                           xc.ap()[:, dot * CH:(dot + 1) * CH], ALU.add))
                em["vector"](vo)

                def so(s, dot=dot, c=c):
                    VV.wait(s)
                    ST.inc(s.dma_start(out=o_x1p[dot * 128:(dot + 1) * 128, c * CH:(c + 1) * CH],
                                       in_=outb.ap()[:, :]), 16)
                em["sync"](so)

        def fin(s):
            ST.wait(s)
        em["sync"](fin)
    return nc


def run_L3(p, parks, c_core):
    in_maps = [{"qn": parks[i]["qn"], "cfm": c_core[i], "xfm": p["xfm"][i], "WoT": p["WoT"]}
               for i in range(NCORE)]
    return _run("L3", build_L3, in_maps)




def build_L4():
    nc = bass.Bass("TRN2", target_bir_lowering=False, debug=False)
    xn_i = nc.dram_tensor("xn", [D, TPC], BF16, kind="ExternalInput").ap()
    x1p_i = nc.dram_tensor("x1p", [D, TPC], F32, kind="ExternalInput").ap()
    wg_i = nc.dram_tensor("WgT", [D, FH], F32, kind="ExternalInput").ap()
    wu_i = nc.dram_tensor("WuT", [D, FH], F32, kind="ExternalInput").ap()
    wd_i = nc.dram_tensor("WdT", [FH, D], F32, kind="ExternalInput").ap()
    o_x1 = nc.dram_tensor("x1", [D, TPC], F32, kind="ExternalOutput").ap()
    HT = 1024
    import contextlib
    with contextlib.ExitStack() as ex:
        sb = lambda *a: ex.enter_context(nc.sbuf_tensor(*a))
        pt = lambda *a: ex.enter_context(nc.psum_tensor(*a))
        xn_sb = sb("xn_sb", [128, NFT * TPC], BF16)
        acc = sb("acc", [128, NFT * HT], F32)
        gated = sb("gated", [128, 16 * HT], BF16)
        wg_t = sb("wg_t", [128, NFT * 128], BF16)
        wu_t = sb("wu_t", [128, NFT * 128], BF16)
        wd_t = sb("wd_t", [128, 16 * 128], BF16)
        gs = sb("gs", [128, CH], F32)
        ps_g = pt("ps_g", [128, CH], F32)
        ps_u = pt("ps_u", [128, CH], F32)
        ps_d0 = pt("ps_d0", [128, CH], F32)
        ps_d1 = pt("ps_d1", [128, CH], F32)
        sems = {k: ex.enter_context(nc.semaphore(k)) for k in ["LD", "MM", "VV", "SS", "GG", "ST"]}
        block = ex.enter_context(nc.Block())
        ctrs = {k: Ctr(s) for k, s in sems.items()}
        LD, MM, VV, SS, GG, ST = (ctrs[k] for k in ["LD", "MM", "VV", "SS", "GG", "ST"])
        em = _mkemit(block, ctrs)

        def g0(g):
            for j in range(4):
                GG.inc(g.dma_start(
                    out=xn_sb.ap().rearrange("p (ft t) -> p ft t", ft=NFT)[:, j * 4:(j + 1) * 4, :],
                    in_=xn_i.rearrange("(ft p) t -> p ft t", p=128)[:, j * 4:(j + 1) * 4, :]), 16)
        em["gpsimd"](g0)

        for hf in range(2):
            def lda(s, hf=hf):
                ST.wait(s)
                LD.inc(s.dma_start(
                    out=acc.ap().rearrange("p (ft t) -> p ft t", ft=NFT),
                    in_=x1p_i.rearrange("(ft p) t -> p ft t", p=128)[:, :, hf * HT:(hf + 1) * HT]), 16)
            em["sync"](lda)

            for sl in range(4):
                for ht in range(16):
                    hid0 = sl * 2048 + ht * 128

                    def ldw(g, hid0=hid0):
                        MM.wait(g)
                        GG.inc(g.dma_start(
                            out=wg_t.ap().rearrange("p (ft o) -> p ft o", ft=NFT),
                            in_=wg_i.rearrange("(ft p) h -> p ft h", p=128)[:, :, hid0:hid0 + 128]), 16)
                        GG.inc(g.dma_start(
                            out=wu_t.ap().rearrange("p (ft o) -> p ft o", ft=NFT),
                            in_=wu_i.rearrange("(ft p) h -> p ft h", p=128)[:, :, hid0:hid0 + 128]), 16)
                    em["gpsimd"](ldw)

                    for tc in range(2):
                        def tg(t, hf=hf, tc=tc):
                            GG.wait(t); VV.wait(t); SS.wait(t)
                            tok0 = hf * HT + tc * CH
                            for ft in range(NFT):
                                MM.inc(t.matmul(ps_g.ap()[:, :], wg_t.ap()[:, ft * 128:(ft + 1) * 128],
                                                xn_sb.ap()[:, ft * TPC + tok0: ft * TPC + tok0 + CH],
                                                start=(ft == 0), stop=(ft == NFT - 1)))
                            for ft in range(NFT):
                                MM.inc(t.matmul(ps_u.ap()[:, :], wu_t.ap()[:, ft * 128:(ft + 1) * 128],
                                                xn_sb.ap()[:, ft * TPC + tok0: ft * TPC + tok0 + CH],
                                                start=(ft == 0), stop=(ft == NFT - 1)))
                        em["tensor"](tg)

                        def sg(s):
                            MM.wait(s)
                            SS.inc(s.activation(gs.ap()[:, :], ps_g.ap()[:, :], AF.Silu))
                        em["scalar"](sg)

                        def vg(v, ht=ht, tc=tc):
                            SS.wait(v); MM.wait(v)
                            VV.inc(v.tensor_tensor(gated.ap()[:, ht * HT + tc * CH: ht * HT + tc * CH + CH],
                                                   gs.ap()[:, :], ps_u.ap()[:, :], ALU.mult))
                        em["vector"](vg)

                for dot in range(16):
                    def ldd(g, sl=sl, dot=dot):
                        MM.wait(g)
                        GG.inc(g.dma_start(
                            out=wd_t.ap().rearrange("p (ht o) -> p ht o", ht=16),
                            in_=wd_i[sl * 2048:(sl + 1) * 2048, dot * 128:(dot + 1) * 128]
                                .rearrange("(ht p) o -> p ht o", p=128)), 16)
                    em["gpsimd"](ldd)

                    for tc in range(2):
                        ps = (ps_d0, ps_d1)[tc]

                        def td(t, hf=hf, tc=tc, ps=ps):
                            GG.wait(t); VV.wait(t)
                            for ht in range(16):
                                MM.inc(t.matmul(ps.ap()[:, :], wd_t.ap()[:, ht * 128:(ht + 1) * 128],
                                                gated.ap()[:, ht * HT + tc * CH: ht * HT + tc * CH + CH],
                                                start=(ht == 0), stop=(ht == 15)))
                        em["tensor"](td)

                        def vd(v, dot=dot, tc=tc, ps=ps):
                            MM.wait(v); LD.wait(v)
                            sl_ = slice(dot * HT + tc * CH, dot * HT + tc * CH + CH)
                            VV.inc(v.tensor_tensor(acc.ap()[:, sl_], acc.ap()[:, sl_], ps.ap()[:, :], ALU.add))
                        em["vector"](vd)

            def sta(s, hf=hf):
                VV.wait(s)
                ST.inc(s.dma_start(
                    out=o_x1.rearrange("(ft p) t -> p ft t", p=128)[:, :, hf * HT:(hf + 1) * HT],
                    in_=acc.ap().rearrange("p (ft t) -> p ft t", ft=NFT)), 16)
            em["sync"](sta)

        def fin(s):
            ST.wait(s)
        em["sync"](fin)
    return nc


def run_L4(p, parks, r3):
    in_maps = [{"xn": parks[i]["xn"], "x1p": r3[i]["x1p"], "WgT": p["WgT"], "WuT": p["WuT"],
                "WdT": p["WdT"]} for i in range(NCORE)]
    return _run("L4", build_L4, in_maps)




def build_L5():
    nc = bass.Bass("TRN2", target_bir_lowering=False, debug=False)
    x1_i = nc.dram_tensor("x1", [D, TPC], F32, kind="ExternalInput").ap()
    WQ_i = nc.dram_tensor("WQ", [D, D], F32, kind="ExternalInput").ap()
    WK_i = nc.dram_tensor("WK", [D, D], F32, kind="ExternalInput").ap()
    WV_i = nc.dram_tensor("WV", [D, D], F32, kind="ExternalInput").ap()
    gw_i = nc.dram_tensor("gwT", [D, NE], F32, kind="ExternalInput").ap()
    bias_i = nc.dram_tensor("biasT", [128, NE], F32, kind="ExternalInput").ap()
    fc1_i = nc.dram_tensor("fc1T", [D, LAT], F32, kind="ExternalInput").ap()
    qc_i = nc.dram_tensor("qcos", [D, TPC], F32, kind="ExternalInput").ap()
    qs_i = nc.dram_tensor("qsin", [D, TPC], F32, kind="ExternalInput").ap()
    kc_i = nc.dram_tensor("kcos", [D, TPC], F32, kind="ExternalInput").ap()
    ks_i = nc.dram_tensor("ksin", [D, TPC], F32, kind="ExternalInput").ap()
    o_xn1 = nc.dram_tensor("xn1", [D, TPC], BF16, kind="ExternalOutput").ap()
    o_Qx = nc.dram_tensor("Qx", [D, TPC], BF16, kind="ExternalOutput").ap()
    o_Kx = nc.dram_tensor("Kx", [D, TPC], BF16, kind="ExternalOutput").ap()
    o_V = nc.dram_tensor("Vtm", [TPC, D], BF16, kind="ExternalOutput").ap()
    o_cw = nc.dram_tensor("cw", [TPC, NE], F32, kind="ExternalOutput").ap()
    o_xlat = nc.dram_tensor("xlat", [LAT, TPC], BF16, kind="ExternalOutput").ap()
    BIGC = 65536.0
    import contextlib
    with contextlib.ExitStack() as ex:
        sb = lambda *a: ex.enter_context(nc.sbuf_tensor(*a))
        pt = lambda *a: ex.enter_context(nc.psum_tensor(*a))
        xn1_sb = sb("xn1_sb", [128, NFT * TPC], BF16)
        w_sb = sb("w_sb", [128, NFT * D], BF16)
        xc = sb("xc", [128, NFT * CH], F32)
        sq = sb("sq", [128, NFT * CH], BF16)
        stats = sb("stats", [1, 5 * CH], F32)
        stats_bf = sb("stats_bf", [1, 4 * CH], BF16)
        ones_col = sb("ones_col", [128, 1], BF16)
        ones_row = sb("ones_row", [1, 128], BF16)
        rt_sb = sb("rt_sb", [128, 128], BF16)
        gw_sb = sb("gw_sb", [128, NFT * NE], BF16)
        bias_sb = sb("bias_sb", [128, NE], F32)
        scores = sb("scores", [128, NE], F32)
        sbt = sb("sbt", [128, NE], F32)
        m8 = sb("m8", [128, 8], F32)
        gsb = sb("gsb", [128, 8], F32)
        g4 = sb("g4", [128, 8], F32)
        gmask = sb("gmask", [128, 4], F32)
        ngm = sb("ngm", [128, 4], F32)
        me = sb("me", [128, NE], F32)
        selm = sb("selm", [128, NE], F32)
        cwt = sb("cwt", [128, NE], F32)
        rsum = sb("rsum", [128, 2], F32)
        qsb = sb("qsb", [128, CH], BF16)
        tq1 = sb("tq1", [128, CH], F32)
        tq2 = sb("tq2", [128, CH], F32)
        qxb = sb("qxb", [128, CH], BF16)
        ctab = sb("ctab", [128, CH], F32)
        stab = sb("stab", [128, CH], F32)
        ps_q = pt("ps_q", [128, CH], F32)
        ps_rot = pt("ps_rot", [128, CH], F32)
        ps_lg = pt("ps_lg", [128, NE], F32)
        ps_s1 = pt("ps_s1", [1, CH], F32)
        ps_s2 = pt("ps_s2", [1, CH], F32)
        ps_bm = pt("ps_bm", [128, CH], F32)
        ps_br = pt("ps_br", [128, CH], F32)
        sems = {k: ex.enter_context(nc.semaphore(k)) for k in ["LD", "MM", "VV", "SS", "GG", "ST"]}
        block = ex.enter_context(nc.Block())
        ctrs = {k: Ctr(s) for k, s in sems.items()}
        LD, MM, VV, SS, GG, ST = (ctrs[k] for k in ["LD", "MM", "VV", "SS", "GG", "ST"])
        em = _mkemit(block, ctrs)
        ones_c_d = nc.inline_tensor(np.ones((128, 1), dtype=np.float32), name="onescol").ap()
        ones_r_d = nc.inline_tensor(np.ones((1, 128), dtype=np.float32), name="onesrow").ap()
        rt_d = nc.inline_tensor(_rot_const(), name="rotT").ap()

        def g0(g):
            GG.inc(g.dma_start(out=ones_col.ap()[:, :], in_=ones_c_d[:, :]), 16)
            GG.inc(g.dma_start(out=ones_row.ap()[:, :], in_=ones_r_d[:, :]), 16)
            GG.inc(g.dma_start(out=rt_sb.ap()[:, :], in_=rt_d), 16)
            GG.inc(g.dma_start(out=gw_sb.ap().rearrange("p (ft e) -> p ft e", ft=NFT),
                               in_=gw_i.rearrange("(ft p) e -> p ft e", p=128)), 16)
        em["gpsimd"](g0)

        def s0(s):
            LD.inc(s.dma_start(out=bias_sb.ap()[:, :], in_=bias_i), 16)
        em["sync"](s0)

        def v0(v):
            VV.inc(v.memset(gsb.ap()[:, 4:8], -1e30))
        em["vector"](v0)

        bufs = dict(xc=xc, xb2=sq, ones_col=ones_col, ones_row=ones_row, stats=stats,
                    stats_bf=stats_bf, ps_s1=ps_s1, ps_s2=ps_s2, ps_bm=ps_bm, ps_br=ps_br)
        for c in range(NCH):
            def ld(s, c=c):
                ST.wait(s); VV.wait(s); SS.wait(s); MM.wait(s)
                LD.inc(s.dma_start(
                    out=xc.ap().rearrange("p (ft t) -> p ft t", ft=NFT),
                    in_=x1_i.rearrange("(ft p) t -> p ft t", p=128)[:, :, c * CH:(c + 1) * CH]), 16)
            em["sync"](ld)
            _ln_stats_chunk(nc, em, bufs, ctrs, c)
            def vcp(v, c=c):
                VV.wait(v)
                for ft in range(NFT):
                    VV.inc(v.tensor_copy(xn1_sb.ap()[:, ft * TPC + c * CH: ft * TPC + (c + 1) * CH],
                                         sq.ap()[:, ft * CH:(ft + 1) * CH]))
            em["vector"](vcp)

            def st_xn(s, c=c):
                VV.wait(s)
                ST.inc(s.dma_start(
                    out=o_xn1.rearrange("(ft p) t -> p ft t", p=128)[:, :, c * CH:(c + 1) * CH],
                    in_=sq.ap().rearrange("p (ft t) -> p ft t", ft=NFT)), 16)
            em["sync"](st_xn)

        for tt in range(TPC // 128):
            def tr(t, tt=tt):
                GG.wait(t); VV.wait(t); SS.wait(t); LD.wait(t)
                for ft in range(NFT):
                    MM.inc(t.matmul(ps_lg.ap()[:, :],
                                    xn1_sb.ap()[:, ft * TPC + tt * 128: ft * TPC + tt * 128 + 128],
                                    gw_sb.ap()[:, ft * NE:(ft + 1) * NE],
                                    start=(ft == 0), stop=(ft == NFT - 1)))
            em["tensor"](tr)

            def sr(s):
                MM.wait(s)
                SS.inc(s.activation(scores.ap()[:, :], ps_lg.ap()[:, :], AF.Sigmoid))
            em["scalar"](sr)

            def vr(v, tt=tt):
                SS.wait(v); ST.wait(v)
                VV.inc(v.tensor_tensor(sbt.ap()[:, :], scores.ap()[:, :], bias_sb.ap()[:, :], ALU.add))
                VV.wait(v)
                for g in range(4):
                    VV.inc(v.max(m8.ap()[:, 0:8], sbt.ap()[:, g * 8:(g + 1) * 8]))
                    VV.wait(v)
                    VV.inc(v.tensor_tensor(gsb.ap()[:, g:g + 1], m8.ap()[:, 0:1], m8.ap()[:, 1:2], ALU.add))
                    VV.wait(v)
                VV.inc(v.max(g4.ap()[:, 0:8], gsb.ap()[:, 0:8]))
                VV.wait(v)
                VV.inc(v.tensor_scalar(gmask.ap()[:, :], gsb.ap()[:, 0:4], g4.ap()[:, 1:2], None, ALU.is_ge))
                VV.wait(v)
                VV.inc(v.tensor_scalar(ngm.ap()[:, :], gmask.ap()[:, :], -1.0, 1.0, ALU.mult, ALU.add))
                VV.wait(v)
                gmv = gmask.ap()[:, :].unsqueeze(2).broadcast_to([128, 4, 8])
                ngv = ngm.ap()[:, :].unsqueeze(2).broadcast_to([128, 4, 8])
                mev = me.ap()[:, :].rearrange("p (g e) -> p g e", g=4)
                sbv = sbt.ap()[:, :].rearrange("p (g e) -> p g e", g=4)
                VV.inc(v.tensor_tensor(mev, sbv, gmv, ALU.mult))
                VV.wait(v)
                VV.inc(v.scalar_tensor_tensor(mev, ngv, -BIGC, mev, ALU.mult, ALU.add))
                VV.wait(v)
                VV.inc(v.max(m8.ap()[:, 0:8], me.ap()[:, :]))
                VV.wait(v)
                VV.inc(v.tensor_scalar(selm.ap()[:, :], me.ap()[:, :], m8.ap()[:, 7:8], None, ALU.is_ge))
                VV.wait(v)
                VV.inc(v.tensor_tensor(cwt.ap()[:, :], scores.ap()[:, :], selm.ap()[:, :], ALU.mult))
                VV.wait(v)
                VV.inc(v.reduce_sum(rsum.ap()[:, 0:1], cwt.ap()[:, :], AX))
                VV.wait(v)
                VV.inc(v.reciprocal(rsum.ap()[:, 1:2], rsum.ap()[:, 0:1]))
                VV.wait(v)
                VV.inc(v.tensor_scalar(cwt.ap()[:, :], cwt.ap()[:, :], rsum.ap()[:, 1:2], 2.5,
                                       ALU.mult, ALU.mult))
            em["vector"](vr)

            def str_(s, tt=tt):
                VV.wait(s)
                ST.inc(s.dma_start(out=o_cw[tt * 128:(tt + 1) * 128, :], in_=cwt.ap()[:, :]), 16)
            em["sync"](str_)

        for (wi, ci, si, park) in ((WQ_i, qc_i, qs_i, o_Qx), (WK_i, kc_i, ks_i, o_Kx)):
            def gw(g, wi=wi):
                MM.wait(g); VV.wait(g); SS.wait(g)
                for j in range(4):
                    GG.inc(g.dma_start(
                        out=w_sb.ap().rearrange("p (ft o) -> p ft o", ft=NFT)[:, j * 4:(j + 1) * 4, :],
                        in_=wi.rearrange("(ft p) o -> p ft o", p=128)[:, j * 4:(j + 1) * 4, :]), 16)
            em["gpsimd"](gw)
            for c in range(NCH):
                for dqt in range(NFT):
                    def ldt(s, ci=ci, si=si, c=c, dqt=dqt):
                        VV.wait(s)
                        LD.inc(s.dma_start(out=ctab.ap()[:, :],
                                           in_=ci[dqt * 128:(dqt + 1) * 128, c * CH:(c + 1) * CH]), 16)
                        LD.inc(s.dma_start(out=stab.ap()[:, :],
                                           in_=si[dqt * 128:(dqt + 1) * 128, c * CH:(c + 1) * CH]), 16)
                    em["sync"](ldt)

                    def tp(t, c=c, dqt=dqt):
                        GG.wait(t); VV.wait(t); SS.wait(t)
                        for ft in range(NFT):
                            MM.inc(t.matmul(ps_q.ap()[:, :],
                                            w_sb.ap()[:, ft * D + dqt * 128: ft * D + dqt * 128 + 128],
                                            xn1_sb.ap()[:, ft * TPC + c * CH: ft * TPC + (c + 1) * CH],
                                            start=(ft == 0), stop=(ft == NFT - 1)))
                    em["tensor"](tp)

                    def sq_(s):
                        MM.wait(s)
                        SS.inc(s.copy(qsb.ap()[:, :], ps_q.ap()[:, :]))
                    em["scalar"](sq_)

                    def trot(t):
                        SS.wait(t)
                        MM.inc(t.matmul(ps_rot.ap()[:, :], rt_sb.ap()[:, :], qsb.ap()[:, :],
                                        start=True, stop=True))
                    em["tensor"](trot)

                    def vx(v):
                        MM.wait(v); LD.wait(v); ST.wait(v)
                        VV.inc(v.tensor_tensor(tq1.ap()[:, :], qsb.ap()[:, :], ctab.ap()[:, :], ALU.mult))
                        VV.inc(v.tensor_tensor(tq2.ap()[:, :], ps_rot.ap()[:, :], stab.ap()[:, :], ALU.mult))
                        VV.wait(v)
                        VV.inc(v.tensor_tensor(qxb.ap()[:, :], tq1.ap()[:, :], tq2.ap()[:, :], ALU.add))
                    em["vector"](vx)

                    def stq(s, park=park, c=c, dqt=dqt):
                        VV.wait(s)
                        ST.inc(s.dma_start(out=park[dqt * 128:(dqt + 1) * 128, c * CH:(c + 1) * CH],
                                           in_=qxb.ap()[:, :]), 16)
                    em["sync"](stq)

        def gv(g):
            MM.wait(g); VV.wait(g); SS.wait(g)
            for j in range(4):
                GG.inc(g.dma_start(
                    out=w_sb.ap().rearrange("p (ft o) -> p ft o", ft=NFT)[:, j * 4:(j + 1) * 4, :],
                    in_=WV_i.rearrange("(ft p) o -> p ft o", p=128)[:, j * 4:(j + 1) * 4, :]), 16)
        em["gpsimd"](gv)
        for tt in range(TPC // 128):
            for dvs in range(4):
                def tv(t, tt=tt, dvs=dvs):
                    GG.wait(t); VV.wait(t)
                    for ft in range(NFT):
                        MM.inc(t.matmul(ps_q.ap()[:, :],
                                        xn1_sb.ap()[:, ft * TPC + tt * 128: ft * TPC + tt * 128 + 128],
                                        w_sb.ap()[:, ft * D + dvs * CH: ft * D + (dvs + 1) * CH],
                                        start=(ft == 0), stop=(ft == NFT - 1)))
                em["tensor"](tv)

                def vv_(v):
                    MM.wait(v); ST.wait(v)
                    VV.inc(v.tensor_copy(qxb.ap()[:, :], ps_q.ap()[:, :]))
                em["vector"](vv_)

                def sv(s, tt=tt, dvs=dvs):
                    VV.wait(s)
                    ST.inc(s.dma_start(out=o_V[tt * 128:(tt + 1) * 128, dvs * CH:(dvs + 1) * CH],
                                       in_=qxb.ap()[:, :]), 16)
                em["sync"](sv)

        def gl(g):
            MM.wait(g); VV.wait(g)
            GG.inc(g.dma_start(
                out=w_sb.ap()[:, 0:NFT * LAT].rearrange("p (ft o) -> p ft o", ft=NFT),
                in_=fc1_i.rearrange("(ft p) o -> p ft o", p=128)), 16)
        em["gpsimd"](gl)
        for latt in range(LAT // 128):
            for c in range(NCH):
                def tl(t, latt=latt, c=c):
                    GG.wait(t); VV.wait(t)
                    for ft in range(NFT):
                        MM.inc(t.matmul(ps_q.ap()[:, :],
                                        w_sb.ap()[:, ft * LAT + latt * 128: ft * LAT + latt * 128 + 128],
                                        xn1_sb.ap()[:, ft * TPC + c * CH: ft * TPC + (c + 1) * CH],
                                        start=(ft == 0), stop=(ft == NFT - 1)))
                em["tensor"](tl)

                def vl(v):
                    MM.wait(v); ST.wait(v)
                    VV.inc(v.tensor_copy(qxb.ap()[:, :], ps_q.ap()[:, :]))
                em["vector"](vl)

                def sl_(s, latt=latt, c=c):
                    VV.wait(s)
                    ST.inc(s.dma_start(out=o_xlat[latt * 128:(latt + 1) * 128, c * CH:(c + 1) * CH],
                                       in_=qxb.ap()[:, :]), 16)
                em["sync"](sl_)

        def fin(s):
            ST.wait(s)
        em["sync"](fin)
    return nc


def run_L5(p, r4):
    tabs = _xpos_tables()
    in_maps = []
    for i in range(NCORE):
        h = i % 2
        qc, qs, kc, ks = tabs[h]
        in_maps.append({"x1": r4[i]["x1"], "WQ": p["WQ"], "WK": p["WK"], "WV": p["WV"],
                        "gwT": p["gwT"], "biasT": p["biasT"], "fc1T": p["fc1T"],
                        "qcos": qc, "qsin": qs, "kcos": kc, "ksin": ks})
    return _run("L5", build_L5, in_maps)




def _decay_tiles():
    lg = np.log(np.float64(GAMMA))
    dm = np.arange(128)[:, None]
    dq = np.arange(512)[None, :]
    full = np.exp(lg * (dq - dm)).astype(np.float32)
    diags = []
    for o in range(4):
        v = np.exp(lg * (dq - 128 * o - dm))
        v = np.where(dm + 128 * o <= dq, v, 0.0)
        diags.append(v.astype(np.float32))
    return full, diags


def build_L6():
    nc = bass.Bass("TRN2", target_bir_lowering=False, debug=False)
    Q_i = nc.dram_tensor("Qb", [D, S], BF16, kind="ExternalInput").ap()
    K_i = nc.dram_tensor("Kb", [D, S], BF16, kind="ExternalInput").ap()
    V_i = nc.dram_tensor("Vbh", [S, 1024], BF16, kind="ExternalInput").ap()
    o_att = nc.dram_tensor("attc", [1024, S], F32, kind="ExternalOutput").ap()
    Tfull, Tdiags = _decay_tiles()
    NMT = S // 128
    import contextlib
    with contextlib.ExitStack() as ex:
        sb = lambda *a: ex.enter_context(nc.sbuf_tensor(*a))
        pt = lambda *a: ex.enter_context(nc.psum_tensor(*a))
        qc_sb = sb("qc_sb", [128, NFT * CH], BF16)
        kt_sb = sb("kt_sb", [128, NFT * 128], BF16)
        vt_sb = sb("vt_sb", [128, 1024], BF16)
        slab = sb("slab", [128, NMT * CH], BF16)
        tdk = sb("tdk", [128, 5 * CH], F32)
        ob = sb("ob", [128, CH], F32)
        ps_at = pt("ps_at", [128, CH], F32)
        ps_o = [pt(f"ps_o{j}", [128, CH], F32) for j in range(4)]
        sems = {k: ex.enter_context(nc.semaphore(k)) for k in ["LD", "MM", "VV", "SS", "GG", "ST"]}
        block = ex.enter_context(nc.Block())
        ctrs = {k: Ctr(s) for k, s in sems.items()}
        LD, MM, VV, SS, GG, ST = (ctrs[k] for k in ["LD", "MM", "VV", "SS", "GG", "ST"])
        em = _mkemit(block, ctrs)
        td_d = nc.inline_tensor(np.concatenate([Tfull] + Tdiags, axis=1), name="dectiles").ap()

        def g0(g):
            GG.inc(g.dma_start(out=tdk.ap()[:, :], in_=td_d), 16)
        em["gpsimd"](g0)

        lg = math.log(GAMMA)
        for qc in range(S // CH):
            nmt = 4 * (qc + 1)

            def ldq(s, qc=qc):
                MM.wait(s)
                LD.inc(s.dma_start(
                    out=qc_sb.ap().rearrange("p (ft t) -> p ft t", ft=NFT),
                    in_=Q_i.rearrange("(ft p) t -> p ft t", p=128)[:, :, qc * CH:(qc + 1) * CH]), 16)
            em["sync"](ldq)

            for mt in range(nmt):
                def ldk(g, mt=mt):
                    MM.wait(g)
                    GG.inc(g.dma_start(
                        out=kt_sb.ap().rearrange("p (ft t) -> p ft t", ft=NFT),
                        in_=K_i.rearrange("(ft p) t -> p ft t", p=128)[:, :, mt * 128:(mt + 1) * 128]), 16)
                em["gpsimd"](ldk)

                def ta(t, mt=mt):
                    GG.wait(t); LD.wait(t); VV.wait(t)
                    for ft in range(NFT):
                        MM.inc(t.matmul(ps_at.ap()[:, :],
                                        kt_sb.ap()[:, ft * 128:(ft + 1) * 128],
                                        qc_sb.ap()[:, ft * CH:(ft + 1) * CH],
                                        start=(ft == 0), stop=(ft == NFT - 1)))
                em["tensor"](ta)

                dist = qc * 512 - mt * 128
                if mt < 4 * qc:
                    cimm = float(np.float32(np.exp(np.float64(lg) * dist)))
                    tsl = tdk.ap()[:, 0:CH]
                else:
                    cimm = 1.0
                    o = mt - 4 * qc
                    tsl = tdk.ap()[:, (1 + o) * CH:(2 + o) * CH]

                def va(v, mt=mt, cimm=cimm, tsl=tsl):
                    MM.wait(v)
                    VV.inc(v.scalar_tensor_tensor(slab.ap()[:, mt * CH:(mt + 1) * CH],
                                                  ps_at.ap()[:, :], cimm, tsl, ALU.mult, ALU.mult))
                em["vector"](va)

            for hp in range(2):
                for mt in range(nmt):
                    def ldv(g, mt=mt, hp=hp):
                        MM.wait(g)
                        GG.inc(g.dma_start(out=vt_sb.ap()[:, 0:CH],
                                           in_=V_i[mt * 128:(mt + 1) * 128, hp * CH:(hp + 1) * CH]), 16)
                    em["gpsimd"](ldv)

                    def tb(t, mt=mt, hp=hp, first=(mt == 0), last=(mt == nmt - 1)):
                        GG.wait(t); VV.wait(t); SS.wait(t)
                        for j in range(4):
                            MM.inc(t.matmul(ps_o[j].ap()[:, :],
                                            vt_sb.ap()[:, j * 128:(j + 1) * 128],
                                            slab.ap()[:, mt * CH:(mt + 1) * CH],
                                            start=first, stop=last))
                    em["tensor"](tb)

                for j in range(4):
                    def vo(v, j=j):
                        MM.wait(v); ST.wait(v)
                        VV.inc(v.tensor_copy(ob.ap()[:, :], ps_o[j].ap()[:, :]))
                    em["vector"](vo)

                    def so(s, j=j, hp=hp, qc=qc):
                        VV.wait(s)
                        dv0 = hp * CH + j * 128
                        ST.inc(s.dma_start(out=o_att[dv0:dv0 + 128, qc * CH:(qc + 1) * CH],
                                           in_=ob.ap()[:, :]), 16)
                    em["sync"](so)

        def fin(s):
            ST.wait(s)
        em["sync"](fin)
    return nc


def run_L6(p, r5):
    in_maps = []
    for i in range(NCORE):
        b, h = i // 2, i % 2
        Qb = np.concatenate([np.asarray(r5[2 * b]["Qx"]), np.asarray(r5[2 * b + 1]["Qx"])], axis=1)
        Kb = np.concatenate([np.asarray(r5[2 * b]["Kx"]), np.asarray(r5[2 * b + 1]["Kx"])], axis=1)
        Vb = np.concatenate([np.asarray(r5[2 * b]["Vtm"]), np.asarray(r5[2 * b + 1]["Vtm"])], axis=0)
        in_maps.append({"Qb": np.ascontiguousarray(Qb), "Kb": np.ascontiguousarray(Kb),
                        "Vbh": np.ascontiguousarray(Vb[:, h * 1024:(h + 1) * 1024])})
    r = _run("L6", build_L6, in_maps)
    attx = []
    for i in range(NCORE):
        b, p0 = _tok_owner(i)
        af = np.concatenate([r[2 * b]["attc"], r[2 * b + 1]["attc"]], axis=0)
        attx.append(np.ascontiguousarray(af[:, p0:p0 + TPC]))
    return attx




def build_L7():
    nc = bass.Bass("TRN2", target_bir_lowering=False, debug=False)
    xn1_i = nc.dram_tensor("xn1", [D, TPC], BF16, kind="ExternalInput").ap()
    xlat_i = nc.dram_tensor("xlat", [LAT, TPC], BF16, kind="ExternalInput").ap()
    cw_i = nc.dram_tensor("cw", [TPC, NE], F32, kind="ExternalInput").ap()
    w1_i = nc.dram_tensor("w1T", [NE, LAT, INTER], F32, kind="ExternalInput").ap()
    w2_i = nc.dram_tensor("w2T", [NE, INTER, LAT], F32, kind="ExternalInput").ap()
    fc2_i = nc.dram_tensor("fc2T", [LAT, D], F32, kind="ExternalInput").ap()
    su_i = nc.dram_tensor("suT", [D, SHH], F32, kind="ExternalInput").ap()
    sd_i = nc.dram_tensor("sdT", [SHH, D], F32, kind="ExternalInput").ap()
    x1_i = nc.dram_tensor("x1", [D, TPC], F32, kind="ExternalInput").ap()
    att_i = nc.dram_tensor("attx", [D, TPC], F32, kind="ExternalInput").ap()
    o_x2 = nc.dram_tensor("x2", [D, TPC], F32, kind="ExternalOutput").ap()
    HT = 1024
    NTT = TPC // 128
    import contextlib
    with contextlib.ExitStack() as ex:
        sb = lambda *a: ex.enter_context(nc.sbuf_tensor(*a))
        pt = lambda *a: ex.enter_context(nc.psum_tensor(*a))
        yfm = sb("yfm", [128, 8 * TPC], BF16)
        cw_sb = sb("cw_sb", [128, NTT * NE], F32)
        hr = sb("hr", [128, CH], F32)
        ob = sb("ob", [128, CH], F32)
        ident = sb("ident", [128, 128], BF16)
        ps_a = pt("ps_a", [128, CH], F32)
        ps_b = pt("ps_b", [128, CH], F32)
        ps_t = pt("ps_t", [128, 128], BF16)
        sems = {k: ex.enter_context(nc.semaphore(k)) for k in ["LD", "MM", "VV", "SS", "GG", "ST"]}
        block = ex.enter_context(nc.Block())
        ctrs = {k: Ctr(s) for k, s in sems.items()}
        LD, MM, VV, SS, GG, ST = (ctrs[k] for k in ["LD", "MM", "VV", "SS", "GG", "ST"])
        em = _mkemit(block, ctrs)
        id_d = nc.inline_tensor(np.eye(128, dtype=np.float32), name="ident128").ap()

        def g0(g):
            GG.inc(g.dma_start(out=ident.ap()[:, :], in_=id_d), 16)
            GG.inc(g.dma_start(out=cw_sb.ap().rearrange("p (tt e) -> p tt e", tt=NTT),
                               in_=cw_i.rearrange("(tt p) e -> p tt e", p=128)), 16)
        em["gpsimd"](g0)

        with contextlib.ExitStack() as ex2:
            sb2 = lambda *a: ex2.enter_context(nc.sbuf_tensor(*a))
            y_acc = sb2("y_acc", [128, NTT * LAT], F32)
            xlat_sb = sb2("xlat_sb", [128, 8 * TPC], BF16)
            h_sb = sb2("h_sb", [128, 8 * TPC], BF16)
            w1_sb = sb2("w1_sb", [128, 8 * INTER], BF16)
            w2_sb = sb2("w2_sb", [128, 8 * LAT], BF16)

            def g1(g):
                for j in range(2):
                    GG.inc(g.dma_start(
                        out=xlat_sb.ap().rearrange("p (lt t) -> p lt t", lt=8)[:, j * 4:(j + 1) * 4, :],
                        in_=xlat_i.rearrange("(lt p) t -> p lt t", p=128)[:, j * 4:(j + 1) * 4, :]), 16)
            em["gpsimd"](g1)

            def vz(v):
                VV.inc(v.memset(y_acc.ap()[:, :], 0.0))
            em["vector"](vz)

            for e in range(NE):
                def gw(g, e=e):
                    MM.wait(g)
                    GG.inc(g.dma_start(out=w1_sb.ap().rearrange("p (lt i) -> p lt i", lt=8),
                                       in_=w1_i[e].rearrange("(lt p) i -> p lt i", p=128)), 16)
                    GG.inc(g.dma_start(out=w2_sb.ap().rearrange("p (it l) -> p it l", it=8),
                                       in_=w2_i[e].rearrange("(it p) l -> p it l", p=128)), 16)
                em["gpsimd"](gw)

                for it in range(8):
                    for tc in range(NCH):
                        def th(t, it=it, tc=tc):
                            GG.wait(t); VV.wait(t); SS.wait(t)
                            for lt in range(8):
                                MM.inc(t.matmul(ps_a.ap()[:, :],
                                                w1_sb.ap()[:, lt * INTER + it * 128: lt * INTER + it * 128 + 128],
                                                xlat_sb.ap()[:, lt * TPC + tc * CH: lt * TPC + (tc + 1) * CH],
                                                start=(lt == 0), stop=(lt == 7)))
                        em["tensor"](th)

                        def sh(s):
                            MM.wait(s)
                            SS.inc(s.activation(hr.ap()[:, :], ps_a.ap()[:, :], AF.Relu))
                        em["scalar"](sh)

                        def vh(v, it=it, tc=tc):
                            SS.wait(v)
                            VV.inc(v.tensor_tensor(h_sb.ap()[:, it * TPC + tc * CH: it * TPC + (tc + 1) * CH],
                                                   hr.ap()[:, :], hr.ap()[:, :], ALU.mult))
                        em["vector"](vh)

                for tt in range(NTT):
                    for ls in range(2):
                        def ty(t, tt=tt, ls=ls):
                            VV.wait(t)
                            for it in range(8):
                                MM.inc(t.matmul(ps_b.ap()[:, :],
                                                h_sb.ap()[:, it * TPC + tt * 128: it * TPC + tt * 128 + 128],
                                                w2_sb.ap()[:, it * LAT + ls * CH: it * LAT + (ls + 1) * CH],
                                                start=(it == 0), stop=(it == 7)))
                        em["tensor"](ty)

                        def vy(v, tt=tt, ls=ls, e=e):
                            MM.wait(v)
                            dst = y_acc.ap()[:, tt * LAT + ls * CH: tt * LAT + (ls + 1) * CH]
                            VV.inc(v.scalar_tensor_tensor(dst, ps_b.ap()[:, :],
                                                          cw_sb.ap()[:, tt * NE + e: tt * NE + e + 1],
                                                          dst, ALU.mult, ALU.add))
                        em["vector"](vy)

            for tt in range(NTT):
                def vcast(v, tt=tt):
                    MM.wait(v); VV.wait(v)
                    VV.inc(v.tensor_copy(h_sb.ap()[:, tt * LAT:(tt + 1) * LAT],
                                         y_acc.ap()[:, tt * LAT:(tt + 1) * LAT]))
                em["vector"](vcast)
            for tt in range(NTT):
                for lt in range(8):
                    def tt_(t, tt=tt, lt=lt):
                        VV.wait(t); SS.wait(t)
                        MM.inc(t.transpose(ps_t.ap()[:, :],
                                           h_sb.ap()[:, tt * LAT + lt * 128: tt * LAT + lt * 128 + 128],
                                           ident.ap()[:, :]))
                    em["tensor"](tt_)

                    def vtr(v, tt=tt, lt=lt):
                        MM.wait(v)
                        VV.inc(v.tensor_copy(yfm.ap()[:, lt * TPC + tt * 128: lt * TPC + tt * 128 + 128],
                                             ps_t.ap()[:, :]))
                    em["vector"](vtr)

        with contextlib.ExitStack() as ex3:
            sb3 = lambda *a: ex3.enter_context(nc.sbuf_tensor(*a))
            acc = sb3("acc", [128, NFT * HT], F32)
            shg = sb3("shg", [128, 16 * CH], BF16)
            fc2_sb = sb3("fc2_sb", [128, 8 * D], BF16)
            xn1_sl = sb3("xn1_sl", [128, NFT * CH], BF16)
            su_t = sb3("su_t", [128, NFT * 128], BF16)
            sd_t = sb3("sd_t", [128, 16 * 128], BF16)
            atile = sb3("atile", [128, HT], F32)

            def g2(g):
                for j in range(4):
                    GG.inc(g.dma_start(
                        out=fc2_sb.ap().rearrange("p (lt o) -> p lt o", lt=8)[:, j * 2:(j + 1) * 2, :],
                        in_=fc2_i.rearrange("(lt p) o -> p lt o", p=128)[:, j * 2:(j + 1) * 2, :]), 16)
            em["gpsimd"](g2)

            for hf in range(2):
                def lda(s, hf=hf):
                    ST.wait(s); VV.wait(s)
                    LD.inc(s.dma_start(
                        out=acc.ap().rearrange("p (ft t) -> p ft t", ft=NFT),
                        in_=x1_i.rearrange("(ft p) t -> p ft t", p=128)[:, :, hf * HT:(hf + 1) * HT]), 16)
                em["sync"](lda)
                for ft in range(NFT):
                    def lat_(s, hf=hf, ft=ft):
                        VV.wait(s)
                        LD.inc(s.dma_start(out=atile.ap()[:, :],
                                           in_=att_i[ft * 128:(ft + 1) * 128, hf * HT:(hf + 1) * HT]), 16)
                    em["sync"](lat_)

                    def vat(v, ft=ft):
                        LD.wait(v)
                        sl_ = slice(ft * HT, (ft + 1) * HT)
                        VV.inc(v.tensor_tensor(acc.ap()[:, sl_], acc.ap()[:, sl_], atile.ap()[:, :], ALU.add))
                    em["vector"](vat)

                for tc in range(2):
                    tok0 = hf * HT + tc * CH

                    def ldx(s, tok0=tok0):
                        MM.wait(s)
                        LD.inc(s.dma_start(
                            out=xn1_sl.ap().rearrange("p (ft t) -> p ft t", ft=NFT),
                            in_=xn1_i.rearrange("(ft p) t -> p ft t", p=128)[:, :, tok0:tok0 + CH]), 16)
                    em["sync"](ldx)

                    for s2 in range(2):
                        for ht in range(16):
                            hid0 = s2 * 2048 + ht * 128

                            def lsu(g, hid0=hid0):
                                MM.wait(g)
                                GG.inc(g.dma_start(
                                    out=su_t.ap().rearrange("p (ft o) -> p ft o", ft=NFT),
                                    in_=su_i.rearrange("(ft p) h -> p ft h", p=128)[:, :, hid0:hid0 + 128]), 16)
                            em["gpsimd"](lsu)

                            def tsu(t, ht=ht):
                                GG.wait(t); LD.wait(t); VV.wait(t); SS.wait(t)
                                for ft in range(NFT):
                                    MM.inc(t.matmul(ps_a.ap()[:, :],
                                                    su_t.ap()[:, ft * 128:(ft + 1) * 128],
                                                    xn1_sl.ap()[:, ft * CH:(ft + 1) * CH],
                                                    start=(ft == 0), stop=(ft == NFT - 1)))
                            em["tensor"](tsu)

                            def ssu(s):
                                MM.wait(s)
                                SS.inc(s.activation(hr.ap()[:, :], ps_a.ap()[:, :], AF.Relu))
                            em["scalar"](ssu)

                            def vsu(v, ht=ht):
                                SS.wait(v)
                                VV.inc(v.tensor_tensor(shg.ap()[:, ht * CH:(ht + 1) * CH],
                                                       hr.ap()[:, :], hr.ap()[:, :], ALU.mult))
                            em["vector"](vsu)

                        for dot in range(16):
                            def lsd(g, s2=s2, dot=dot):
                                MM.wait(g)
                                GG.inc(g.dma_start(
                                    out=sd_t.ap().rearrange("p (ht o) -> p ht o", ht=16),
                                    in_=sd_i[s2 * 2048:(s2 + 1) * 2048, dot * 128:(dot + 1) * 128]
                                        .rearrange("(ht p) o -> p ht o", p=128)), 16)
                            em["gpsimd"](lsd)

                            def tsd(t, dot=dot):
                                GG.wait(t); VV.wait(t)
                                for ht in range(16):
                                    MM.inc(t.matmul(ps_b.ap()[:, :],
                                                    sd_t.ap()[:, ht * 128:(ht + 1) * 128],
                                                    shg.ap()[:, ht * CH:(ht + 1) * CH],
                                                    start=(ht == 0), stop=(ht == 15)))
                            em["tensor"](tsd)

                            def vsd(v, dot=dot, tc=tc):
                                MM.wait(v); LD.wait(v)
                                sl_ = slice(dot * HT + tc * CH, dot * HT + tc * CH + CH)
                                VV.inc(v.tensor_tensor(acc.ap()[:, sl_], acc.ap()[:, sl_],
                                                       ps_b.ap()[:, :], ALU.add))
                            em["vector"](vsd)

                    for dot in range(16):
                        def tr2(t, dot=dot, tok0=tok0):
                            GG.wait(t); VV.wait(t)
                            for lt in range(8):
                                MM.inc(t.matmul(ps_a.ap()[:, :],
                                                fc2_sb.ap()[:, lt * D + dot * 128: lt * D + dot * 128 + 128],
                                                yfm.ap()[:, lt * TPC + tok0: lt * TPC + tok0 + CH],
                                                start=(lt == 0), stop=(lt == 7)))
                        em["tensor"](tr2)

                        def vr2(v, dot=dot, tc=tc):
                            MM.wait(v)
                            sl_ = slice(dot * HT + tc * CH, dot * HT + tc * CH + CH)
                            VV.inc(v.tensor_tensor(acc.ap()[:, sl_], acc.ap()[:, sl_],
                                                   ps_a.ap()[:, :], ALU.add))
                        em["vector"](vr2)

                def sta(s, hf=hf):
                    VV.wait(s)
                    ST.inc(s.dma_start(
                        out=o_x2.rearrange("(ft p) t -> p ft t", p=128)[:, :, hf * HT:(hf + 1) * HT],
                        in_=acc.ap().rearrange("p (ft t) -> p ft t", ft=NFT)), 16)
                em["sync"](sta)

            def fin(s):
                ST.wait(s)
            em["sync"](fin)
    return nc


def run_L7(p, r5, r4, attx):
    in_maps = [{"xn1": r5[i]["xn1"], "xlat": r5[i]["xlat"], "cw": r5[i]["cw"],
                "w1T": p["w1T"], "w2T": p["w2T"], "fc2T": p["fc2T"], "suT": p["suT"],
                "sdT": p["sdT"], "x1": r4[i]["x1"], "attx": attx[i]} for i in range(NCORE)]
    return _run("L7", build_L7, in_maps)



def _tok_owner(i):
    return i // 2, (i % 2) * TPC


def _prep_basic(inputs):
    p = {}
    x = np.ascontiguousarray(inputs["L0_attn_x"], dtype=np.float32)
    p["xfm"] = []
    for i in range(NCORE):
        b, p0 = _tok_owner(i)
        p["xfm"].append(np.ascontiguousarray(x[b, p0:p0 + TPC, :].T))
    p["WqT"] = np.ascontiguousarray(inputs["L0_attn_W_q"].T.astype(np.float32))
    p["WkvT"] = np.ascontiguousarray(inputs["L0_attn_W_kv"].T.astype(np.float32))
    p["WoT"] = np.ascontiguousarray(inputs["L0_attn_W_o"].T.astype(np.float32))
    p["WgT"] = np.ascontiguousarray(inputs["L0_ffn_w_gate"].T.astype(np.float32))
    p["WuT"] = np.ascontiguousarray(inputs["L0_ffn_w_up"].T.astype(np.float32))
    p["WdT"] = np.ascontiguousarray(inputs["L0_ffn_w_down"].T.astype(np.float32))
    p["WQ"] = np.ascontiguousarray(inputs["L1_attn_W_Q"].astype(np.float32))
    p["WK"] = np.ascontiguousarray(inputs["L1_attn_W_K"].astype(np.float32))
    p["WV"] = np.ascontiguousarray(inputs["L1_attn_W_V"].astype(np.float32))
    p["gwT"] = np.ascontiguousarray(inputs["L1_ffn_gate_w"].T.astype(np.float32))
    p["biasT"] = np.ascontiguousarray(np.tile(inputs["L1_ffn_gate_bias"].astype(np.float32)[None, :], (128, 1)))
    p["fc1T"] = np.ascontiguousarray(inputs["L1_ffn_fc1_w"].T.astype(np.float32))
    p["fc2T"] = np.ascontiguousarray(inputs["L1_ffn_fc2_w"].T.astype(np.float32))
    p["w1T"] = np.ascontiguousarray(inputs["L1_ffn_w1"].transpose(0, 2, 1).astype(np.float32))
    p["w2T"] = np.ascontiguousarray(inputs["L1_ffn_w2"].transpose(0, 2, 1).astype(np.float32))
    p["suT"] = np.ascontiguousarray(inputs["L1_ffn_shared_up_w"].T.astype(np.float32))
    p["sdT"] = np.ascontiguousarray(inputs["L1_ffn_shared_down_w"].T.astype(np.float32))
    return p


def _xpos_tables():
    half = D // 2
    j = np.arange(half, dtype=np.float64)
    scale_vec = (2.0 * j + 0.4 * D) / (1.4 * D)
    inv_freq = 1.0 / (10000.0 ** (j / half))
    out = []
    for h in range(2):
        pos = np.arange(h * TPC, (h + 1) * TPC, dtype=np.float64)
        scale = scale_vec[None, :] ** (pos[:, None] / 512.0)
        sinus = pos[:, None] * inv_freq[None, :]
        sin, cos = np.sin(sinus), np.cos(sinus)
        qc = np.repeat(cos * scale, 2, axis=1).T
        qs = np.repeat(sin * scale, 2, axis=1).T
        kc = np.repeat(cos / scale, 2, axis=1).T
        ks = np.repeat(sin / scale, 2, axis=1).T
        out.append(tuple(np.ascontiguousarray(a.astype(np.float32)) for a in (qc, qs, kc, ks)))
    return out


def _run(name, build_fn, in_maps):
    if name not in _prog_cache:
        _prog_cache[name] = build_fn()
    res = run_bass_kernel_spmd(_prog_cache[name], in_maps, core_ids=list(range(NCORE)))
    return res.results


def run_L1(p):
    in_maps = [{"xfm": p["xfm"][i], "WqT": p["WqT"], "WkvT": p["WkvT"]} for i in range(NCORE)]
    return _run("L1", build_L1, in_maps)


def kernel(**inputs):
    out, _ = kernel_fused(**inputs)
    return out



FDEBUG = False

RGPAIR = [[0, 1], [2, 3], [4, 5], [6, 7]]
NJOB = [4 * (qc + 1) + 16 for qc in range(4)]
NJOBS = sum(NJOB)


def _dtiles_fused(h):
    lg = np.log(np.float64(GAMMA))
    dm = np.arange(128)[:, None]
    dq = np.arange(512)[None, :]
    out = np.zeros((NJOBS, 128, 512), dtype=np.float32)
    j = 0
    for qc in range(4):
        q0g = h * TPC + qc * 512
        for mt in range(NJOB[qc]):
            m0 = mt * 128
            expo = (q0g + dq) - (m0 + dm)
            v = np.where(expo >= 0, np.exp(lg * expo), 0.0)
            out[j] = v.astype(np.float32)
            j += 1
    return np.ascontiguousarray(out)


def build_FUSED():
    nc = bass.Bass("TRN2", target_bir_lowering=False, debug=False)
    import contextlib
    EI = lambda n, s, d: nc.dram_tensor(n, s, d, kind="ExternalInput").ap()
    xfm = EI("xfm", [D, TPC], F32)
    WqT = EI("WqT", [D, D], F32); WkvT = EI("WkvT", [D, D], F32); WoT = EI("WoT", [D, D], F32)
    WgT = EI("WgT", [D, FH], F32); WuT = EI("WuT", [D, FH], F32); WdT = EI("WdT", [FH, D], F32)
    WQ = EI("WQ", [D, D], F32); WK = EI("WK", [D, D], F32); WV = EI("WV", [D, D], F32)
    gwT = EI("gwT", [D, NE], F32); biasT = EI("biasT", [128, NE], F32)
    fc1T = EI("fc1T", [D, LAT], F32); fc2T = EI("fc2T", [LAT, D], F32)
    w1T = EI("w1T", [NE, LAT, INTER], F32); w2T = EI("w2T", [NE, INTER, LAT], F32)
    suT = EI("suT", [D, SHH], F32); sdT = EI("sdT", [SHH, D], F32)
    qc_i = EI("qcos", [D, TPC], F32); qs_i = EI("qsin", [D, TPC], F32)
    kc_i = EI("kcos", [D, TPC], F32); ks_i = EI("ksin", [D, TPC], F32)
    dt_i = EI("dtiles", [NJOBS, 128, 512], F32)
    isc_i = EI("isc", [128, 1], F32)
    o_x2 = nc.dram_tensor("x2", [D, TPC], F32, kind="ExternalOutput").ap()
    pkind = "ExternalOutput" if FDEBUG else "Internal"
    PK = lambda n, s, d: nc.dram_tensor(n, s, d, kind=pkind).ap()
    xn_p = PK("xn_p", [D, TPC], BF16)
    qn_p = PK("qn_p", [TPC, D], BF16)
    kvn_p = PK("kvn_p", [TPC, D], BF16)
    c_p = PK("c_p", [D, TPC], F32)
    x1p_p = PK("x1p_p", [D, TPC], F32)
    x1_p = PK("x1_p", [D, TPC], F32)
    xn1_p = PK("xn1_p", [D, TPC], BF16)
    Qx_p = PK("Qx_p", [D, TPC], BF16)
    cw_p = PK("cw_p", [TPC, NE], F32)
    xlat_p = PK("xlat_p", [LAT, TPC], BF16)
    attx_p = PK("attx_p", [D, TPC], F32)
    kc_c = nc.dram_tensor("kc_c", [D, TPC], BF16).ap()
    vc_c = nc.dram_tensor("vc_c", [TPC, D], BF16).ap()
    ctot_c = nc.dram_tensor("ctot_c", [1, D], F32).ap()
    KG = nc.dram_tensor("KG", [2 * D, TPC], BF16).ap()
    VG = nc.dram_tensor("VG", [2 * TPC, D], BF16).ap()
    GTOT = nc.dram_tensor("GTOT", [2, D], F32).ap()
    Cf, SN, Ci, SNi = _dft_consts()
    HT = 1024
    NTT = TPC // 128

    with contextlib.ExitStack() as ex:
        sems = {k: ex.enter_context(nc.semaphore(k)) for k in ["LD", "MM", "VV", "SS", "GG", "ST", "CC"]}
        block = ex.enter_context(nc.Block())
        ctrs = {k: Ctr(s) for k, s in sems.items()}
        LD, MM, VV, SS, GG, ST, CC = (ctrs[k] for k in ["LD", "MM", "VV", "SS", "GG", "ST", "CC"])
        em = _mkemit(block, ctrs)
        ones_c_d = nc.inline_tensor(np.ones((128, 1), dtype=np.float32), name="onescol").ap()
        ones_r_d = nc.inline_tensor(np.ones((1, 128), dtype=np.float32), name="onesrow").ap()
        rt_d = nc.inline_tensor(_rot_const(), name="rotT").ap()
        dft_d = nc.inline_tensor(np.concatenate([Cf, SN, Ci, SNi], axis=1), name="dftc").ap()
        id_d = nc.inline_tensor(np.eye(128, dtype=np.float32), name="ident128").ap()

        with contextlib.ExitStack() as px:
            px.enter_context(nc.named_scope("P1"))
            _phase_barrier(em, ctrs)
            sb = lambda *a: px.enter_context(nc.sbuf_tensor(*a))
            pt = lambda *a: px.enter_context(nc.psum_tensor(*a))
            wq_sb = sb("p1wq", [128, NFT * D], BF16)
            wkv_sb = sb("p1wkv", [128, NFT * D], BF16)
            xc = sb("p1xc", [128, NFT * CH], F32)
            xb2 = sb("p1xb2", [128, NFT * CH], BF16)
            stats = sb("p1stats", [1, 5 * CH], F32)
            stats_bf = sb("p1statsbf", [1, 4 * CH], BF16)
            ones_col = sb("p1onesc", [128, 1], BF16)
            ones_row = sb("p1onesr", [1, 128], BF16)
            qbuf = sb("p1qbuf", [128, D], BF16)
            qnbuf = sb("p1qnbuf", [128, D], BF16)
            qsq = sb("p1qsq", [128, D], F32)
            s16 = sb("p1s16", [128, 16], F32)
            rn16 = sb("p1rn16", [128, 16], F32)
            ps_s1 = pt("p1s1", [1, CH], F32)
            ps_s2 = pt("p1s2", [1, CH], F32)
            ps_bm = pt("p1bm", [128, CH], F32)
            ps_br = pt("p1br", [128, CH], F32)
            ps_q0 = pt("p1q0", [128, CH], F32)
            ps_q1 = pt("p1q1", [128, CH], F32)
            bufs = dict(xc=xc, xb2=xb2, ones_col=ones_col, ones_row=ones_row, stats=stats,
                        stats_bf=stats_bf, ps_s1=ps_s1, ps_s2=ps_s2, ps_bm=ps_bm, ps_br=ps_br)

            def g0(g):
                for j in range(4):
                    GG.inc(g.dma_start(
                        out=wq_sb.ap()[:, j * 4 * D:(j + 1) * 4 * D],
                        in_=WqT.rearrange("(ft p) o -> p ft o", p=128)[:, j * 4:(j + 1) * 4, :]), 16)
                    GG.inc(g.dma_start(
                        out=wkv_sb.ap()[:, j * 4 * D:(j + 1) * 4 * D],
                        in_=WkvT.rearrange("(ft p) o -> p ft o", p=128)[:, j * 4:(j + 1) * 4, :]), 16)
                GG.inc(g.dma_start(out=ones_col.ap()[:, :], in_=ones_c_d[:, :]), 16)
                GG.inc(g.dma_start(out=ones_row.ap()[:, :], in_=ones_r_d[:, :]), 16)
            em["gpsimd"](g0)

            for c in range(NCH):
                def ld(s, c=c):
                    ST.wait(s); VV.wait(s); SS.wait(s); MM.wait(s)
                    LD.inc(s.dma_start(
                        out=xc.ap().rearrange("p (ft c) -> p ft c", ft=NFT),
                        in_=xfm.rearrange("(ft p) t -> p ft t", p=128)[:, :, c * CH:(c + 1) * CH]), 16)
                em["sync"](ld)

                _ln_stats_chunk(nc, em, bufs, ctrs, c)

                def st_xn(s, c=c):
                    VV.wait(s)
                    ST.inc(s.dma_start(
                        out=xn_p.rearrange("(ft p) t -> p ft t", p=128)[:, :, c * CH:(c + 1) * CH],
                        in_=xb2.ap().rearrange("p (ft c) -> p ft c", ft=NFT)), 16)
                em["sync"](st_xn)

                for (wsb, park) in ((wq_sb, qn_p), (wkv_sb, kvn_p)):
                    for ts in range(CH // 128):
                        for fq in range(4):
                            ps = (ps_q0, ps_q1)[fq % 2]

                            def tq(t, wsb=wsb, ts=ts, c=c, fq=fq, ps=ps):
                                GG.wait(t); VV.wait(t); SS.wait(t)
                                for ft in range(NFT):
                                    MM.inc(t.matmul(
                                        ps.ap()[:, 0:CH],
                                        xb2.ap()[:, ft * CH + ts * 128: ft * CH + ts * 128 + 128],
                                        wsb.ap()[:, ft * D + fq * CH: ft * D + (fq + 1) * CH],
                                        start=(ft == 0), stop=(ft == NFT - 1)))
                            em["tensor"](tq)

                            def ev(s, fq=fq, ps=ps):
                                MM.wait(s)
                                SS.inc(s.copy(qbuf.ap()[:, fq * CH:(fq + 1) * CH], ps.ap()[:, 0:CH]))
                            em["scalar"](ev)

                        def vn(v, ts=ts):
                            SS.wait(v)
                            VV.inc(v.tensor_tensor(qsq.ap()[:, :], qbuf.ap()[:, :], qbuf.ap()[:, :], ALU.mult))
                            VV.wait(v)
                            VV.inc(v.tensor_reduce(s16.ap()[:, :],
                                                   qsq.ap().rearrange("p (h f) -> p h f", h=H),
                                                   AX, ALU.add))
                        em["vector"](vn)

                        def sn(s, ts=ts):
                            VV.wait(s)
                            SS.inc(s.activation(rn16.ap()[:, :], s16.ap()[:, :], AF.Ln, scale=float(HD)))
                            SS.wait(s)
                            SS.inc(s.activation(rn16.ap()[:, :], rn16.ap()[:, :], AF.Exp, scale=-0.5))
                        em["scalar"](sn)

                        def vq(v, ts=ts):
                            SS.wait(v)
                            ST.wait(v)
                            for h in range(H):
                                VV.inc(v.tensor_scalar(qnbuf.ap()[:, h * HD:(h + 1) * HD],
                                                       qbuf.ap()[:, h * HD:(h + 1) * HD],
                                                       rn16.ap()[:, h:h + 1], None, ALU.mult))
                        em["vector"](vq)

                        def stq(s, park=park, ts=ts, c=c):
                            VV.wait(s)
                            ST.inc(s.dma_start(out=park[c * CH + ts * 128: c * CH + ts * 128 + 128, :],
                                               in_=qnbuf.ap()[:, :]), 16)
                            ST.wait(s)
                        em["sync"](stq)

        with contextlib.ExitStack() as px:
            px.enter_context(nc.named_scope("PC"))
            _phase_barrier(em, ctrs)
            sb = lambda *a: px.enter_context(nc.sbuf_tensor(*a))
            tin = sb("pcin", [128, TPC], BF16)
            tout = sb("pcout", [128, TPC], F32)
            carry = sb("pccar", [128, NFT], F32)
            iscs = sb("pcisc", [128, 1], F32)
            for ft in range(NFT):
                def ldk(s, ft=ft):
                    VV.wait(s); ST.wait(s)
                    LD.inc(s.dma_start(out=tin.ap()[:, :],
                                       in_=kvn_p[0:TPC, ft * 128:(ft + 1) * 128], transpose=True), 16)
                em["sync"](ldk)

                def vs(v, ft=ft):
                    LD.wait(v); ST.wait(v)
                    VV.inc(v.tensor_tensor_scan(tout.ap()[:, :], tin.ap()[:, :], tin.ap()[:, :],
                                                0.0, ALU.add, ALU.bypass))
                em["vector"](vs)

                def stc(s, ft=ft):
                    VV.wait(s)
                    ST.inc(s.dma_start(out=c_p[ft * 128:(ft + 1) * 128, :], in_=tout.ap()[:, :]), 16)
                    ST.inc(s.dma_start(out=ctot_c[0:1, ft * 128:(ft + 1) * 128].rearrange("a p -> p a"),
                                       in_=tout.ap()[:, TPC - 1:TPC],
                                       allow_slow_non_contiguous=True), 16)
                em["sync"](stc)

            def cc1(g):
                ST.wait(g)
                CC.inc(g.collective_compute("AllGather", ALU.bypass, replica_groups=RGPAIR,
                                            ins=[ctot_c.opt()], outs=[GTOT.opt()]))
            em["gpsimd"](cc1)

            def ldcar(s):
                CC.wait(s)
                LD.inc(s.dma_start(out=carry.ap()[:, :],
                                   in_=GTOT[0:1, :].rearrange("a (ft p) -> p (a ft)", p=128),
                                   allow_slow_non_contiguous=True), 16)
                LD.inc(s.dma_start(out=iscs.ap()[:, :], in_=isc_i), 16)
            em["sync"](ldcar)

            def vcar(v):
                LD.wait(v)
                VV.inc(v.tensor_scalar(carry.ap()[:, :], carry.ap()[:, :], iscs.ap()[:, 0:1], None, ALU.mult))
            em["vector"](vcar)

            for ft in range(NFT):
                def ldc2(s, ft=ft):
                    VV.wait(s); ST.wait(s)
                    LD.inc(s.dma_start(out=tout.ap()[:, :], in_=c_p[ft * 128:(ft + 1) * 128, :]), 16)
                em["sync"](ldc2)

                def vad(v, ft=ft):
                    LD.wait(v); ST.wait(v)
                    VV.inc(v.tensor_scalar(tout.ap()[:, :], tout.ap()[:, :],
                                           carry.ap()[:, ft:ft + 1], None, ALU.add))
                em["vector"](vad)

                def stc2(s, ft=ft):
                    VV.wait(s)
                    ST.inc(s.dma_start(out=c_p[ft * 128:(ft + 1) * 128, :], in_=tout.ap()[:, :]), 16)
                em["sync"](stc2)

        with contextlib.ExitStack() as px:
            px.enter_context(nc.named_scope("P3"))
            _phase_barrier(em, ctrs)
            sb = lambda *a: px.enter_context(nc.sbuf_tensor(*a))
            pt = lambda *a: px.enter_context(nc.psum_tensor(*a))
            wo_sb = sb("p3wo", [128, NFT * D], BF16)
            qn_fm = sb("p3qn", [128, NFT * CH], BF16)
            c_bf = sb("p3c", [128, NFT * CH], BF16)
            xc = sb("p3xc", [128, NFT * CH], F32)
            vh_bf = sb("p3vh", [128, NFT * CH], BF16)
            cfr = sb("p3cfr", [128, CH], F32)
            cfi = sb("p3cfi", [128, CH], F32)
            t1 = sb("p3t1", [128, CH], F32)
            t2 = sb("p3t2", [128, CH], F32)
            qvr = sb("p3qvr", [128, CH], BF16)
            qvi = sb("p3qvi", [128, CH], BF16)
            outb = sb("p3outb", [128, CH], F32)
            cdft = sb("p3cdft", [128, 4 * 128], BF16)
            ps_qr = pt("p3qr", [128, CH], F32)
            ps_qi = pt("p3qi", [128, CH], F32)
            ps_cr = pt("p3cr", [128, CH], F32)
            ps_ci = pt("p3ci", [128, CH], F32)
            ps_vh = pt("p3vhp", [128, CH], F32)
            ps_a0 = pt("p3a0", [128, CH], F32)
            ps_a1 = pt("p3a1", [128, CH], F32)
            CfS = cdft.ap()[:, 0:128]
            SNS = cdft.ap()[:, 128:256]
            CiS = cdft.ap()[:, 256:384]
            SNiS = cdft.ap()[:, 384:512]

            def g0(g):
                GG.inc(g.dma_start(out=cdft.ap()[:, :], in_=dft_d), 16)
                for j in range(4):
                    GG.inc(g.dma_start(
                        out=wo_sb.ap()[:, j * 4 * D:(j + 1) * 4 * D],
                        in_=WoT.rearrange("(ft p) o -> p ft o", p=128)[:, j * 4:(j + 1) * 4, :]), 16)
            em["gpsimd"](g0)

            for c in range(NCH):
                def ldq(s, c=c):
                    MM.wait(s); VV.wait(s)
                    for ft in range(NFT):
                        LD.inc(s.dma_start(out=qn_fm.ap()[:, ft * CH:(ft + 1) * CH],
                                           in_=qn_p[c * CH:(c + 1) * CH, ft * 128:(ft + 1) * 128],
                                           transpose=True), 16)
                    LD.inc(s.dma_start(
                        out=xc.ap().rearrange("p (ft t) -> p ft t", ft=NFT),
                        in_=xfm.rearrange("(ft p) t -> p ft t", p=128)[:, :, c * CH:(c + 1) * CH]), 16)
                em["sync"](ldq)

                def ldc(g, c=c):
                    MM.wait(g)
                    GG.inc(g.dma_start(
                        out=c_bf.ap().rearrange("p (ft t) -> p ft t", ft=NFT),
                        in_=c_p.rearrange("(ft p) t -> p ft t", p=128)[:, :, c * CH:(c + 1) * CH]), 16)
                em["gpsimd"](ldc)

                for ft in range(NFT):
                    def tf(t, ft=ft):
                        LD.wait(t); GG.wait(t); VV.wait(t); SS.wait(t)
                        MM.inc(t.matmul(ps_qr.ap()[:, :], CfS, qn_fm.ap()[:, ft * CH:(ft + 1) * CH],
                                        start=True, stop=True))
                        MM.inc(t.matmul(ps_qi.ap()[:, :], SNS, qn_fm.ap()[:, ft * CH:(ft + 1) * CH],
                                        start=True, stop=True))
                        MM.inc(t.matmul(ps_cr.ap()[:, :], CfS, c_bf.ap()[:, ft * CH:(ft + 1) * CH],
                                        start=True, stop=True))
                        MM.inc(t.matmul(ps_ci.ap()[:, :], SNS, c_bf.ap()[:, ft * CH:(ft + 1) * CH],
                                        start=True, stop=True))
                    em["tensor"](tf)

                    def se(s, ft=ft):
                        MM.wait(s)
                        SS.inc(s.copy(cfr.ap()[:, :], ps_cr.ap()[:, :]))
                        SS.inc(s.copy(cfi.ap()[:, :], ps_ci.ap()[:, :]))
                    em["scalar"](se)

                    def vc(v, ft=ft):
                        SS.wait(v); MM.wait(v)
                        VV.inc(v.tensor_tensor(t1.ap()[:, :], ps_qr.ap()[:, :], cfr.ap()[:, :], ALU.mult))
                        VV.inc(v.tensor_tensor(t2.ap()[:, :], ps_qi.ap()[:, :], cfi.ap()[:, :], ALU.mult))
                        VV.wait(v)
                        VV.inc(v.tensor_tensor(qvr.ap()[:, :], t1.ap()[:, :], t2.ap()[:, :], ALU.subtract))
                        VV.inc(v.tensor_tensor(t1.ap()[:, :], ps_qr.ap()[:, :], cfi.ap()[:, :], ALU.mult))
                        VV.inc(v.tensor_tensor(t2.ap()[:, :], ps_qi.ap()[:, :], cfr.ap()[:, :], ALU.mult))
                        VV.wait(v)
                        VV.inc(v.tensor_tensor(qvi.ap()[:, :], t1.ap()[:, :], t2.ap()[:, :], ALU.add))
                    em["vector"](vc)

                    def ti(t, ft=ft):
                        VV.wait(t)
                        MM.inc(t.matmul(ps_vh.ap()[:, :], CiS, qvr.ap()[:, :], start=True, stop=False))
                        MM.inc(t.matmul(ps_vh.ap()[:, :], SNiS, qvi.ap()[:, :], start=False, stop=True))
                    em["tensor"](ti)

                    def ve(v, ft=ft):
                        MM.wait(v)
                        VV.inc(v.tensor_copy(vh_bf.ap()[:, ft * CH:(ft + 1) * CH], ps_vh.ap()[:, :]))
                    em["vector"](ve)

                for dot in range(NFT):
                    ps = (ps_a0, ps_a1)[dot % 2]

                    def tw(t, dot=dot, ps=ps):
                        VV.wait(t); SS.wait(t)
                        for ft in range(NFT):
                            MM.inc(t.matmul(ps.ap()[:, :],
                                            wo_sb.ap()[:, ft * D + dot * 128: ft * D + dot * 128 + 128],
                                            vh_bf.ap()[:, ft * CH:(ft + 1) * CH],
                                            start=(ft == 0), stop=(ft == NFT - 1)))
                    em["tensor"](tw)

                    def vo(v, dot=dot, ps=ps):
                        MM.wait(v); ST.wait(v)
                        VV.inc(v.tensor_tensor(outb.ap()[:, :], ps.ap()[:, :],
                                               xc.ap()[:, dot * CH:(dot + 1) * CH], ALU.add))
                    em["vector"](vo)

                    def so(s, dot=dot, c=c):
                        VV.wait(s)
                        ST.inc(s.dma_start(out=x1p_p[dot * 128:(dot + 1) * 128, c * CH:(c + 1) * CH],
                                           in_=outb.ap()[:, :]), 16)
                    em["sync"](so)

        with contextlib.ExitStack() as px:
            px.enter_context(nc.named_scope("P4"))
            _phase_barrier(em, ctrs)
            sb = lambda *a: px.enter_context(nc.sbuf_tensor(*a))
            pt = lambda *a: px.enter_context(nc.psum_tensor(*a))
            HT4 = 1024
            xn_sb = sb("p4xn", [128, NFT * TPC], BF16)
            acc = sb("p4acc", [128, NFT * HT4], F32)
            gated = sb("p4gt", [128, 16 * HT4], BF16)
            wg_t = [sb(f"p4wg{b}", [128, NFT * 256], BF16) for b in range(2)]
            wu_t = [sb(f"p4wu{b}", [128, NFT * 256], BF16) for b in range(2)]
            wd_t = [sb(f"p4wd{b}", [128, 16 * 128], BF16) for b in range(2)]
            gsb_ = [sb(f"p4gs{b}", [128, CH], F32) for b in range(2)]
            ps_g = [pt(f"p4g{b}", [128, CH], F32) for b in range(2)]
            ps_u = [pt(f"p4u{b}", [128, CH], F32) for b in range(2)]
            ps_d = [pt(f"p4d{b}", [128, CH], F32) for b in range(2)]

            def g0(g):
                for j in range(4):
                    GG.inc(g.dma_start(
                        out=xn_sb.ap().rearrange("p (ft t) -> p ft t", ft=NFT)[:, j * 4:(j + 1) * 4, :],
                        in_=xn_p.rearrange("(ft p) t -> p ft t", p=128)[:, j * 4:(j + 1) * 4, :]), 16)
            em["gpsimd"](g0)
            xnsnap = GG.n

            wsnap = [0, 0]; musers = [0, 0]
            dsnap = [0, 0]; dusers = [0, 0]
            ggate = [0, 0]
            ugate = [0, 0]
            srel = [0, 0]
            dgate = [0, 0]
            dlast = 0
            accld = 0

            def ldw2(g, hf, sl, hg, b, gate):
                MM.wait_at(g, gate)
                hid0 = sl * 2048 + hg * 256
                GG.inc(g.dma_start(
                    out=wg_t[b].ap().rearrange("p (ft o) -> p ft o", ft=NFT),
                    in_=WgT.rearrange("(ft p) h -> p ft h", p=128)[:, :, hid0:hid0 + 256]), 16)
                GG.inc(g.dma_start(
                    out=wu_t[b].ap().rearrange("p (ft o) -> p ft o", ft=NFT),
                    in_=WuT.rearrange("(ft p) h -> p ft h", p=128)[:, :, hid0:hid0 + 256]), 16)

            def ldd(g, sl, dot, b, gate):
                MM.wait_at(g, gate)
                GG.inc(g.dma_start(
                    out=wd_t[b].ap().rearrange("p (ht o) -> p ht o", ht=16),
                    in_=WdT[sl * 2048:(sl + 1) * 2048, dot * 128:(dot + 1) * 128]
                        .rearrange("(ht p) o -> p ht o", p=128)), 16)

            for hf in range(2):
                def lda(s, hf=hf):
                    ST.wait(s)
                    LD.inc(s.dma_start(
                        out=acc.ap().rearrange("p (ft t) -> p ft t", ft=NFT),
                        in_=x1p_p.rearrange("(ft p) t -> p ft t", p=128)[:, :, hf * HT4:(hf + 1) * HT4]), 16)
                em["sync"](lda)
                accld = LD.n

                for sl in range(4):
                    em["gpsimd"](lambda g, hf=hf, sl=sl, gate=musers[0]: ldw2(g, hf, sl, 0, 0, gate))
                    wsnap[0] = GG.n
                    for hg in range(8):
                        b = hg % 2
                        if hg + 1 < 8:
                            em["gpsimd"](lambda g, hf=hf, sl=sl, hg=hg, b=b,
                                         gate=musers[1 - b]: ldw2(g, hf, sl, hg + 1, 1 - b, gate))
                            wsnap[1 - b] = GG.n
                        for htl in range(2):
                            ht = hg * 2 + htl
                            for tc in range(2):
                                pb = tc
                                tok0 = hf * HT4 + tc * CH

                                def tg(t, b=b, htl=htl, tok0=tok0, pb=pb,
                                       wgate=wsnap[b], g1=ggate[pb], g2=ugate[pb], xg=xnsnap):
                                    GG.wait_at(t, wgate)
                                    GG.wait_at(t, xg)
                                    SS.wait_at(t, g1)
                                    VV.wait_at(t, g2)
                                    for ft in range(NFT):
                                        MM.inc(t.matmul(ps_g[pb].ap()[:, :],
                                                        wg_t[b].ap()[:, ft * 256 + htl * 128: ft * 256 + htl * 128 + 128],
                                                        xn_sb.ap()[:, ft * TPC + tok0: ft * TPC + tok0 + CH],
                                                        start=(ft == 0), stop=(ft == NFT - 1)))
                                    for ft in range(NFT):
                                        MM.inc(t.matmul(ps_u[pb].ap()[:, :],
                                                        wu_t[b].ap()[:, ft * 256 + htl * 128: ft * 256 + htl * 128 + 128],
                                                        xn_sb.ap()[:, ft * TPC + tok0: ft * TPC + tok0 + CH],
                                                        start=(ft == 0), stop=(ft == NFT - 1)))
                                em["tensor"](tg)
                                musers[b] = MM.n

                                def sg(s, pb=pb, snap=MM.n, vg_=srel[pb]):
                                    MM.wait_at(s, snap)
                                    VV.wait_at(s, vg_)
                                    SS.inc(s.activation(gsb_[pb].ap()[:, :], ps_g[pb].ap()[:, :], AF.Silu))
                                em["scalar"](sg)
                                ggate[pb] = SS.n

                                def vg(v, ht=ht, tc=tc, pb=pb, snap=SS.n, mm_=MM.n, dl=dlast):
                                    SS.wait_at(v, snap)
                                    MM.wait_at(v, mm_)
                                    MM.wait_at(v, dl)
                                    VV.inc(v.tensor_tensor(gated.ap()[:, ht * HT4 + tc * CH: ht * HT4 + tc * CH + CH],
                                                           gsb_[pb].ap()[:, :], ps_u[pb].ap()[:, :], ALU.mult))
                                em["vector"](vg)
                                ugate[pb] = VV.n
                                srel[pb] = VV.n
                    vgdone = VV.n

                    em["gpsimd"](lambda g, sl=sl, gate=dusers[0]: ldd(g, sl, 0, 0, gate))
                    dsnap[0] = GG.n
                    for dot in range(16):
                        b = dot % 2
                        if dot + 1 < 16:
                            em["gpsimd"](lambda g, sl=sl, dot=dot, b=b,
                                         gate=dusers[1 - b]: ldd(g, sl, dot + 1, 1 - b, gate))
                            dsnap[1 - b] = GG.n
                        for tc in range(2):
                            pb = tc

                            def td(t, hf=hf, tc=tc, b=b, pb=pb, wgate=dsnap[b], vgd=vgdone, pg=dgate[pb]):
                                GG.wait_at(t, wgate)
                                VV.wait_at(t, vgd)
                                VV.wait_at(t, pg)
                                for ht in range(16):
                                    MM.inc(t.matmul(ps_d[pb].ap()[:, :], wd_t[b].ap()[:, ht * 128:(ht + 1) * 128],
                                                    gated.ap()[:, ht * HT4 + tc * CH: ht * HT4 + tc * CH + CH],
                                                    start=(ht == 0), stop=(ht == 15)))
                            em["tensor"](td)
                            dusers[b] = MM.n

                            def vd(v, dot=dot, tc=tc, pb=pb, snap=MM.n, al=accld):
                                MM.wait_at(v, snap)
                                LD.wait_at(v, al)
                                sl_ = slice(dot * HT4 + tc * CH, dot * HT4 + tc * CH + CH)
                                VV.inc(v.tensor_tensor(acc.ap()[:, sl_], acc.ap()[:, sl_], ps_d[pb].ap()[:, :], ALU.add))
                            em["vector"](vd)
                            dgate[pb] = VV.n
                    dlast = MM.n

                def sta(s, hf=hf):
                    VV.wait(s)
                    ST.inc(s.dma_start(
                        out=x1_p.rearrange("(ft p) t -> p ft t", p=128)[:, :, hf * HT4:(hf + 1) * HT4],
                        in_=acc.ap().rearrange("p (ft t) -> p ft t", ft=NFT)), 16)
                em["sync"](sta)

        with contextlib.ExitStack() as px:
            px.enter_context(nc.named_scope("P5"))
            _phase_barrier(em, ctrs)
            sb = lambda *a: px.enter_context(nc.sbuf_tensor(*a))
            pt = lambda *a: px.enter_context(nc.psum_tensor(*a))
            BIGC = 65536.0
            xn1_sb = sb("p5xn1", [128, NFT * TPC], BF16)
            w_sb = sb("p5w", [128, NFT * D], BF16)
            xc = sb("p5xc", [128, NFT * CH], F32)
            sq = sb("p5sq", [128, NFT * CH], BF16)
            stats = sb("p5stats", [1, 5 * CH], F32)
            stats_bf = sb("p5statsbf", [1, 4 * CH], BF16)
            ones_col = sb("p5onesc", [128, 1], BF16)
            ones_row = sb("p5onesr", [1, 128], BF16)
            rt_sb = sb("p5rt", [128, 128], BF16)
            gw_sb = sb("p5gw", [128, NFT * NE], BF16)
            bias_sb = sb("p5bias", [128, NE], F32)
            scores = sb("p5sc", [128, NE], F32)
            sbt = sb("p5sbt", [128, NE], F32)
            m8 = sb("p5m8", [128, 8], F32)
            gsb = sb("p5gsb", [128, 8], F32)
            g4 = sb("p5g4", [128, 8], F32)
            gmask = sb("p5gm", [128, 4], F32)
            ngm = sb("p5ngm", [128, 4], F32)
            me = sb("p5me", [128, NE], F32)
            selm = sb("p5sel", [128, NE], F32)
            cwt = sb("p5cwt", [128, NE], F32)
            rsum = sb("p5rs", [128, 2], F32)
            qsb = sb("p5qsb", [128, CH], BF16)
            tq1 = sb("p5tq1", [128, CH], F32)
            tq2 = sb("p5tq2", [128, CH], F32)
            qxb = sb("p5qxb", [128, CH], BF16)
            ctab = sb("p5ct", [128, CH], F32)
            stab = sb("p5st", [128, CH], F32)
            ps_q = pt("p5q", [128, CH], F32)
            ps_rot = pt("p5rot", [128, CH], F32)
            ps_lg = pt("p5lg", [128, NE], F32)
            ps_s1 = pt("p5s1", [1, CH], F32)
            ps_s2 = pt("p5s2", [1, CH], F32)
            ps_bm = pt("p5bm", [128, CH], F32)
            ps_br = pt("p5br", [128, CH], F32)

            def g0(g):
                ST.wait(g)
                GG.inc(g.dma_start(out=ones_col.ap()[:, :], in_=ones_c_d[:, :]), 16)
                GG.inc(g.dma_start(out=ones_row.ap()[:, :], in_=ones_r_d[:, :]), 16)
                GG.inc(g.dma_start(out=rt_sb.ap()[:, :], in_=rt_d), 16)
                GG.inc(g.dma_start(out=gw_sb.ap().rearrange("p (ft e) -> p ft e", ft=NFT),
                                   in_=gwT.rearrange("(ft p) e -> p ft e", p=128)), 16)
            em["gpsimd"](g0)

            def s0(s):
                LD.inc(s.dma_start(out=bias_sb.ap()[:, :], in_=biasT), 16)
            em["sync"](s0)

            def v0(v):
                VV.inc(v.memset(gsb.ap()[:, 4:8], -1e30))
            em["vector"](v0)

            bufs = dict(xc=xc, xb2=sq, ones_col=ones_col, ones_row=ones_row, stats=stats,
                        stats_bf=stats_bf, ps_s1=ps_s1, ps_s2=ps_s2, ps_bm=ps_bm, ps_br=ps_br)
            for c in range(NCH):
                def ld(s, c=c):
                    ST.wait(s); VV.wait(s); SS.wait(s); MM.wait(s)
                    LD.inc(s.dma_start(
                        out=xc.ap().rearrange("p (ft t) -> p ft t", ft=NFT),
                        in_=x1_p.rearrange("(ft p) t -> p ft t", p=128)[:, :, c * CH:(c + 1) * CH]), 16)
                em["sync"](ld)
                _ln_stats_chunk(nc, em, bufs, ctrs, c)

                def vcp(v, c=c):
                    VV.wait(v)
                    for ft in range(NFT):
                        VV.inc(v.tensor_copy(xn1_sb.ap()[:, ft * TPC + c * CH: ft * TPC + (c + 1) * CH],
                                             sq.ap()[:, ft * CH:(ft + 1) * CH]))
                em["vector"](vcp)

                def st_xn(s, c=c):
                    VV.wait(s)
                    ST.inc(s.dma_start(
                        out=xn1_p.rearrange("(ft p) t -> p ft t", p=128)[:, :, c * CH:(c + 1) * CH],
                        in_=sq.ap().rearrange("p (ft t) -> p ft t", ft=NFT)), 16)
                em["sync"](st_xn)

            for tt in range(TPC // 128):
                def tr(t, tt=tt):
                    GG.wait(t); VV.wait(t); SS.wait(t); LD.wait(t)
                    for ft in range(NFT):
                        MM.inc(t.matmul(ps_lg.ap()[:, :],
                                        xn1_sb.ap()[:, ft * TPC + tt * 128: ft * TPC + tt * 128 + 128],
                                        gw_sb.ap()[:, ft * NE:(ft + 1) * NE],
                                        start=(ft == 0), stop=(ft == NFT - 1)))
                em["tensor"](tr)

                def sr(s):
                    MM.wait(s)
                    SS.inc(s.activation(scores.ap()[:, :], ps_lg.ap()[:, :], AF.Sigmoid))
                em["scalar"](sr)

                def vr(v, tt=tt):
                    SS.wait(v); ST.wait(v)
                    VV.inc(v.tensor_tensor(sbt.ap()[:, :], scores.ap()[:, :], bias_sb.ap()[:, :], ALU.add))
                    VV.wait(v)
                    for g in range(4):
                        VV.inc(v.max(m8.ap()[:, 0:8], sbt.ap()[:, g * 8:(g + 1) * 8]))
                        VV.wait(v)
                        VV.inc(v.tensor_tensor(gsb.ap()[:, g:g + 1], m8.ap()[:, 0:1], m8.ap()[:, 1:2], ALU.add))
                        VV.wait(v)
                    VV.inc(v.max(g4.ap()[:, 0:8], gsb.ap()[:, 0:8]))
                    VV.wait(v)
                    VV.inc(v.tensor_scalar(gmask.ap()[:, :], gsb.ap()[:, 0:4], g4.ap()[:, 1:2], None, ALU.is_ge))
                    VV.wait(v)
                    VV.inc(v.tensor_scalar(ngm.ap()[:, :], gmask.ap()[:, :], -1.0, 1.0, ALU.mult, ALU.add))
                    VV.wait(v)
                    gmv = gmask.ap()[:, :].unsqueeze(2).broadcast_to([128, 4, 8])
                    ngv = ngm.ap()[:, :].unsqueeze(2).broadcast_to([128, 4, 8])
                    mev = me.ap()[:, :].rearrange("p (g e) -> p g e", g=4)
                    sbv = sbt.ap()[:, :].rearrange("p (g e) -> p g e", g=4)
                    VV.inc(v.tensor_tensor(mev, sbv, gmv, ALU.mult))
                    VV.wait(v)
                    VV.inc(v.scalar_tensor_tensor(mev, ngv, -BIGC, mev, ALU.mult, ALU.add))
                    VV.wait(v)
                    VV.inc(v.max(m8.ap()[:, 0:8], me.ap()[:, :]))
                    VV.wait(v)
                    VV.inc(v.tensor_scalar(selm.ap()[:, :], me.ap()[:, :], m8.ap()[:, 7:8], None, ALU.is_ge))
                    VV.wait(v)
                    VV.inc(v.tensor_tensor(cwt.ap()[:, :], scores.ap()[:, :], selm.ap()[:, :], ALU.mult))
                    VV.wait(v)
                    VV.inc(v.reduce_sum(rsum.ap()[:, 0:1], cwt.ap()[:, :], AX))
                    VV.wait(v)
                    VV.inc(v.reciprocal(rsum.ap()[:, 1:2], rsum.ap()[:, 0:1]))
                    VV.wait(v)
                    VV.inc(v.tensor_scalar(cwt.ap()[:, :], cwt.ap()[:, :], rsum.ap()[:, 1:2], 2.5,
                                           ALU.mult, ALU.mult))
                em["vector"](vr)

                def str_(s, tt=tt):
                    VV.wait(s)
                    ST.inc(s.dma_start(out=cw_p[tt * 128:(tt + 1) * 128, :], in_=cwt.ap()[:, :]), 16)
                em["sync"](str_)

            for (wi, ci, si, park) in ((WQ, qc_i, qs_i, Qx_p), (WK, kc_i, ks_i, kc_c)):
                def gw(g, wi=wi):
                    MM.wait(g); VV.wait(g); SS.wait(g)
                    for j in range(4):
                        GG.inc(g.dma_start(
                            out=w_sb.ap().rearrange("p (ft o) -> p ft o", ft=NFT)[:, j * 4:(j + 1) * 4, :],
                            in_=wi.rearrange("(ft p) o -> p ft o", p=128)[:, j * 4:(j + 1) * 4, :]), 16)
                em["gpsimd"](gw)
                for c in range(NCH):
                    for dqt in range(NFT):
                        def ldt(s, ci=ci, si=si, c=c, dqt=dqt):
                            VV.wait(s)
                            LD.inc(s.dma_start(out=ctab.ap()[:, :],
                                               in_=ci[dqt * 128:(dqt + 1) * 128, c * CH:(c + 1) * CH]), 16)
                            LD.inc(s.dma_start(out=stab.ap()[:, :],
                                               in_=si[dqt * 128:(dqt + 1) * 128, c * CH:(c + 1) * CH]), 16)
                        em["sync"](ldt)

                        def tp(t, c=c, dqt=dqt):
                            GG.wait(t); VV.wait(t); SS.wait(t)
                            for ft in range(NFT):
                                MM.inc(t.matmul(ps_q.ap()[:, :],
                                                w_sb.ap()[:, ft * D + dqt * 128: ft * D + dqt * 128 + 128],
                                                xn1_sb.ap()[:, ft * TPC + c * CH: ft * TPC + (c + 1) * CH],
                                                start=(ft == 0), stop=(ft == NFT - 1)))
                        em["tensor"](tp)

                        def sq_(s):
                            MM.wait(s)
                            SS.inc(s.copy(qsb.ap()[:, :], ps_q.ap()[:, :]))
                        em["scalar"](sq_)

                        def trot(t):
                            SS.wait(t); VV.wait(t)
                            MM.inc(t.matmul(ps_rot.ap()[:, :], rt_sb.ap()[:, :], qsb.ap()[:, :],
                                            start=True, stop=True))
                        em["tensor"](trot)

                        def vx(v):
                            MM.wait(v); LD.wait(v); ST.wait(v)
                            VV.inc(v.tensor_tensor(tq1.ap()[:, :], qsb.ap()[:, :], ctab.ap()[:, :], ALU.mult))
                            VV.inc(v.tensor_tensor(tq2.ap()[:, :], ps_rot.ap()[:, :], stab.ap()[:, :], ALU.mult))
                            VV.wait(v)
                            VV.inc(v.tensor_tensor(qxb.ap()[:, :], tq1.ap()[:, :], tq2.ap()[:, :], ALU.add))
                        em["vector"](vx)

                        def stq(s, park=park, c=c, dqt=dqt):
                            VV.wait(s)
                            ST.inc(s.dma_start(out=park[dqt * 128:(dqt + 1) * 128, c * CH:(c + 1) * CH],
                                               in_=qxb.ap()[:, :]), 16)
                        em["sync"](stq)

            def gv(g):
                MM.wait(g); VV.wait(g); SS.wait(g)
                for j in range(4):
                    GG.inc(g.dma_start(
                        out=w_sb.ap().rearrange("p (ft o) -> p ft o", ft=NFT)[:, j * 4:(j + 1) * 4, :],
                        in_=WV.rearrange("(ft p) o -> p ft o", p=128)[:, j * 4:(j + 1) * 4, :]), 16)
            em["gpsimd"](gv)
            for tt in range(TPC // 128):
                for dvs in range(4):
                    def tv(t, tt=tt, dvs=dvs):
                        GG.wait(t); VV.wait(t)
                        for ft in range(NFT):
                            MM.inc(t.matmul(ps_q.ap()[:, :],
                                            xn1_sb.ap()[:, ft * TPC + tt * 128: ft * TPC + tt * 128 + 128],
                                            w_sb.ap()[:, ft * D + dvs * CH: ft * D + (dvs + 1) * CH],
                                            start=(ft == 0), stop=(ft == NFT - 1)))
                    em["tensor"](tv)

                    def vv_(v):
                        MM.wait(v); ST.wait(v)
                        VV.inc(v.tensor_copy(qxb.ap()[:, :], ps_q.ap()[:, :]))
                    em["vector"](vv_)

                    def sv(s, tt=tt, dvs=dvs):
                        VV.wait(s)
                        ST.inc(s.dma_start(out=vc_c[tt * 128:(tt + 1) * 128, dvs * CH:(dvs + 1) * CH],
                                           in_=qxb.ap()[:, :]), 16)
                    em["sync"](sv)

            def gl(g):
                MM.wait(g); VV.wait(g)
                GG.inc(g.dma_start(
                    out=w_sb.ap()[:, 0:NFT * LAT].rearrange("p (ft o) -> p ft o", ft=NFT),
                    in_=fc1T.rearrange("(ft p) o -> p ft o", p=128)), 16)
            em["gpsimd"](gl)
            for latt in range(LAT // 128):
                for c in range(NCH):
                    def tl(t, latt=latt, c=c):
                        GG.wait(t); VV.wait(t)
                        for ft in range(NFT):
                            MM.inc(t.matmul(ps_q.ap()[:, :],
                                            w_sb.ap()[:, ft * LAT + latt * 128: ft * LAT + latt * 128 + 128],
                                            xn1_sb.ap()[:, ft * TPC + c * CH: ft * TPC + (c + 1) * CH],
                                            start=(ft == 0), stop=(ft == NFT - 1)))
                    em["tensor"](tl)

                    def vl(v):
                        MM.wait(v); ST.wait(v)
                        VV.inc(v.tensor_copy(qxb.ap()[:, :], ps_q.ap()[:, :]))
                    em["vector"](vl)

                    def sl_(s, latt=latt, c=c):
                        VV.wait(s)
                        ST.inc(s.dma_start(out=xlat_p[latt * 128:(latt + 1) * 128, c * CH:(c + 1) * CH],
                                           in_=qxb.ap()[:, :]), 16)
                    em["sync"](sl_)

            def cckv(g):
                ST.wait(g)
                CC.inc(g.collective_compute("AllGather", ALU.bypass, replica_groups=RGPAIR,
                                            ins=[kc_c.opt()], outs=[KG.opt()]))
                CC.inc(g.collective_compute("AllGather", ALU.bypass, replica_groups=RGPAIR,
                                            ins=[vc_c.opt()], outs=[VG.opt()]))
            em["gpsimd"](cckv)

        with contextlib.ExitStack() as px:
            px.enter_context(nc.named_scope("P6"))
            _phase_barrier(em, ctrs)
            sb = lambda *a: px.enter_context(nc.sbuf_tensor(*a))
            pt = lambda *a: px.enter_context(nc.psum_tensor(*a))
            qc_sb = sb("p6q", [128, NFT * CH], BF16)
            kt_sb = sb("p6k", [128, NFT * 128], BF16)
            vt_sb = sb("p6v", [128, D], BF16)
            slab = sb("p6slab", [128, 32 * CH], BF16)
            dtl = sb("p6dt", [128, CH], F32)
            ob = sb("p6ob", [128, CH], F32)
            ps_o = [pt(f"p6o{j}", [128, CH], F32) for j in range(8)]
            ps_at = ps_o[0]

            job = 0
            for qcx in range(4):
                nmt = NJOB[qcx]

                def ldq(s, qcx=qcx):
                    MM.wait(s); CC.wait(s)
                    LD.inc(s.dma_start(
                        out=qc_sb.ap().rearrange("p (ft t) -> p ft t", ft=NFT),
                        in_=Qx_p.rearrange("(ft p) t -> p ft t", p=128)[:, :, qcx * CH:(qcx + 1) * CH]), 16)
                em["sync"](ldq)

                for mt in range(nmt):
                    blk = mt // 16
                    col0 = (mt % 16) * 128

                    def ldk(g, blk=blk, col0=col0):
                        MM.wait(g); CC.wait(g)
                        GG.inc(g.dma_start(
                            out=kt_sb.ap().rearrange("p (ft t) -> p ft t", ft=NFT),
                            in_=KG[blk * D:(blk + 1) * D, col0:col0 + 128]
                                .rearrange("(ft p) t -> p ft t", p=128)), 16)
                    em["gpsimd"](ldk)

                    def ldd(s, job=job):
                        VV.wait(s)
                        LD.inc(s.dma_start(out=dtl.ap()[:, :], in_=dt_i[job]), 16)
                    em["sync"](ldd)

                    def ta(t, mt=mt):
                        GG.wait(t); LD.wait(t); VV.wait(t)
                        for ft in range(NFT):
                            MM.inc(t.matmul(ps_at.ap()[:, :],
                                            kt_sb.ap()[:, ft * 128:(ft + 1) * 128],
                                            qc_sb.ap()[:, ft * CH:(ft + 1) * CH],
                                            start=(ft == 0), stop=(ft == NFT - 1)))
                    em["tensor"](ta)

                    def va(v, mt=mt):
                        MM.wait(v); LD.wait(v)
                        VV.inc(v.tensor_tensor(slab.ap()[:, mt * CH:(mt + 1) * CH],
                                               ps_at.ap()[:, :], dtl.ap()[:, :], ALU.mult))
                    em["vector"](va)
                    job += 1

                for hp in range(2):
                    for mt in range(nmt):
                        blk = mt // 16
                        row0 = (mt % 16) * 128

                        def ldv(g, blk=blk, row0=row0):
                            MM.wait(g)
                            GG.inc(g.dma_start(out=vt_sb.ap()[:, :],
                                               in_=VG[blk * TPC + row0: blk * TPC + row0 + 128, :]), 16)
                        em["gpsimd"](ldv)

                        def tb(t, mt=mt, hp=hp, first=(mt == 0), last=(mt == nmt - 1)):
                            GG.wait(t); VV.wait(t); SS.wait(t)
                            for j in range(8):
                                MM.inc(t.matmul(ps_o[j].ap()[:, :],
                                                vt_sb.ap()[:, hp * 1024 + j * 128: hp * 1024 + (j + 1) * 128],
                                                slab.ap()[:, mt * CH:(mt + 1) * CH],
                                                start=first, stop=last))
                        em["tensor"](tb)

                    for j in range(8):
                        def vo(v, j=j):
                            MM.wait(v); ST.wait(v)
                            VV.inc(v.tensor_copy(ob.ap()[:, :], ps_o[j].ap()[:, :]))
                        em["vector"](vo)

                        def so(s, j=j, hp=hp, qcx=qcx):
                            VV.wait(s)
                            dv0 = hp * 1024 + j * 128
                            ST.inc(s.dma_start(out=attx_p[dv0:dv0 + 128, qcx * CH:(qcx + 1) * CH],
                                               in_=ob.ap()[:, :]), 16)
                        em["sync"](so)

        with contextlib.ExitStack() as px:
            px.enter_context(nc.named_scope("P7"))
            _phase_barrier(em, ctrs)
            sb = lambda *a: px.enter_context(nc.sbuf_tensor(*a))
            pt = lambda *a: px.enter_context(nc.psum_tensor(*a))
            HT7 = 1024
            yfm = sb("p7yfm", [128, 8 * TPC], BF16)
            cw_sb = sb("p7cw", [128, NTT * NE], F32)
            hr = sb("p7hr", [128, CH], F32)
            ident = sb("p7id", [128, 128], BF16)
            ps_a = pt("p7a", [128, CH], F32)
            ps_b = pt("p7b", [128, CH], F32)
            ps_t = pt("p7t", [128, 128], BF16)

            def g0(g):
                ST.wait(g)
                GG.inc(g.dma_start(out=ident.ap()[:, :], in_=id_d), 16)
                GG.inc(g.dma_start(out=cw_sb.ap().rearrange("p (tt e) -> p tt e", tt=NTT),
                                   in_=cw_p.rearrange("(tt p) e -> p tt e", p=128)), 16)
            em["gpsimd"](g0)

            with contextlib.ExitStack() as ex2:
                _phase_barrier(em, ctrs)
                sb2 = lambda *a: ex2.enter_context(nc.sbuf_tensor(*a))
                pt2 = lambda *a: ex2.enter_context(nc.psum_tensor(*a))
                y_acc = sb2("p7ya", [128, NTT * LAT], F32)
                xlat_sb = sb2("p7xl", [128, 8 * TPC], BF16)
                h_sb = sb2("p7h", [128, 8 * CH], BF16)
                w1_sb = [sb2(f"p7w1{b}", [128, 8 * INTER], BF16) for b in range(2)]
                w2_sb = [sb2(f"p7w2{b}", [128, 8 * LAT], BF16) for b in range(2)]
                hrb0 = sb2("p7hr0", [128, CH], F32)
                hrb = [hrb0, hrb0]
                ps_h = [pt2(f"p7ph{b}", [128, CH], F32) for b in range(2)]
                ps_y = [pt2(f"p7py{b}", [128, CH], F32) for b in range(2)]

                def g1(g):
                    for j in range(2):
                        GG.inc(g.dma_start(
                            out=xlat_sb.ap().rearrange("p (lt t) -> p lt t", lt=8)[:, j * 4:(j + 1) * 4, :],
                            in_=xlat_p.rearrange("(lt p) t -> p lt t", p=128)[:, j * 4:(j + 1) * 4, :]), 16)
                em["gpsimd"](g1)

                def vz(v):
                    VV.inc(v.memset(y_acc.ap()[:, :], 0.0))
                em["vector"](vz)

                wsnap = [0, 0]
                musers = [0, 0]
                hgate = [0, 0]
                hrsq = [0, 0]
                ygate = [0, 0]
                ylast = 0
                vsq = 0

                def ldw(g, e, b, gate):
                    MM.wait_at(g, gate)
                    GG.inc(g.dma_start(out=w1_sb[b].ap().rearrange("p (lt i) -> p lt i", lt=8),
                                       in_=w1T[e].rearrange("(lt p) i -> p lt i", p=128)), 16)
                    GG.inc(g.dma_start(out=w2_sb[b].ap().rearrange("p (it l) -> p it l", it=8),
                                       in_=w2T[e].rearrange("(it p) l -> p it l", p=128)), 16)

                em["gpsimd"](lambda g: ldw(g, 0, 0, 0))
                wsnap[0] = GG.n

                for e in range(NE):
                    wb = e % 2
                    if e + 1 < NE:
                        em["gpsimd"](lambda g, e=e, wb=wb, gate=musers[1 - wb]: ldw(g, e + 1, 1 - wb, gate))
                        wsnap[1 - wb] = GG.n
                    for tc in range(NCH):
                        for it in range(8):
                            hb = it % 2

                            def th(t, it=it, tc=tc, wb=wb, hb=hb,
                                   wgate=wsnap[wb], pgate=hgate[hb]):
                                GG.wait_at(t, wgate)
                                SS.wait_at(t, pgate)
                                for lt in range(8):
                                    MM.inc(t.matmul(ps_h[hb].ap()[:, :],
                                                    w1_sb[wb].ap()[:, lt * INTER + it * 128: lt * INTER + it * 128 + 128],
                                                    xlat_sb.ap()[:, lt * TPC + tc * CH: lt * TPC + (tc + 1) * CH],
                                                    start=(lt == 0), stop=(lt == 7)))
                            em["tensor"](th)
                            musers[wb] = MM.n

                            def sh(s, hb=hb, snap=MM.n, vgate=hrsq[hb]):
                                MM.wait_at(s, snap)
                                VV.wait_at(s, vgate)
                                SS.inc(s.activation(hrb[hb].ap()[:, :], ps_h[hb].ap()[:, :], AF.Relu))
                            em["scalar"](sh)
                            hgate[hb] = SS.n

                            def vh(v, it=it, hb=hb, snap=SS.n, mgate=ylast):
                                SS.wait_at(v, snap)
                                MM.wait_at(v, mgate)
                                VV.inc(v.tensor_tensor(h_sb.ap()[:, it * CH:(it + 1) * CH],
                                                       hrb[hb].ap()[:, :], hrb[hb].ap()[:, :], ALU.mult))
                            em["vector"](vh)
                            hrsq[0] = hrsq[1] = VV.n
                        vsq = VV.n
                        for tl in range(4):
                            tt = tc * 4 + tl
                            for ls in range(2):
                                yb = (tl * 2 + ls) % 2

                                def ty(t, tl=tl, ls=ls, wb=wb, yb=yb,
                                       vgate=vsq, pgate=ygate[yb], wgate=wsnap[wb]):
                                    VV.wait_at(t, pgate)
                                    VV.wait_at(t, vgate)
                                    GG.wait_at(t, wgate)
                                    for it in range(8):
                                        MM.inc(t.matmul(ps_y[yb].ap()[:, :],
                                                        h_sb.ap()[:, it * CH + tl * 128: it * CH + tl * 128 + 128],
                                                        w2_sb[wb].ap()[:, it * LAT + ls * CH: it * LAT + (ls + 1) * CH],
                                                        start=(it == 0), stop=(it == 7)))
                                em["tensor"](ty)
                                musers[wb] = MM.n

                                def vy(v, tt=tt, ls=ls, e=e, yb=yb, snap=MM.n):
                                    MM.wait_at(v, snap)
                                    dst = y_acc.ap()[:, tt * LAT + ls * CH: tt * LAT + (ls + 1) * CH]
                                    VV.inc(v.scalar_tensor_tensor(dst, ps_y[yb].ap()[:, :],
                                                                  cw_sb.ap()[:, tt * NE + e: tt * NE + e + 1],
                                                                  dst, ALU.mult, ALU.add))
                                em["vector"](vy)
                                ygate[yb] = VV.n
                        ylast = MM.n

                _phase_barrier(em, ctrs)
                for tt in range(NTT):
                    def vcast(v, tt=tt):
                        VV.inc(v.tensor_copy(xlat_sb.ap()[:, tt * LAT:(tt + 1) * LAT],
                                             y_acc.ap()[:, tt * LAT:(tt + 1) * LAT]))
                    em["vector"](vcast)
                vc_done = VV.n
                tsn = 0
                for tt in range(NTT):
                    for lt in range(8):
                        def tt_(t, tt=tt, lt=lt, gate=tsn):
                            VV.wait_at(t, vc_done)
                            VV.wait_at(t, gate)
                            MM.inc(t.transpose(ps_t.ap()[:, :],
                                               xlat_sb.ap()[:, tt * LAT + lt * 128: tt * LAT + lt * 128 + 128],
                                               ident.ap()[:, :]))
                        em["tensor"](tt_)

                        def vtr(v, tt=tt, lt=lt, snap=MM.n):
                            MM.wait_at(v, snap)
                            VV.inc(v.tensor_copy(yfm.ap()[:, lt * TPC + tt * 128: lt * TPC + tt * 128 + 128],
                                                 ps_t.ap()[:, :]))
                        em["vector"](vtr)
                        tsn = VV.n

            with contextlib.ExitStack() as ex3:
                _phase_barrier(em, ctrs)
                sb3 = lambda *a: ex3.enter_context(nc.sbuf_tensor(*a))
                acc = sb3("p7acc", [128, NFT * HT7], F32)
                shg = sb3("p7shg", [128, 16 * CH], BF16)
                fc2_sb = sb3("p7fc2", [128, 8 * D], BF16)
                xn1_sl = sb3("p7xsl", [128, NFT * CH], BF16)
                su_t = sb3("p7su", [128, NFT * 128], BF16)
                sd_t = sb3("p7sd", [128, 16 * 128], BF16)
                atile = sb3("p7at", [128, HT7], F32)

                def g2(g):
                    VV.wait(g); MM.wait(g)
                    for j in range(4):
                        GG.inc(g.dma_start(
                            out=fc2_sb.ap().rearrange("p (lt o) -> p lt o", lt=8)[:, j * 2:(j + 1) * 2, :],
                            in_=fc2T.rearrange("(lt p) o -> p lt o", p=128)[:, j * 2:(j + 1) * 2, :]), 16)
                em["gpsimd"](g2)

                for hf in range(2):
                    def lda(s, hf=hf):
                        ST.wait(s); VV.wait(s)
                        LD.inc(s.dma_start(
                            out=acc.ap().rearrange("p (ft t) -> p ft t", ft=NFT),
                            in_=x1_p.rearrange("(ft p) t -> p ft t", p=128)[:, :, hf * HT7:(hf + 1) * HT7]), 16)
                    em["sync"](lda)
                    for ft in range(NFT):
                        def lat_(s, hf=hf, ft=ft):
                            VV.wait(s)
                            LD.inc(s.dma_start(out=atile.ap()[:, :],
                                               in_=attx_p[ft * 128:(ft + 1) * 128, hf * HT7:(hf + 1) * HT7]), 16)
                        em["sync"](lat_)

                        def vat(v, ft=ft):
                            LD.wait(v)
                            sl_ = slice(ft * HT7, (ft + 1) * HT7)
                            VV.inc(v.tensor_tensor(acc.ap()[:, sl_], acc.ap()[:, sl_], atile.ap()[:, :], ALU.add))
                        em["vector"](vat)

                    for tc in range(2):
                        tok0 = hf * HT7 + tc * CH

                        def ldx(s, tok0=tok0):
                            MM.wait(s)
                            LD.inc(s.dma_start(
                                out=xn1_sl.ap().rearrange("p (ft t) -> p ft t", ft=NFT),
                                in_=xn1_p.rearrange("(ft p) t -> p ft t", p=128)[:, :, tok0:tok0 + CH]), 16)
                        em["sync"](ldx)

                        for s2 in range(2):
                            for ht in range(16):
                                hid0 = s2 * 2048 + ht * 128

                                def lsu(g, hid0=hid0):
                                    MM.wait(g)
                                    GG.inc(g.dma_start(
                                        out=su_t.ap().rearrange("p (ft o) -> p ft o", ft=NFT),
                                        in_=suT.rearrange("(ft p) h -> p ft h", p=128)[:, :, hid0:hid0 + 128]), 16)
                                em["gpsimd"](lsu)

                                def tsu(t, ht=ht):
                                    GG.wait(t); LD.wait(t); VV.wait(t); SS.wait(t)
                                    for ft in range(NFT):
                                        MM.inc(t.matmul(ps_a.ap()[:, :],
                                                        su_t.ap()[:, ft * 128:(ft + 1) * 128],
                                                        xn1_sl.ap()[:, ft * CH:(ft + 1) * CH],
                                                        start=(ft == 0), stop=(ft == NFT - 1)))
                                em["tensor"](tsu)

                                def ssu(s):
                                    MM.wait(s)
                                    SS.inc(s.activation(hr.ap()[:, :], ps_a.ap()[:, :], AF.Relu))
                                em["scalar"](ssu)

                                def vsu(v, ht=ht):
                                    SS.wait(v)
                                    VV.inc(v.tensor_tensor(shg.ap()[:, ht * CH:(ht + 1) * CH],
                                                           hr.ap()[:, :], hr.ap()[:, :], ALU.mult))
                                em["vector"](vsu)

                            for dot in range(16):
                                def lsd(g, s2=s2, dot=dot):
                                    MM.wait(g)
                                    GG.inc(g.dma_start(
                                        out=sd_t.ap().rearrange("p (ht o) -> p ht o", ht=16),
                                        in_=sdT[s2 * 2048:(s2 + 1) * 2048, dot * 128:(dot + 1) * 128]
                                            .rearrange("(ht p) o -> p ht o", p=128)), 16)
                                em["gpsimd"](lsd)

                                def tsd(t, dot=dot):
                                    GG.wait(t); VV.wait(t)
                                    for ht in range(16):
                                        MM.inc(t.matmul(ps_b.ap()[:, :],
                                                        sd_t.ap()[:, ht * 128:(ht + 1) * 128],
                                                        shg.ap()[:, ht * CH:(ht + 1) * CH],
                                                        start=(ht == 0), stop=(ht == 15)))
                                em["tensor"](tsd)

                                def vsd(v, dot=dot, tc=tc):
                                    MM.wait(v); LD.wait(v)
                                    sl_ = slice(dot * HT7 + tc * CH, dot * HT7 + tc * CH + CH)
                                    VV.inc(v.tensor_tensor(acc.ap()[:, sl_], acc.ap()[:, sl_],
                                                           ps_b.ap()[:, :], ALU.add))
                                em["vector"](vsd)

                        for dot in range(16):
                            def tr2(t, dot=dot, tok0=tok0):
                                GG.wait(t); VV.wait(t); SS.wait(t)
                                for lt in range(8):
                                    MM.inc(t.matmul(ps_a.ap()[:, :],
                                                    fc2_sb.ap()[:, lt * D + dot * 128: lt * D + dot * 128 + 128],
                                                    yfm.ap()[:, lt * TPC + tok0: lt * TPC + tok0 + CH],
                                                    start=(lt == 0), stop=(lt == 7)))
                            em["tensor"](tr2)

                            def vr2(v, dot=dot, tc=tc):
                                MM.wait(v)
                                sl_ = slice(dot * HT7 + tc * CH, dot * HT7 + tc * CH + CH)
                                VV.inc(v.tensor_tensor(acc.ap()[:, sl_], acc.ap()[:, sl_],
                                                       ps_a.ap()[:, :], ALU.add))
                            em["vector"](vr2)

                    def sta(s, hf=hf):
                        VV.wait(s)
                        ST.inc(s.dma_start(
                            out=o_x2.rearrange("(ft p) t -> p ft t", p=128)[:, :, hf * HT7:(hf + 1) * HT7],
                            in_=acc.ap().rearrange("p (ft t) -> p ft t", ft=NFT)), 16)
                    em["sync"](sta)

                def fin(s):
                    ST.wait(s)
                em["sync"](fin)
    return nc


def kernel_fused(**inputs):
    p = _prep_basic(inputs)
    tabs = _xpos_tables()
    dts = [_dtiles_fused(h) for h in range(2)]
    iscs = [np.zeros((128, 1), np.float32), np.ones((128, 1), np.float32)]
    in_maps = []
    for i in range(NCORE):
        h = i % 2
        qc, qs, kc, ks = tabs[h]
        in_maps.append({
            "xfm": p["xfm"][i], "WqT": p["WqT"], "WkvT": p["WkvT"], "WoT": p["WoT"],
            "WgT": p["WgT"], "WuT": p["WuT"], "WdT": p["WdT"],
            "WQ": p["WQ"], "WK": p["WK"], "WV": p["WV"],
            "gwT": p["gwT"], "biasT": p["biasT"], "fc1T": p["fc1T"], "fc2T": p["fc2T"],
            "w1T": p["w1T"], "w2T": p["w2T"], "suT": p["suT"], "sdT": p["sdT"],
            "qcos": qc, "qsin": qs, "kcos": kc, "ksin": ks,
            "dtiles": dts[h], "isc": iscs[h],
        })
    r = _run("FUSED", build_FUSED, in_maps)
    out = np.empty((B, S, D), dtype=np.float32)
    for i in range(NCORE):
        b, p0 = _tok_owner(i)
        out[b, p0:p0 + TPC, :] = np.asarray(r[i]["x2"], dtype=np.float32).T
    return out, r
